# version 07
# baseline (speedup 1.0000x reference)
import numpy as np
import concourse.bass as bass
import concourse.mybir as mybir

F32 = mybir.dt.float32
BF16 = mybir.dt.bfloat16
U16 = mybir.dt.uint16
U32 = mybir.dt.uint32
AF = mybir.ActivationFunctionType
ALU = mybir.AluOpType
AX = mybir.AxisListType

B = 2
S = 2048
D = 1024
T = B * S
NC = 8
H, N8 = 4, 8
DIN = 2048
DST = 64
E, TOPK, CAP = 64, 6, 512
EPERC = E // NC
I_EXP = 704
SI = 1408
EPS = 1e-5
NDT = D // 128

ENGINES = ("sync", "vector", "scalar", "tensor", "gpsimd")

_BREG = {}

def bounds_reg(E):
    k = id(E)
    if k not in _BREG:
        r = E.alloc_register()
        E.reg_mov(r, 32767)
        _BREG[k] = r
    return _BREG[k]


NJSLOTS = 8
DRAIN_PHASES = {"init", "aa"}


class Tok:
    __slots__ = ("eng", "is_dma", "seq", "dma_seq", "cc_seq", "nojoin",
                 "nj_slot", "nj_cnt")
    def __init__(self, eng, is_dma, seq):
        self.eng, self.is_dma, self.seq = eng, is_dma, seq
        self.dma_seq = None
        self.cc_seq = None
        self.nojoin = False
        self.nj_slot = None
        self.nj_cnt = None


class Sched:
    def __init__(self):
        self.items = []
        self.cc = {e: 0 for e in ENGINES}
        self.cd = {e: 0 for e in ENGINES}
        self.ccc = {}
        self.cdn = {e: 0 for e in ENGINES}
        self.last = {e: None for e in ENGINES}
        self._stage = []
        self._phase = "init"

    def set_phase(self, p):
        self._phase = p

    def emit(self, eng, fn, deps=(), is_dma=False, name="", is_cc=False,
             join=True):
        if is_dma and not join:
            i = self.cdn[eng]
            self.cdn[eng] += 1
            tok = Tok(eng, False, self.cc[eng])
            tok.nojoin = True
            tok.nj_slot = i % NJSLOTS
            tok.nj_cnt = i // NJSLOTS + 1
            deps = tuple(d for d in deps if d is not None) + tuple(self._stage)
            self.items.append((eng, fn, deps, "dmanj", name, tok, self._phase))
            self.last[eng] = tok
            return tok
        if is_dma:
            self.cd[eng] += 1
            self.cc[eng] += 1
            tok = Tok(eng, False, self.cc[eng])
            tok.dma_seq = self.cd[eng]
        elif is_cc:
            self.ccc[eng] = self.ccc.get(eng, 0) + 1
            self.cc[eng] += 1
            tok = Tok(eng, False, self.cc[eng])
            tok.dma_seq = None
            tok.cc_seq = self.ccc[eng]
        else:
            self.cc[eng] += 1
            tok = Tok(eng, False, self.cc[eng])
            tok.dma_seq = None
        deps = tuple(d for d in deps if d is not None) + tuple(self._stage)
        self.items.append((eng, fn, deps, ("dma" if is_dma else "cc" if is_cc else ""),
                           name, tok, self._phase))
        self.last[eng] = tok
        return tok


    def dma(self, eng, out, in_, deps=(), name="", slow=False, join=True):
        if slow:
            return self.emit(eng, lambda E, o=out, i=in_: E.dma_start(
                out=o, in_=i, allow_slow_non_contiguous=True),
                deps=deps, is_dma=True, name=name, join=join)
        return self.emit(eng, lambda E, o=out, i=in_: E.dma_start(out=o, in_=i),
                         deps=deps, is_dma=True, name=name, join=join)

    def stage(self):
        self._stage = [t for t in self.last.values() if t is not None]

    def build(self, nc):
        sem_c = {e: nc.semaphore(f"sc_{e}").__enter__() for e in ENGINES}
        sem_d = {e: nc.semaphore(f"sd_{e}").__enter__() for e in ENGINES}
        sem_g = nc.semaphore("sg_cc").__enter__()
        sem_n = {}
        for e in ENGINES:
            if self.cdn[e]:
                sem_n[e] = [nc.semaphore(f"sn_{e}{j}").__enter__()
                            for j in range(NJSLOTS)]

        def section(ename):
            def run(E):
                floor = {}
                floor_d = {}
                for (eng, fn, deps, kind, name, tok, phase) in self.items:
                    if eng != ename:
                        continue
                    w = {}
                    wd = {}
                    for d in deps:
                        if d.nojoin:
                            wd[(d.eng, d.nj_slot)] = max(
                                wd.get((d.eng, d.nj_slot), 0), d.nj_cnt)
                            continue
                        w[d.eng] = max(w.get(d.eng, 0), d.seq)
                    for deng, seq in w.items():
                        if seq <= floor.get(deng, 0):
                            continue
                        floor[deng] = seq
                        E.wait_ge(sem_c[deng], seq)
                    for dk, cnt in wd.items():
                        if cnt <= floor_d.get(dk, 0):
                            continue
                        floor_d[dk] = cnt
                        E.wait_ge(sem_n[dk[0]][dk[1]], 16 * cnt)
                    with nc.named_scope(phase):
                        if kind == "dmanj":
                            if tok.nj_cnt > 1:
                                k0 = (ename, tok.nj_slot)
                                if floor_d.get(k0, 0) < tok.nj_cnt - 1:
                                    floor_d[k0] = tok.nj_cnt - 1
                                    E.wait_ge(sem_n[ename][tok.nj_slot],
                                              16 * (tok.nj_cnt - 1))
                        inst = fn(E)
                        if kind == "dmanj":
                            inst.then_inc(sem_n[ename][tok.nj_slot], 16)
                        elif kind == "dma":
                            inst.then_inc(sem_d[ename], 16)
                            E.wait_ge(sem_d[ename], 16 * tok.dma_seq)
                            E.nop().then_inc(sem_c[ename], 1)
                        elif kind == "cc":
                            inst.then_inc(sem_g, 1)
                            E.wait_ge(sem_g, tok.cc_seq)
                            E.nop().then_inc(sem_c[ename], 1)
                        else:
                            if inst is None:
                                E.nop().then_inc(sem_c[ename], 1)
                            elif phase in DRAIN_PHASES:
                                E.maybe_drain_then_inc((sem_c[ename], 1))
                            else:
                                inst.then_inc(sem_c[ename], 1)
            return run
        return section, sem_c, sem_d


class Ctx:
    def __init__(self, nc, sch):
        self.nc = nc
        self.sch = sch
        self.sb = {}
        self.ps = {}
        self.dram = {}
        self._exits = []
        self._scopes = []

    def push_scope(self):
        self._scopes.append([])

    def pop_scope(self):
        for name, h in reversed(self._scopes.pop()):
            try:
                h.__exit__(None, None, None)
            except Exception:
                pass

    def init_arena(self, cols_f32):
        self.arena = self.sbuf("ARENA", (128, cols_f32), F32)
        self._bump = 0
        self._top = cols_f32
        self._arena_cols = cols_f32

    def scratch_reset(self):
        self._bump = 0

    def scr_top(self, name, shape, dtype=F32):
        p, n = shape
        if dtype in (BF16, U16, U32):
            units = (n + 1) // 2 if dtype in (BF16, U16) else n
        else:
            units = n
        units = (units + 15) & ~15
        self._top -= units
        assert self._bump <= self._top, (name, self._bump, self._top)
        view = self.arena[0:p, self._top:self._top + units]
        if dtype in (BF16, U16, U32):
            view = view.bitcast(dtype)
        view = view[0:p, 0:n]
        self.sb[name] = view
        return view

    def top_mark(self):
        return self._top

    def top_reset(self, mark):
        self._top = mark

    def mark(self):
        return self._bump

    def reset_to(self, mark):
        self._bump = mark

    def scr(self, name, shape, dtype=F32):
        p, n = shape
        if dtype in (BF16, U16):
            units = (n + 1) // 2
        else:
            units = n
        units = (units + 15) & ~15
        off = self._bump
        self._bump += units
        assert self._bump <= self._top, (name, self._bump, self._top)
        view = self.arena[0:p, off:off + units]
        if dtype in (BF16, U16, U32):
            view = view.bitcast(dtype)
        view = view[0:p, 0:n]
        self.sb[name] = view
        return view

    def sbuf(self, name, shape, dtype=F32):
        h = self.nc.sbuf_tensor(name, list(shape), dtype)
        t = h.__enter__()
        if self._scopes:
            self._scopes[-1].append((name, h))
        else:
            self._exits.append(h)
        self.sb[name] = t
        return t

    def psum(self, name, shape, dtype=F32):
        h = self.nc.psum_tensor(list(shape), dtype)
        t = h.__enter__()
        self._exits.append(h)
        self.ps[name] = t
        return t

    def dram_in(self, name, shape, dtype=F32):
        t = self.nc.dram_tensor(name, list(shape), dtype, kind="ExternalInput").ap()
        self.dram[name] = t
        return t

    def dram_out(self, name, shape, dtype=F32):
        t = self.nc.dram_tensor(name, list(shape), dtype, kind="ExternalOutput").ap()
        self.dram[name] = t
        return t

    def dram_tmp(self, name, shape, dtype=F32, shared=False):
        kw = dict(kind="Internal")
        if shared:
            kw["addr_space"] = "Shared"
        t = self.nc.dram_tensor(name, list(shape), dtype, **kw).ap()
        self.dram[name] = t
        return t


def host_consts():
    c = {}
    c["ident128"] = np.eye(128, dtype=np.float32)
    c["ones_col"] = np.ones((128, 1), np.float32)
    c["ones_row"] = np.ones((1, 128), np.float32)
    lt = (np.arange(8)[None, :] < np.arange(8)[:, None]).astype(np.float32)
    c["lt8"] = np.broadcast_to(lt.reshape(1, 64), (128, 64)).copy()
    c["ident8row"] = np.eye(8, dtype=np.float32).reshape(1, 64)
    c["ident8rep"] = np.broadcast_to(np.eye(8, dtype=np.float32).reshape(1, 64), (128, 64)).copy()
    c["ustri128"] = (np.arange(128)[:, None] < np.arange(128)[None, :]).astype(np.float32)
    c["erow"] = np.broadcast_to(np.arange(64, dtype=np.float32).reshape(1, 64), (128, 64)).copy()
    lt6 = (np.arange(6)[None, :] < np.arange(6)[:, None]).astype(np.float32)
    c["lt6"] = np.broadcast_to(lt6.reshape(1, 36), (128, 36)).copy()
    return c


def alloc_common(ctx):
    ctx.sbuf("ident128", (128, 128), F32)
    ctx.sbuf("ones_col", (128, 1), F32)
    ctx.sbuf("ones_row", (1, 128), F32)
    ctx.sbuf("lt8", (128, 64), F32)
    ctx.sbuf("ustri128", (128, 128), F32)
    ctx.sbuf("erow", (128, 64), F32)
    ctx.sbuf("lt6", (128, 36), F32)
    ctx.sbuf("eps_col", (128, 1), F32)
    ctx.sbuf("ident128b", (128, 128), BF16)
    ctx.psum("psA", (128, 512))
    ctx.psum("psB", (128, 512))
    ctx.psum("psC", (128, 512))
    ctx.psum("psD", (128, 512))
    ctx.psum("psE", (128, 2048))


def alloc_stats_top(ctx):
    sr = ctx.scr_top("stat_rows", (33, 2048), F32)
    ctx.sb["mu_row"] = sr[0:1, :]
    ctx.sb["var_row"] = sr[32:33, :]
    ctx.sb["r_row"] = sr[32:33, :]
    ctx.sb["negmu_row"] = sr[0:1, :]


def load_consts(ctx, deps=()):
    sch = ctx.sch
    toks = []
    for nme in ("ident128", "ones_col", "ones_row", "lt8", "ustri128", "erow", "lt6"):
        toks.append(sch.dma("sync", ctx.sb[nme][:], ctx.dram[f"c_{nme}"][:], deps=deps))
    toks.append(sch.emit("gpsimd", lambda E: E.memset(ctx.sb["eps_col"][:, :], EPS),
                         deps=deps, name="eps"))
    toks.append(ctx.sch.emit("vector", lambda E: E.tensor_copy(
        ctx.sb["ident128b"][:, :], ctx.sb["ident128"][:, :]), deps=[toks[0]], name="id_b"))
    return toks



def emit_ln_stats(ctx, src_fn, deps, tag):
    nc, sch = ctx.nc, ctx.sch
    ones = ctx.sb["ones_col"]
    psE = ctx.ps["psE"]
    sqs = [ctx.sb["sq0"], ctx.sb["sq1"]]
    sq_free = [None, None]
    mm_toks = []
    for dt in range(NDT):
        sq = sqs[dt % 2]
        dep_sq = [sq_free[dt % 2]] if sq_free[dt % 2] else []
        t_sq = sch.emit("scalar", lambda E, dt=dt, sq=sq: E.activation(
            out=sq[:], in_=src_fn(dt), func=AF.Square), deps=list(deps) + dep_sq,
            name=f"{tag}_sq{dt}")
        last = None
        for c0 in range(0, S, 512):
            m1 = sch.emit("tensor", lambda E, dt=dt, c0=c0: E.matmul(
                psE[0:1, c0:c0 + 512], ones[:, 0:1], src_fn(dt)[:, c0:c0 + 512],
                start=(dt == 0), stop=(dt == NDT - 1)), deps=deps, name=f"{tag}_s")
            m2 = sch.emit("tensor", lambda E, sq=sq, c0=c0, dt=dt: E.matmul(
                psE[32:33, c0:c0 + 512], ones[:, 0:1], sq[:, c0:c0 + 512],
                start=(dt == 0), stop=(dt == NDT - 1)), deps=[t_sq], name=f"{tag}_q")
            last = m2
            if dt == NDT - 1:
                mm_toks += [m1, m2]
        sq_free[dt % 2] = last
    mu, var, r, negmu = (ctx.sb["mu_row"], ctx.sb["var_row"], ctx.sb["r_row"],
                         ctx.sb["negmu_row"])
    t1 = sch.emit("vector", lambda E: E.tensor_scalar(mu[0:1, :], psE[0:1, :], 1.0 / D,
                  None, ALU.mult), deps=mm_toks, name=f"{tag}_mu")
    t2 = sch.emit("vector", lambda E: E.tensor_tensor(var[0:1, :], mu[0:1, :], mu[0:1, :],
                  ALU.mult), deps=[t1], name=f"{tag}_mu2")
    t3 = sch.emit("vector", lambda E: E.scalar_tensor_tensor(var[0:1, :], psE[32:33, :],
                  1.0 / D, var[0:1, :], ALU.mult, ALU.subtract), deps=[t2] + mm_toks,
                  name=f"{tag}_var")
    t4 = sch.emit("scalar", lambda E: E.activation(out=r[0:1, :], in_=var[0:1, :],
                  func=AF.Sqrt, bias=sb["eps_col"][0:1, 0:1]), deps=[t3], name=f"{tag}_sqrt")
    t5 = sch.emit("vector", lambda E: E.reciprocal(r[0:1, :], r[0:1, :]), deps=[t4],
                  name=f"{tag}_rcp")
    t6 = sch.emit("vector", lambda E: E.tensor_scalar(negmu[0:1, :], mu[0:1, :], -1.0,
                  None, ALU.mult), deps=[t1], name=f"{tag}_negmu")
    return [t5, t6]


def emit_bcast_plane(ctx, dst_sb, row_ap, deps, tag):
    nc, sch = ctx.nc, ctx.sch
    onesr = ctx.sb["ones_row"]
    psE = ctx.ps["psE"]
    toks = []
    for c0 in range(0, 2048, 512):
        toks.append(sch.emit("tensor", lambda E, c0=c0: E.matmul(
            psE[:, c0:c0 + 512], onesr[0:1, :], row_ap[0:1, c0:c0 + 512],
            start=True, stop=True), deps=deps, name=f"{tag}_bc"))
    ev = sch.emit("vector", lambda E: E.tensor_copy(dst_sb[:, :], psE[:, :]),
                  deps=toks, name=f"{tag}_ev")
    return [ev]



def emit_attn_a_weights(ctx, deps):
    nc, sch = ctx.nc, ctx.sch
    d = ctx.dram
    sb = ctx.sb
    wst = sb["wstage"]
    lnw, lnb = sb["lnw_col"], sb["lnb_col"]
    psA, psB = ctx.ps["psA"], ctx.ps["psB"]
    ob1 = sch.emit("gpsimd", lambda E: E.memset(sb["ones_col_b"][:, :], 1.0), deps=deps,
                   name="onesb")
    tok_ln = [sch.dma("sync", lnw[:, :], d["L0_attn_a_ln_w"].rearrange("(a p) -> p a", p=128), deps=deps, slow=True),
              sch.dma("sync", lnb[:, :], d["L0_attn_a_ln_b"].rearrange("(a p) -> p a", p=128), deps=deps, slow=True)]
    const_mm = []
    prev_users = []
    for dt in range(NDT):
        tks = [sch.dma("sync", wst[:, 0:32], d["L0_attn_a_sign_W"][dt * 128:(dt + 1) * 128, :], deps=prev_users),
               sch.dma("scalar", wst[:, 32:288], d["L0_attn_a_perm_W"][dt * 128:(dt + 1) * 128, :], deps=prev_users),
               sch.dma("gpsimd", wst[:, 288:320], d["L0_attn_a_gate_W"][dt * 128:(dt + 1) * 128, :], deps=prev_users),
               sch.dma("gpsimd", wst[:, 320:352], d["L0_attn_a_in_W"][dt * 128:(dt + 1) * 128, :], deps=prev_users)]
        tks += tok_ln
        psE = ctx.ps["psE"]
        psC, psD = ctx.ps["psC"], ctx.ps["psD"]
        c1 = sch.emit("tensor", lambda E, dt=dt: E.matmul(
            psA[0:128, 0:1], wst[:, 32:160], lnb[:, dt:dt + 1],
            start=(dt == 0), stop=(dt == NDT - 1)), deps=tks, name="aw_c0")
        c2 = sch.emit("tensor", lambda E, dt=dt: E.matmul(
            psB[0:128, 0:1], wst[:, 160:288], lnb[:, dt:dt + 1],
            start=(dt == 0), stop=(dt == NDT - 1)), deps=tks, name="aw_c1")
        c3 = sch.emit("tensor", lambda E, dt=dt: E.matmul(
            psC[0:32, 0:1], wst[:, 0:32], lnb[:, dt:dt + 1],
            start=(dt == 0), stop=(dt == NDT - 1)), deps=tks, name="aw_c2")
        c4 = sch.emit("tensor", lambda E, dt=dt: E.matmul(
            psD[32:64, 0:1], wst[:, 288:320], lnb[:, dt:dt + 1],
            start=(dt == 0), stop=(dt == NDT - 1)), deps=tks, name="aw_c3")
        c5 = sch.emit("tensor", lambda E, dt=dt: E.matmul(
            psE[64:96, 0:1], wst[:, 320:352], lnb[:, dt:dt + 1],
            start=(dt == 0), stop=(dt == NDT - 1)), deps=tks, name="aw_c4")
        const_mm = [c1, c2, c3, c4, c5]
        t_p = sch.emit("vector", lambda E, dt=dt: E.tensor_scalar(
            sb["aw_perm"][:, dt * 256:(dt + 1) * 256], wst[:, 32:288],
            lnw[:, dt:dt + 1], None, ALU.mult), deps=tks + const_mm, name="aw_cp")
        t_s = sch.emit("vector", lambda E, dt=dt: E.tensor_scalar(
            sb["aw_sgi"][:, dt * 96: dt * 96 + 32], wst[:, 0:32],
            lnw[:, dt:dt + 1], None, ALU.mult), deps=tks + const_mm, name="aw_cs")
        t_g = sch.emit("vector", lambda E, dt=dt: E.tensor_scalar(
            sb["aw_sgi"][:, dt * 96 + 32: dt * 96 + 64], wst[:, 288:320],
            lnw[:, dt:dt + 1], None, ALU.mult), deps=tks + const_mm, name="aw_cg")
        t_i = sch.emit("vector", lambda E, dt=dt: E.tensor_scalar(
            sb["aw_sgi"][:, dt * 96 + 64: dt * 96 + 96], wst[:, 320:352],
            lnw[:, dt:dt + 1], None, ALU.mult), deps=tks + const_mm, name="aw_ci")
        prev_users = [t_p, t_s, t_g, t_i]
    bias_tiles = sb["a_bias"]
    tz = sch.emit("gpsimd", lambda E: E.memset(bias_tiles[:, :], 0.0), deps=deps, name="bz")
    deps2 = list(deps) + [tz]
    tb = [sch.dma("sync", bias_tiles[0:128, 0:1], d["L0_attn_a_perm_b"][0:128].unsqueeze(1), deps=deps2, slow=True),
          sch.dma("sync", bias_tiles[0:128, 1:2], d["L0_attn_a_perm_b"][128:256].unsqueeze(1), deps=deps2, slow=True),
          sch.dma("scalar", bias_tiles[0:32, 2:3], d["L0_attn_a_sign_b"][:].unsqueeze(1), deps=deps2, slow=True),
          sch.dma("scalar", bias_tiles[32:64, 2:3], d["L0_attn_a_gate_b"][:].unsqueeze(1), deps=deps2, slow=True),
          sch.dma("scalar", bias_tiles[64:96, 2:3], d["L0_attn_a_in_b"][:].unsqueeze(1), deps=deps2, slow=True)]
    tz3 = sch.emit("gpsimd", lambda E: E.memset(sb["a_const"][:, :], 0.0), deps=deps, name="cz")
    tca = sch.emit("vector", lambda E: E.tensor_tensor(sb["a_const"][:, 0:1],
                   psA[:, 0:1], bias_tiles[:, 0:1], ALU.add), deps=const_mm + tb + [tz3], name="ac0")
    tcb = sch.emit("vector", lambda E: E.tensor_tensor(sb["a_const"][:, 1:2],
                   psB[:, 0:1], bias_tiles[:, 1:2], ALU.add), deps=const_mm + tb + [tz3], name="ac1")
    tcc = sch.emit("vector", lambda E: E.tensor_tensor(sb["a_const"][0:32, 2:3],
                   ctx.ps["psC"][0:32, 0:1], bias_tiles[0:32, 2:3], ALU.add),
                   deps=const_mm + tb + [tz3], name="ac2")
    tcd = sch.emit("vector", lambda E: E.tensor_tensor(sb["a_const"][32:64, 2:3],
                   ctx.ps["psD"][32:64, 0:1], bias_tiles[32:64, 2:3], ALU.add),
                   deps=const_mm + tb + [tz3], name="ac3")
    tce = sch.emit("vector", lambda E: E.tensor_tensor(sb["a_const"][64:96, 2:3],
                   ctx.ps["psE"][64:96, 0:1], bias_tiles[64:96, 2:3], ALU.add),
                   deps=const_mm + tb + [tz3], name="ac4")
    tc = sch.emit("vector", lambda E: None, deps=[tca, tcb, tcc, tcd, tce], name="acj")
    tcx = tc
    onesb = sb["ones_col_b"]
    wtw_toks = []
    for dt in range(NDT):
        w1 = sch.emit("tensor", lambda E, dt=dt: E.matmul(
            psA[0:128, 2:3], sb["aw_perm"][:, dt * 256: dt * 256 + 128], onesb[:, 0:1],
            start=(dt == 0), stop=(dt == NDT - 1)), deps=prev_users + [tcx, ob1], name="wtw0")
        w2 = sch.emit("tensor", lambda E, dt=dt: E.matmul(
            psB[0:128, 2:3], sb["aw_perm"][:, dt * 256 + 128: dt * 256 + 256], onesb[:, 0:1],
            start=(dt == 0), stop=(dt == NDT - 1)), deps=prev_users + [tcx], name="wtw1")
        w3 = sch.emit("tensor", lambda E, dt=dt: E.matmul(
            ctx.ps["psC"][0:96, 2:3], sb["aw_sgi"][:, dt * 96: dt * 96 + 96], onesb[:, 0:1],
            start=(dt == 0), stop=(dt == NDT - 1)), deps=prev_users + [tcx], name="wtw2")
        wtw_toks = [w1, w2, w3]
    te1 = sch.emit("vector", lambda E: E.tensor_copy(sb["a_wtw"][:, 0:1], psA[:, 2:3]),
                   deps=wtw_toks, name="wtw_e1")
    te2 = sch.emit("vector", lambda E: E.tensor_copy(sb["a_wtw"][:, 1:2], psB[:, 2:3]),
                   deps=wtw_toks, name="wtw_e2")
    tz4 = sch.emit("gpsimd", lambda E: E.memset(sb["a_wtw"][:, 2:3], 0.0), deps=deps, name="wz")
    te3 = sch.emit("vector", lambda E: E.tensor_copy(sb["a_wtw"][0:96, 2:3],
                   ctx.ps["psC"][0:96, 2:3]), deps=wtw_toks + [tz4], name="wtw_e3")
    te = sch.emit("vector", lambda E: None, deps=[te1, te2, te3], name="wtw_j")
    tf = []
    for col in range(3):
        tf.append(sch.dma("gpsimd", sb["a_wtw_row"][0:1, col * 128:(col + 1) * 128],
                          sb["a_wtw"][0:128, col:col + 1], deps=[te], name="wtwf"))
    tg = sch.emit("vector", lambda E: E.tensor_copy(sb["a_wtw_rowb"][0:1, :],
                  sb["a_wtw_row"][0:1, :]), deps=tf, name="wtwb")
    tow1 = sch.dma("sync", wst[0:32, 0:512], d["L0_attn_a_out_W"][:, 0:512],
                   deps=prev_users + const_mm)
    tcw1 = sch.emit("vector", lambda E: E.tensor_copy(sb["aw_outb"][0:32, 0:512],
                    wst[0:32, 0:512]), deps=[tow1], name="owc1")
    tow2 = sch.dma("sync", wst[0:32, 0:512], d["L0_attn_a_out_W"][:, 512:1024],
                   deps=[tcw1])
    tcw2 = sch.emit("vector", lambda E: E.tensor_copy(sb["aw_outb"][0:32, 512:1024],
                    wst[0:32, 0:512]), deps=[tow2], name="owc2")
    tob = sch.dma("sync", sb["a_outb"][:, 0:8],
                  d["L0_attn_a_out_b"].rearrange("(a p) -> p a", p=128), deps=deps, slow=True)
    return [tc, tg, tcw2, tob]




def emit_stats(ctx, src_fn, deps, tag, negmu=False, sq_buf=None, mean_of_sq=False):
    nc, sch = ctx.nc, ctx.sch
    sb = ctx.sb
    psA, psB, psC, psD, psE = (ctx.ps[k] for k in ("psA", "psB", "psC", "psD", "psE"))
    ones = sb["ones_col"]
    sq = sq_buf if sq_buf is not None else sb["sq0"]
    mu, var = sb["mu_row"], sb["var_row"]
    mm_toks = []
    prev = list(deps)
    for dt in range(NDT):
        xap = src_fn(dt)
        tq = sch.emit("scalar", lambda E, xap=xap, sq=sq: E.activation(
            out=sq[:, :], in_=xap, func=AF.Square), deps=prev, name=f"{tag}_sq")
        m1 = m2 = None
        for c0 in range(0, S, 512):
            m1 = sch.emit("tensor", lambda E, c0=c0, dt=dt, xap=xap: E.matmul(
                psE[0:1, c0:c0 + 512], ones[:, 0:1], xap[:, c0:c0 + 512],
                start=(dt == 0), stop=(dt == NDT - 1)), deps=prev, name=f"{tag}_s")
            m2 = sch.emit("tensor", lambda E, c0=c0, dt=dt, sq=sq: E.matmul(
                [psA, psB, psC, psD][c0 // 512][0:1, 0:512], ones[:, 0:1],
                sq[:, c0:c0 + 512],
                start=(dt == 0), stop=(dt == NDT - 1)), deps=[tq], name=f"{tag}_q")
            if dt == NDT - 1:
                mm_toks += [m1, m2]
        prev = [tq, m1, m2]
    t1 = sch.emit("vector", lambda E: E.tensor_scalar(mu[0:1, :], psE[0:1, :],
                  1.0 / D, None, ALU.mult), deps=mm_toks, name=f"{tag}_mu")
    sqc = []
    for c0 in range(0, 2048, 512):
        sqc.append(sch.emit("vector", lambda E, c0=c0: E.tensor_copy(
            var[0:1, c0:c0 + 512], [psA, psB, psC, psD][c0 // 512][0:1, 0:512]),
            deps=mm_toks, name=f"{tag}_sc"))
    t2 = sch.emit("vector", lambda E: E.tensor_tensor(psE[0:1, :], mu[0:1, :],
                  mu[0:1, :], ALU.mult), deps=[t1], name=f"{tag}_m2")
    t3 = sch.emit("vector", lambda E: E.scalar_tensor_tensor(var[0:1, :], var[0:1, :],
                  1.0 / D, psE[0:1, :], ALU.mult, ALU.subtract),
                  deps=[t2] + sqc, name=f"{tag}_v")
    t4 = sch.emit("scalar", lambda E: E.activation(out=var[0:1, :], in_=var[0:1, :],
                  func=AF.Sqrt, bias=sb["eps_col"][0:1, 0:1]), deps=[t3], name=f"{tag}_st")
    t5 = sch.emit("vector", lambda E: E.reciprocal(var[0:1, :], var[0:1, :]), deps=[t4],
                  name=f"{tag}_rc")
    toks = [t5]
    if negmu:
        toks.append(sch.emit("vector", lambda E: E.tensor_scalar(mu[0:1, :], mu[0:1, :],
                    -1.0, None, ALU.mult), deps=[t1, t2], name=f"{tag}_nm"))
    return toks, t1


def emit_bcast(ctx, dst_sb, row_ap, deps, tag, via_row0=None):
    nc, sch = ctx.nc, ctx.sch
    sb = ctx.sb
    psE = ctx.ps["psE"]
    toks = []
    for c0 in range(0, 2048, 512):
        toks.append(sch.emit("tensor", lambda E, c0=c0: E.matmul(
            psE[:, c0:c0 + 512], sb["ones_row"][0:1, :], row_ap[0:1, c0:c0 + 512],
            start=True, stop=True), deps=deps, name=f"{tag}_b"))
    ev = sch.emit("vector", lambda E: E.tensor_copy(dst_sb[:, 0:2048], psE[:, :]),
                  deps=toks, name=f"{tag}_e")
    return [ev]



def alloc_attn_a_body(ctx):
    ctx.scr("aw_perm", (128, NDT * 256), BF16)
    ctx.scr("aw_sgi", (128, NDT * 96), BF16)
    ctx.scr("aw_outb", (32, 1024), BF16)
    ctx.scr("a_outb", (128, 8), F32)
    ctx.scr("lnw_col", (128, NDT), F32)
    ctx.scr("lnb_col", (128, NDT), F32)
    ctx.scr("ones_col_b", (128, 1), BF16)
    ctx.scr("poolA", (128, 4096), F32)
    ctx.scr("poolC", (128, 4096), F32)
    ctx.scr("xn1d", (128, 2048), F32)
    ctx.scr("xn1bt", (128, 2048), BF16)
    ctx.scr("xtile", (128, 2048), F32)
    ctx.scr("pl_mu", (128, 2048), F32)
    ctx.scr("pl_r", (128, 2048), F32)
    ctx.scr("perm_fm", (128, 2 * 2048), F32)
    ctx.scr("sgi_fm", (96, 2048), F32)
    ctx.scr("perm_tm", (128, 16 * 256), F32)
    ctx.scr("lse_s", (128, 512), F32)
    ctx.scr("grd_rank", (128, 512), F32)
    ctx.scr("grd_sel", (128, 512), F32)
    ctx.scr("grd_rv", (128, 512), F32)
    ctx.scr("grd_oh", (128, 512), F32)
    ctx.scr("grd_cm", (128, 64), F32)
    ctx.scr("grd_tk", (128, 512), F32)
    ctx.scr("ss_P", (128, 64 * 64), BF16)
    ctx.scr("ss_G", (128, 64 * 24), F32)
    ctx.scr("scanT_P", (128, 128 * 64), BF16)
    ctx.scr("scanT_G", (128, 128 * 24), F32)
    ctx.scr("sc_p32", (128, 64), F32)
    ctx.scr("scM", (128, 64), F32)
    ctx.scr("scV", (128, 8), F32)
    ctx.scr("sc_gs", (128, 8), F32)
    ctx.scr("sc_c", (128, 8), F32)
    ctx.scr("sc_om", (128, 8), F32)
    ctx.scr("sc_t1", (128, 512), F32)
    ctx.scr("sc_pm", (128, 64), F32)
    ctx.scr("sc_t3", (128, 64), F32)
    ctx.scr("sc_pv", (128, 8), F32)
    ctx.scr("sc_pv2", (128, 8), F32)
    ctx.scr("summ", (128, 72), F32)
    ctx.scr("pf", (8, 16 * 72), F32)
    ctx.scr("pf_h", (8, 8), F32)
    ctx.scr("pf_h2", (8, 8), F32)
    ctx.scr("pf_t", (8, 64), F32)
    ctx.scr("hstarts", (8, 16 * 8), F32)
    ctx.scr("hst", (128, 8), F32)
    ctx.scr("ys", (128, 128 * 8), F32)
    ctx.scr("ys_b", (128, 128 * 8), BF16)
    ctx.scr("ys_r", (32, 4096), BF16)
    ctx.scr("ident8rep", (128, 64), F32)
    ctx.scr("a_bias", (128, 4), F32)
    ctx.scr("a_const", (128, 4), F32)
    ctx.scr("a_wtw", (128, 4), F32)
    ctx.scr("a_wtw_row", (1, 384), F32)
    ctx.scr("a_wtw_rowb", (1, 384), BF16)
    ctx.sb["wstage"] = ctx.sb["poolA"][:, 0:512]
    ctx.sb["sq0"] = ctx.sb["poolA"][:, 0:2048]
    ctx.sb["sq1"] = ctx.sb["poolA"][:, 2048:4096]
    ctx.sb["grd_m"] = ctx.sb["lse_s"]
    ctx.sb["negmu_b"] = ctx.sb["ys_r"][0:1, 0:2048]
    ctx.sb["evtmp"] = ctx.sb["perm_fm"][:, 0:1024]


def emit_attn_a(ctx, x_fm, x1_fm, xn1b_dram, deps, dbg=None):
    nc, sch = ctx.nc, ctx.sch
    psA, psB, psC, psD, psE = (ctx.ps[k] for k in ("psA", "psB", "psC", "psD", "psE"))
    sb = ctx.sb
    ident = sb["ident128"]
    qs = ["sync", "scalar", "gpsimd"]

    sch.set_phase("aa")
    wt = emit_attn_a_weights(ctx, deps)
    t_i8 = sch.dma("gpsimd", sb["ident8rep"][:, :], ctx.dram["c_ident8rep"][:], deps=deps)
    sch.stage()

    for b in range(B):
        col0 = b * S
        xt = sb["xtile"]
        sq = sb["sq0"]
        mm_toks = []
        prev = []
        for dt in range(NDT):
            tl = sch.dma("sync", xt[:, :], x_fm[dt * 128:(dt + 1) * 128, col0:col0 + S],
                         deps=prev, name="x_ld")
            tq = sch.emit("scalar", lambda E: E.activation(
                out=sq[:, :], in_=xt[:, :], func=AF.Square), deps=[tl] + prev, name="sq")
            m1 = None
            m2 = None
            for c0 in range(0, S, 512):
                m1 = sch.emit("tensor", lambda E, c0=c0, dt=dt: E.matmul(
                    psE[0:1, c0:c0 + 512], sb["ones_col"][:, 0:1], xt[:, c0:c0 + 512],
                    start=(dt == 0), stop=(dt == NDT - 1)), deps=[tl], name="lnS")
                m2 = sch.emit("tensor", lambda E, c0=c0, dt=dt: E.matmul(
                    [psA, psB, psC, psD][c0 // 512][0:1, 0:512], sb["ones_col"][:, 0:1],
                    sq[:, c0:c0 + 512],
                    start=(dt == 0), stop=(dt == NDT - 1)), deps=[tq], name="lnQ")
                if dt == NDT - 1:
                    mm_toks += [m1, m2]
            prev = [tq, m1, m2]
        mu, var, r = sb["mu_row"], sb["var_row"], sb["r_row"]
        t1 = sch.emit("vector", lambda E: E.tensor_scalar(mu[0:1, :], psE[0:1, :],
                      1.0 / D, None, ALU.mult), deps=mm_toks, name="mu")
        sqc = []
        for c0 in range(0, 2048, 512):
            sqc.append(sch.emit("vector", lambda E, c0=c0: E.tensor_copy(
                var[0:1, c0:c0 + 512], [psA, psB, psC, psD][c0 // 512][0:1, 0:512]),
                deps=mm_toks, name="sqc"))
        t2 = sch.emit("vector", lambda E: E.tensor_tensor(psE[0:1, :], mu[0:1, :],
                      mu[0:1, :], ALU.mult), deps=[t1], name="mu2")
        t3 = sch.emit("vector", lambda E: E.scalar_tensor_tensor(var[0:1, :], var[0:1, :],
                      1.0 / D, psE[0:1, :], ALU.mult, ALU.subtract),
                      deps=[t2] + sqc, name="var")
        t4 = sch.emit("scalar", lambda E: E.activation(out=r[0:1, :], in_=var[0:1, :],
                      func=AF.Sqrt, bias=sb["eps_col"][0:1, 0:1]), deps=[t3], name="sr")
        t5 = sch.emit("vector", lambda E: E.reciprocal(r[0:1, :], r[0:1, :]), deps=[t4],
                      name="rc")
        bc1 = []
        for c0 in range(0, 2048, 512):
            bc1.append(sch.emit("tensor", lambda E, c0=c0: E.matmul(
                psE[:, c0:c0 + 512], sb["ones_row"][0:1, :], mu[0:1, c0:c0 + 512],
                start=True, stop=True), deps=[t3, t1], name="bmu"))
        ev1 = sch.emit("vector", lambda E: E.tensor_copy(sb["pl_mu"][:, :], psE[:, :]),
                       deps=bc1, name="bmu_ev")
        t5b = sch.emit("vector", lambda E: E.tensor_copy(mu[0:1, :], r[0:1, :]),
                       deps=[t5] + bc1, name="r2row0")
        bc2 = []
        for c0 in range(0, 2048, 512):
            bc2.append(sch.emit("tensor", lambda E, c0=c0: E.matmul(
                psE[:, c0:c0 + 512], sb["ones_row"][0:1, :], mu[0:1, c0:c0 + 512],
                start=True, stop=True), deps=[t5b, ev1], name="br"))
        ev2 = sch.emit("vector", lambda E: E.tensor_copy(sb["pl_r"][:, :], psE[:, :]),
                       deps=bc2, name="br_ev")
        sch.stage()
        xn1d = sb["xn1d"]
        prev = []
        mm_toks = []
        for dt in range(NDT):
            tl = sch.dma("sync", xt[:, :], x_fm[dt * 128:(dt + 1) * 128, col0:col0 + S],
                         deps=prev, name="x_ld2")
            a1 = sch.emit("vector", lambda E: E.tensor_tensor(
                xn1d[:, :], xt[:, :], sb["pl_mu"][:, :], ALU.subtract),
                deps=[tl] + prev, name="xsub")
            a2 = sch.emit("vector", lambda E: E.tensor_tensor(
                xn1d[:, :], xn1d[:, :], sb["pl_r"][:, :], ALU.mult), deps=[a1], name="xmul")
            a3 = sch.emit("vector", lambda E: E.tensor_tensor(
                xt[:, :], xt[:, :], xn1d[:, :], ALU.add), deps=[a2], name="xadd")
            st = sch.dma("scalar", x1_fm[dt * 128:(dt + 1) * 128, col0:col0 + S],
                         xt[:, :], deps=[a3], name="x1_st")
            tq = sch.emit("scalar", lambda E: E.activation(
                out=sq[:, :], in_=xn1d[:, :], func=AF.Square), deps=[a2] + prev, name="sq2")
            m1 = None
            m2 = None
            for c0 in range(0, S, 512):
                m1 = sch.emit("tensor", lambda E, c0=c0, dt=dt: E.matmul(
                    psE[0:1, c0:c0 + 512], sb["ones_col"][:, 0:1], xn1d[:, c0:c0 + 512],
                    start=(dt == 0), stop=(dt == NDT - 1)), deps=[a2], name="ln2S")
                m2 = sch.emit("tensor", lambda E, c0=c0, dt=dt: E.matmul(
                    [psA, psB, psC, psD][c0 // 512][0:1, 0:512], sb["ones_col"][:, 0:1],
                    sq[:, c0:c0 + 512],
                    start=(dt == 0), stop=(dt == NDT - 1)), deps=[tq], name="ln2Q")
                if dt == NDT - 1:
                    mm_toks += [m1, m2]
            cst = sch.emit("vector", lambda E: E.tensor_copy(sb["xn1bt"][:, :],
                           xn1d[:, :]), deps=[a2], name="xnbc")
            sd = sch.dma("gpsimd", xn1b_dram[dt * 128:(dt + 1) * 128, col0:col0 + S],
                         sb["xn1bt"][:, :], deps=[cst], name="xnb_sd")
            prev = [st, tq, m1, m2, cst, sd]
        t1 = sch.emit("vector", lambda E: E.tensor_scalar(mu[0:1, :], psE[0:1, :],
                      1.0 / D, None, ALU.mult), deps=mm_toks, name="mu_2")
        sqc = []
        for c0 in range(0, 2048, 512):
            sqc.append(sch.emit("vector", lambda E, c0=c0: E.tensor_copy(
                var[0:1, c0:c0 + 512], [psA, psB, psC, psD][c0 // 512][0:1, 0:512]),
                deps=mm_toks, name="sqc2"))
        t2 = sch.emit("vector", lambda E: E.tensor_tensor(psE[0:1, :], mu[0:1, :],
                      mu[0:1, :], ALU.mult), deps=[t1], name="mu2_2")
        t3 = sch.emit("vector", lambda E: E.scalar_tensor_tensor(var[0:1, :], var[0:1, :],
                      1.0 / D, psE[0:1, :], ALU.mult, ALU.subtract),
                      deps=[t2] + sqc, name="var_2")
        t4 = sch.emit("scalar", lambda E: E.activation(out=r[0:1, :], in_=var[0:1, :],
                      func=AF.Sqrt, bias=sb["eps_col"][0:1, 0:1]), deps=[t3], name="sr_2")
        t5 = sch.emit("vector", lambda E: E.reciprocal(r[0:1, :], r[0:1, :]), deps=[t4],
                      name="rc_2")
        t6 = sch.emit("vector", lambda E: E.tensor_scalar(mu[0:1, :],
                      mu[0:1, :], -1.0, None, ALU.mult), deps=[t3], name="negmu")
        cnm = sch.emit("vector", lambda E: E.tensor_copy(sb["negmu_b"][0:1, :],
                       mu[0:1, :]), deps=[t6], name="negmu_b")
        t6b = sch.emit("vector", lambda E: E.tensor_copy(mu[0:1, :], r[0:1, :]),
                       deps=[t5, cnm], name="r2row0b")
        bc3 = []
        for c0 in range(0, 2048, 512):
            bc3.append(sch.emit("tensor", lambda E, c0=c0: E.matmul(
                psE[:, c0:c0 + 512], sb["ones_row"][0:1, :], mu[0:1, c0:c0 + 512],
                start=True, stop=True), deps=[t6b], name="br2"))
        ev3 = sch.emit("vector", lambda E: E.tensor_copy(sb["pl_r"][:, :], psE[:, :]),
                       deps=bc3, name="br2_ev")
        cwt = sch.emit("vector", lambda E: E.tensor_copy(sb["a_wtw_rowb"][0:1, :],
                       sb["a_wtw_row"][0:1, :]), deps=wt, name="wtw_b")
        sch.stage()
        pss = [psA, psB, psC]
        ld_prev = [None, None]
        ev_prev = []
        for tc in range(4):
            mm_last = [None, None, None]
            for dt in range(NDT):
                k = dt % 2
                xb = sb["xn1bt"][:, k * 512:(k + 1) * 512]
                tl = sch.dma("sync", xb,
                             xn1b_dram[dt * 128:(dt + 1) * 128, col0 + tc * 512: col0 + tc * 512 + 512],
                             deps=[ld_prev[k]] + ([sd] if dt == 0 and tc == 0 else []),
                             name="xnb_ld")
                for mt in range(3):
                    nrow = 128 if mt < 2 else 96
                    if mt == 0:
                        lh = sb["aw_perm"][:, dt * 256: dt * 256 + 128]
                    elif mt == 1:
                        lh = sb["aw_perm"][:, dt * 256 + 128: dt * 256 + 256]
                    else:
                        lh = sb["aw_sgi"][:, dt * 96: dt * 96 + 96]
                    mm_last[mt] = sch.emit("tensor", lambda E, lh=lh, dt=dt, mt=mt, nrow=nrow, xb=xb: E.matmul(
                        pss[mt][0:nrow, 0:512], lh, xb,
                        start=(dt == 0), stop=False, skip_group_check=True),
                        deps=[tl] + ev_prev, name=f"pj{mt}")
                ld_prev[k] = mm_last[2]
            ev_prev = []
            for mt in range(3):
                nrow = 128 if mt < 2 else 96
                wr = sb["a_wtw_rowb"][0:1, mt * 128: mt * 128 + nrow]
                mmc = sch.emit("tensor", lambda E, wr=wr, tc=tc, mt=mt, nrow=nrow: E.matmul(
                    pss[mt][0:nrow, 0:512], wr, sb["negmu_b"][0:1, tc * 512: tc * 512 + 512],
                    start=False, stop=True, skip_group_check=True),
                    deps=[mm_last[mt], cnm, cwt], name="pjc")
                if mt < 2:
                    dst = sb["perm_fm"][:, mt * 2048 + tc * 512: mt * 2048 + tc * 512 + 512]
                else:
                    dst = sb["sgi_fm"][0:96, tc * 512: tc * 512 + 512]
                e1 = sch.emit("vector", lambda E, mt=mt, dst=dst, nrow=nrow, tc=tc: E.tensor_tensor(
                    dst, pss[mt][0:nrow, 0:512], sb["pl_r"][0:nrow, tc * 512: tc * 512 + 512],
                    ALU.mult), deps=[mmc, ev3], name="pj_e1")
                e2 = sch.emit("vector", lambda E, dst=dst, nrow=nrow, mt=mt: E.tensor_scalar(
                    dst, dst, sb["a_const"][0:nrow, mt:mt + 1], None, ALU.add),
                    deps=[e1], name="pj_e2")
                ev_prev.append(e2)
        sch.stage()
        tr_prev = {0: None, 1: None}
        for ft in range(2):
            for tt in range(16):
                k = (ft * 16 + tt) % 2
                ps = [psC, psD][k]
                tm = sch.emit("tensor", lambda E, ft=ft, tt=tt, ps=ps: E.transpose(
                    ps[:, 0:128], sb["perm_fm"][:, ft * 2048 + tt * 128: ft * 2048 + tt * 128 + 128],
                    ident[:, :]), deps=[tr_prev[k]], name="ptr")
                ev = sch.emit("vector", lambda E, ft=ft, tt=tt, ps=ps: E.tensor_copy(
                    sb["perm_tm"][:, tt * 256 + ft * 128: tt * 256 + ft * 128 + 128],
                    ps[:, 0:128]), deps=[tm], name="ptr_ev")
                tr_prev[k] = ev
        for tt in range(16):
            k = tt % 2
            ps = [psC, psD][k]
            tm = sch.emit("tensor", lambda E, tt=tt, ps=ps: E.transpose(
                ps[:, 0:96], sb["sgi_fm"][0:96, tt * 128: tt * 128 + 128], ident[0:96, 0:96]),
                deps=[tr_prev[k]], name="str")
            ssg4 = sb["ss_G"][:, :].rearrange("p (h tt c) -> p h tt c", h=4, tt=16, c=24)
            e1 = sch.emit("scalar", lambda E, tt=tt, ps=ps, ssg4=ssg4: E.activation(
                out=ssg4[:, :, tt, 0:8], in_=ps[:, 0:32].rearrange("p (h n) -> p h n", n=8),
                func=AF.Tanh, scale=0.5), deps=[tm], name="s_tanh")
            e2 = sch.emit("scalar", lambda E, tt=tt, ps=ps, ssg4=ssg4: E.activation(
                out=ssg4[:, :, tt, 8:16], in_=ps[:, 32:64].rearrange("p (h n) -> p h n", n=8),
                func=AF.Sigmoid), deps=[e1, tm], name="g_sig")
            e3 = sch.emit("vector", lambda E, tt=tt, ps=ps, ssg4=ssg4: E.tensor_copy(
                ssg4[:, :, tt, 16:24], ps[:, 64:96].rearrange("p (h n) -> p h n", n=8)),
                deps=[tm], name="bI_cp")
            tr_prev[k] = sch.emit("vector", lambda E: None, deps=[e2, e3], name="join")
        sch.stage()
        la = sb["perm_tm"]
        ebuf = sb["poolA"]
        x8 = None
        for it in range(5):
            x1 = sch.emit("scalar", lambda E: E.activation(out=ebuf[:, :], in_=la[:, :],
                          func=AF.Exp), deps=[x8], name="sk_e")
            x2 = sch.emit("vector", lambda E: E.tensor_reduce(
                sb["lse_s"][:, :], ebuf[:, :].rearrange("p (a j) -> p a j", j=8),
                axis=AX.X, op=ALU.add), deps=[x1], name="sk_rs")
            x3 = sch.emit("scalar", lambda E: E.activation(out=sb["lse_s"][:, :],
                          in_=sb["lse_s"][:, :], func=AF.Ln), deps=[x2], name="sk_ln")
            x4 = sch.emit("vector", lambda E: E.tensor_tensor(
                la[:, :].rearrange("p (a j) -> p a j", j=8),
                la[:, :].rearrange("p (a j) -> p a j", j=8),
                sb["lse_s"][:, :].to_broadcast((128, 512, 8)), ALU.subtract),
                deps=[x3], name="sk_sub")
            x5 = sch.emit("scalar", lambda E: E.activation(out=ebuf[:, :], in_=la[:, :],
                          func=AF.Exp), deps=[x4], name="sk_e2")
            x6 = sch.emit("vector", lambda E: E.tensor_reduce(
                sb["lse_s"][:, :], ebuf[:, :].rearrange("p (g i j) -> p g j i", i=8, j=8),
                axis=AX.X, op=ALU.add), deps=[x5], name="sk_cs")
            x7 = sch.emit("scalar", lambda E: E.activation(out=sb["lse_s"][:, :],
                          in_=sb["lse_s"][:, :], func=AF.Ln), deps=[x6], name="sk_ln2")
            x8 = sch.emit("vector", lambda E: E.tensor_tensor(
                la[:, :].rearrange("p (g i j) -> p g j i", i=8, j=8),
                la[:, :].rearrange("p (g i j) -> p g j i", i=8, j=8),
                sb["lse_s"][:, :].to_broadcast((128, 512, 8)), ALU.subtract),
                deps=[x7], name="sk_sub2")
        soft = sb["perm_tm"]
        x9 = sch.emit("scalar", lambda E: E.activation(out=soft[:, :], in_=soft[:, :],
                      func=AF.Exp), deps=[x8], name="sk_soft")
        sch.stage()
        m = sb["grd_m"]
        g1 = sch.emit("vector", lambda E: E.tensor_reduce(
            m[:, :], soft[:, :].rearrange("p (a j) -> p a j", j=8), axis=AX.X,
            op=ALU.max), deps=[], name="g_max")
        mv_r = m[:, :].rearrange("p (g r) -> p g r", r=8).to_broadcast((128, 64, 8, 8))
        e1 = sch.emit("vector", lambda E: E.tensor_tensor(
            sb["poolA"][:, :].rearrange("p (g r q) -> p g r q", r=8, q=8),
            mv_r.transpose((0, 1, 3, 2)), mv_r, ALU.is_gt), deps=[g1], name="g_gt")
        e2 = sch.emit("vector", lambda E: E.tensor_tensor(
            sb["poolC"][:, :].rearrange("p (g r q) -> p g r q", r=8, q=8),
            mv_r.transpose((0, 1, 3, 2)), mv_r, ALU.is_equal), deps=[e1], name="g_eq")
        lt_v = sb["lt8"][:, :].rearrange("p (r q) -> p r q", q=8).to_broadcast(
            (128, 8, 8, 64)).transpose((0, 3, 1, 2))
        e3 = sch.emit("vector", lambda E: E.tensor_tensor(
            sb["poolC"][:, :].rearrange("p (g r q) -> p g r q", r=8, q=8),
            sb["poolC"][:, :].rearrange("p (g r q) -> p g r q", r=8, q=8),
            lt_v, ALU.mult), deps=[e2], name="g_lt")
        e4 = sch.emit("vector", lambda E: E.tensor_tensor(
            sb["poolA"][:, :], sb["poolA"][:, :], sb["poolC"][:, :], ALU.add),
            deps=[e3], name="g_add")
        e5 = sch.emit("vector", lambda E: E.tensor_reduce(
            sb["grd_rank"][:, :], sb["poolA"][:, :].rearrange("p (a q) -> p a q", q=8),
            axis=AX.X, op=ALU.add), deps=[e4], name="g_rank")
        z1 = sch.emit("gpsimd", lambda E: E.memset(sb["grd_tk"][:, :], 0.0), deps=[],
                      name="g_tk0")
        hard = sb["poolC"]
        z2 = sch.emit("gpsimd", lambda E: E.memset(hard[:, :], 0.0), deps=[e3, e4],
                      name="g_h0")
        prev = [e5, z1, z2]
        for step in range(8):
            s1 = sch.emit("vector", lambda E, step=step: E.tensor_scalar(
                sb["grd_sel"][:, :], sb["grd_rank"][:, :], float(step), None,
                ALU.is_equal), deps=prev, name="g_sel")
            selv = sb["grd_sel"][:, :].rearrange("p (g r) -> p g r", r=8).to_broadcast(
                (128, 64, 8, 8))
            s2 = sch.emit("vector", lambda E, selv=selv: E.tensor_tensor(
                sb["poolA"][:, :].rearrange("p (g j r) -> p g j r", j=8, r=8),
                soft[:, :].rearrange("p (g r j) -> p g j r", r=8, j=8),
                selv.transpose((0, 1, 3, 2)), ALU.mult), deps=[s1], name="g_rv1")
            s3 = sch.emit("vector", lambda E: E.tensor_reduce(
                sb["grd_rv"][:, :], sb["poolA"][:, :].rearrange("p (a r) -> p a r", r=8),
                axis=AX.X, op=ALU.add), deps=[s2], name="g_rv2")
            s4 = sch.emit("vector", lambda E: E.scalar_tensor_tensor(
                sb["grd_rv"][:, :], sb["grd_tk"][:, :], -1e30, sb["grd_rv"][:, :],
                ALU.mult, ALU.add), deps=[s3], name="g_sc")
            s5 = sch.emit("vector", lambda E: E.tensor_reduce(
                sb["grd_cm"][:, :], sb["grd_rv"][:, :].rearrange("p (g j) -> p g j", j=8),
                axis=AX.X, op=ALU.max), deps=[s4], name="g_cm")
            s6 = sch.emit("vector", lambda E: E.tensor_tensor(
                sb["grd_oh"][:, :].rearrange("p (g j) -> p g j", j=8),
                sb["grd_rv"][:, :].rearrange("p (g j) -> p g j", j=8),
                sb["grd_cm"][:, :].to_broadcast((128, 64, 8)), ALU.is_equal),
                deps=[s5], name="g_oh")
            s7 = sch.emit("vector", lambda E: E.tensor_tensor(
                sb["grd_tk"][:, :], sb["grd_tk"][:, :], sb["grd_oh"][:, :], ALU.add),
                deps=[s6], name="g_tk")
            ohv = sb["grd_oh"][:, :].rearrange("p (g j) -> p g j", j=8).to_broadcast(
                (128, 64, 8, 8)).transpose((0, 1, 3, 2))
            s8 = sch.emit("vector", lambda E, selv=selv, ohv=ohv: E.tensor_tensor(
                sb["poolA"][:, :].rearrange("p (g r j) -> p g r j", r=8, j=8),
                selv, ohv, ALU.mult), deps=[s6], name="g_hu1")
            s9 = sch.emit("vector", lambda E: E.tensor_tensor(
                hard[:, :], hard[:, :], sb["poolA"][:, :], ALU.add), deps=[s8], name="g_hu2")
            prev = [s7, s9]
        sch.stage()
        c1 = sch.emit("vector", lambda E: E.tensor_copy(
            sb["ss_P"][:, :].rearrange("p (h tt c) -> p h tt c", h=4, tt=16, c=64),
            hard[:, :].rearrange("p (tt h c) -> p h tt c", tt=16, h=4, c=64)),
            deps=[], name="ss_h")
        for g2 in range(64):
            sp = b * 64 + g2
            sch.dma(qs[g2 % 3], sb["scanT_P"][sp:sp + 1, :],
                    sb["ss_P"][:, g2 * 64:(g2 + 1) * 64], deps=[c1], name="scTP")
            sch.dma(qs[(g2 + 1) % 3], sb["scanT_G"][sp:sp + 1, :],
                    sb["ss_G"][:, g2 * 24:(g2 + 1) * 24], deps=[], name="scTG")
        sch.stage()
    sb_ = sb
    i1 = sch.emit("vector", lambda E: E.tensor_copy(sb_["scM"][:, :], sb_["ident8rep"][:, :]),
                  deps=[], name="scM0")
    i2 = sch.emit("gpsimd", lambda E: E.memset(sb_["scV"][:, :], 0.0), deps=[], name="scV0")
    prev = [i1, i2]
    for tau in range(128):
        gbase = tau * 24
        cp = sch.emit("vector", lambda E, tau=tau: E.tensor_copy(
            sb_["sc_p32"][:, :], sb_["scanT_P"][:, tau * 64:(tau + 1) * 64]),
            deps=prev, name="p32")
        P_ij = sb_["sc_p32"][:, :].rearrange("p (i j) -> p i j", j=8)
        gsl = sb_["sc_gs"][:, :]
        csl = sb_["sc_c"][:, :]
        u1 = sch.emit("vector", lambda E, gbase=gbase, gsl=gsl: E.tensor_tensor(
            gsl, sb_["scanT_G"][:, gbase + 8:gbase + 16], sb_["scanT_G"][:, gbase:gbase + 8],
            ALU.mult), deps=prev, name="gs")
        u2 = sch.emit("vector", lambda E, gbase=gbase: E.tensor_scalar(
            sb_["sc_om"][:, :], sb_["scanT_G"][:, gbase + 8:gbase + 16], -1.0, 1.0,
            ALU.mult, ALU.add), deps=prev, name="om")
        u3 = sch.emit("vector", lambda E, gbase=gbase, csl=csl: E.tensor_tensor(
            csl, sb_["sc_om"][:, :], sb_["scanT_G"][:, gbase + 16:gbase + 24], ALU.mult),
            deps=[u2], name="cc")
        Pb = P_ij.to_broadcast((128, 8, 8, 8)).transpose((0, 1, 3, 2))
        Mb = sb_["scM"][:, :].rearrange("p (j k) -> p j k", k=8).to_broadcast(
            (128, 8, 8, 8)).transpose((0, 3, 2, 1))
        u4 = sch.emit("vector", lambda E, Pb=Pb, Mb=Mb: E.tensor_tensor(
            sb_["sc_t1"][:, :].rearrange("p (i k j) -> p i k j", i=8, k=8, j=8),
            Pb, Mb, ALU.mult), deps=[cp] + prev, name="pm1")
        u5 = sch.emit("vector", lambda E: E.tensor_reduce(
            sb_["sc_pm"][:, :], sb_["sc_t1"][:, :].rearrange("p (a j) -> p a j", j=8),
            axis=AX.X, op=ALU.add), deps=[u4], name="pm2")
        u6 = sch.emit("vector", lambda E, gsl=gsl: E.tensor_tensor(
            sb_["scM"][:, :].rearrange("p (i k) -> p i k", k=8),
            sb_["sc_pm"][:, :].rearrange("p (i k) -> p i k", k=8),
            gsl.to_broadcast((128, 8, 8)), ALU.mult), deps=[u5, u1], name="pm3")
        vb = sb_["scV"][:, :].to_broadcast((128, 8, 8)).transpose((0, 2, 1))
        u7 = sch.emit("vector", lambda E, P_ij=P_ij, vb=vb: E.tensor_tensor(
            sb_["sc_t3"][:, :].rearrange("p (i j) -> p i j", j=8), P_ij, vb, ALU.mult),
            deps=[cp] + prev, name="pv1")
        u8 = sch.emit("vector", lambda E: E.tensor_reduce(
            sb_["sc_pv"][:, :], sb_["sc_t3"][:, :].rearrange("p (i j) -> p i j", j=8),
            axis=AX.X, op=ALU.add), deps=[u7], name="pv2")
        u9 = sch.emit("vector", lambda E, gsl=gsl: E.tensor_tensor(
            sb_["sc_pv2"][:, :], sb_["sc_pv"][:, :], gsl, ALU.mult), deps=[u8, u1],
            name="pv3")
        u10 = sch.emit("vector", lambda E, csl=csl: E.tensor_tensor(
            sb_["scV"][:, :], sb_["sc_pv2"][:, :], csl, ALU.add), deps=[u9, u3, u7],
            name="pv4")
        prev = [u6, u10]
    f1 = sch.emit("vector", lambda E: E.tensor_copy(sb_["summ"][:, 0:64], sb_["scM"][:, :]),
                  deps=prev, name="sm")
    f2 = sch.emit("vector", lambda E: E.tensor_copy(sb_["summ"][:, 64:72], sb_["scV"][:, :]),
                  deps=prev, name="sv")
    sch.stage()
    qs = ["sync", "scalar", "gpsimd"]
    dts = []
    for b2 in range(2):
        for h in range(4):
            base_p = b2 * 64 + h * 16
            dts.append(sch.dma(qs[(b2 * 4 + h) % 3], sb_["pf"][b2 * 4 + h:b2 * 4 + h + 1, :],
                               sb_["summ"][base_p:base_p + 16, :], deps=[f1, f2], name="pfd"))
    z3 = sch.emit("gpsimd", lambda E: E.memset(sb_["pf_h"][:, :], 0.0), deps=[], name="pfh0")
    prev = dts + [z3]
    for c in range(16):
        w1 = sch.emit("vector", lambda E, c=c: E.tensor_copy(
            sb_["hstarts"][:, c * 8:(c + 1) * 8], sb_["pf_h"][:, :]), deps=prev, name="hs_cp")
        Mc = sb_["pf"][:, c * 72:c * 72 + 64].rearrange("p (i k) -> p i k", k=8)
        hb = sb_["pf_h"][:, :].to_broadcast((8, 8, 8)).transpose((0, 2, 1))
        w2 = sch.emit("vector", lambda E, Mc=Mc, hb=hb: E.tensor_tensor(
            sb_["pf_t"][:, :].rearrange("p (i k) -> p i k", k=8), Mc, hb, ALU.mult),
            deps=prev, name="hs_m")
        w3 = sch.emit("vector", lambda E: E.tensor_reduce(
            sb_["pf_h2"][:, :], sb_["pf_t"][:, :].rearrange("p (i k) -> p i k", k=8),
            axis=AX.X, op=ALU.add), deps=[w2], name="hs_r")
        w4 = sch.emit("vector", lambda E, c=c: E.tensor_tensor(
            sb_["pf_h"][:, :], sb_["pf_h2"][:, :], sb_["pf"][:, c * 72 + 64:c * 72 + 72],
            ALU.add), deps=[w3, w1], name="hs_a")
        prev = [w4]
    dts2 = []
    for b2 in range(2):
        for h in range(4):
            base_p = b2 * 64 + h * 16
            dts2.append(sch.dma(qs[(b2 * 4 + h) % 3], sb_["hst"][base_p:base_p + 16, :],
                                sb_["hstarts"][b2 * 4 + h:b2 * 4 + h + 1, :],
                                deps=prev, name="hsd"))
    sch.stage()
    prev = list(dts2)
    hprev_ap = sb_["hst"][:, :]
    for tau in range(128):
        cp = sch.emit("vector", lambda E, tau=tau: E.tensor_copy(
            sb_["sc_p32"][:, :], sb_["scanT_P"][:, tau * 64:(tau + 1) * 64]),
            deps=prev, name="p32b")
        P_ij = sb_["sc_p32"][:, :].rearrange("p (i j) -> p i j", j=8)
        gbase = tau * 24
        g1b = sch.emit("vector", lambda E, gbase=gbase: E.tensor_tensor(
            sb_["sc_gs"][:, :], sb_["scanT_G"][:, gbase + 8:gbase + 16],
            sb_["scanT_G"][:, gbase:gbase + 8], ALU.mult), deps=prev, name="gs2")
        g2b = sch.emit("vector", lambda E, gbase=gbase: E.tensor_scalar(
            sb_["sc_om"][:, :], sb_["scanT_G"][:, gbase + 8:gbase + 16], -1.0, 1.0,
            ALU.mult, ALU.add), deps=prev, name="om2")
        g3b = sch.emit("vector", lambda E, gbase=gbase: E.tensor_tensor(
            sb_["sc_c"][:, :], sb_["sc_om"][:, :], sb_["scanT_G"][:, gbase + 16:gbase + 24],
            ALU.mult), deps=[g2b], name="cc2")
        hb = hprev_ap.to_broadcast((128, 8, 8)).transpose((0, 2, 1))
        v1 = sch.emit("vector", lambda E, P_ij=P_ij, hb=hb: E.tensor_tensor(
            sb_["sc_t3"][:, :].rearrange("p (i j) -> p i j", j=8), P_ij, hb, ALU.mult),
            deps=[cp] + prev, name="ap1")
        v2 = sch.emit("vector", lambda E: E.tensor_reduce(
            sb_["sc_pv"][:, :], sb_["sc_t3"][:, :].rearrange("p (i j) -> p i j", j=8),
            axis=AX.X, op=ALU.add), deps=[v1], name="ap2")
        v3 = sch.emit("vector", lambda E: E.tensor_tensor(
            sb_["sc_pv2"][:, :], sb_["sc_pv"][:, :], sb_["sc_gs"][:, :],
            ALU.mult), deps=[v2, g1b], name="ap3")
        yslot = sb_["ys"][:, :].rearrange("p (n tau) -> p n tau", n=8)[:, :, tau]
        v4 = sch.emit("vector", lambda E, yslot=yslot: E.tensor_tensor(
            yslot, sb_["sc_pv2"][:, :],
            sb_["sc_c"][:, :], ALU.add), deps=[v3, g3b], name="ap4")
        hprev_ap = yslot
        prev = [v4]
    cb = sch.emit("vector", lambda E: E.tensor_copy(sb_["ys_b"][:, :], sb_["ys"][:, :]),
                  deps=prev, name="ysb")
    sch.stage()
    dts3 = []
    for b2 in range(2):
        for h in range(4):
            for tt in range(16):
                sp = b2 * 64 + h * 16 + tt
                src2 = sb_["ys_b"][sp:sp + 1, :].rearrange("p (n tau) -> p n tau", n=8)
                dst = sb_["ys_r"][h * 8:(h + 1) * 8, b2 * 2048 + tt * 128: b2 * 2048 + tt * 128 + 128]
                dts3.append(sch.dma(qs[sp % 3], dst, src2, deps=[cb], name="ysr"))
    sch.stage()
    ev_prev = {0: None, 1: None}
    for b2 in range(2):
        col0 = b2 * S
        for ft in range(8):
            for tc in range(4):
                k = (ft * 4 + tc) % 2
                ps = [psA, psB][k]
                mm = sch.emit("tensor", lambda E, ft=ft, tc=tc, b2=b2, ps=ps: E.matmul(
                    ps[:, 0:512], sb_["aw_outb"][0:32, ft * 128:(ft + 1) * 128],
                    sb_["ys_r"][0:32, b2 * 2048 + tc * 512: b2 * 2048 + tc * 512 + 512],
                    start=True, stop=True), deps=[ev_prev[k]] + dts3, name="ow_mm")
                xl = sb_["xtile"][:, k * 512:(k + 1) * 512]
                ev = sb_["evtmp"][:, k * 512:(k + 1) * 512]
                tl = sch.dma("sync", xl,
                             x1_fm[ft * 128:(ft + 1) * 128, col0 + tc * 512: col0 + tc * 512 + 512],
                             deps=[ev_prev[k]], name="x1_ld")
                e1 = sch.emit("vector", lambda E, ps=ps, ft=ft, xl=xl, ev=ev: E.scalar_tensor_tensor(
                    ev, ps[:, 0:512], sb_["a_outb"][:, ft:ft + 1],
                    xl, ALU.add, ALU.add), deps=[mm, tl], name="ow_ev")
                st = sch.dma("scalar", x1_fm[ft * 128:(ft + 1) * 128, col0 + tc * 512: col0 + tc * 512 + 512],
                             ev, deps=[e1], name="x1_st2")
                ev_prev[k] = st
    sch.stage()
    return []



def alloc_moe(ctx):
    ctx.scr_top("m_wst", (128, 1024), F32)
    ctx.scr_top("m_wv", (128, 2 * 16 * 6), F32)
    ctx.scr_top("m_s2u", (128, 2 * 16 * 6), U32)
    ctx.scr_top("m_mrows", (33, 2048), F32)
    ctx.scr_top("m_cnt8", (8, 64), F32)
    ctx.scr_top("m_rmask", (8, 1), F32)
    ctx.scr_top("m_tokid", (128, 32), F32)
    ctx.scr_top("m_rsum", (128, 1024), F32)
    ctx.scr_top("m_ctm", (128, 1024), F32)
    ctx.scr_top("m_xidx", (128, 32), U32)
    ctx.scr_top("m_xnbt", (128, 1024), BF16)


def alloc_moe_routing(ctx):
    ctx.scr("m_plmu", (128, 2048), F32)
    ctx.scr("m_plr", (128, 2048), F32)
    ctx.scr("m_t1", (128, 2048), F32)
    ctx.scr("m_xnb", (128, NDT * 2048), BF16)
    ctx.scr("m_xtile", (128, 2048), F32)
    ctx.scr("m_gw", (128, NDT * 64), BF16)
    ctx.scr("m_gwl", (128, NDT * 64), BF16)
    ctx.scr("m_xlo", (128, 2048), BF16)
    ctx.scr("m_gbrow", (1, 64), F32)
    ctx.scr("m_gbrep", (128, 64), F32)
    ctx.scr("m_sc", (128, 16 * 64), F32)
    ctx.scr("m_sc2", (128, 16 * 64), F32)
    ctx.scr("m_srow", (128, 16), F32)
    ctx.scr("m_oh", (128, 64), F32)
    ctx.scr("m_ohp", (128, 64), F32)
    ctx.scr("m_topv", (128, 16 * 8), F32)
    ctx.scr("m_topi", (128, 16 * 8), U16)
    ctx.scr("m_ef", (128, 2 * 16 * 6), F32)
    ctx.scr("m_r1", (128, 16 * 6), F32)
    ctx.scr("m_p6", (128, 16 * 36), F32)
    ctx.scr("m_I", (128, 4 * 6 * 64), F32)
    ctx.scr("m_c", (128, 2 * 16 * 64), F32)
    ctx.scr("m_pfx", (128, 2 * 16 * 64), F32)
    ctx.scr("m_fp", (128, 16 * 64), F32)
    ctx.scr("m_pos", (128, 16 * 6), F32)
    ctx.scr("m_slot", (128, 16 * 6), F32)
    ctx.scr("m_val", (128, 16 * 6), F32)
    ctx.scr("m_rec", (128, 16 * 6 * 4), F32)


def emit_moe(ctx, x_fm, x2_fm, z_dram, pass_id, deps, cut=10):
    nc, sch = ctx.nc, ctx.sch
    sb = ctx.sb
    d = ctx.dram
    psA, psB, psC, psD, psE = (ctx.ps[k] for k in ("psA", "psB", "psC", "psD", "psE"))
    ident = sb["ident128"]
    P = pass_id
    qs = ["sync", "scalar", "gpsimd"]

    ag_cnt_i = ctx.dram_tmp(f"agci{P}", (1, 64), F32)
    ag_cnt_o = ctx.dram_tmp(f"agco{P}", (8, 64), F32, shared=True)
    ag_rec_i = ctx.dram_tmp(f"agri{P}", (32768, 4), F32)
    ag_rec_o = ctx.dram_tmp(f"agro{P}", (8 * 32768, 4), F32, shared=True)
    ag_xn_i = ctx.dram_tmp(f"agxi{P}", (4096, 1024), BF16)
    ag_xn_o = ctx.dram_tmp(f"agxo{P}", (8 * 4096, 1024), BF16, shared=True)
    ag_cb_i = ctx.dram_tmp(f"agbi{P}", (4096, 1024), F32)
    ag_cb_o = ctx.dram_tmp(f"agbo{P}", (8 * 4096, 1024), F32, shared=True)

    sch.set_phase(f"m{P}_ld")
    ctx.scratch_reset()
    alloc_moe_routing(ctx)
    zoff = ctx._bump
    prev_w = list(deps)
    wl = []
    for dt in range(NDT):
        t1 = sch.dma("sync", sb["m_wst"][:, 0:64], d["gate_wT"][dt * 128:(dt + 1) * 128, :],
                     deps=prev_w, name="gw_l")
        t2 = sch.emit("vector", lambda E, dt=dt: E.tensor_copy(
            sb["m_gw"][:, dt * 64:(dt + 1) * 64], sb["m_wst"][:, 0:64]), deps=[t1],
            name="gw_c")
        t2u = sch.emit("vector", lambda E, dt=dt: E.tensor_copy(
            sb["m_wst"][:, 128:192], sb["m_gw"][:, dt * 64:(dt + 1) * 64]),
            deps=[t2], name="gw_up")
        t2b = sch.emit("vector", lambda E: E.tensor_tensor(
            sb["m_wst"][:, 64:128], sb["m_wst"][:, 0:64],
            sb["m_wst"][:, 128:192], ALU.subtract), deps=[t2u],
            name="gw_lo1")
        t2c = sch.emit("vector", lambda E, dt=dt: E.tensor_copy(
            sb["m_gwl"][:, dt * 64:(dt + 1) * 64], sb["m_wst"][:, 64:128]),
            deps=[t2b], name="gw_lo2")
        prev_w = [t2b]
        wl += [t2, t2c]
    tgb = sch.dma("sync", sb["m_gbrow"][0:1, :], d["L0_ffn_gate_b"][:].unsqueeze(0),
                  deps=deps, name="gb_l")
    bcg = sch.emit("tensor", lambda E: E.matmul(psD[:, 0:64], sb["ones_row"][0:1, :],
                   sb["m_gbrow"][0:1, :], start=True, stop=True), deps=[tgb], name="gb_b")
    tgb2 = sch.emit("vector", lambda E: E.tensor_copy(sb["m_gbrep"][:, :], psD[:, 0:64]),
                    deps=[bcg], name="gb_e")
    trm = sch.dma("sync", sb["m_rmask"][:, :], d["rankmask"][:, :], deps=deps, name="rm")
    tti = sch.dma("sync", sb["m_tokid"][:, :], d["tokid"][:, :], deps=deps, name="ti")
    sch.stage()
    if cut <= 1:
        return

    wsz = ctx.dram_tmp(f"wsz{P}", (11, 128, 8, 256), BF16)
    ws2z = ctx.dram_tmp(f"ws2z{P}", (8, 128, 11, 128), BF16)
    ctx.scr("z_wk", (128, 2 * 2048), BF16)
    ctx.scr("z_w2k", (128, 2 * 1408), BF16)
    ctx.scr("z_zh", (128, 11 * 512), BF16)
    ctx.scr("z_stmp", (128, 512), F32)
    ctx.scr("z_stg", (128, 256), BF16)
    prev_w = []
    zl = []
    for dt in range(NDT):
        for scn in range(11):
            t1 = sch.dma("scalar", sb["m_wst"][:, 0:128],
                         d["ws1T"][dt * 128:(dt + 1) * 128, scn * 128:(scn + 1) * 128],
                         deps=prev_w, name="zc_l1")
            c1 = sch.emit("vector", lambda E: E.tensor_copy(sb["z_stg"][:, 0:128],
                          sb["m_wst"][:, 0:128]), deps=[t1], name="zc_c1")
            t2 = sch.dma("scalar", sb["m_wst"][:, 0:128],
                         d["ws3T"][dt * 128:(dt + 1) * 128, scn * 128:(scn + 1) * 128],
                         deps=[c1], name="zc_l2")
            c2 = sch.emit("vector", lambda E: E.tensor_copy(sb["z_stg"][:, 128:256],
                          sb["m_wst"][:, 0:128]), deps=[t2], name="zc_c2")
            s1 = sch.dma("gpsimd", wsz[scn, :, dt, :], sb["z_stg"][:, :], deps=[c2],
                         name="zc_s")
            prev_w = [c2, s1]
            zl.append(s1)
    for dt in range(NDT):
        for scn in range(11):
            t1 = sch.dma("scalar", sb["m_wst"][:, 0:128],
                         d["ws2T"][scn * 128:(scn + 1) * 128, dt * 128:(dt + 1) * 128],
                         deps=prev_w, name="z2_l")
            c1 = sch.emit("vector", lambda E: E.tensor_copy(sb["z_stg"][:, 0:128],
                          sb["m_wst"][:, 0:128]), deps=[t1], name="z2_c")
            s1 = sch.dma("gpsimd", ws2z[dt, :, scn, :], sb["z_stg"][:, 0:128], deps=[c1],
                         name="z2_s")
            prev_w = [c1, s1]
            zl.append(s1)
    sch.stage()
    if cut <= 2:
        return

    xn_dma = []
    ts_evac = []
    for b in range(B):
        sch.set_phase(f"m{P}_rt{b}")
        col0 = b * S
        xt = sb["m_xtile"]
        mm_toks = []
        prev = []
        for dt in range(NDT):
            tl = sch.dma("sync", xt[:, :], x_fm[dt * 128:(dt + 1) * 128, col0:col0 + S],
                         deps=prev, name="mx_ld")
            tq = sch.emit("scalar", lambda E: E.activation(out=sb["m_t1"][:, :],
                          in_=xt[:, :], func=AF.Square), deps=[tl] + prev, name="m_sq")
            m1 = m2 = None
            for c0 in range(0, S, 512):
                m1 = sch.emit("tensor", lambda E, c0=c0, dt=dt: E.matmul(
                    psE[0:1, c0:c0 + 512], sb["ones_col"][:, 0:1], xt[:, c0:c0 + 512],
                    start=(dt == 0), stop=(dt == NDT - 1)), deps=[tl], name="m_s")
                m2 = sch.emit("tensor", lambda E, c0=c0, dt=dt: E.matmul(
                    [psA, psB, psC, psD][c0 // 512][0:1, 0:512], sb["ones_col"][:, 0:1],
                    sb["m_t1"][:, c0:c0 + 512],
                    start=(dt == 0), stop=(dt == NDT - 1)), deps=[tq], name="m_q")
                if dt == NDT - 1:
                    mm_toks += [m1, m2]
            prev = [tq, m1, m2]
        mu, var = sb["mu_row"], sb["var_row"]
        t1 = sch.emit("vector", lambda E: E.tensor_scalar(mu[0:1, :], psE[0:1, :],
                      1.0 / D, None, ALU.mult), deps=mm_toks, name="m_mu")
        sqc = []
        for c0 in range(0, 2048, 512):
            sqc.append(sch.emit("vector", lambda E, c0=c0: E.tensor_copy(
                var[0:1, c0:c0 + 512], [psA, psB, psC, psD][c0 // 512][0:1, 0:512]),
                deps=mm_toks, name="m_sc"))
        t2 = sch.emit("vector", lambda E: E.tensor_tensor(psE[0:1, :], mu[0:1, :],
                      mu[0:1, :], ALU.mult), deps=[t1], name="m_m2")
        t3 = sch.emit("vector", lambda E: E.scalar_tensor_tensor(var[0:1, :], var[0:1, :],
                      1.0 / D, psE[0:1, :], ALU.mult, ALU.subtract),
                      deps=[t2] + sqc, name="m_v")
        t4 = sch.emit("scalar", lambda E: E.activation(out=var[0:1, :], in_=var[0:1, :],
                      func=AF.Sqrt, bias=sb["eps_col"][0:1, 0:1]), deps=[t3], name="m_st")
        t5 = sch.emit("vector", lambda E: E.reciprocal(var[0:1, :], var[0:1, :]),
                      deps=[t4], name="m_rc")
        bmu = emit_bcast(ctx, sb["m_plmu"], mu, [t1, t3], "m_bm")
        t5b = sch.emit("vector", lambda E: E.tensor_copy(mu[0:1, :], var[0:1, :]),
                       deps=[t5] + bmu, name="m_r0")
        brr = emit_bcast(ctx, sb["m_plr"], mu, [t5b], "m_br")
        sch.stage()
        prev = []
        g_last = None
        for dt in range(NDT):
            tl = sch.dma("sync", xt[:, :], x_fm[dt * 128:(dt + 1) * 128, col0:col0 + S],
                         deps=prev, name="mx_ld2")
            a1 = sch.emit("vector", lambda E: E.tensor_tensor(sb["m_t1"][:, :], xt[:, :],
                          sb["m_plmu"][:, :], ALU.subtract), deps=[tl], name="m_a1")
            a1b = sch.emit("vector", lambda E: E.tensor_tensor(sb["m_t1"][:, :],
                           sb["m_t1"][:, :], sb["m_plr"][:, :], ALU.mult), deps=[a1],
                           name="m_a1b")
            a2 = sch.emit("vector", lambda E, dt=dt: E.tensor_copy(
                sb["m_xnb"][:, dt * 2048:(dt + 1) * 2048], sb["m_t1"][:, :]),
                deps=[a1b], name="m_a2")
            a2u = sch.emit("vector", lambda E, dt=dt: E.tensor_copy(
                xt[:, :], sb["m_xnb"][:, dt * 2048:(dt + 1) * 2048]),
                deps=[a2], name="m_a2u")
            a3 = sch.emit("vector", lambda E: E.tensor_tensor(
                sb["m_t1"][:, :], sb["m_t1"][:, :], xt[:, :], ALU.subtract),
                deps=[a2u] + ([g_last] if g_last else []), name="m_a3")
            a4 = sch.emit("vector", lambda E: E.tensor_copy(sb["m_xlo"][:, :],
                          sb["m_t1"][:, :]), deps=[a3], name="m_a4")
            for c0 in range(0, S, 512):
                sch.emit("tensor", lambda E, c0=c0, dt=dt: E.matmul(
                    psE[0:64, c0:c0 + 512], sb["m_gw"][:, dt * 64:(dt + 1) * 64],
                    sb["m_xnb"][:, dt * 2048 + c0: dt * 2048 + c0 + 512],
                    start=(dt == 0), stop=False), deps=wl + [a2, t3], name="m_gmm")
                sch.emit("tensor", lambda E, c0=c0, dt=dt: E.matmul(
                    psE[0:64, c0:c0 + 512], sb["m_gwl"][:, dt * 64:(dt + 1) * 64],
                    sb["m_xnb"][:, dt * 2048 + c0: dt * 2048 + c0 + 512],
                    start=False, stop=False), deps=[a2], name="m_gml")
                g_last = sch.emit("tensor", lambda E, c0=c0, dt=dt: E.matmul(
                    psE[0:64, c0:c0 + 512], sb["m_gw"][:, dt * 64:(dt + 1) * 64],
                    sb["m_xlo"][:, c0:c0 + 512],
                    start=False, stop=(dt == NDT - 1)), deps=[a4], name="m_gmx")
            prev = [a3]
        sch.stage()
        gev = []
        for c0 in range(0, 2048, 512):
            gev.append(sch.emit("vector", lambda E, c0=c0: E.tensor_copy(
                sb["m_ctm"][0:64, 0:512], psE[0:64, c0:c0 + 512]),
                deps=[g_last] + gev[-1:], name="m_gev"))
            for q in range(4):
                tt = c0 // 128 + q
                k = (tt) % 2
                ps = [psC, psD][k]
                tmx = sch.emit("tensor", lambda E, q=q, ps=ps: E.transpose(
                    ps[:, 0:64], sb["m_ctm"][0:64, q * 128:(q + 1) * 128],
                    ident[0:64, 0:64]), deps=[gev[-1]], name="m_gtr")
                evx = sch.emit("vector", lambda E, tt=tt, ps=ps: E.tensor_copy(
                    sb["m_sc"][:, tt * 64:(tt + 1) * 64], ps[:, 0:64]), deps=[tmx],
                    name="m_gte")
                gev.append(evx)
        if "dbg_sc" in d and b == 0 and P == 0:
            sch.dma("sync", d["dbg_sc"][:, :], sb["m_sc"][:, :], deps=list(gev),
                    name="dbg_sc")
        sch.stage()
        tr_prev = {0: None, 1: None}
        for tt in range(16):
            evs = []
            for dt in range(NDT):
                k = dt % 2
                ps = [psC, psD][k]
                tm = sch.emit("tensor", lambda E, dt=dt, tt=tt, ps=ps: E.transpose(
                    ps[:, 0:64].bitcast(BF16), sb["m_xnb"][:, dt * 2048 + tt * 128: dt * 2048 + tt * 128 + 128],
                    sb["ident128b"][:, :]), deps=[tr_prev[k]], name="m_tr")
                ev = sch.emit("vector", lambda E, dt=dt, ps=ps: E.tensor_copy(
                    sb["m_ctm"][:, dt * 128:(dt + 1) * 128], ps[:, 0:64].bitcast(BF16)), deps=[tm],
                    name="m_tre")
                tr_prev[k] = ev
                evs.append(ev)
            cb = sch.emit("vector", lambda E: E.tensor_copy(
                sb["m_rsum"][:, :].bitcast(BF16)[:, 0:1024] if False else sb["m_rec"][:, 0:384],
                sb["m_ctm"][:, 0:384]), deps=evs, name="m_dum") if False else None
            cb = sch.emit("vector", lambda E: E.tensor_copy(sb["m_xnbt"][:, :],
                          sb["m_ctm"][:, :]), deps=evs, name="m_cb")
            xd = sch.dma(qs[tt % 3], ag_xn_i[b * 2048 + tt * 128: b * 2048 + tt * 128 + 128, :],
                         sb["m_xnbt"][:, :], deps=[cb], name="m_xnd")
            xn_dma.append(xd)
            tr_prev = {0: xd, 1: xd}
        sch.stage()
        sch.set_phase(f"m{P}_z{b}")
        for tc in range(4):
            zh_toks = []
            wk_free = [None, None]
            for scn in range(11):
                kb = scn % 2
                wkl = sch.dma("sync", sb["z_wk"][:, kb * 2048:(kb + 1) * 2048],
                              wsz[scn, :, :, :], deps=[wk_free[kb]] + zl, name="m_zwl")
                mm1 = mm3 = None
                for dt in range(NDT):
                    mm1 = sch.emit("tensor", lambda E, scn=scn, dt=dt, tc=tc, kb=kb: E.matmul(
                        psA[:, 0:512],
                        sb["z_wk"][:, kb * 2048 + dt * 256: kb * 2048 + dt * 256 + 128],
                        sb["m_xnb"][:, dt * 2048 + tc * 512: dt * 2048 + tc * 512 + 512],
                        start=(dt == 0), stop=(dt == NDT - 1), skip_group_check=True),
                        deps=[wkl] + zh_toks, name="m_z1m")
                    mm3 = sch.emit("tensor", lambda E, scn=scn, dt=dt, tc=tc, kb=kb: E.matmul(
                        psB[:, 0:512],
                        sb["z_wk"][:, kb * 2048 + dt * 256 + 128: kb * 2048 + dt * 256 + 256],
                        sb["m_xnb"][:, dt * 2048 + tc * 512: dt * 2048 + tc * 512 + 512],
                        start=(dt == 0), stop=(dt == NDT - 1), skip_group_check=True),
                        deps=[wkl] + zh_toks, name="m_z3m")
                e1 = sch.emit("scalar", lambda E: E.activation(
                    out=sb["z_stmp"][:, :], in_=psA[:, 0:512], func=AF.Silu),
                    deps=[mm1], name="m_zsil")
                e2 = sch.emit("vector", lambda E, scn=scn: E.tensor_tensor(
                    sb["z_zh"][:, scn * 512:(scn + 1) * 512], sb["z_stmp"][:, :],
                    psB[:, 0:512], ALU.mult), deps=[e1, mm3], name="m_zhm")
                zh_toks.append(e2)
                wk_free[kb] = mm3
            ps_free = None
            w2_free = [None, None]
            for dt in range(NDT):
                kb = dt % 2
                w2l = sch.dma("sync", sb["z_w2k"][:, kb * 1408:(kb + 1) * 1408],
                              ws2z[dt, :, :, :], deps=[w2_free[kb]] + zl, name="m_z2wl")
                mmz = None
                for scn in range(11):
                    mmz = sch.emit("tensor", lambda E, dt=dt, scn=scn, kb=kb: E.matmul(
                        psE[:, (dt % 4) * 512:(dt % 4) * 512 + 512],
                        sb["z_w2k"][:, kb * 1408 + scn * 128: kb * 1408 + scn * 128 + 128],
                        sb["z_zh"][:, scn * 512:(scn + 1) * 512],
                        start=(scn == 0), stop=(scn == 10), skip_group_check=True),
                        deps=zh_toks + [w2l] + ([ps_free] if dt >= 4 else []), name="m_zmm")
                ze = sch.emit("vector", lambda E, dt=dt: E.tensor_copy(
                    sb["m_t1"][:, (dt % 4) * 512:(dt % 4) * 512 + 512],
                    psE[:, (dt % 4) * 512:(dt % 4) * 512 + 512]), deps=[mmz], name="m_ze")
                zd2 = sch.dma(qs[dt % 3], z_dram[dt * 128:(dt + 1) * 128, col0 + tc * 512: col0 + tc * 512 + 512],
                              sb["m_t1"][:, (dt % 4) * 512:(dt % 4) * 512 + 512],
                              deps=[ze], name="m_zd")
                ps_free = zd2
                w2_free[kb] = mmz
            sch.stage()
        sch.set_phase(f"m{P}_tk{b}")
        s1 = sch.emit("vector", lambda E: E.tensor_reduce(
            sb["m_srow"][:, :], sb["m_sc"][:, :].rearrange("p (a e) -> p a e", e=64),
            axis=AX.X, op=ALU.max), deps=[], name="m_sm1")
        s2 = sch.emit("vector", lambda E: E.tensor_tensor(
            sb["m_sc"][:, :].rearrange("p (a e) -> p a e", e=64),
            sb["m_sc"][:, :].rearrange("p (a e) -> p a e", e=64),
            sb["m_srow"][:, :].to_broadcast((128, 16, 64)), ALU.subtract), deps=[s1],
            name="m_sm2")
        s3 = sch.emit("scalar", lambda E: E.activation(out=sb["m_sc"][:, :],
                      in_=sb["m_sc"][:, :], func=AF.Exp), deps=[s2], name="m_sm3")
        s4 = sch.emit("vector", lambda E: E.tensor_reduce(
            sb["m_srow"][:, :], sb["m_sc"][:, :].rearrange("p (a e) -> p a e", e=64),
            axis=AX.X, op=ALU.add), deps=[s3], name="m_sm4")
        s5 = sch.emit("vector", lambda E: E.reciprocal(sb["m_srow"][:, :],
                      sb["m_srow"][:, :]), deps=[s4], name="m_sm5")
        s6 = sch.emit("vector", lambda E: E.tensor_tensor(
            sb["m_sc"][:, :].rearrange("p (a e) -> p a e", e=64),
            sb["m_sc"][:, :].rearrange("p (a e) -> p a e", e=64),
            sb["m_srow"][:, :].to_broadcast((128, 16, 64)), ALU.mult), deps=[s5],
            name="m_sm6")
        s7 = sch.emit("vector", lambda E: E.tensor_tensor(
            sb["m_sc2"][:, :].rearrange("p (a e) -> p a e", e=64),
            sb["m_sc"][:, :].rearrange("p (a e) -> p a e", e=64),
            sb["m_gbrep"][:, :].to_broadcast((128, 64, 16)).transpose((0, 2, 1)),
            ALU.add), deps=[s6, tgb2], name="m_sc2")
        prevv = [s7]
        eftoks = []
        for tt in range(16):
            tv = sch.emit("vector", lambda E, tt=tt: E.max_with_indices(
                sb["m_topv"][:, tt * 8:(tt + 1) * 8], sb["m_topi"][:, tt * 8:(tt + 1) * 8],
                sb["m_sc2"][:, tt * 64:(tt + 1) * 64]), deps=prevv, name="m_top")
            for j in range(TOPK):
                oh = sch.emit("vector", lambda E, tt=tt, j=j: E.tensor_tensor(
                    sb["m_oh"][:, :].rearrange("p (a e) -> p a e", a=1),
                    sb["m_sc2"][:, tt * 64:(tt + 1) * 64].rearrange("p (a e) -> p a e", a=1),
                    sb["m_topv"][:, tt * 8 + j: tt * 8 + j + 1].to_broadcast((128, 1, 64)),
                    ALU.is_equal), deps=[tv], name="m_oh")
                wj1 = sch.emit("vector", lambda E, tt=tt: E.tensor_tensor(
                    sb["m_ohp"][:, :], sb["m_sc"][:, tt * 64:(tt + 1) * 64],
                    sb["m_oh"][:, :], ALU.mult), deps=[oh], name="m_wj1")
                wj = sch.emit("vector", lambda E, tt=tt, j=j, b=b: E.tensor_reduce(
                    sb["m_wv"][:, (b * 16 + tt) * 6 + j: (b * 16 + tt) * 6 + j + 1],
                    sb["m_ohp"][:, :].rearrange("p (a e) -> p a e", a=1),
                    axis=AX.X, op=ALU.add), deps=[wj1], name="m_wj2")
                ef1 = sch.emit("vector", lambda E: E.tensor_tensor(
                    sb["m_ohp"][:, :], sb["m_oh"][:, :], sb["erow"][:, :],
                    ALU.mult), deps=[wj], name="m_ef1")
                ef2 = sch.emit("vector", lambda E, tt=tt, j=j, b=b: E.tensor_reduce(
                    sb["m_ef"][:, b * 96 + tt * 6 + j: b * 96 + tt * 6 + j + 1],
                    sb["m_ohp"][:, :].rearrange("p (a e) -> p a e", a=1),
                    axis=AX.X, op=ALU.add), deps=[ef1], name="m_ef2")
                prevv = [ef2]
                eftoks.append(ef2)
        ef = eftoks[-1]
        ctoks = []
        for g in range(4):
            efg = sb["m_ef"][:, b * 96 + g * 24: b * 96 + (g + 1) * 24]
            i1 = sch.emit("vector", lambda E, efg=efg: E.tensor_tensor(
                sb["m_I"][:, :].rearrange("p (t j e) -> p t j e", t=4, j=6, e=64),
                efg.rearrange("p (t j) -> p t j", j=6).to_broadcast((128, 4, 6, 64)),
                sb["erow"][:, :].to_broadcast((128, 64, 6)).to_broadcast((128, 64, 6, 4)).transpose((0, 3, 2, 1)),
                ALU.is_equal), deps=[ef], name="m_I")
            c1 = sch.emit("vector", lambda E, g=g, b=b: E.tensor_reduce(
                sb["m_c"][:, (b * 16 + g * 4) * 64: (b * 16 + (g + 1) * 4) * 64],
                sb["m_I"][:, :].rearrange("p (t j e) -> p t e j", t=4, j=6, e=64),
                axis=AX.X, op=ALU.add), deps=[i1], name="m_c")
            ctoks.append(c1)
        ms_toks = []
        for tt in range(16):
            k = tt % 2
            ps = [psA, psB][k]
            mp = sch.emit("tensor", lambda E, tt=tt, b=b, ps=ps: E.matmul(
                ps[:, 0:64], sb["ustri128"][:, :],
                sb["m_c"][:, (b * 16 + tt) * 64:(b * 16 + tt + 1) * 64],
                start=True, stop=True), deps=ctoks, name="m_mp")
            ev = sch.emit("vector", lambda E, tt=tt, b=b, ps=ps: E.tensor_copy(
                sb["m_pfx"][:, (b * 16 + tt) * 64:(b * 16 + tt + 1) * 64], ps[:, 0:64]),
                deps=[mp], name="m_mpe")
            ms = sch.emit("tensor", lambda E, tt=tt, b=b: E.matmul(
                [psC, psD][tt // 8][0:1, (tt % 8) * 64:(tt % 8) * 64 + 64],
                sb["ones_col"][:, 0:1],
                sb["m_c"][:, (b * 16 + tt) * 64:(b * 16 + tt + 1) * 64],
                start=True, stop=True), deps=ctoks, name="m_ms")
            ms_toks.append(ms)
        e1 = sch.emit("vector", lambda E, b=b: E.tensor_copy(
            sb["m_mrows"][0:1, b * 1024: b * 1024 + 512], psC[0:1, 0:512]),
            deps=ms_toks, name="m_tse1")
        e2 = sch.emit("vector", lambda E, b=b: E.tensor_copy(
            sb["m_mrows"][0:1, b * 1024 + 512: b * 1024 + 1024], psD[0:1, 0:512]),
            deps=ms_toks, name="m_tse2")
        ts_evac += [e1, e2]
        sch.stage()
    pass
    sch.stage()
    if cut <= 3:
        return
    sch.set_phase(f"m{P}_r3")
    tot = sch.emit("vector", lambda E: E.tensor_reduce(
        sb["m_mrows"][32:33, 0:64],
        sb["m_mrows"][0:1, 0:2048].rearrange("p (t e) -> p e t", t=32, e=64),
        axis=AX.X, op=ALU.add), deps=ts_evac, name="m_tot")
    cp = sch.emit("vector", lambda E: E.tensor_copy(sb["mu_row"][0:1, 64:128],
                  sb["m_mrows"][32:33, 0:64]), deps=[tot], name="m_totc")
    d1 = sch.dma("sync", ag_cnt_i[0:1, :], sb["mu_row"][0:1, 64:128], deps=[cp],
                 name="m_cd")
    if cut == 41:
        sch.stage()
        return
    ag1 = sch.emit("gpsimd", lambda E: E.collective_compute(
        "AllGather", ALU.bypass, replica_groups=[list(range(8))],
        ins=[ag_cnt_i[:, :]], outs=[ag_cnt_o[:, :]]), deps=[d1], name="m_ag1", is_cc=True)
    if cut == 42:
        sch.stage()
        return
    d2 = sch.dma("sync", sb["m_cnt8"][:, :], ag_cnt_o[:, :], deps=[ag1], name="m_cl")
    if cut == 43:
        sch.stage()
        return
    msm = sch.emit("tensor", lambda E: E.matmul(psA[0:1, 0:64], sb["m_rmask"][:, :],
                   sb["m_cnt8"][:, :], start=True, stop=True), deps=[d2, trm], name="m_cs")
    bp0 = sch.emit("vector", lambda E: E.tensor_copy(sb["mu_row"][0:1, 0:64],
                   psA[0:1, 0:64]), deps=[msm], name="m_bp0")
    sch.stage()
    if cut <= 4:
        return
    z1 = sch.emit("gpsimd", lambda E: E.memset(sb["m_rsum"][:, :], 0.0), deps=[], name="m_z1")
    zd = sch.dma("gpsimd", ag_rec_i[:, :].rearrange("(a b) c -> a (b c)", a=128),
                 sb["m_rsum"][:, :], deps=[z1], name="m_zd")
    prev_bp = [bp0, zd]
    for b in range(B):
        for tt in range(16):
            f1 = sch.emit("tensor", lambda E: E.matmul(
                psE[:, 0:64], sb["ones_row"][0:1, :], sb["mu_row"][0:1, 0:64],
                start=True, stop=True), deps=prev_bp, name="m_f1")
            f2 = sch.emit("vector", lambda E, b=b, tt=tt: E.tensor_tensor(
                sb["m_fp"][:, tt * 64:(tt + 1) * 64],
                sb["m_pfx"][:, (b * 16 + tt) * 64:(b * 16 + tt + 1) * 64],
                psE[:, 0:64], ALU.add), deps=[f1], name="m_f2")
            f3 = sch.emit("vector", lambda E, b=b, tt=tt: E.tensor_tensor(
                sb["mu_row"][0:1, 0:64], sb["mu_row"][0:1, 0:64],
                sb["m_mrows"][0:1, (b * 16 + tt) * 64:(b * 16 + tt + 1) * 64], ALU.add),
                deps=[f2], name="m_f3")
            prev_bp = [f3]
        for g in range(4):
            efg = sb["m_ef"][:, b * 96 + g * 24: b * 96 + (g + 1) * 24]
            i1 = sch.emit("vector", lambda E, efg=efg: E.tensor_tensor(
                sb["m_I"][:, :].rearrange("p (t j e) -> p t j e", t=4, j=6, e=64),
                efg.rearrange("p (t j) -> p t j", j=6).to_broadcast((128, 4, 6, 64)),
                sb["erow"][:, :].to_broadcast((128, 64, 6)).to_broadcast((128, 64, 6, 4)).transpose((0, 3, 2, 1)),
                ALU.is_equal), deps=prev_bp, name="m_I2")
            i2 = sch.emit("vector", lambda E, g=g: E.tensor_tensor(
                sb["m_I"][:, :].rearrange("p (t j e) -> p t j e", t=4, j=6, e=64),
                sb["m_I"][:, :].rearrange("p (t j e) -> p t j e", t=4, j=6, e=64),
                sb["m_fp"][:, g * 256:(g + 1) * 256].rearrange("p (t e) -> p t e", e=64).to_broadcast((128, 4, 64, 6)).transpose((0, 1, 3, 2)),
                ALU.mult), deps=[i1], name="m_I3")
            i3 = sch.emit("vector", lambda E, g=g: E.tensor_reduce(
                sb["m_pos"][:, g * 24:(g + 1) * 24],
                sb["m_I"][:, :].rearrange("p (t j e) -> p t j e", t=4, j=6, e=64),
                axis=AX.X, op=ALU.add), deps=[i2], name="m_I4")
            prev_bp = [i3]
        efb = sb["m_ef"][:, b * 96:(b + 1) * 96]
        q1 = sch.emit("vector", lambda E, efb=efb: E.tensor_tensor(
            sb["m_p6"][:, :].rearrange("p (t j q) -> p t j q", t=16, j=6, q=6),
            efb.rearrange("p (t j) -> p t j", j=6).to_broadcast((128, 16, 6, 6)),
            efb.rearrange("p (t q) -> p t q", q=6).to_broadcast((128, 16, 6, 6)).transpose((0, 1, 3, 2)),
            ALU.is_equal), deps=prev_bp, name="m_q1b")
        q2 = sch.emit("vector", lambda E: E.tensor_tensor(
            sb["m_p6"][:, :].rearrange("p (t jq) -> p t jq", jq=36),
            sb["m_p6"][:, :].rearrange("p (t jq) -> p t jq", jq=36),
            sb["lt6"][:, :].to_broadcast((128, 36, 16)).transpose((0, 2, 1)),
            ALU.mult), deps=[q1], name="m_q2b")
        q3 = sch.emit("vector", lambda E: E.tensor_reduce(
            sb["m_r1"][:, :], sb["m_p6"][:, :].rearrange("p (t j q) -> p t j q", j=6, q=6),
            axis=AX.X, op=ALU.add), deps=[q2], name="m_q3b")
        p1 = sch.emit("vector", lambda E: E.tensor_tensor(sb["m_pos"][:, :],
                      sb["m_pos"][:, :], sb["m_r1"][:, :], ALU.add), deps=[q3],
                      name="m_p1b")
        v1 = sch.emit("vector", lambda E: E.tensor_scalar(sb["m_val"][:, :],
                      sb["m_pos"][:, :], 512.0, None, ALU.is_lt), deps=[p1], name="m_v1")
        v2 = sch.emit("vector", lambda E, b=b: E.tensor_tensor(
            sb["m_wv"][:, b * 96:(b + 1) * 96], sb["m_wv"][:, b * 96:(b + 1) * 96],
            sb["m_val"][:, :], ALU.mult), deps=[v1], name="m_v2")
        sl1 = sch.emit("vector", lambda E, b=b, efb=efb: E.scalar_tensor_tensor(
            sb["m_slot"][:, :], efb, 512.0, sb["m_pos"][:, :],
            ALU.mult, ALU.add), deps=[p1], name="m_sl1")
        sl2 = sch.emit("vector", lambda E: E.tensor_scalar(
            sb["m_val"][:, :], sb["m_val"][:, :], -1e9, 1e9, ALU.mult, ALU.add),
            deps=[v2, sl1], name="m_sl2")
        sl3 = sch.emit("vector", lambda E: E.tensor_tensor(
            sb["m_pos"][:, :], sb["m_slot"][:, :], sb["m_val"][:, :],
            ALU.add), deps=[sl2], name="m_sl3")
        sl4 = sch.emit("vector", lambda E, b=b: E.tensor_copy(
            sb["m_s2u"][:, b * 96:(b + 1) * 96], sb["m_pos"][:, :]), deps=[sl3],
            name="m_s2u")
        r0 = sch.emit("gpsimd", lambda E: E.memset(sb["m_rec"][:, :], 0.0), deps=[sl4],
                      name="m_r0")
        r1t = sch.emit("vector", lambda E, b=b: E.tensor_copy(
            sb["m_rec"][:, :].rearrange("p (t j c) -> p t j c", t=16, j=6, c=4)[:, :, :, 0:1],
            sb["m_tokid"][:, b * 16:(b + 1) * 16].to_broadcast((128, 16, 6)).to_broadcast((128, 16, 6, 1))),
            deps=[r0], name="m_r1")
        r2t = sch.emit("vector", lambda E, b=b: E.tensor_copy(
            sb["m_rec"][:, :].rearrange("p (t j c) -> p t j c", t=16, j=6, c=4)[:, :, :, 1:2],
            sb["m_wv"][:, b * 96:(b + 1) * 96].rearrange("p (t j) -> p t j", j=6).to_broadcast((128, 16, 6, 1))),
            deps=[r0, v2], name="m_r2")
        prev_sc = [r1t, r2t, sl4, zd]
        for col in range(96):
            st = sch.emit("gpsimd", lambda E, col=col, b=b: E.indirect_dma_start(
                ag_rec_i[:, :],
                bass.IndirectOffsetOnAxis(ap=sb["m_s2u"][:, b * 96 + col: b * 96 + col + 1], axis=0),
                sb["m_rec"][:, col * 4:(col + 1) * 4], None,
                bounds_check=bounds_reg(E), oob_is_err=False),
                is_dma=True, deps=prev_sc, name="m_rsc")
            prev_sc = [st]
        sch.stage()
    sch.stage()
    if cut <= 5:
        return
    if "dbg_s2u" in d and P == 0:
        sch.dma("sync", d["dbg_s2u"][:, :], sb["m_s2u"][:, :], deps=[], name="dbg_s2u")
        sch.dma("sync", d["dbg_wv"][:, :], sb["m_wv"][:, :], deps=[], name="dbg_wv")
        sch.dma("scalar", d["dbg_reci"][:, :].rearrange("(a b) c -> a (b c)", a=128),
                ag_rec_i[:, :].rearrange("(a b) c -> a (b c)", a=128), deps=[],
                name="dbg_reci")
        sch.stage()
    agx = sch.emit("gpsimd", lambda E: E.collective_compute(
        "AllGather", ALU.bypass, replica_groups=[list(range(8))],
        ins=[ag_xn_i[:, :]], outs=[ag_xn_o[:, :]]), deps=xn_dma, name="m_agx", is_cc=True)
    agr = sch.emit("gpsimd", lambda E: E.collective_compute(
        "AllGather", ALU.bypass, replica_groups=[list(range(8))],
        ins=[ag_rec_i[:, :]], outs=[ag_rec_o[:, :]]), deps=[], name="m_agr", is_cc=True)
    sch.stage()
    if cut <= 6:
        return
    acc = sch.dma("sync", sb["m_rsum"][:, :],
                  ag_rec_o[0:32768, :].rearrange("(a b) c -> a (b c)", a=128),
                  deps=[agr], name="m_rl0")
    for c in range(1, 8):
        tl = sch.dma("sync", sb["m_ctm"][:, :],
                     ag_rec_o[c * 32768:(c + 1) * 32768, :].rearrange("(a b) c -> a (b c)", a=128),
                     deps=[acc], name="m_rl")
        acc = sch.emit("vector", lambda E: E.tensor_tensor(
            sb["m_rsum"][:, :], sb["m_rsum"][:, :], sb["m_ctm"][:, :], ALU.add),
            deps=[tl], name="m_racc")
    mw = sch.dma("sync", ag_rec_i[:, :].rearrange("(a b) c -> a (b c)", a=128),
                 sb["m_rsum"][:, :], deps=[acc], name="m_rw")
    if "dbg_rec" in d and P == 0:
        sch.dma("scalar", d["dbg_rec"][:, :].rearrange("(a b) c -> a (b c)", a=128),
                sb["m_rsum"][:, :], deps=[acc], name="dbg_rec")
    sch.stage()
    if cut <= 7:
        return
    sch.set_phase(f"m{P}_ex")
    ctx.scratch_reset()
    ctx.scr("e_w13", (128, NDT * 1408), BF16)
    ctx.scr("e_w2", (128, 6 * 1024), BF16)
    ctx.scr("e_xbt", (128, 4 * 1024), BF16)
    ctx.scr("e_xbf", (128, NDT * 512), BF16)
    ctx.scr("e_h", (128, 6 * 512), BF16)
    ctx.scr("e_stmp", (128, 512), F32)
    ctx.scr("e_ybs", (128, 512), F32)
    ctx.scr("e_ct4", (128, 4096), F32)
    gsl = sch.dma("sync", sb["m_xidx"][:, :], d["myslots"][:, :], deps=[], name="m_msl")
    gts = []
    for g in range(32):
        gt = sch.emit("gpsimd", lambda E, g=g: E.indirect_dma_start(
            sb["m_rsum"][:, g * 4:(g + 1) * 4], None,
            ag_rec_i[:, :],
            bass.IndirectOffsetOnAxis(ap=sb["m_xidx"][:, g:g + 1], axis=0),
            bounds_check=bounds_reg(E), oob_is_err=False),
            is_dma=True, deps=[gsl, mw], name="m_grec")
        gts.append(gt)
    x1 = sch.emit("vector", lambda E: E.tensor_scalar(
        sb["m_ctm"][:, 0:32].rearrange("p (g c) -> p g c", c=1),
        sb["m_rsum"][:, 0:128].rearrange("p (g c) -> p g c", c=4)[:, :, 0:1],
        -1.0, None, ALU.add), deps=gts, name="m_xi1")
    x2 = sch.emit("vector", lambda E: E.tensor_scalar(
        sb["m_ctm"][:, 32:64], sb["m_ctm"][:, 0:32], 0.0, None, ALU.is_lt),
        deps=[x1], name="m_xi2")
    x3 = sch.emit("vector", lambda E: E.scalar_tensor_tensor(
        sb["m_ctm"][:, 0:32], sb["m_ctm"][:, 32:64], 1e9, sb["m_ctm"][:, 0:32],
        ALU.mult, ALU.add), deps=[x2], name="m_xi3")
    x4 = sch.emit("vector", lambda E: E.tensor_copy(sb["m_xidx"][:, :],
                  sb["m_ctm"][:, 0:32]), deps=[x3], name="m_xi4")
    sch.stage()
    prev_e = []
    cb_dma = []
    for e in range(EPERC):
        wts = []
        prev_l = list(prev_e)
        for dt in range(NDT):
            l1 = sch.dma("sync", sb["m_wst"][:, 0:704],
                         d["w1T_my"][e, dt * 128:(dt + 1) * 128, :], deps=prev_l, name="m_w1l")
            c1 = sch.emit("vector", lambda E, dt=dt: E.tensor_copy(
                sb["e_w13"][:, dt * 1408: dt * 1408 + 704], sb["m_wst"][:, 0:704]),
                deps=[l1], name="m_w1c")
            l2 = sch.dma("scalar", sb["m_wst"][:, 0:704],
                         d["w3T_my"][e, dt * 128:(dt + 1) * 128, :], deps=[c1], name="m_w3l")
            c2 = sch.emit("vector", lambda E, dt=dt: E.tensor_copy(
                sb["e_w13"][:, dt * 1408 + 704: dt * 1408 + 1408], sb["m_wst"][:, 0:704]),
                deps=[l2], name="m_w3c")
            prev_l = [c2]
            wts.append(c2)
        for ic in range(6):
            kk = 128 if ic < 5 else 64
            l3 = sch.dma("gpsimd", sb["m_wst"][0:kk, :],
                         d["w2T_my"][e, ic * 128: ic * 128 + kk, :], deps=prev_l, name="m_w2l")
            c3 = sch.emit("vector", lambda E, ic=ic, kk=kk: E.tensor_copy(
                sb["e_w2"][0:kk, ic * 1024:(ic + 1) * 1024], sb["m_wst"][0:kk, :]),
                deps=[l3], name="m_w2c")
            prev_l = [c3]
            wts.append(c3)
        if cut == 71:
            sch.stage()
            return
        z2 = sch.emit("gpsimd", lambda E: E.memset(sb["e_xbt"][:, :], 0.0),
                      deps=prev_e + [x4], name="m_xz")
        gats = []
        for sg in range(4):
            gx = sch.emit("gpsimd", lambda E, sg=sg, e=e: E.indirect_dma_start(
                sb["e_xbt"][:, sg * 1024:(sg + 1) * 1024], None,
                ag_xn_o[:, :],
                bass.IndirectOffsetOnAxis(ap=sb["m_xidx"][:, e * 4 + sg: e * 4 + sg + 1], axis=0),
                bounds_check=bounds_reg(E), oob_is_err=False),
                is_dma=True, deps=[z2, x4, agx], name="m_gx")
            gats.append(gx)
        if cut == 72:
            sch.stage()
            return
        trp = {0: None, 1: None}
        tr_toks = []
        for sg in range(4):
            for dt in range(NDT):
                k = (sg * NDT + dt) % 2
                ps = [psC, psD][k]
                tmt = sch.emit("tensor", lambda E, sg=sg, dt=dt, ps=ps: E.transpose(
                    ps[:, 0:64].bitcast(BF16), sb["e_xbt"][:, sg * 1024 + dt * 128: sg * 1024 + dt * 128 + 128],
                    sb["ident128b"][:, :]), deps=[trp[k]] + gats, name="m_xtr")
                ev = sch.emit("vector", lambda E, sg=sg, dt=dt, ps=ps: E.tensor_copy(
                    sb["e_xbf"][:, dt * 512 + sg * 128: dt * 512 + sg * 128 + 128],
                    ps[:, 0:64].bitcast(BF16)), deps=[tmt], name="m_xte")
                trp[k] = ev
                tr_toks.append(ev)
        if cut == 73:
            sch.stage()
            return
        hts = []
        for ic in range(6):
            ni = 128 if ic < 5 else 64
            mm1 = mm3 = None
            for dt in range(NDT):
                mm1 = sch.emit("tensor", lambda E, ic=ic, dt=dt, ni=ni: E.matmul(
                    psA[0:ni, 0:512], sb["e_w13"][:, dt * 1408 + ic * 128: dt * 1408 + ic * 128 + ni],
                    sb["e_xbf"][:, dt * 512:(dt + 1) * 512],
                    start=(dt == 0), stop=(dt == NDT - 1), skip_group_check=True),
                    deps=wts + tr_toks + hts, name="m_h1")
                mm3 = sch.emit("tensor", lambda E, ic=ic, dt=dt, ni=ni: E.matmul(
                    psB[0:ni, 0:512], sb["e_w13"][:, dt * 1408 + 704 + ic * 128: dt * 1408 + 704 + ic * 128 + ni],
                    sb["e_xbf"][:, dt * 512:(dt + 1) * 512],
                    start=(dt == 0), stop=(dt == NDT - 1), skip_group_check=True),
                    deps=wts + tr_toks + hts, name="m_h3")
            e1 = sch.emit("scalar", lambda E, ni=ni: E.activation(
                out=sb["e_stmp"][0:ni, :], in_=psA[0:ni, 0:512], func=AF.Silu),
                deps=[mm1], name="m_sil")
            e2 = sch.emit("vector", lambda E, ic=ic, ni=ni: E.tensor_tensor(
                sb["e_h"][0:ni, ic * 512:(ic + 1) * 512], sb["e_stmp"][0:ni, :],
                psB[0:ni, 0:512], ALU.mult), deps=[e1, mm3], name="m_hm")
            hts.append(e2)
        if cut == 74:
            sch.stage()
            return
        ct_toks = []
        ytp = {0: None, 1: None}
        for dt in range(NDT):
            mm2 = None
            for ic in range(6):
                kk = 128 if ic < 5 else 64
                mm2 = sch.emit("tensor", lambda E, dt=dt, ic=ic, kk=kk: E.matmul(
                    psE[:, (dt % 4) * 512:(dt % 4) * 512 + 512],
                    sb["e_w2"][0:kk, ic * 1024 + dt * 128: ic * 1024 + dt * 128 + 128],
                    sb["e_h"][0:kk, ic * 512:(ic + 1) * 512],
                    start=(ic == 0), stop=(ic == 5), skip_group_check=True),
                    deps=hts + ct_toks, name="m_ybm")
            ev = sch.emit("vector", lambda E, dt=dt: E.tensor_copy(
                sb["e_ybs"][:, :], psE[:, (dt % 4) * 512:(dt % 4) * 512 + 512]),
                deps=[mm2] + ct_toks, name="m_ybe")
            sg_evs = []
            for sg in range(4):
                k = sg % 2
                ps = [psC, psD][k]
                tmt = sch.emit("tensor", lambda E, sg=sg, ps=ps: E.transpose(
                    ps[:, 0:128], sb["e_ybs"][:, sg * 128:(sg + 1) * 128], ident[:, :]),
                    deps=[ev, ytp.get(k)] + tr_toks, name="m_ytr")
                ev2 = sch.emit("vector", lambda E, sg=sg, dt=dt, ps=ps: E.tensor_copy(
                    sb["e_ct4"][:, sg * 1024 + dt * 128: sg * 1024 + dt * 128 + 128],
                    ps[:, 0:128]), deps=[tmt], name="m_yte")
                ytp[k] = ev2
                sg_evs.append(ev2)
            ct_toks = sg_evs
        eds = []
        for sg in range(4):
            ed = sch.dma(qs[sg % 3], ag_cb_i[e * 512 + sg * 128: e * 512 + (sg + 1) * 128, :],
                         sb["e_ct4"][:, sg * 1024:(sg + 1) * 1024], deps=ct_toks, name="m_cbd")
            eds.append(ed)
            cb_dma.append(ed)
            if "dbg_cb" in d and P == 0:
                sch.dma(qs[sg % 3], d["dbg_cb"][e * 512 + sg * 128: e * 512 + (sg + 1) * 128, :],
                        sb["e_ct4"][:, sg * 1024:(sg + 1) * 1024], deps=ct_toks, name="dbg_cb")
        prev_e = eds
        sch.stage()
    sch.stage()
    if cut <= 8:
        return
    agc = sch.emit("gpsimd", lambda E: E.collective_compute(
        "AllGather", ALU.bypass, replica_groups=[list(range(8))],
        ins=[ag_cb_i[:, :]], outs=[ag_cb_o[:, :]]), deps=cb_dma, name="m_agc", is_cc=True)
    sch.stage()
    if cut <= 9:
        return
    sch.set_phase(f"m{P}_cb")
    ctx.scratch_reset()
    ctx.scr("c_gat", (128, 1024), F32)
    ctx.scr("c_ytm4", (128, 4096), F32)
    ctx.scr("c_x2", (128, 512), F32)
    ctx.scr("c_zt", (128, 512), F32)
    for b in range(B):
        col0 = b * S
        for tc in range(4):
            ycols = []
            for q in range(4):
                tt = tc * 4 + q
                yq = sb["c_ytm4"][:, q * 1024:(q + 1) * 1024]
                z3 = sch.emit("gpsimd", lambda E, yq=yq: E.memset(yq, 0.0),
                              deps=ycols, name="m_yz2")
                acc2 = [z3]
                for j in range(6):
                    col = b * 96 + tt * 6 + j
                    gx = sch.emit("gpsimd", lambda E, col=col: E.indirect_dma_start(
                        sb["c_gat"][:, :], None,
                        ag_cb_o[:, :],
                        bass.IndirectOffsetOnAxis(ap=sb["m_s2u"][:, col:col + 1], axis=0),
                        bounds_check=bounds_reg(E), oob_is_err=False),
                        is_dma=True, deps=acc2 + [agc], name="m_yg")
                    a1 = sch.emit("vector", lambda E, col=col, yq=yq: E.scalar_tensor_tensor(
                        yq, sb["c_gat"][:, :],
                        sb["m_wv"][:, col:col + 1], yq,
                        ALU.mult, ALU.add), deps=[gx], name="m_ya")
                    acc2 = [a1]
                ycols.append(acc2[0])
            prev_x = []
            for dt in range(NDT):
                tlx = sch.dma("sync", sb["c_x2"][:, :],
                              x_fm[dt * 128:(dt + 1) * 128, col0 + tc * 512: col0 + tc * 512 + 512],
                              deps=prev_x, name="m_xl")
                tlz = sch.dma("scalar", sb["c_zt"][:, :],
                              z_dram[dt * 128:(dt + 1) * 128, col0 + tc * 512: col0 + tc * 512 + 512],
                              deps=prev_x, name="m_zl")
                ax = sch.emit("vector", lambda E: E.tensor_tensor(
                    sb["c_x2"][:, :], sb["c_x2"][:, :], sb["c_zt"][:, :], ALU.add),
                    deps=[tlx, tlz], name="m_ax")
                ay = [ax]
                for q in range(4):
                    k = q % 2
                    ps = [psC, psD][k]
                    tmt = sch.emit("tensor", lambda E, q=q, dt=dt, ps=ps: E.transpose(
                        ps[:, 0:128], sb["c_ytm4"][:, q * 1024 + dt * 128: q * 1024 + dt * 128 + 128],
                        ident[:, :]), deps=ycols + ay[-1:], name="m_ytr2")
                    av = sch.emit("vector", lambda E, q=q, ps=ps: E.tensor_tensor(
                        sb["c_x2"][:, q * 128:(q + 1) * 128], sb["c_x2"][:, q * 128:(q + 1) * 128],
                        ps[:, 0:128], ALU.add), deps=[tmt] + ay, name="m_ay")
                    ay.append(av)
                sx = sch.dma("gpsimd", x2_fm[dt * 128:(dt + 1) * 128, col0 + tc * 512: col0 + tc * 512 + 512],
                             sb["c_x2"][:, :], deps=ay, name="m_x2s")
                prev_x = [sx]
            sch.stage()
    sch.stage()
    return {}



def emit_attn_b(ctx, x_fm, x4_fm, deps):
    nc, sch = ctx.nc, ctx.sch
    sb = ctx.sb
    d = ctx.dram
    psA, psB, psC, psD, psE = (ctx.ps[k] for k in ("psA", "psB", "psC", "psD", "psE"))
    banks8 = [psA, psB, psC, psD] + [psE[:, k * 512:(k + 1) * 512] for k in range(4)]
    qs = ["sync", "scalar", "gpsimd"]

    wzi = ctx.dram_tmp("wzi", (32, 8, 128, 128), BF16)
    wzx = ctx.dram_tmp("wzx", (17, 16, 128, 128), BF16)
    wzo = ctx.dram_tmp("wzo", (8, 16, 128, 128), BF16)
    xs_d = ctx.dram_tmp("xs_d", (16, 128, 2048), BF16)
    slr_d = ctx.dram_tmp("slr_d", (16, 128, 2048), BF16)
    cse_d = ctx.dram_tmp("cse_d", (16, 128, 2048), BF16)
    dlt_d = ctx.dram_tmp("dlt_d", (16, 128, 2048), BF16)
    y2_d = ctx.dram_tmp("y2_d", (16, 128, 2048), BF16)
    xn3_d = ctx.dram_tmp("xn3_d", (8, 128, 2048), F32)
    bsel_d = ctx.dram_tmp("bsel_d", (64, 2048), BF16)

    sch.set_phase("ab_stage")
    ctx.scratch_reset()
    aA = ctx.scr("b_aA", (128, 1024), F32)
    bcols = ctx.scr("b_cols", (128, 112), F32)
    CBI, CBX, CBDT, CDP, CBO, CGAM, CBET = 0, 32, 49, 65, 81, 89, 97
    wdt_sb = ctx.scr("b_wdt", (64, 2048), BF16)
    wt = [ctx.scr("b_wt0", (128, 2048), BF16), ctx.scr("b_wt1", (128, 2048), BF16)]
    wstg = [ctx.scr("b_wstg0", (128, 128), BF16), ctx.scr("b_wstg1", (128, 128), BF16)]
    wldf = [ctx.scr("b_wldf0", (128, 128), F32), ctx.scr("b_wldf1", (128, 128), F32)]
    xt = ctx.scr("b_xt", (128, 2048), F32)
    sqb = ctx.scr("b_sqb", (128, 2048), F32)
    xstg = ctx.scr("b_xstg", (128, 2048), BF16)
    cstg = [ctx.scr("b_cstg0", (128, 512), BF16), ctx.scr("b_cstg1", (128, 512), BF16)]
    b64 = ctx.scr("b_b64", (64, 2048), BF16)
    dbr = ctx.scr("b_dbr", (128, 2048), F32)
    dtmp = ctx.scr("b_dtmp", (128, 512), F32)
    ctx.scr("b_hbn", (64, 64 * 128), BF16)
    BASE = ctx.mark()

    stage_jobs = []
    for ft in range(32):
        for dt in range(NDT):
            stage_jobs.append((d["b_Wi"], dt, ft, wzi[ft, dt]))
    for ft in range(17):
        for dti in range(16):
            stage_jobs.append((d["b_Wx"], dti, ft, wzx[ft, dti]))
    for ft in range(8):
        for dblk in range(16):
            stage_jobs.append((d["b_Wo"], dblk, ft, wzo[ft, dblk]))
    prev_l = {0: None, 1: None}
    prev_s = {0: None, 1: None}
    wz_done = []
    for i, (src, kt, ft, dst) in enumerate(stage_jobs):
        k = i % 2
        q = ["sync", "scalar"][k]
        l = sch.dma(q, wldf[k][:, :], src[kt * 128:(kt + 1) * 128, ft * 128:(ft + 1) * 128],
                    deps=[prev_l[k]] + list(deps), name="bw_l")
        c = sch.emit("vector", lambda E, k=k: E.tensor_copy(wstg[k][:, :], wldf[k][:, :]),
                     deps=[l, prev_s[k]], name="bw_c")
        s = sch.dma(q, dst, wstg[k][:, :], deps=[c], name="bw_s")
        prev_l[k] = c
        prev_s[k] = s
    wz_done = [prev_s[0], prev_s[1]]
    t_bi = sch.dma("gpsimd", bcols[:, CBI:CBI + 32], d["b_bi_c"][:, :], deps=deps, name="b_bi")
    t_bx = sch.dma("gpsimd", bcols[:, CBX:CBX + 17], d["b_bx_c"][:, :], deps=deps, name="b_bx")
    t_bdt = sch.dma("gpsimd", bcols[:, CBDT:CBDT + 16], d["b_bdt_c"][:, :], deps=deps, name="b_bdt")
    t_dp = sch.dma("gpsimd", bcols[:, CDP:CDP + 16], d["b_Dp_c"][:, :], deps=deps, name="b_dp")
    t_bo = sch.dma("gpsimd", bcols[:, CBO:CBO + 8], d["b_bo_c"][:, :], deps=deps, name="b_bo")
    t_gm = sch.dma("gpsimd", bcols[:, CGAM:CGAM + 8], d["b_gam_c"][:, :], deps=deps, name="b_gm")
    t_bt = sch.dma("gpsimd", bcols[:, CBET:CBET + 8], d["b_bet_c"][:, :], deps=deps, name="b_bt")
    t_al = sch.dma("gpsimd", sqb[:, 0:1024], d["b_Alog_r"][:, :], deps=deps, name="b_al")
    t_a1 = sch.emit("scalar", lambda E: E.activation(out=aA[:, :], in_=sqb[:, 0:1024],
                    func=AF.Exp), deps=[t_al], name="b_a1")
    t_a2 = sch.emit("vector", lambda E: E.tensor_scalar(aA[:, :], aA[:, :], -1.0, None,
                    ALU.mult), deps=[t_a1], name="b_a2")
    prev = [t_a2]
    for h in range(2):
        l = sch.dma("sync", xt[0:64, 0:1024], d["b_Wdt"][:, h * 1024:(h + 1) * 1024],
                    deps=prev, name="b_wdl")
        c = sch.emit("vector", lambda E, h=h: E.tensor_copy(
            wdt_sb[:, h * 1024:(h + 1) * 1024], xt[0:64, 0:1024]), deps=[l], name="b_wdc")
        prev = [c]
    for n in range(64):
        sch.emit("vector", lambda E, n=n: E.tensor_copy(
            sb["b_hbn"][0:64, n * 128:(n + 1) * 128],
            sb["ident128b"][0:64, n:n + 1].to_broadcast((64, 128))),
            deps=deps, name="b_hb")
    sch.stage()

    for b in range(B):
        col0 = b * S
        ctx.reset_to(BASE)
        plmu = ctx.scr("b_plmu", (128, 2048), F32)
        plr = ctx.scr("b_plr", (128, 2048), F32)
        MARKB = ctx.mark()
        xnb = ctx.scr("b_xnb", (128, 8 * 2048), BF16)

        sch.set_phase(f"ab_L{b}")
        mm_toks = []
        prev = []
        for dt in range(NDT):
            tl = sch.dma("sync", xt[:, :], x_fm[dt * 128:(dt + 1) * 128, col0:col0 + S],
                         deps=prev, name="bL_x")
            tq = sch.emit("scalar", lambda E: E.activation(out=sqb[:, :], in_=xt[:, :],
                          func=AF.Square), deps=[tl] + prev, name="bL_sq")
            m1 = m2 = None
            for c0 in range(0, S, 512):
                m1 = sch.emit("tensor", lambda E, c0=c0, dt=dt: E.matmul(
                    psE[0:1, c0:c0 + 512], sb["ones_col"][:, 0:1], xt[:, c0:c0 + 512],
                    start=(dt == 0), stop=(dt == NDT - 1)), deps=[tl], name="bL_s")
                m2 = sch.emit("tensor", lambda E, c0=c0, dt=dt: E.matmul(
                    [psA, psB, psC, psD][c0 // 512][0:1, 0:512], sb["ones_col"][:, 0:1],
                    sqb[:, c0:c0 + 512],
                    start=(dt == 0), stop=(dt == NDT - 1)), deps=[tq], name="bL_q")
                if dt == NDT - 1:
                    mm_toks += [m1, m2]
            prev = [tq, m1, m2]
        mu, var = sb["mu_row"], sb["var_row"]
        t1 = sch.emit("vector", lambda E: E.tensor_scalar(mu[0:1, :], psE[0:1, :],
                      1.0 / D, None, ALU.mult), deps=mm_toks, name="bL_mu")
        sqc = []
        for c0 in range(0, 2048, 512):
            sqc.append(sch.emit("vector", lambda E, c0=c0: E.tensor_copy(
                var[0:1, c0:c0 + 512], [psA, psB, psC, psD][c0 // 512][0:1, 0:512]),
                deps=mm_toks, name="bL_sc"))
        t2 = sch.emit("vector", lambda E: E.tensor_tensor(psE[0:1, :], mu[0:1, :],
                      mu[0:1, :], ALU.mult), deps=[t1], name="bL_m2")
        t3 = sch.emit("vector", lambda E: E.scalar_tensor_tensor(var[0:1, :], var[0:1, :],
                      1.0 / D, psE[0:1, :], ALU.mult, ALU.subtract),
                      deps=[t2] + sqc, name="bL_v")
        t4 = sch.emit("scalar", lambda E: E.activation(out=var[0:1, :], in_=var[0:1, :],
                      func=AF.Sqrt, bias=sb["eps_col"][0:1, 0:1]), deps=[t3], name="bL_st")
        t5 = sch.emit("vector", lambda E: E.reciprocal(var[0:1, :], var[0:1, :]),
                      deps=[t4], name="bL_rc")
        bmu = emit_bcast(ctx, plmu, mu, [t1, t3], "bL_bm")
        t5b = sch.emit("vector", lambda E: E.tensor_copy(mu[0:1, :], var[0:1, :]),
                       deps=[t5] + bmu, name="bL_r0")
        brr = emit_bcast(ctx, plr, mu, [t5b], "bL_br")
        sch.stage()
        prev = []
        for dt in range(NDT):
            tl = sch.dma("sync", xt[:, :], x_fm[dt * 128:(dt + 1) * 128, col0:col0 + S],
                         deps=prev, name="bLx2")
            a1 = sch.emit("vector", lambda E: E.tensor_tensor(sqb[:, :], xt[:, :],
                          plmu[:, :], ALU.subtract), deps=[tl], name="bLa1")
            a2 = sch.emit("vector", lambda E: E.tensor_tensor(sqb[:, :], sqb[:, :],
                          plr[:, :], ALU.mult), deps=[a1], name="bLa2")
            sd = sch.dma("scalar", xn3_d[dt, :, :], sqb[:, :], deps=[a2], name="bLxn")
            cb = sch.emit("vector", lambda E, dt=dt: E.tensor_copy(
                xnb[:, dt * 2048:(dt + 1) * 2048], sqb[:, :]), deps=[a2], name="bLcb")
            prev = [tl, sd, cb]
        sch.stage()

        sch.set_phase(f"ab_I{b}")
        prev_mm = {0: [], 1: []}
        prev_ev = []
        prev_st = None
        for ft in range(32):
            k = ft % 2
            wl = []
            for dt in range(NDT):
                wl.append(sch.dma(qs[dt % 3], wt[k][:, dt * 128:(dt + 1) * 128],
                          wzi[ft, dt], deps=prev_mm[k] + wz_done, name="bI_w"))
            evs = []
            for c in range(4):
                mm = None
                for dt in range(NDT):
                    mm = sch.emit("tensor", lambda E, c=c, dt=dt, k=k: E.matmul(
                        [psA, psB, psC, psD][c][:, 0:512],
                        wt[k][:, dt * 128:(dt + 1) * 128],
                        xnb[:, dt * 2048 + c * 512: dt * 2048 + c * 512 + 512],
                        start=(dt == 0), stop=(dt == NDT - 1)),
                        deps=wl + prev_ev, name="bI_mm")
                ev = sch.emit("scalar", lambda E, c=c, ft=ft: E.activation(
                    out=xstg[:, c * 512:(c + 1) * 512], in_=[psA, psB, psC, psD][c][:, 0:512],
                    func=AF.Silu, bias=bcols[:, CBI + ft:CBI + ft + 1]),
                    deps=[mm, prev_st], name="bI_ev")
                evs.append(ev)
            dst = xs_d[ft, :, :] if ft < 16 else slr_d[ft - 16, :, :]
            st = sch.dma(qs[ft % 3], dst, xstg[:, :], deps=evs, name="bI_st")
            prev_mm[k] = [mm]
            prev_ev = evs
            prev_st = st
        sch.stage()

        sch.set_phase(f"ab_X{b}")
        ctx.reset_to(MARKB)
        wtbig = ctx.scr("b_wtbig", (128, 16384), BF16)
        xsc = [ctx.scr("b_xsc0", (128, 512), BF16), ctx.scr("b_xsc1", (128, 512), BF16)]
        passes = [list(range(0, 8)), list(range(8, 16)), [16]]
        prev_pass = []
        for pss in passes:
            wl = []
            for fi, ft in enumerate(pss):
                for dti in range(16):
                    wl.append(sch.dma(qs[(fi + dti) % 3],
                              wtbig[:, (fi * 16 + dti) * 128:(fi * 16 + dti) * 128 + 128],
                              wzx[ft, dti], deps=prev_pass + wz_done, name="bX_w"))
            evs_all = []
            prev_x = {0: None, 1: None}
            for c in range(4):
                mms = {}
                for dti in range(16):
                    k = dti % 2
                    xl = sch.dma("sync", xsc[k][:, :], xs_d[dti, :, c * 512:(c + 1) * 512],
                                 deps=[prev_x[k]], name="bX_x")
                    for fi, ft in enumerate(pss):
                        mms[fi] = sch.emit("tensor", lambda E, fi=fi, dti=dti, k=k: E.matmul(
                            banks8[fi][:, 0:512],
                            wtbig[:, (fi * 16 + dti) * 128:(fi * 16 + dti) * 128 + 128],
                            xsc[k][:, :], start=(dti == 0), stop=(dti == 15)),
                            deps=wl + [xl] + evs_all, name="bX_mm")
                    prev_x[k] = mms[len(pss) - 1]
                evs = []
                for fi, ft in enumerate(pss):
                    if ft == 0:
                        ev = sch.emit("scalar", lambda E, c=c, fi=fi: E.activation(
                            out=dbr[:, c * 512:(c + 1) * 512], in_=banks8[fi][:, 0:512],
                            func=AF.Identity, bias=bcols[:, CBX:CBX + 1]),
                            deps=[mms[fi]], name="bX_e0")
                        evs.append(ev)
                    else:
                        kk = fi % 2
                        ev = sch.emit("scalar", lambda E, fi=fi, ft=ft, kk=kk: E.activation(
                            out=cstg[kk][:, :], in_=banks8[fi][:, 0:512],
                            func=AF.Identity, bias=bcols[:, CBX + ft:CBX + ft + 1]),
                            deps=[mms[fi]] + ([evs[fi - 2]] if fi >= 2 else []), name="bX_ec")
                        st = sch.dma(qs[fi % 3], cse_d[ft - 1, :, c * 512:(c + 1) * 512],
                                     cstg[kk][:, :], deps=[ev], name="bX_sc")
                        evs.append(st)
                evs_all = evs
            prev_pass = evs_all
        cb = sch.emit("vector", lambda E: E.tensor_copy(b64[:, :], dbr[64:128, :]),
                      deps=prev_pass, name="bX_bs")
        bsd = sch.dma("sync", bsel_d[:, :], b64[:, :], deps=[cb], name="bX_bd")
        sch.stage()

        sch.set_phase(f"ab_D{b}")
        cbd = sch.emit("vector", lambda E: E.tensor_copy(b64[:, :], dbr[0:64, :]),
                       deps=[bsd], name="bD_c")
        prev_ev = {}
        prev_ln = None
        for ft in range(16):
            for c in range(4):
                k = (ft * 4 + c) % 2
                ps = [psA, psB][k]
                mm = sch.emit("tensor", lambda E, ft=ft, c=c, ps=ps: E.matmul(
                    ps[:, 0:512], wdt_sb[:, ft * 128:(ft + 1) * 128],
                    b64[:, c * 512:(c + 1) * 512], start=True, stop=True),
                    deps=[cbd, prev_ev.get(k)], name="bD_mm")
                ev1 = sch.emit("scalar", lambda E, ft=ft, ps=ps: E.activation(
                    out=dtmp[:, :], in_=ps[:, 0:512], func=AF.Exp,
                    bias=bcols[:, CBDT + ft:CBDT + ft + 1]),
                    deps=[mm, prev_ev.get(k), prev_ln], name="bD_e1")
                ev = sch.emit("scalar", lambda E, k=k: E.activation(
                    out=cstg[k][:, :], in_=dtmp[:, :], func=AF.Ln, bias=1.0),
                    deps=[ev1], name="bD_e2")
                prev_ln = ev
                st = sch.dma(qs[c % 3], dlt_d[ft, :, c * 512:(c + 1) * 512],
                             cstg[k][:, :], deps=[ev], name="bD_st")
                prev_ev[k] = st
        sch.stage()

        sch.set_phase(f"ab_S{b}")
        ctx.reset_to(MARKB)
        pdl = [ctx.scr("bS_dl0", (128, 2048), BF16), ctx.scr("bS_dl1", (128, 2048), BF16)]
        pxs = [ctx.scr("bS_xs0", (128, 2048), BF16), ctx.scr("bS_xs1", (128, 2048), BF16)]
        pcs = [ctx.scr("bS_cs0", (128, 2048), BF16), ctx.scr("bS_cs1", (128, 2048), BF16)]
        psl = [ctx.scr("bS_sl0", (128, 2048), BF16), ctx.scr("bS_sl1", (128, 2048), BF16)]
        dxp = [ctx.scr("bS_dx0", (128, 2048), BF16), ctx.scr("bS_dx1", (128, 2048), BF16)]
        btil = [ctx.scr("bS_bt0", (128, 2048), BF16), ctx.scr("bS_bt1", (128, 2048), BF16)]
        dat = [[ctx.scr(f"bS_da{dd}{k}", (128, 2048), BF16) for k in range(2)] for dd in range(2)]
        dxt = [[ctx.scr(f"bS_dxt{dd}{k}", (128, 2048), BF16) for k in range(2)] for dd in range(2)]
        hti = [[ctx.scr(f"bS_h{dd}{k}", (128, 2048), BF16) for k in range(2)] for dd in range(2)]
        ytm = ctx.scr("bS_yt", (128, 2048), BF16)
        y2s = [ctx.scr("bS_y20", (128, 2048), BF16), ctx.scr("bS_y21", (128, 2048), BF16)]
        accs = [[psA, psB, psC, psD], [psE[:, k * 512:(k + 1) * 512] for k in range(4)]]
        prev_y2 = {0: [], 1: []}
        prev_ld = []
        for pair in range(8):
            lds = {}
            for dd in range(2):
                dblk = pair * 2 + dd
                l1 = sch.dma("sync", pdl[dd][:, :], dlt_d[dblk, :, :], deps=prev_ld, name="bS_l1")
                l2 = sch.dma("scalar", pxs[dd][:, :], xs_d[dblk, :, :], deps=prev_ld, name="bS_l2")
                l3 = sch.dma("gpsimd", pcs[dd][:, :], cse_d[dblk, :, :], deps=prev_ld, name="bS_l3")
                l4 = sch.dma("sync", psl[dd][:, :], slr_d[dblk, :, :], deps=prev_ld, name="bS_l4")
                dx = sch.emit("vector", lambda E, dd=dd: E.tensor_tensor(
                    dxp[dd][:, :], pdl[dd][:, :], pxs[dd][:, :], ALU.mult),
                    deps=[l1, l2], name="bS_dx")
                lds[dd] = [l1, l2, l3, l4, dx]
            prev_pe = {0: [], 1: []}
            prev_tts = {0: None, 1: None}
            prev_dxt = {}
            prev_bt = {}
            for n in range(64):
                k = n % 2
                bt0 = sch.dma("sync", btil[k][0:64, :],
                              bsel_d[n:n + 1, :].to_broadcast((64, 2048)),
                              deps=[bsd, prev_dxt.get((1, k))], name="bS_bt",
                              join=False)
                bt = sch.dma("gpsimd", btil[k][64:128, :],
                             bsel_d[n:n + 1, :].to_broadcast((64, 2048)),
                             deps=[bsd, prev_dxt.get((1, k))], name="bS_bt2",
                             join=False)
                prev_bt[k] = bt
                prev_bt[(k, 0)] = bt0
                for dd in range(2):
                    dblk = pair * 2 + dd
                    da = sch.emit("scalar", lambda E, dd=dd, k=k, dblk=dblk, n=n: E.activation(
                        out=dat[dd][k][:, :], in_=pdl[dd][:, :], func=AF.Exp,
                        scale=aA[:, dblk * 64 + n: dblk * 64 + n + 1]),
                        deps=lds[dd] + [prev_tts[dd]], name="bS_da")
                    dxe = sch.emit("gpsimd", lambda E, dd=dd, k=k: E.tensor_tensor(
                        dxt[dd][k][:, :], dxp[dd][:, :], btil[k][:, :], ALU.mult),
                        deps=[bt0, bt, lds[dd][4], prev_tts[dd]], name="bS_dxe")
                    prev_dxt[(dd, k)] = dxe
                    tts = sch.emit("vector", lambda E, dd=dd, k=k: E.tensor_tensor_scan(
                        hti[dd][k][:, :], dat[dd][k][:, :], dxt[dd][k][:, :], 0.0,
                        ALU.mult, ALU.add), deps=[da, dxe] + prev_pe[dd], name="bS_tts")
                    prev_tts[dd] = tts
                    pes = []
                    for c in range(4):
                        pes.append(sch.emit("tensor", lambda E, dd=dd, k=k, c=c, n=n: E.matmul(
                            accs[dd][c][:, 0:512], sb["ident128b"][:, :],
                            hti[dd][k][:, c * 512:(c + 1) * 512],
                            start=(n == 0), stop=(n == 63)),
                            deps=[tts] + prev_y2[dd], name="bS_pe"))
                    prev_pe[dd] = pes
            for dd in range(2):
                dblk = pair * 2 + dd
                w1s = []
                w3 = None
                for c in range(4):
                    w1 = sch.emit("vector", lambda E, dd=dd, c=c: E.tensor_tensor(
                        ytm[:, c * 512:(c + 1) * 512], accs[dd][c][:, 0:512],
                        pcs[dd][:, c * 512:(c + 1) * 512], ALU.mult),
                        deps=prev_pe[dd] + [lds[dd][2]] + prev_y2[dd], name="bS_w1")
                    w2 = sch.emit("vector", lambda E, dd=dd, c=c, dblk=dblk: E.scalar_tensor_tensor(
                        ytm[:, c * 512:(c + 1) * 512], pxs[dd][:, c * 512:(c + 1) * 512],
                        bcols[:, CDP + dblk:CDP + dblk + 1],
                        ytm[:, c * 512:(c + 1) * 512], ALU.mult, ALU.add),
                        deps=[w1], name="bS_w2")
                    w3 = sch.emit("vector", lambda E, dd=dd, c=c: E.tensor_tensor(
                        y2s[dd][:, c * 512:(c + 1) * 512], ytm[:, c * 512:(c + 1) * 512],
                        psl[dd][:, c * 512:(c + 1) * 512], ALU.mult),
                        deps=[w2, lds[dd][3]], name="bS_w3")
                    w1s.append(w1)
                    w1s.append(w3)
                st = sch.dma(qs[dd], y2_d[dblk, :, :], y2s[dd][:, :], deps=[w3] + w1s,
                             name="bS_y2d")
                prev_y2[dd] = w1s + [st]
            prev_ld = [prev_y2[0][-1], prev_y2[1][-1], prev_tts[0], prev_tts[1],
                       prev_bt[0], prev_bt[1]]
        sch.stage()

        sch.set_phase(f"ab_O{b}")
        ctx.reset_to(MARKB)
        wob = ctx.scr("bO_w", (128, 16384), BF16)
        ysc = [ctx.scr("bO_y0", (128, 512), BF16), ctx.scr("bO_y1", (128, 512), BF16)]
        xnt = [ctx.scr("bO_xn0", (128, 512), F32), ctx.scr("bO_xn1", (128, 512), F32)]
        zpl = ctx.scr("bO_z", (128, 8 * 2048), F32)
        wl = []
        for ft in range(8):
            for dblk in range(16):
                wl.append(sch.dma(qs[(ft + dblk) % 3],
                          wob[:, (ft * 16 + dblk) * 128:(ft * 16 + dblk) * 128 + 128],
                          wzo[ft, dblk], deps=wz_done, name="bO_w"))
        prev_ev = []
        prev_yl = {0: None, 1: None}
        prev_xl = {0: None, 1: None}
        for c in range(4):
            mms = {}
            for dblk in range(16):
                k = dblk % 2
                yl = sch.dma("sync", ysc[k][:, :], y2_d[dblk, :, c * 512:(c + 1) * 512],
                             deps=[prev_yl[k]], name="bO_yl")
                for ft in range(8):
                    mms[ft] = sch.emit("tensor", lambda E, ft=ft, dblk=dblk, k=k: E.matmul(
                        banks8[ft][:, 0:512],
                        wob[:, (ft * 16 + dblk) * 128:(ft * 16 + dblk) * 128 + 128],
                        ysc[k][:, :], start=(dblk == 0), stop=(dblk == 15)),
                        deps=wl + [yl] + prev_ev, name="bO_mm")
                prev_yl[k] = mms[7]
            evs = []
            for ft in range(8):
                k = ft % 2
                xl = sch.dma("scalar", xnt[k][:, :], xn3_d[ft, :, c * 512:(c + 1) * 512],
                             deps=[prev_xl[k]], name="bO_xl")
                e1 = sch.emit("scalar", lambda E, ft=ft, c=c: E.activation(
                    out=zpl[:, ft * 2048 + c * 512: ft * 2048 + c * 512 + 512],
                    in_=banks8[ft][:, 0:512], func=AF.Identity,
                    bias=bcols[:, CBO + ft:CBO + ft + 1]), deps=[mms[ft]], name="bO_e1")
                e2 = sch.emit("vector", lambda E, ft=ft, c=c, k=k: E.tensor_tensor(
                    zpl[:, ft * 2048 + c * 512: ft * 2048 + c * 512 + 512],
                    zpl[:, ft * 2048 + c * 512: ft * 2048 + c * 512 + 512],
                    xnt[k][:, :], ALU.add), deps=[e1, xl], name="bO_e2")
                prev_xl[k] = e2
                evs.append(e2)
            prev_ev = evs
        sch.stage()

        sch.set_phase(f"ab_F{b}")
        stoks, _ = emit_stats(ctx, lambda dt: zpl[:, dt * 2048:(dt + 1) * 2048],
                              prev_ev, "bF", sq_buf=sqb)
        bmu = emit_bcast(ctx, plmu, sb["mu_row"], stoks, "bF_bm")
        t5b = sch.emit("vector", lambda E: E.tensor_copy(sb["mu_row"][0:1, :],
                       sb["var_row"][0:1, :]), deps=stoks + bmu, name="bF_r0")
        brr = emit_bcast(ctx, plr, sb["mu_row"], [t5b], "bF_br")
        sch.stage()
        prev = []
        for dt in range(NDT):
            tl = sch.dma("sync", xt[:, :], x_fm[dt * 128:(dt + 1) * 128, col0:col0 + S],
                         deps=prev, name="bF_x")
            f1 = sch.emit("vector", lambda E, dt=dt: E.tensor_tensor(
                sqb[:, :], zpl[:, dt * 2048:(dt + 1) * 2048], plmu[:, :], ALU.subtract),
                deps=[tl], name="bF_f1")
            f2 = sch.emit("vector", lambda E: E.tensor_tensor(sqb[:, :], sqb[:, :],
                          plr[:, :], ALU.mult), deps=[f1], name="bF_f2")
            f3 = sch.emit("vector", lambda E, dt=dt: E.scalar_tensor_tensor(
                sqb[:, :], sqb[:, :], bcols[:, CGAM + dt:CGAM + dt + 1], xt[:, :],
                ALU.mult, ALU.add), deps=[f2], name="bF_f3")
            f4 = sch.emit("scalar", lambda E, dt=dt: E.activation(
                out=xt[:, :], in_=sqb[:, :], func=AF.Identity,
                bias=bcols[:, CBET + dt:CBET + dt + 1]), deps=[f3], name="bF_f4")
            st = sch.dma(qs[dt % 3], x4_fm[dt * 128:(dt + 1) * 128, col0:col0 + S],
                         xt[:, :], deps=[f4], name="bF_st")
            prev = [f1, st, f4]
        sch.stage()
    return {}



def _build_program(phases="am0bm1"):
    import concourse.bass as bass
    nc = bass.Bass("TRN2", target_bir_lowering=False)
    sch = Sched()
    ctx = Ctx(nc, sch)
    x_fm = ctx.dram_in("x_fm", (1024, T), F32)
    x4_fm = ctx.dram_out("x4_fm", (1024, T), F32)
    x1_fm = ctx.dram_tmp("x1_fm", (1024, T), F32)
    x2_fm = ctx.dram_tmp("x2_fm", (1024, T), F32)
    x3_fm = ctx.dram_tmp("x3_fm", (1024, T), F32)
    xn1b_dram = ctx.dram_tmp("xn1b_dram", (1024, T), BF16)
    z_dram = ctx.dram_tmp("z_dram", (1024, T), F32)
    for nme, arr in host_consts().items():
        ctx.dram_in(f"c_{nme}", arr.shape, F32)
    aa_shapes = {
        "L0_attn_a_ln_w": (1024,), "L0_attn_a_ln_b": (1024,),
        "L0_attn_a_sign_W": (1024, 32), "L0_attn_a_sign_b": (32,),
        "L0_attn_a_perm_W": (1024, 256), "L0_attn_a_perm_b": (256,),
        "L0_attn_a_gate_W": (1024, 32), "L0_attn_a_gate_b": (32,),
        "L0_attn_a_in_W": (1024, 32), "L0_attn_a_in_b": (32,),
        "L0_attn_a_out_W": (32, 1024), "L0_attn_a_out_b": (1024,),
    }
    for nm, shp in aa_shapes.items():
        ctx.dram_in(nm, shp, F32)
    ctx.dram_in("gate_wT", (1024, 64), F32)
    ctx.dram_in("L0_ffn_gate_b", (64,), F32)
    ctx.dram_in("ws1T", (1024, 1408), F32)
    ctx.dram_in("ws2T", (1408, 1024), F32)
    ctx.dram_in("ws3T", (1024, 1408), F32)
    ctx.dram_in("w1T_my", (8, 1024, 704), F32)
    ctx.dram_in("w2T_my", (8, 704, 1024), F32)
    ctx.dram_in("w3T_my", (8, 1024, 704), F32)
    ctx.dram_in("tokid", (128, 32), F32)
    ctx.dram_in("rankmask", (8, 1), F32)
    ctx.dram["myslots"] = nc.dram_tensor("myslots", [128, 32], U32,
                                         kind="ExternalInput").ap()
    ctx.dram_in("b_Wi", (1024, 4096), F32)
    ctx.dram_in("b_bi_c", (128, 32), F32)
    ctx.dram_in("b_Wx", (2048, 2176), F32)
    ctx.dram_in("b_bx_c", (128, 17), F32)
    ctx.dram_in("b_Wdt", (64, 2048), F32)
    ctx.dram_in("b_bdt_c", (128, 16), F32)
    ctx.dram_in("b_Alog_r", (128, 1024), F32)
    ctx.dram_in("b_Dp_c", (128, 16), F32)
    ctx.dram_in("b_Wo", (2048, 1024), F32)
    ctx.dram_in("b_bo_c", (128, 8), F32)
    ctx.dram_in("b_gam_c", (128, 8), F32)
    ctx.dram_in("b_bet_c", (128, 8), F32)

    alloc_common(ctx)
    ctx.init_arena(52500)
    alloc_stats_top(ctx)
    TOPM = ctx.top_mark()
    alloc_attn_a_body(ctx)
    load_consts(ctx)
    sch.stage()
    cur = x_fm
    if "a" in phases.replace("m0", "").replace("m1", "").replace("b", "") or phases.startswith("a"):
        emit_attn_a(ctx, cur, x1_fm, xn1b_dram, deps=[])
        cur = x1_fm
        sch.stage()
    if "m0" in phases:
        ctx.scratch_reset()
        alloc_moe(ctx)
        emit_moe(ctx, cur, x2_fm, z_dram, 0, deps=[])
        cur = x2_fm
        ctx.top_reset(TOPM)
        sch.stage()
    if "b" in phases.replace("m0", "").replace("m1", ""):
        emit_attn_b(ctx, cur, x3_fm, deps=[])
        cur = x3_fm
        sch.stage()
    if "m1" in phases:
        ctx.scratch_reset()
        alloc_moe(ctx)
        emit_moe(ctx, cur, x4_fm, z_dram, 1, deps=[])
        cur = x4_fm
        sch.stage()
    if cur is not x4_fm:
        ctx.scratch_reset()
        cp = ctx.scr("fin_cp", (128, 2048), F32)
        prev = []
        for dt in range(NDT):
            for cb2 in range(2):
                tl = sch.dma("sync", cp[:, :], cur[dt * 128:(dt + 1) * 128, cb2 * 2048:(cb2 + 1) * 2048], deps=prev, name="fin_l")
                st = sch.dma("scalar", x4_fm[dt * 128:(dt + 1) * 128, cb2 * 2048:(cb2 + 1) * 2048], cp[:, :], deps=[tl], name="fin_s")
                prev = [st]
    sch.stage()
    section, sem_c, sem_d = sch.build(nc)
    with nc.Block() as block:
        block.sync(section("sync"))
        block.vector(section("vector"))
        block.scalar(section("scalar"))
        block.tensor(section("tensor"))
        block.gpsimd(section("gpsimd"))
    return nc, ctx


def _make_in_maps(inp):
    X = inp["L0_attn_a_x"].astype(np.float32)
    consts = host_consts()
    in_maps = []
    for c in range(8):
        im = {"x_fm": np.ascontiguousarray(
            np.concatenate([X[2 * c].T, X[2 * c + 1].T], axis=1))}
        for nme, arr in consts.items():
            im[f"c_{nme}"] = arr
        for k in list(inp.keys()):
            if k.startswith("L0_attn_a") and k != "L0_attn_a_x":
                im[k] = inp[k]
        im["gate_wT"] = np.ascontiguousarray(inp["L0_ffn_gate_w"].T)
        im["L0_ffn_gate_b"] = inp["L0_ffn_gate_b"]
        im["ws1T"] = np.ascontiguousarray(inp["L0_ffn_ws1"].T)
        im["ws2T"] = np.ascontiguousarray(inp["L0_ffn_ws2"].T)
        im["ws3T"] = np.ascontiguousarray(inp["L0_ffn_ws3"].T)
        im["w1T_my"] = np.ascontiguousarray(inp["L0_ffn_w1"][8 * c:8 * c + 8].transpose(0, 2, 1))
        im["w2T_my"] = np.ascontiguousarray(inp["L0_ffn_w2"][8 * c:8 * c + 8].transpose(0, 2, 1))
        im["w3T_my"] = np.ascontiguousarray(inp["L0_ffn_w3"][8 * c:8 * c + 8].transpose(0, 2, 1))
        tokid = np.zeros((128, 32), np.float32)
        for b in range(2):
            for tt in range(16):
                tokid[:, b * 16 + tt] = c * 4096 + b * 2048 + tt * 128 + np.arange(128) + 1
        im["tokid"] = tokid
        im["rankmask"] = (np.arange(8) < c).astype(np.float32).reshape(8, 1)
        mys = np.zeros((128, 32), np.uint32)
        for g in range(32):
            mys[:, g] = c * 4096 + g * 128 + np.arange(128)
        im["myslots"] = mys
        im["b_Wi"] = inp["L0_attn_b_Wi"]
        im["b_bi_c"] = np.ascontiguousarray(inp["L0_attn_b_bi"].reshape(32, 128).T)
        im["b_Wx"] = inp["L0_attn_b_Wx"]
        im["b_bx_c"] = np.ascontiguousarray(inp["L0_attn_b_bx"].reshape(17, 128).T)
        im["b_Wdt"] = inp["L0_attn_b_Wdt"]
        im["b_bdt_c"] = np.ascontiguousarray(inp["L0_attn_b_bdt"].reshape(16, 128).T)
        im["b_Alog_r"] = np.ascontiguousarray(
            inp["L0_attn_b_A_log"].reshape(16, 128, 64).transpose(1, 0, 2).reshape(128, 1024))
        im["b_Dp_c"] = np.ascontiguousarray(inp["L0_attn_b_Dp"].reshape(16, 128).T)
        im["b_Wo"] = inp["L0_attn_b_Wo"]
        im["b_bo_c"] = np.ascontiguousarray(inp["L0_attn_b_bo"].reshape(8, 128).T)
        im["b_gam_c"] = np.ascontiguousarray(inp["L0_attn_b_gamma"].reshape(8, 128).T)
        im["b_bet_c"] = np.ascontiguousarray(inp["L0_attn_b_beta"].reshape(8, 128).T)
        in_maps.append(im)
    return in_maps


def kernel(**inputs):
    from concourse.bass_utils import run_bass_kernel_spmd
    inp = {k: np.asarray(v) for k, v in inputs.items()}
    nc, ctx = _build_program()
    in_maps = _make_in_maps(inp)
    res = run_bass_kernel_spmd(nc, in_maps, core_ids=list(range(8)))
    out = np.empty((16, 2048, 1024), np.float32)
    for c in range(8):
        pl = res.results[c]["x4_fm"]
        out[2 * c] = pl[:, 0:2048].T
        out[2 * c + 1] = pl[:, 2048:4096].T
    return out


# version 09
# speedup vs baseline: 1.0437x; 1.0437x over previous
import numpy as np
import concourse.bass as bass
import concourse.mybir as mybir

F32 = mybir.dt.float32
BF16 = mybir.dt.bfloat16
U16 = mybir.dt.uint16
U32 = mybir.dt.uint32
AF = mybir.ActivationFunctionType
ALU = mybir.AluOpType
AX = mybir.AxisListType

B = 2
S = 2048
D = 1024
T = B * S
NC = 8
H, N8 = 4, 8
DIN = 2048
DST = 64
E, TOPK, CAP = 64, 6, 512
EPERC = E // NC
I_EXP = 704
SI = 1408
EPS = 1e-5
NDT = D // 128

ENGINES = ("sync", "vector", "scalar", "tensor", "gpsimd")

_BREG = {}

def bounds_reg(E):
    k = id(E)
    if k not in _BREG:
        r = E.alloc_register()
        E.reg_mov(r, 32767)
        _BREG[k] = r
    return _BREG[k]


NJSLOTS = 8
DRAIN_PHASES = {"init", "aa"}


class Tok:
    __slots__ = ("eng", "is_dma", "seq", "dma_seq", "cc_seq", "nojoin",
                 "nj_slot", "nj_cnt")
    def __init__(self, eng, is_dma, seq):
        self.eng, self.is_dma, self.seq = eng, is_dma, seq
        self.dma_seq = None
        self.cc_seq = None
        self.nojoin = False
        self.nj_slot = None
        self.nj_cnt = None


class Sched:
    def __init__(self):
        self.items = []
        self.cc = {e: 0 for e in ENGINES}
        self.cd = {e: 0 for e in ENGINES}
        self.ccc = {}
        self.cdn = {e: 0 for e in ENGINES}
        self.last = {e: None for e in ENGINES}
        self._stage = []
        self._phase = "init"

    def set_phase(self, p):
        self._phase = p

    def emit(self, eng, fn, deps=(), is_dma=False, name="", is_cc=False,
             join=True):
        if is_dma and not join:
            i = self.cdn[eng]
            self.cdn[eng] += 1
            tok = Tok(eng, False, self.cc[eng])
            tok.nojoin = True
            tok.nj_slot = i % NJSLOTS
            tok.nj_cnt = i // NJSLOTS + 1
            deps = tuple(d for d in deps if d is not None) + tuple(self._stage)
            self.items.append((eng, fn, deps, "dmanj", name, tok, self._phase))
            self.last[eng] = tok
            return tok
        if is_dma:
            self.cd[eng] += 1
            self.cc[eng] += 1
            tok = Tok(eng, False, self.cc[eng])
            tok.dma_seq = self.cd[eng]
        elif is_cc:
            self.ccc[eng] = self.ccc.get(eng, 0) + 1
            self.cc[eng] += 1
            tok = Tok(eng, False, self.cc[eng])
            tok.dma_seq = None
            tok.cc_seq = self.ccc[eng]
        else:
            self.cc[eng] += 1
            tok = Tok(eng, False, self.cc[eng])
            tok.dma_seq = None
        deps = tuple(d for d in deps if d is not None) + tuple(self._stage)
        self.items.append((eng, fn, deps, ("dma" if is_dma else "cc" if is_cc else ""),
                           name, tok, self._phase))
        self.last[eng] = tok
        return tok


    def dma(self, eng, out, in_, deps=(), name="", slow=False, join=True):
        if slow:
            return self.emit(eng, lambda E, o=out, i=in_: E.dma_start(
                out=o, in_=i, allow_slow_non_contiguous=True),
                deps=deps, is_dma=True, name=name, join=join)
        return self.emit(eng, lambda E, o=out, i=in_: E.dma_start(out=o, in_=i),
                         deps=deps, is_dma=True, name=name, join=join)

    def stage(self):
        self._stage = [t for t in self.last.values() if t is not None]

    def build(self, nc):
        sem_c = {e: nc.semaphore(f"sc_{e}").__enter__() for e in ENGINES}
        sem_d = {e: nc.semaphore(f"sd_{e}").__enter__() for e in ENGINES}
        sem_g = nc.semaphore("sg_cc").__enter__()
        sem_n = {}
        for e in ENGINES:
            if self.cdn[e]:
                sem_n[e] = [nc.semaphore(f"sn_{e}{j}").__enter__()
                            for j in range(NJSLOTS)]

        def section(ename):
            def run(E):
                floor = {}
                floor_d = {}
                for (eng, fn, deps, kind, name, tok, phase) in self.items:
                    if eng != ename:
                        continue
                    w = {}
                    wd = {}
                    for d in deps:
                        if d.nojoin:
                            wd[(d.eng, d.nj_slot)] = max(
                                wd.get((d.eng, d.nj_slot), 0), d.nj_cnt)
                            continue
                        w[d.eng] = max(w.get(d.eng, 0), d.seq)
                    for deng, seq in w.items():
                        if seq <= floor.get(deng, 0):
                            continue
                        floor[deng] = seq
                        E.wait_ge(sem_c[deng], seq)
                    for dk, cnt in wd.items():
                        if cnt <= floor_d.get(dk, 0):
                            continue
                        floor_d[dk] = cnt
                        E.wait_ge(sem_n[dk[0]][dk[1]], 16 * cnt)
                    with nc.named_scope(phase):
                        if kind == "dmanj":
                            if tok.nj_cnt > 1:
                                k0 = (ename, tok.nj_slot)
                                if floor_d.get(k0, 0) < tok.nj_cnt - 1:
                                    floor_d[k0] = tok.nj_cnt - 1
                                    E.wait_ge(sem_n[ename][tok.nj_slot],
                                              16 * (tok.nj_cnt - 1))
                        inst = fn(E)
                        if kind == "dmanj":
                            inst.then_inc(sem_n[ename][tok.nj_slot], 16)
                        elif kind == "dma":
                            inst.then_inc(sem_d[ename], 16)
                            E.wait_ge(sem_d[ename], 16 * tok.dma_seq)
                            E.nop().then_inc(sem_c[ename], 1)
                        elif kind == "cc":
                            inst.then_inc(sem_g, 1)
                            E.wait_ge(sem_g, tok.cc_seq)
                            E.nop().then_inc(sem_c[ename], 1)
                        else:
                            if inst is None:
                                E.nop().then_inc(sem_c[ename], 1)
                            elif phase in DRAIN_PHASES:
                                E.maybe_drain_then_inc((sem_c[ename], 1))
                            else:
                                inst.then_inc(sem_c[ename], 1)
            return run
        return section, sem_c, sem_d


class Ctx:
    def __init__(self, nc, sch):
        self.nc = nc
        self.sch = sch
        self.sb = {}
        self.ps = {}
        self.dram = {}
        self._exits = []
        self._scopes = []

    def push_scope(self):
        self._scopes.append([])

    def pop_scope(self):
        for name, h in reversed(self._scopes.pop()):
            try:
                h.__exit__(None, None, None)
            except Exception:
                pass

    def init_arena(self, cols_f32):
        self.arena = self.sbuf("ARENA", (128, cols_f32), F32)
        self._bump = 0
        self._top = cols_f32
        self._arena_cols = cols_f32

    def scratch_reset(self):
        self._bump = 0

    def scr_top(self, name, shape, dtype=F32):
        p, n = shape
        if dtype in (BF16, U16, U32):
            units = (n + 1) // 2 if dtype in (BF16, U16) else n
        else:
            units = n
        units = (units + 15) & ~15
        self._top -= units
        assert self._bump <= self._top, (name, self._bump, self._top)
        view = self.arena[0:p, self._top:self._top + units]
        if dtype in (BF16, U16, U32):
            view = view.bitcast(dtype)
        view = view[0:p, 0:n]
        self.sb[name] = view
        return view

    def top_mark(self):
        return self._top

    def top_reset(self, mark):
        self._top = mark

    def mark(self):
        return self._bump

    def reset_to(self, mark):
        self._bump = mark

    def scr(self, name, shape, dtype=F32):
        p, n = shape
        if dtype in (BF16, U16):
            units = (n + 1) // 2
        else:
            units = n
        units = (units + 15) & ~15
        off = self._bump
        self._bump += units
        assert self._bump <= self._top, (name, self._bump, self._top)
        view = self.arena[0:p, off:off + units]
        if dtype in (BF16, U16, U32):
            view = view.bitcast(dtype)
        view = view[0:p, 0:n]
        self.sb[name] = view
        return view

    def sbuf(self, name, shape, dtype=F32):
        h = self.nc.sbuf_tensor(name, list(shape), dtype)
        t = h.__enter__()
        if self._scopes:
            self._scopes[-1].append((name, h))
        else:
            self._exits.append(h)
        self.sb[name] = t
        return t

    def psum(self, name, shape, dtype=F32):
        h = self.nc.psum_tensor(list(shape), dtype)
        t = h.__enter__()
        self._exits.append(h)
        self.ps[name] = t
        return t

    def dram_in(self, name, shape, dtype=F32):
        t = self.nc.dram_tensor(name, list(shape), dtype, kind="ExternalInput").ap()
        self.dram[name] = t
        return t

    def dram_out(self, name, shape, dtype=F32):
        t = self.nc.dram_tensor(name, list(shape), dtype, kind="ExternalOutput").ap()
        self.dram[name] = t
        return t

    def dram_tmp(self, name, shape, dtype=F32, shared=False):
        kw = dict(kind="Internal")
        if shared:
            kw["addr_space"] = "Shared"
        t = self.nc.dram_tensor(name, list(shape), dtype, **kw).ap()
        self.dram[name] = t
        return t


def host_consts():
    c = {}
    c["ident128"] = np.eye(128, dtype=np.float32)
    c["ones_col"] = np.ones((128, 1), np.float32)
    c["ones_row"] = np.ones((1, 128), np.float32)
    lt = (np.arange(8)[None, :] < np.arange(8)[:, None]).astype(np.float32)
    c["lt8"] = np.broadcast_to(lt.reshape(1, 64), (128, 64)).copy()
    c["ident8row"] = np.eye(8, dtype=np.float32).reshape(1, 64)
    c["ident8rep"] = np.broadcast_to(np.eye(8, dtype=np.float32).reshape(1, 64), (128, 64)).copy()
    c["ustri128"] = (np.arange(128)[:, None] < np.arange(128)[None, :]).astype(np.float32)
    c["erow"] = np.broadcast_to(np.arange(64, dtype=np.float32).reshape(1, 64), (128, 64)).copy()
    lt6 = (np.arange(6)[None, :] < np.arange(6)[:, None]).astype(np.float32)
    c["lt6"] = np.broadcast_to(lt6.reshape(1, 36), (128, 36)).copy()
    return c


def alloc_common(ctx):
    ctx.sbuf("ident128", (128, 128), F32)
    ctx.sbuf("ones_col", (128, 1), F32)
    ctx.sbuf("ones_row", (1, 128), F32)
    ctx.sbuf("lt8", (128, 64), F32)
    ctx.sbuf("ustri128", (128, 128), F32)
    ctx.sbuf("erow", (128, 64), F32)
    ctx.sbuf("lt6", (128, 36), F32)
    ctx.sbuf("eps_col", (128, 1), F32)
    ctx.sbuf("ident128b", (128, 128), BF16)
    ctx.psum("psA", (128, 512))
    ctx.psum("psB", (128, 512))
    ctx.psum("psC", (128, 512))
    ctx.psum("psD", (128, 512))
    ctx.psum("psE", (128, 2048))


def alloc_stats_top(ctx):
    sr = ctx.scr_top("stat_rows", (33, 2048), F32)
    ctx.sb["mu_row"] = sr[0:1, :]
    ctx.sb["var_row"] = sr[32:33, :]
    ctx.sb["r_row"] = sr[32:33, :]
    ctx.sb["negmu_row"] = sr[0:1, :]


def load_consts(ctx, deps=()):
    sch = ctx.sch
    toks = []
    for nme in ("ident128", "ones_col", "ones_row", "lt8", "ustri128", "erow", "lt6"):
        toks.append(sch.dma("sync", ctx.sb[nme][:], ctx.dram[f"c_{nme}"][:], deps=deps))
    toks.append(sch.emit("gpsimd", lambda E: E.memset(ctx.sb["eps_col"][:, :], EPS),
                         deps=deps, name="eps"))
    toks.append(ctx.sch.emit("vector", lambda E: E.tensor_copy(
        ctx.sb["ident128b"][:, :], ctx.sb["ident128"][:, :]), deps=[toks[0]], name="id_b"))
    return toks



def emit_ln_stats(ctx, src_fn, deps, tag):
    nc, sch = ctx.nc, ctx.sch
    ones = ctx.sb["ones_col"]
    psE = ctx.ps["psE"]
    sqs = [ctx.sb["sq0"], ctx.sb["sq1"]]
    sq_free = [None, None]
    mm_toks = []
    for dt in range(NDT):
        sq = sqs[dt % 2]
        dep_sq = [sq_free[dt % 2]] if sq_free[dt % 2] else []
        t_sq = sch.emit("scalar", lambda E, dt=dt, sq=sq: E.activation(
            out=sq[:], in_=src_fn(dt), func=AF.Square), deps=list(deps) + dep_sq,
            name=f"{tag}_sq{dt}")
        last = None
        for c0 in range(0, S, 512):
            m1 = sch.emit("tensor", lambda E, dt=dt, c0=c0: E.matmul(
                psE[0:1, c0:c0 + 512], ones[:, 0:1], src_fn(dt)[:, c0:c0 + 512],
                start=(dt == 0), stop=(dt == NDT - 1)), deps=deps, name=f"{tag}_s")
            m2 = sch.emit("tensor", lambda E, sq=sq, c0=c0, dt=dt: E.matmul(
                psE[32:33, c0:c0 + 512], ones[:, 0:1], sq[:, c0:c0 + 512],
                start=(dt == 0), stop=(dt == NDT - 1)), deps=[t_sq], name=f"{tag}_q")
            last = m2
            if dt == NDT - 1:
                mm_toks += [m1, m2]
        sq_free[dt % 2] = last
    mu, var, r, negmu = (ctx.sb["mu_row"], ctx.sb["var_row"], ctx.sb["r_row"],
                         ctx.sb["negmu_row"])
    t1 = sch.emit("vector", lambda E: E.tensor_scalar(mu[0:1, :], psE[0:1, :], 1.0 / D,
                  None, ALU.mult), deps=mm_toks, name=f"{tag}_mu")
    t2 = sch.emit("vector", lambda E: E.tensor_tensor(var[0:1, :], mu[0:1, :], mu[0:1, :],
                  ALU.mult), deps=[t1], name=f"{tag}_mu2")
    t3 = sch.emit("vector", lambda E: E.scalar_tensor_tensor(var[0:1, :], psE[32:33, :],
                  1.0 / D, var[0:1, :], ALU.mult, ALU.subtract), deps=[t2] + mm_toks,
                  name=f"{tag}_var")
    t4 = sch.emit("scalar", lambda E: E.activation(out=r[0:1, :], in_=var[0:1, :],
                  func=AF.Sqrt, bias=sb["eps_col"][0:1, 0:1]), deps=[t3], name=f"{tag}_sqrt")
    t5 = sch.emit("vector", lambda E: E.reciprocal(r[0:1, :], r[0:1, :]), deps=[t4],
                  name=f"{tag}_rcp")
    t6 = sch.emit("vector", lambda E: E.tensor_scalar(negmu[0:1, :], mu[0:1, :], -1.0,
                  None, ALU.mult), deps=[t1], name=f"{tag}_negmu")
    return [t5, t6]


def emit_bcast_plane(ctx, dst_sb, row_ap, deps, tag):
    nc, sch = ctx.nc, ctx.sch
    onesr = ctx.sb["ones_row"]
    psE = ctx.ps["psE"]
    toks = []
    for c0 in range(0, 2048, 512):
        toks.append(sch.emit("tensor", lambda E, c0=c0: E.matmul(
            psE[:, c0:c0 + 512], onesr[0:1, :], row_ap[0:1, c0:c0 + 512],
            start=True, stop=True), deps=deps, name=f"{tag}_bc"))
    ev = sch.emit("vector", lambda E: E.tensor_copy(dst_sb[:, :], psE[:, :]),
                  deps=toks, name=f"{tag}_ev")
    return [ev]



def emit_attn_a_weights(ctx, deps):
    nc, sch = ctx.nc, ctx.sch
    d = ctx.dram
    sb = ctx.sb
    wst = sb["wstage"]
    lnw, lnb = sb["lnw_col"], sb["lnb_col"]
    psA, psB = ctx.ps["psA"], ctx.ps["psB"]
    ob1 = sch.emit("gpsimd", lambda E: E.memset(sb["ones_col_b"][:, :], 1.0), deps=deps,
                   name="onesb")
    tok_ln = [sch.dma("sync", lnw[:, :], d["L0_attn_a_ln_w"].rearrange("(a p) -> p a", p=128), deps=deps, slow=True),
              sch.dma("sync", lnb[:, :], d["L0_attn_a_ln_b"].rearrange("(a p) -> p a", p=128), deps=deps, slow=True)]
    const_mm = []
    prev_users = []
    for dt in range(NDT):
        tks = [sch.dma("sync", wst[:, 0:32], d["L0_attn_a_sign_W"][dt * 128:(dt + 1) * 128, :], deps=prev_users),
               sch.dma("scalar", wst[:, 32:288], d["L0_attn_a_perm_W"][dt * 128:(dt + 1) * 128, :], deps=prev_users),
               sch.dma("gpsimd", wst[:, 288:320], d["L0_attn_a_gate_W"][dt * 128:(dt + 1) * 128, :], deps=prev_users),
               sch.dma("gpsimd", wst[:, 320:352], d["L0_attn_a_in_W"][dt * 128:(dt + 1) * 128, :], deps=prev_users)]
        tks += tok_ln
        psE = ctx.ps["psE"]
        psC, psD = ctx.ps["psC"], ctx.ps["psD"]
        c1 = sch.emit("tensor", lambda E, dt=dt: E.matmul(
            psA[0:128, 0:1], wst[:, 32:160], lnb[:, dt:dt + 1],
            start=(dt == 0), stop=(dt == NDT - 1)), deps=tks, name="aw_c0")
        c2 = sch.emit("tensor", lambda E, dt=dt: E.matmul(
            psB[0:128, 0:1], wst[:, 160:288], lnb[:, dt:dt + 1],
            start=(dt == 0), stop=(dt == NDT - 1)), deps=tks, name="aw_c1")
        c3 = sch.emit("tensor", lambda E, dt=dt: E.matmul(
            psC[0:32, 0:1], wst[:, 0:32], lnb[:, dt:dt + 1],
            start=(dt == 0), stop=(dt == NDT - 1)), deps=tks, name="aw_c2")
        c4 = sch.emit("tensor", lambda E, dt=dt: E.matmul(
            psD[32:64, 0:1], wst[:, 288:320], lnb[:, dt:dt + 1],
            start=(dt == 0), stop=(dt == NDT - 1)), deps=tks, name="aw_c3")
        c5 = sch.emit("tensor", lambda E, dt=dt: E.matmul(
            psE[64:96, 0:1], wst[:, 320:352], lnb[:, dt:dt + 1],
            start=(dt == 0), stop=(dt == NDT - 1)), deps=tks, name="aw_c4")
        const_mm = [c1, c2, c3, c4, c5]
        t_p = sch.emit("vector", lambda E, dt=dt: E.tensor_scalar(
            sb["aw_perm"][:, dt * 256:(dt + 1) * 256], wst[:, 32:288],
            lnw[:, dt:dt + 1], None, ALU.mult), deps=tks + const_mm, name="aw_cp")
        t_s = sch.emit("vector", lambda E, dt=dt: E.tensor_scalar(
            sb["aw_sgi"][:, dt * 96: dt * 96 + 32], wst[:, 0:32],
            lnw[:, dt:dt + 1], None, ALU.mult), deps=tks + const_mm, name="aw_cs")
        t_g = sch.emit("vector", lambda E, dt=dt: E.tensor_scalar(
            sb["aw_sgi"][:, dt * 96 + 32: dt * 96 + 64], wst[:, 288:320],
            lnw[:, dt:dt + 1], None, ALU.mult), deps=tks + const_mm, name="aw_cg")
        t_i = sch.emit("vector", lambda E, dt=dt: E.tensor_scalar(
            sb["aw_sgi"][:, dt * 96 + 64: dt * 96 + 96], wst[:, 320:352],
            lnw[:, dt:dt + 1], None, ALU.mult), deps=tks + const_mm, name="aw_ci")
        prev_users = [t_p, t_s, t_g, t_i]
    bias_tiles = sb["a_bias"]
    tz = sch.emit("gpsimd", lambda E: E.memset(bias_tiles[:, :], 0.0), deps=deps, name="bz")
    deps2 = list(deps) + [tz]
    tb = [sch.dma("sync", bias_tiles[0:128, 0:1], d["L0_attn_a_perm_b"][0:128].unsqueeze(1), deps=deps2, slow=True),
          sch.dma("sync", bias_tiles[0:128, 1:2], d["L0_attn_a_perm_b"][128:256].unsqueeze(1), deps=deps2, slow=True),
          sch.dma("scalar", bias_tiles[0:32, 2:3], d["L0_attn_a_sign_b"][:].unsqueeze(1), deps=deps2, slow=True),
          sch.dma("scalar", bias_tiles[32:64, 2:3], d["L0_attn_a_gate_b"][:].unsqueeze(1), deps=deps2, slow=True),
          sch.dma("scalar", bias_tiles[64:96, 2:3], d["L0_attn_a_in_b"][:].unsqueeze(1), deps=deps2, slow=True)]
    tz3 = sch.emit("gpsimd", lambda E: E.memset(sb["a_const"][:, :], 0.0), deps=deps, name="cz")
    tca = sch.emit("vector", lambda E: E.tensor_tensor(sb["a_const"][:, 0:1],
                   psA[:, 0:1], bias_tiles[:, 0:1], ALU.add), deps=const_mm + tb + [tz3], name="ac0")
    tcb = sch.emit("vector", lambda E: E.tensor_tensor(sb["a_const"][:, 1:2],
                   psB[:, 0:1], bias_tiles[:, 1:2], ALU.add), deps=const_mm + tb + [tz3], name="ac1")
    tcc = sch.emit("vector", lambda E: E.tensor_tensor(sb["a_const"][0:32, 2:3],
                   ctx.ps["psC"][0:32, 0:1], bias_tiles[0:32, 2:3], ALU.add),
                   deps=const_mm + tb + [tz3], name="ac2")
    tcd = sch.emit("vector", lambda E: E.tensor_tensor(sb["a_const"][32:64, 2:3],
                   ctx.ps["psD"][32:64, 0:1], bias_tiles[32:64, 2:3], ALU.add),
                   deps=const_mm + tb + [tz3], name="ac3")
    tce = sch.emit("vector", lambda E: E.tensor_tensor(sb["a_const"][64:96, 2:3],
                   ctx.ps["psE"][64:96, 0:1], bias_tiles[64:96, 2:3], ALU.add),
                   deps=const_mm + tb + [tz3], name="ac4")
    tc = sch.emit("vector", lambda E: None, deps=[tca, tcb, tcc, tcd, tce], name="acj")
    tcx = tc
    onesb = sb["ones_col_b"]
    wtw_toks = []
    for dt in range(NDT):
        w1 = sch.emit("tensor", lambda E, dt=dt: E.matmul(
            psA[0:128, 2:3], sb["aw_perm"][:, dt * 256: dt * 256 + 128], onesb[:, 0:1],
            start=(dt == 0), stop=(dt == NDT - 1)), deps=prev_users + [tcx, ob1], name="wtw0")
        w2 = sch.emit("tensor", lambda E, dt=dt: E.matmul(
            psB[0:128, 2:3], sb["aw_perm"][:, dt * 256 + 128: dt * 256 + 256], onesb[:, 0:1],
            start=(dt == 0), stop=(dt == NDT - 1)), deps=prev_users + [tcx], name="wtw1")
        w3 = sch.emit("tensor", lambda E, dt=dt: E.matmul(
            ctx.ps["psC"][0:96, 2:3], sb["aw_sgi"][:, dt * 96: dt * 96 + 96], onesb[:, 0:1],
            start=(dt == 0), stop=(dt == NDT - 1)), deps=prev_users + [tcx], name="wtw2")
        wtw_toks = [w1, w2, w3]
    te1 = sch.emit("vector", lambda E: E.tensor_copy(sb["a_wtw"][:, 0:1], psA[:, 2:3]),
                   deps=wtw_toks, name="wtw_e1")
    te2 = sch.emit("vector", lambda E: E.tensor_copy(sb["a_wtw"][:, 1:2], psB[:, 2:3]),
                   deps=wtw_toks, name="wtw_e2")
    tz4 = sch.emit("gpsimd", lambda E: E.memset(sb["a_wtw"][:, 2:3], 0.0), deps=deps, name="wz")
    te3 = sch.emit("vector", lambda E: E.tensor_copy(sb["a_wtw"][0:96, 2:3],
                   ctx.ps["psC"][0:96, 2:3]), deps=wtw_toks + [tz4], name="wtw_e3")
    te = sch.emit("vector", lambda E: None, deps=[te1, te2, te3], name="wtw_j")
    tf = []
    for col in range(3):
        tf.append(sch.dma("gpsimd", sb["a_wtw_row"][0:1, col * 128:(col + 1) * 128],
                          sb["a_wtw"][0:128, col:col + 1], deps=[te], name="wtwf"))
    tg = sch.emit("vector", lambda E: E.tensor_copy(sb["a_wtw_rowb"][0:1, :],
                  sb["a_wtw_row"][0:1, :]), deps=tf, name="wtwb")
    tow1 = sch.dma("sync", wst[0:32, 0:512], d["L0_attn_a_out_W"][:, 0:512],
                   deps=prev_users + const_mm)
    tcw1 = sch.emit("vector", lambda E: E.tensor_copy(sb["aw_outb"][0:32, 0:512],
                    wst[0:32, 0:512]), deps=[tow1], name="owc1")
    tow2 = sch.dma("sync", wst[0:32, 0:512], d["L0_attn_a_out_W"][:, 512:1024],
                   deps=[tcw1])
    tcw2 = sch.emit("vector", lambda E: E.tensor_copy(sb["aw_outb"][0:32, 512:1024],
                    wst[0:32, 0:512]), deps=[tow2], name="owc2")
    tob = sch.dma("sync", sb["a_outb"][:, 0:8],
                  d["L0_attn_a_out_b"].rearrange("(a p) -> p a", p=128), deps=deps, slow=True)
    return [tc, tg, tcw2, tob]




def emit_stats(ctx, src_fn, deps, tag, negmu=False, sq_buf=None, mean_of_sq=False):
    nc, sch = ctx.nc, ctx.sch
    sb = ctx.sb
    psA, psB, psC, psD, psE = (ctx.ps[k] for k in ("psA", "psB", "psC", "psD", "psE"))
    ones = sb["ones_col"]
    sq = sq_buf if sq_buf is not None else sb["sq0"]
    mu, var = sb["mu_row"], sb["var_row"]
    mm_toks = []
    prev = list(deps)
    for dt in range(NDT):
        xap = src_fn(dt)
        tq = sch.emit("scalar", lambda E, xap=xap, sq=sq: E.activation(
            out=sq[:, :], in_=xap, func=AF.Square), deps=prev, name=f"{tag}_sq")
        m1 = m2 = None
        for c0 in range(0, S, 512):
            m1 = sch.emit("tensor", lambda E, c0=c0, dt=dt, xap=xap: E.matmul(
                psE[0:1, c0:c0 + 512], ones[:, 0:1], xap[:, c0:c0 + 512],
                start=(dt == 0), stop=(dt == NDT - 1)), deps=prev, name=f"{tag}_s")
            m2 = sch.emit("tensor", lambda E, c0=c0, dt=dt, sq=sq: E.matmul(
                [psA, psB, psC, psD][c0 // 512][0:1, 0:512], ones[:, 0:1],
                sq[:, c0:c0 + 512],
                start=(dt == 0), stop=(dt == NDT - 1)), deps=[tq], name=f"{tag}_q")
            if dt == NDT - 1:
                mm_toks += [m1, m2]
        prev = [tq, m1, m2]
    t1 = sch.emit("vector", lambda E: E.tensor_scalar(mu[0:1, :], psE[0:1, :],
                  1.0 / D, None, ALU.mult), deps=mm_toks, name=f"{tag}_mu")
    sqc = []
    for c0 in range(0, 2048, 512):
        sqc.append(sch.emit("vector", lambda E, c0=c0: E.tensor_copy(
            var[0:1, c0:c0 + 512], [psA, psB, psC, psD][c0 // 512][0:1, 0:512]),
            deps=mm_toks, name=f"{tag}_sc"))
    t2 = sch.emit("vector", lambda E: E.tensor_tensor(psE[0:1, :], mu[0:1, :],
                  mu[0:1, :], ALU.mult), deps=[t1], name=f"{tag}_m2")
    t3 = sch.emit("vector", lambda E: E.scalar_tensor_tensor(var[0:1, :], var[0:1, :],
                  1.0 / D, psE[0:1, :], ALU.mult, ALU.subtract),
                  deps=[t2] + sqc, name=f"{tag}_v")
    t4 = sch.emit("scalar", lambda E: E.activation(out=var[0:1, :], in_=var[0:1, :],
                  func=AF.Sqrt, bias=sb["eps_col"][0:1, 0:1]), deps=[t3], name=f"{tag}_st")
    t5 = sch.emit("vector", lambda E: E.reciprocal(var[0:1, :], var[0:1, :]), deps=[t4],
                  name=f"{tag}_rc")
    toks = [t5]
    if negmu:
        toks.append(sch.emit("vector", lambda E: E.tensor_scalar(mu[0:1, :], mu[0:1, :],
                    -1.0, None, ALU.mult), deps=[t1, t2], name=f"{tag}_nm"))
    return toks, t1


def emit_bcast(ctx, dst_sb, row_ap, deps, tag, via_row0=None):
    nc, sch = ctx.nc, ctx.sch
    sb = ctx.sb
    psE = ctx.ps["psE"]
    toks = []
    for c0 in range(0, 2048, 512):
        toks.append(sch.emit("tensor", lambda E, c0=c0: E.matmul(
            psE[:, c0:c0 + 512], sb["ones_row"][0:1, :], row_ap[0:1, c0:c0 + 512],
            start=True, stop=True), deps=deps, name=f"{tag}_b"))
    ev = sch.emit("vector", lambda E: E.tensor_copy(dst_sb[:, 0:2048], psE[:, :]),
                  deps=toks, name=f"{tag}_e")
    return [ev]



def alloc_attn_a_body(ctx):
    ctx.scr("aw_perm", (128, NDT * 256), BF16)
    ctx.scr("aw_sgi", (128, NDT * 96), BF16)
    ctx.scr("aw_outb", (32, 1024), BF16)
    ctx.scr("a_outb", (128, 8), F32)
    ctx.scr("lnw_col", (128, NDT), F32)
    ctx.scr("lnb_col", (128, NDT), F32)
    ctx.scr("ones_col_b", (128, 1), BF16)
    ctx.scr("poolA", (128, 4096), F32)
    ctx.scr("poolC", (128, 4096), F32)
    ctx.scr("xn1d", (128, 2048), F32)
    ctx.scr("xn1bt", (128, 2048), BF16)
    ctx.scr("xtile", (128, 2048), F32)
    ctx.scr("pl_mu", (128, 2048), F32)
    ctx.scr("pl_r", (128, 2048), F32)
    ctx.scr("perm_fm", (128, 2 * 2048), F32)
    ctx.scr("sgi_fm", (96, 2048), F32)
    ctx.scr("perm_tm", (128, 16 * 256), F32)
    ctx.scr("lse_s", (128, 512), F32)
    ctx.scr("grd_rank", (128, 512), F32)
    ctx.scr("grd_sel", (128, 512), F32)
    ctx.scr("grd_rv", (128, 512), F32)
    ctx.scr("grd_oh", (128, 512), F32)
    ctx.scr("grd_cm", (128, 64), F32)
    ctx.scr("grd_tk", (128, 512), F32)
    ctx.scr("ss_P", (128, 64 * 64), BF16)
    ctx.scr("ss_G", (128, 64 * 24), F32)
    ctx.scr("scanT_P", (128, 128 * 64), BF16)
    ctx.scr("scanT_G", (128, 128 * 24), F32)
    ctx.scr("sc_p32", (128, 64), F32)
    ctx.scr("scM", (128, 64), F32)
    ctx.scr("scV", (128, 8), F32)
    ctx.scr("sc_gs", (128, 8), F32)
    ctx.scr("sc_c", (128, 8), F32)
    ctx.scr("sc_om", (128, 8), F32)
    ctx.scr("sc_t1", (128, 512), F32)
    ctx.scr("sc_pm", (128, 64), F32)
    ctx.scr("sc_t3", (128, 64), F32)
    ctx.scr("sc_pv", (128, 8), F32)
    ctx.scr("sc_pv2", (128, 8), F32)
    ctx.scr("summ", (128, 72), F32)
    ctx.scr("pf", (8, 16 * 72), F32)
    ctx.scr("pf_h", (8, 8), F32)
    ctx.scr("pf_h2", (8, 8), F32)
    ctx.scr("pf_t", (8, 64), F32)
    ctx.scr("hstarts", (8, 16 * 8), F32)
    ctx.scr("hst", (128, 8), F32)
    ctx.scr("ys", (128, 128 * 8), F32)
    ctx.scr("ys_b", (128, 128 * 8), BF16)
    ctx.scr("ys_r", (32, 4096), BF16)
    ctx.scr("ident8rep", (128, 64), F32)
    ctx.scr("a_bias", (128, 4), F32)
    ctx.scr("a_const", (128, 4), F32)
    ctx.scr("a_wtw", (128, 4), F32)
    ctx.scr("a_wtw_row", (1, 384), F32)
    ctx.scr("a_wtw_rowb", (1, 384), BF16)
    ctx.sb["wstage"] = ctx.sb["poolA"][:, 0:512]
    ctx.sb["sq0"] = ctx.sb["poolA"][:, 0:2048]
    ctx.sb["sq1"] = ctx.sb["poolA"][:, 2048:4096]
    ctx.sb["grd_m"] = ctx.sb["lse_s"]
    ctx.sb["negmu_b"] = ctx.sb["ys_r"][0:1, 0:2048]
    ctx.sb["evtmp"] = ctx.sb["perm_fm"][:, 0:1024]


def emit_attn_a(ctx, x_fm, x1_fm, xn1b_dram, deps, dbg=None):
    nc, sch = ctx.nc, ctx.sch
    psA, psB, psC, psD, psE = (ctx.ps[k] for k in ("psA", "psB", "psC", "psD", "psE"))
    sb = ctx.sb
    ident = sb["ident128"]
    qs = ["sync", "scalar", "gpsimd"]

    sch.set_phase("aa")
    wt = emit_attn_a_weights(ctx, deps)
    t_i8 = sch.dma("gpsimd", sb["ident8rep"][:, :], ctx.dram["c_ident8rep"][:], deps=deps)
    sch.stage()

    for b in range(B):
        col0 = b * S
        xt = sb["xtile"]
        sq = sb["sq0"]
        mm_toks = []
        prev = []
        for dt in range(NDT):
            tl = sch.dma("sync", xt[:, :], x_fm[dt * 128:(dt + 1) * 128, col0:col0 + S],
                         deps=prev, name="x_ld")
            tq = sch.emit("scalar", lambda E: E.activation(
                out=sq[:, :], in_=xt[:, :], func=AF.Square), deps=[tl] + prev, name="sq")
            m1 = None
            m2 = None
            for c0 in range(0, S, 512):
                m1 = sch.emit("tensor", lambda E, c0=c0, dt=dt: E.matmul(
                    psE[0:1, c0:c0 + 512], sb["ones_col"][:, 0:1], xt[:, c0:c0 + 512],
                    start=(dt == 0), stop=(dt == NDT - 1)), deps=[tl], name="lnS")
                m2 = sch.emit("tensor", lambda E, c0=c0, dt=dt: E.matmul(
                    [psA, psB, psC, psD][c0 // 512][0:1, 0:512], sb["ones_col"][:, 0:1],
                    sq[:, c0:c0 + 512],
                    start=(dt == 0), stop=(dt == NDT - 1)), deps=[tq], name="lnQ")
                if dt == NDT - 1:
                    mm_toks += [m1, m2]
            prev = [tq, m1, m2]
        mu, var, r = sb["mu_row"], sb["var_row"], sb["r_row"]
        t1 = sch.emit("vector", lambda E: E.tensor_scalar(mu[0:1, :], psE[0:1, :],
                      1.0 / D, None, ALU.mult), deps=mm_toks, name="mu")
        sqc = []
        for c0 in range(0, 2048, 512):
            sqc.append(sch.emit("vector", lambda E, c0=c0: E.tensor_copy(
                var[0:1, c0:c0 + 512], [psA, psB, psC, psD][c0 // 512][0:1, 0:512]),
                deps=mm_toks, name="sqc"))
        t2 = sch.emit("vector", lambda E: E.tensor_tensor(psE[0:1, :], mu[0:1, :],
                      mu[0:1, :], ALU.mult), deps=[t1], name="mu2")
        t3 = sch.emit("vector", lambda E: E.scalar_tensor_tensor(var[0:1, :], var[0:1, :],
                      1.0 / D, psE[0:1, :], ALU.mult, ALU.subtract),
                      deps=[t2] + sqc, name="var")
        t4 = sch.emit("scalar", lambda E: E.activation(out=r[0:1, :], in_=var[0:1, :],
                      func=AF.Sqrt, bias=sb["eps_col"][0:1, 0:1]), deps=[t3], name="sr")
        t5 = sch.emit("vector", lambda E: E.reciprocal(r[0:1, :], r[0:1, :]), deps=[t4],
                      name="rc")
        bc1 = []
        for c0 in range(0, 2048, 512):
            bc1.append(sch.emit("tensor", lambda E, c0=c0: E.matmul(
                psE[:, c0:c0 + 512], sb["ones_row"][0:1, :], mu[0:1, c0:c0 + 512],
                start=True, stop=True), deps=[t3, t1], name="bmu"))
        ev1 = sch.emit("vector", lambda E: E.tensor_copy(sb["pl_mu"][:, :], psE[:, :]),
                       deps=bc1, name="bmu_ev")
        t5b = sch.emit("vector", lambda E: E.tensor_copy(mu[0:1, :], r[0:1, :]),
                       deps=[t5] + bc1, name="r2row0")
        bc2 = []
        for c0 in range(0, 2048, 512):
            bc2.append(sch.emit("tensor", lambda E, c0=c0: E.matmul(
                psE[:, c0:c0 + 512], sb["ones_row"][0:1, :], mu[0:1, c0:c0 + 512],
                start=True, stop=True), deps=[t5b, ev1], name="br"))
        ev2 = sch.emit("vector", lambda E: E.tensor_copy(sb["pl_r"][:, :], psE[:, :]),
                       deps=bc2, name="br_ev")
        sch.stage()
        xn1d = sb["xn1d"]
        prev = []
        mm_toks = []
        for dt in range(NDT):
            tl = sch.dma("sync", xt[:, :], x_fm[dt * 128:(dt + 1) * 128, col0:col0 + S],
                         deps=prev, name="x_ld2")
            a1 = sch.emit("vector", lambda E: E.tensor_tensor(
                xn1d[:, :], xt[:, :], sb["pl_mu"][:, :], ALU.subtract),
                deps=[tl] + prev, name="xsub")
            a2 = sch.emit("vector", lambda E: E.tensor_tensor(
                xn1d[:, :], xn1d[:, :], sb["pl_r"][:, :], ALU.mult), deps=[a1], name="xmul")
            a3 = sch.emit("vector", lambda E: E.tensor_tensor(
                xt[:, :], xt[:, :], xn1d[:, :], ALU.add), deps=[a2], name="xadd")
            st = sch.dma("scalar", x1_fm[dt * 128:(dt + 1) * 128, col0:col0 + S],
                         xt[:, :], deps=[a3], name="x1_st")
            tq = sch.emit("scalar", lambda E: E.activation(
                out=sq[:, :], in_=xn1d[:, :], func=AF.Square), deps=[a2] + prev, name="sq2")
            m1 = None
            m2 = None
            for c0 in range(0, S, 512):
                m1 = sch.emit("tensor", lambda E, c0=c0, dt=dt: E.matmul(
                    psE[0:1, c0:c0 + 512], sb["ones_col"][:, 0:1], xn1d[:, c0:c0 + 512],
                    start=(dt == 0), stop=(dt == NDT - 1)), deps=[a2], name="ln2S")
                m2 = sch.emit("tensor", lambda E, c0=c0, dt=dt: E.matmul(
                    [psA, psB, psC, psD][c0 // 512][0:1, 0:512], sb["ones_col"][:, 0:1],
                    sq[:, c0:c0 + 512],
                    start=(dt == 0), stop=(dt == NDT - 1)), deps=[tq], name="ln2Q")
                if dt == NDT - 1:
                    mm_toks += [m1, m2]
            cst = sch.emit("vector", lambda E: E.tensor_copy(sb["xn1bt"][:, :],
                           xn1d[:, :]), deps=[a2], name="xnbc")
            sd = sch.dma("gpsimd", xn1b_dram[dt * 128:(dt + 1) * 128, col0:col0 + S],
                         sb["xn1bt"][:, :], deps=[cst], name="xnb_sd")
            prev = [st, tq, m1, m2, cst, sd]
        t1 = sch.emit("vector", lambda E: E.tensor_scalar(mu[0:1, :], psE[0:1, :],
                      1.0 / D, None, ALU.mult), deps=mm_toks, name="mu_2")
        sqc = []
        for c0 in range(0, 2048, 512):
            sqc.append(sch.emit("vector", lambda E, c0=c0: E.tensor_copy(
                var[0:1, c0:c0 + 512], [psA, psB, psC, psD][c0 // 512][0:1, 0:512]),
                deps=mm_toks, name="sqc2"))
        t2 = sch.emit("vector", lambda E: E.tensor_tensor(psE[0:1, :], mu[0:1, :],
                      mu[0:1, :], ALU.mult), deps=[t1], name="mu2_2")
        t3 = sch.emit("vector", lambda E: E.scalar_tensor_tensor(var[0:1, :], var[0:1, :],
                      1.0 / D, psE[0:1, :], ALU.mult, ALU.subtract),
                      deps=[t2] + sqc, name="var_2")
        t4 = sch.emit("scalar", lambda E: E.activation(out=r[0:1, :], in_=var[0:1, :],
                      func=AF.Sqrt, bias=sb["eps_col"][0:1, 0:1]), deps=[t3], name="sr_2")
        t5 = sch.emit("vector", lambda E: E.reciprocal(r[0:1, :], r[0:1, :]), deps=[t4],
                      name="rc_2")
        t6 = sch.emit("vector", lambda E: E.tensor_scalar(mu[0:1, :],
                      mu[0:1, :], -1.0, None, ALU.mult), deps=[t3], name="negmu")
        cnm = sch.emit("vector", lambda E: E.tensor_copy(sb["negmu_b"][0:1, :],
                       mu[0:1, :]), deps=[t6], name="negmu_b")
        t6b = sch.emit("vector", lambda E: E.tensor_copy(mu[0:1, :], r[0:1, :]),
                       deps=[t5, cnm], name="r2row0b")
        bc3 = []
        for c0 in range(0, 2048, 512):
            bc3.append(sch.emit("tensor", lambda E, c0=c0: E.matmul(
                psE[:, c0:c0 + 512], sb["ones_row"][0:1, :], mu[0:1, c0:c0 + 512],
                start=True, stop=True), deps=[t6b], name="br2"))
        ev3 = sch.emit("vector", lambda E: E.tensor_copy(sb["pl_r"][:, :], psE[:, :]),
                       deps=bc3, name="br2_ev")
        cwt = sch.emit("vector", lambda E: E.tensor_copy(sb["a_wtw_rowb"][0:1, :],
                       sb["a_wtw_row"][0:1, :]), deps=wt, name="wtw_b")
        sch.stage()
        pss = [psA, psB, psC]
        ld_prev = [None, None]
        ev_prev = []
        for tc in range(4):
            mm_last = [None, None, None]
            for dt in range(NDT):
                k = dt % 2
                xb = sb["xn1bt"][:, k * 512:(k + 1) * 512]
                tl = sch.dma("sync", xb,
                             xn1b_dram[dt * 128:(dt + 1) * 128, col0 + tc * 512: col0 + tc * 512 + 512],
                             deps=[ld_prev[k]] + ([sd] if dt == 0 and tc == 0 else []),
                             name="xnb_ld")
                for mt in range(3):
                    nrow = 128 if mt < 2 else 96
                    if mt == 0:
                        lh = sb["aw_perm"][:, dt * 256: dt * 256 + 128]
                    elif mt == 1:
                        lh = sb["aw_perm"][:, dt * 256 + 128: dt * 256 + 256]
                    else:
                        lh = sb["aw_sgi"][:, dt * 96: dt * 96 + 96]
                    mm_last[mt] = sch.emit("tensor", lambda E, lh=lh, dt=dt, mt=mt, nrow=nrow, xb=xb: E.matmul(
                        pss[mt][0:nrow, 0:512], lh, xb,
                        start=(dt == 0), stop=False, skip_group_check=True),
                        deps=[tl] + ev_prev, name=f"pj{mt}")
                ld_prev[k] = mm_last[2]
            ev_prev = []
            for mt in range(3):
                nrow = 128 if mt < 2 else 96
                wr = sb["a_wtw_rowb"][0:1, mt * 128: mt * 128 + nrow]
                mmc = sch.emit("tensor", lambda E, wr=wr, tc=tc, mt=mt, nrow=nrow: E.matmul(
                    pss[mt][0:nrow, 0:512], wr, sb["negmu_b"][0:1, tc * 512: tc * 512 + 512],
                    start=False, stop=True, skip_group_check=True),
                    deps=[mm_last[mt], cnm, cwt], name="pjc")
                if mt < 2:
                    dst = sb["perm_fm"][:, mt * 2048 + tc * 512: mt * 2048 + tc * 512 + 512]
                else:
                    dst = sb["sgi_fm"][0:96, tc * 512: tc * 512 + 512]
                e1 = sch.emit("vector", lambda E, mt=mt, dst=dst, nrow=nrow, tc=tc: E.tensor_tensor(
                    dst, pss[mt][0:nrow, 0:512], sb["pl_r"][0:nrow, tc * 512: tc * 512 + 512],
                    ALU.mult), deps=[mmc, ev3], name="pj_e1")
                e2 = sch.emit("vector", lambda E, dst=dst, nrow=nrow, mt=mt: E.tensor_scalar(
                    dst, dst, sb["a_const"][0:nrow, mt:mt + 1], None, ALU.add),
                    deps=[e1], name="pj_e2")
                ev_prev.append(e2)
        sch.stage()
        tr_prev = {0: None, 1: None}
        for ft in range(2):
            for tt in range(16):
                k = (ft * 16 + tt) % 2
                ps = [psC, psD][k]
                tm = sch.emit("tensor", lambda E, ft=ft, tt=tt, ps=ps: E.transpose(
                    ps[:, 0:128], sb["perm_fm"][:, ft * 2048 + tt * 128: ft * 2048 + tt * 128 + 128],
                    ident[:, :]), deps=[tr_prev[k]], name="ptr")
                ev = sch.emit("vector", lambda E, ft=ft, tt=tt, ps=ps: E.tensor_copy(
                    sb["perm_tm"][:, tt * 256 + ft * 128: tt * 256 + ft * 128 + 128],
                    ps[:, 0:128]), deps=[tm], name="ptr_ev")
                tr_prev[k] = ev
        for tt in range(16):
            k = tt % 2
            ps = [psC, psD][k]
            tm = sch.emit("tensor", lambda E, tt=tt, ps=ps: E.transpose(
                ps[:, 0:96], sb["sgi_fm"][0:96, tt * 128: tt * 128 + 128], ident[0:96, 0:96]),
                deps=[tr_prev[k]], name="str")
            ssg4 = sb["ss_G"][:, :].rearrange("p (h tt c) -> p h tt c", h=4, tt=16, c=24)
            e1 = sch.emit("scalar", lambda E, tt=tt, ps=ps, ssg4=ssg4: E.activation(
                out=ssg4[:, :, tt, 0:8], in_=ps[:, 0:32].rearrange("p (h n) -> p h n", n=8),
                func=AF.Tanh, scale=0.5), deps=[tm], name="s_tanh")
            e2 = sch.emit("scalar", lambda E, tt=tt, ps=ps, ssg4=ssg4: E.activation(
                out=ssg4[:, :, tt, 8:16], in_=ps[:, 32:64].rearrange("p (h n) -> p h n", n=8),
                func=AF.Sigmoid), deps=[e1, tm], name="g_sig")
            e3 = sch.emit("vector", lambda E, tt=tt, ps=ps, ssg4=ssg4: E.tensor_copy(
                ssg4[:, :, tt, 16:24], ps[:, 64:96].rearrange("p (h n) -> p h n", n=8)),
                deps=[tm], name="bI_cp")
            tr_prev[k] = sch.emit("vector", lambda E: None, deps=[e2, e3], name="join")
        sch.stage()
        la = sb["perm_tm"]
        ebuf = sb["poolA"]
        x8 = None
        for it in range(5):
            x1 = sch.emit("scalar", lambda E: E.activation(out=ebuf[:, :], in_=la[:, :],
                          func=AF.Exp), deps=[x8], name="sk_e")
            x2 = sch.emit("vector", lambda E: E.tensor_reduce(
                sb["lse_s"][:, :], ebuf[:, :].rearrange("p (a j) -> p a j", j=8),
                axis=AX.X, op=ALU.add), deps=[x1], name="sk_rs")
            x3 = sch.emit("scalar", lambda E: E.activation(out=sb["lse_s"][:, :],
                          in_=sb["lse_s"][:, :], func=AF.Ln), deps=[x2], name="sk_ln")
            x4 = sch.emit("vector", lambda E: E.tensor_tensor(
                la[:, :].rearrange("p (a j) -> p a j", j=8),
                la[:, :].rearrange("p (a j) -> p a j", j=8),
                sb["lse_s"][:, :].to_broadcast((128, 512, 8)), ALU.subtract),
                deps=[x3], name="sk_sub")
            x5 = sch.emit("scalar", lambda E: E.activation(out=ebuf[:, :], in_=la[:, :],
                          func=AF.Exp), deps=[x4], name="sk_e2")
            x6 = sch.emit("vector", lambda E: E.tensor_reduce(
                sb["lse_s"][:, :], ebuf[:, :].rearrange("p (g i j) -> p g j i", i=8, j=8),
                axis=AX.X, op=ALU.add), deps=[x5], name="sk_cs")
            x7 = sch.emit("scalar", lambda E: E.activation(out=sb["lse_s"][:, :],
                          in_=sb["lse_s"][:, :], func=AF.Ln), deps=[x6], name="sk_ln2")
            x8 = sch.emit("vector", lambda E: E.tensor_tensor(
                la[:, :].rearrange("p (g i j) -> p g j i", i=8, j=8),
                la[:, :].rearrange("p (g i j) -> p g j i", i=8, j=8),
                sb["lse_s"][:, :].to_broadcast((128, 512, 8)), ALU.subtract),
                deps=[x7], name="sk_sub2")
        soft = sb["perm_tm"]
        x9 = sch.emit("scalar", lambda E: E.activation(out=soft[:, :], in_=soft[:, :],
                      func=AF.Exp), deps=[x8], name="sk_soft")
        sch.stage()
        m = sb["grd_m"]
        g1 = sch.emit("vector", lambda E: E.tensor_reduce(
            m[:, :], soft[:, :].rearrange("p (a j) -> p a j", j=8), axis=AX.X,
            op=ALU.max), deps=[], name="g_max")
        mv_r = m[:, :].rearrange("p (g r) -> p g r", r=8).to_broadcast((128, 64, 8, 8))
        e1 = sch.emit("vector", lambda E: E.tensor_tensor(
            sb["poolA"][:, :].rearrange("p (g r q) -> p g r q", r=8, q=8),
            mv_r.transpose((0, 1, 3, 2)), mv_r, ALU.is_gt), deps=[g1], name="g_gt")
        e2 = sch.emit("vector", lambda E: E.tensor_tensor(
            sb["poolC"][:, :].rearrange("p (g r q) -> p g r q", r=8, q=8),
            mv_r.transpose((0, 1, 3, 2)), mv_r, ALU.is_equal), deps=[e1], name="g_eq")
        lt_v = sb["lt8"][:, :].rearrange("p (r q) -> p r q", q=8).to_broadcast(
            (128, 8, 8, 64)).transpose((0, 3, 1, 2))
        e3 = sch.emit("vector", lambda E: E.tensor_tensor(
            sb["poolC"][:, :].rearrange("p (g r q) -> p g r q", r=8, q=8),
            sb["poolC"][:, :].rearrange("p (g r q) -> p g r q", r=8, q=8),
            lt_v, ALU.mult), deps=[e2], name="g_lt")
        e4 = sch.emit("vector", lambda E: E.tensor_tensor(
            sb["poolA"][:, :], sb["poolA"][:, :], sb["poolC"][:, :], ALU.add),
            deps=[e3], name="g_add")
        e5 = sch.emit("vector", lambda E: E.tensor_reduce(
            sb["grd_rank"][:, :], sb["poolA"][:, :].rearrange("p (a q) -> p a q", q=8),
            axis=AX.X, op=ALU.add), deps=[e4], name="g_rank")
        z1 = sch.emit("gpsimd", lambda E: E.memset(sb["grd_tk"][:, :], 0.0), deps=[],
                      name="g_tk0")
        hard = sb["poolC"]
        z2 = sch.emit("gpsimd", lambda E: E.memset(hard[:, :], 0.0), deps=[e3, e4],
                      name="g_h0")
        prev = [e5, z1, z2]
        for step in range(8):
            s1 = sch.emit("vector", lambda E, step=step: E.tensor_scalar(
                sb["grd_sel"][:, :], sb["grd_rank"][:, :], float(step), None,
                ALU.is_equal), deps=prev, name="g_sel")
            selv = sb["grd_sel"][:, :].rearrange("p (g r) -> p g r", r=8).to_broadcast(
                (128, 64, 8, 8))
            s2 = sch.emit("vector", lambda E, selv=selv: E.tensor_tensor(
                sb["poolA"][:, :].rearrange("p (g j r) -> p g j r", j=8, r=8),
                soft[:, :].rearrange("p (g r j) -> p g j r", r=8, j=8),
                selv.transpose((0, 1, 3, 2)), ALU.mult), deps=[s1], name="g_rv1")
            s3 = sch.emit("vector", lambda E: E.tensor_reduce(
                sb["grd_rv"][:, :], sb["poolA"][:, :].rearrange("p (a r) -> p a r", r=8),
                axis=AX.X, op=ALU.add), deps=[s2], name="g_rv2")
            s4 = sch.emit("vector", lambda E: E.scalar_tensor_tensor(
                sb["grd_rv"][:, :], sb["grd_tk"][:, :], -1e30, sb["grd_rv"][:, :],
                ALU.mult, ALU.add), deps=[s3], name="g_sc")
            s5 = sch.emit("vector", lambda E: E.tensor_reduce(
                sb["grd_cm"][:, :], sb["grd_rv"][:, :].rearrange("p (g j) -> p g j", j=8),
                axis=AX.X, op=ALU.max), deps=[s4], name="g_cm")
            s6 = sch.emit("vector", lambda E: E.tensor_tensor(
                sb["grd_oh"][:, :].rearrange("p (g j) -> p g j", j=8),
                sb["grd_rv"][:, :].rearrange("p (g j) -> p g j", j=8),
                sb["grd_cm"][:, :].to_broadcast((128, 64, 8)), ALU.is_equal),
                deps=[s5], name="g_oh")
            s7 = sch.emit("vector", lambda E: E.tensor_tensor(
                sb["grd_tk"][:, :], sb["grd_tk"][:, :], sb["grd_oh"][:, :], ALU.add),
                deps=[s6], name="g_tk")
            ohv = sb["grd_oh"][:, :].rearrange("p (g j) -> p g j", j=8).to_broadcast(
                (128, 64, 8, 8)).transpose((0, 1, 3, 2))
            s8 = sch.emit("vector", lambda E, selv=selv, ohv=ohv: E.tensor_tensor(
                sb["poolA"][:, :].rearrange("p (g r j) -> p g r j", r=8, j=8),
                selv, ohv, ALU.mult), deps=[s6], name="g_hu1")
            s9 = sch.emit("vector", lambda E: E.tensor_tensor(
                hard[:, :], hard[:, :], sb["poolA"][:, :], ALU.add), deps=[s8], name="g_hu2")
            prev = [s7, s9]
        sch.stage()
        c1 = sch.emit("vector", lambda E: E.tensor_copy(
            sb["ss_P"][:, :].rearrange("p (h tt c) -> p h tt c", h=4, tt=16, c=64),
            hard[:, :].rearrange("p (tt h c) -> p h tt c", tt=16, h=4, c=64)),
            deps=[], name="ss_h")
        for g2 in range(64):
            sp = b * 64 + g2
            sch.dma(qs[g2 % 3], sb["scanT_P"][sp:sp + 1, :],
                    sb["ss_P"][:, g2 * 64:(g2 + 1) * 64], deps=[c1], name="scTP")
            sch.dma(qs[(g2 + 1) % 3], sb["scanT_G"][sp:sp + 1, :],
                    sb["ss_G"][:, g2 * 24:(g2 + 1) * 24], deps=[], name="scTG")
        sch.stage()
    sb_ = sb
    i1 = sch.emit("vector", lambda E: E.tensor_copy(sb_["scM"][:, :], sb_["ident8rep"][:, :]),
                  deps=[], name="scM0")
    i2 = sch.emit("gpsimd", lambda E: E.memset(sb_["scV"][:, :], 0.0), deps=[], name="scV0")
    prev = [i1, i2]
    for tau in range(128):
        gbase = tau * 24
        cp = sch.emit("vector", lambda E, tau=tau: E.tensor_copy(
            sb_["sc_p32"][:, :], sb_["scanT_P"][:, tau * 64:(tau + 1) * 64]),
            deps=prev, name="p32")
        P_ij = sb_["sc_p32"][:, :].rearrange("p (i j) -> p i j", j=8)
        gsl = sb_["sc_gs"][:, :]
        csl = sb_["sc_c"][:, :]
        u1 = sch.emit("vector", lambda E, gbase=gbase, gsl=gsl: E.tensor_tensor(
            gsl, sb_["scanT_G"][:, gbase + 8:gbase + 16], sb_["scanT_G"][:, gbase:gbase + 8],
            ALU.mult), deps=prev, name="gs")
        u2 = sch.emit("vector", lambda E, gbase=gbase: E.tensor_scalar(
            sb_["sc_om"][:, :], sb_["scanT_G"][:, gbase + 8:gbase + 16], -1.0, 1.0,
            ALU.mult, ALU.add), deps=prev, name="om")
        u3 = sch.emit("vector", lambda E, gbase=gbase, csl=csl: E.tensor_tensor(
            csl, sb_["sc_om"][:, :], sb_["scanT_G"][:, gbase + 16:gbase + 24], ALU.mult),
            deps=[u2], name="cc")
        Pb = P_ij.to_broadcast((128, 8, 8, 8)).transpose((0, 1, 3, 2))
        Mb = sb_["scM"][:, :].rearrange("p (j k) -> p j k", k=8).to_broadcast(
            (128, 8, 8, 8)).transpose((0, 3, 2, 1))
        u4 = sch.emit("vector", lambda E, Pb=Pb, Mb=Mb: E.tensor_tensor(
            sb_["sc_t1"][:, :].rearrange("p (i k j) -> p i k j", i=8, k=8, j=8),
            Pb, Mb, ALU.mult), deps=[cp] + prev, name="pm1")
        u5 = sch.emit("vector", lambda E: E.tensor_reduce(
            sb_["sc_pm"][:, :], sb_["sc_t1"][:, :].rearrange("p (a j) -> p a j", j=8),
            axis=AX.X, op=ALU.add), deps=[u4], name="pm2")
        u6 = sch.emit("vector", lambda E, gsl=gsl: E.tensor_tensor(
            sb_["scM"][:, :].rearrange("p (i k) -> p i k", k=8),
            sb_["sc_pm"][:, :].rearrange("p (i k) -> p i k", k=8),
            gsl.to_broadcast((128, 8, 8)), ALU.mult), deps=[u5, u1], name="pm3")
        vb = sb_["scV"][:, :].to_broadcast((128, 8, 8)).transpose((0, 2, 1))
        u7 = sch.emit("vector", lambda E, P_ij=P_ij, vb=vb: E.tensor_tensor(
            sb_["sc_t3"][:, :].rearrange("p (i j) -> p i j", j=8), P_ij, vb, ALU.mult),
            deps=[cp] + prev, name="pv1")
        u8 = sch.emit("vector", lambda E: E.tensor_reduce(
            sb_["sc_pv"][:, :], sb_["sc_t3"][:, :].rearrange("p (i j) -> p i j", j=8),
            axis=AX.X, op=ALU.add), deps=[u7], name="pv2")
        u9 = sch.emit("vector", lambda E, gsl=gsl: E.tensor_tensor(
            sb_["sc_pv2"][:, :], sb_["sc_pv"][:, :], gsl, ALU.mult), deps=[u8, u1],
            name="pv3")
        u10 = sch.emit("vector", lambda E, csl=csl: E.tensor_tensor(
            sb_["scV"][:, :], sb_["sc_pv2"][:, :], csl, ALU.add), deps=[u9, u3, u7],
            name="pv4")
        prev = [u6, u10]
    f1 = sch.emit("vector", lambda E: E.tensor_copy(sb_["summ"][:, 0:64], sb_["scM"][:, :]),
                  deps=prev, name="sm")
    f2 = sch.emit("vector", lambda E: E.tensor_copy(sb_["summ"][:, 64:72], sb_["scV"][:, :]),
                  deps=prev, name="sv")
    sch.stage()
    qs = ["sync", "scalar", "gpsimd"]
    dts = []
    for b2 in range(2):
        for h in range(4):
            base_p = b2 * 64 + h * 16
            dts.append(sch.dma(qs[(b2 * 4 + h) % 3], sb_["pf"][b2 * 4 + h:b2 * 4 + h + 1, :],
                               sb_["summ"][base_p:base_p + 16, :], deps=[f1, f2], name="pfd"))
    z3 = sch.emit("gpsimd", lambda E: E.memset(sb_["pf_h"][:, :], 0.0), deps=[], name="pfh0")
    prev = dts + [z3]
    for c in range(16):
        w1 = sch.emit("vector", lambda E, c=c: E.tensor_copy(
            sb_["hstarts"][:, c * 8:(c + 1) * 8], sb_["pf_h"][:, :]), deps=prev, name="hs_cp")
        Mc = sb_["pf"][:, c * 72:c * 72 + 64].rearrange("p (i k) -> p i k", k=8)
        hb = sb_["pf_h"][:, :].to_broadcast((8, 8, 8)).transpose((0, 2, 1))
        w2 = sch.emit("vector", lambda E, Mc=Mc, hb=hb: E.tensor_tensor(
            sb_["pf_t"][:, :].rearrange("p (i k) -> p i k", k=8), Mc, hb, ALU.mult),
            deps=prev, name="hs_m")
        w3 = sch.emit("vector", lambda E: E.tensor_reduce(
            sb_["pf_h2"][:, :], sb_["pf_t"][:, :].rearrange("p (i k) -> p i k", k=8),
            axis=AX.X, op=ALU.add), deps=[w2], name="hs_r")
        w4 = sch.emit("vector", lambda E, c=c: E.tensor_tensor(
            sb_["pf_h"][:, :], sb_["pf_h2"][:, :], sb_["pf"][:, c * 72 + 64:c * 72 + 72],
            ALU.add), deps=[w3, w1], name="hs_a")
        prev = [w4]
    dts2 = []
    for b2 in range(2):
        for h in range(4):
            base_p = b2 * 64 + h * 16
            dts2.append(sch.dma(qs[(b2 * 4 + h) % 3], sb_["hst"][base_p:base_p + 16, :],
                                sb_["hstarts"][b2 * 4 + h:b2 * 4 + h + 1, :],
                                deps=prev, name="hsd"))
    sch.stage()
    prev = list(dts2)
    hprev_ap = sb_["hst"][:, :]
    for tau in range(128):
        cp = sch.emit("vector", lambda E, tau=tau: E.tensor_copy(
            sb_["sc_p32"][:, :], sb_["scanT_P"][:, tau * 64:(tau + 1) * 64]),
            deps=prev, name="p32b")
        P_ij = sb_["sc_p32"][:, :].rearrange("p (i j) -> p i j", j=8)
        gbase = tau * 24
        g1b = sch.emit("vector", lambda E, gbase=gbase: E.tensor_tensor(
            sb_["sc_gs"][:, :], sb_["scanT_G"][:, gbase + 8:gbase + 16],
            sb_["scanT_G"][:, gbase:gbase + 8], ALU.mult), deps=prev, name="gs2")
        g2b = sch.emit("vector", lambda E, gbase=gbase: E.tensor_scalar(
            sb_["sc_om"][:, :], sb_["scanT_G"][:, gbase + 8:gbase + 16], -1.0, 1.0,
            ALU.mult, ALU.add), deps=prev, name="om2")
        g3b = sch.emit("vector", lambda E, gbase=gbase: E.tensor_tensor(
            sb_["sc_c"][:, :], sb_["sc_om"][:, :], sb_["scanT_G"][:, gbase + 16:gbase + 24],
            ALU.mult), deps=[g2b], name="cc2")
        hb = hprev_ap.to_broadcast((128, 8, 8)).transpose((0, 2, 1))
        v1 = sch.emit("vector", lambda E, P_ij=P_ij, hb=hb: E.tensor_tensor(
            sb_["sc_t3"][:, :].rearrange("p (i j) -> p i j", j=8), P_ij, hb, ALU.mult),
            deps=[cp] + prev, name="ap1")
        v2 = sch.emit("vector", lambda E: E.tensor_reduce(
            sb_["sc_pv"][:, :], sb_["sc_t3"][:, :].rearrange("p (i j) -> p i j", j=8),
            axis=AX.X, op=ALU.add), deps=[v1], name="ap2")
        v3 = sch.emit("vector", lambda E: E.tensor_tensor(
            sb_["sc_pv2"][:, :], sb_["sc_pv"][:, :], sb_["sc_gs"][:, :],
            ALU.mult), deps=[v2, g1b], name="ap3")
        yslot = sb_["ys"][:, :].rearrange("p (n tau) -> p n tau", n=8)[:, :, tau]
        v4 = sch.emit("vector", lambda E, yslot=yslot: E.tensor_tensor(
            yslot, sb_["sc_pv2"][:, :],
            sb_["sc_c"][:, :], ALU.add), deps=[v3, g3b], name="ap4")
        hprev_ap = yslot
        prev = [v4]
    cb = sch.emit("vector", lambda E: E.tensor_copy(sb_["ys_b"][:, :], sb_["ys"][:, :]),
                  deps=prev, name="ysb")
    sch.stage()
    dts3 = []
    for b2 in range(2):
        for h in range(4):
            for tt in range(16):
                sp = b2 * 64 + h * 16 + tt
                src2 = sb_["ys_b"][sp:sp + 1, :].rearrange("p (n tau) -> p n tau", n=8)
                dst = sb_["ys_r"][h * 8:(h + 1) * 8, b2 * 2048 + tt * 128: b2 * 2048 + tt * 128 + 128]
                dts3.append(sch.dma(qs[sp % 3], dst, src2, deps=[cb], name="ysr"))
    sch.stage()
    ev_prev = {0: None, 1: None}
    for b2 in range(2):
        col0 = b2 * S
        for ft in range(8):
            for tc in range(4):
                k = (ft * 4 + tc) % 2
                ps = [psA, psB][k]
                mm = sch.emit("tensor", lambda E, ft=ft, tc=tc, b2=b2, ps=ps: E.matmul(
                    ps[:, 0:512], sb_["aw_outb"][0:32, ft * 128:(ft + 1) * 128],
                    sb_["ys_r"][0:32, b2 * 2048 + tc * 512: b2 * 2048 + tc * 512 + 512],
                    start=True, stop=True), deps=[ev_prev[k]] + dts3, name="ow_mm")
                xl = sb_["xtile"][:, k * 512:(k + 1) * 512]
                ev = sb_["evtmp"][:, k * 512:(k + 1) * 512]
                tl = sch.dma("sync", xl,
                             x1_fm[ft * 128:(ft + 1) * 128, col0 + tc * 512: col0 + tc * 512 + 512],
                             deps=[ev_prev[k]], name="x1_ld")
                e1 = sch.emit("vector", lambda E, ps=ps, ft=ft, xl=xl, ev=ev: E.scalar_tensor_tensor(
                    ev, ps[:, 0:512], sb_["a_outb"][:, ft:ft + 1],
                    xl, ALU.add, ALU.add), deps=[mm, tl], name="ow_ev")
                st = sch.dma("scalar", x1_fm[ft * 128:(ft + 1) * 128, col0 + tc * 512: col0 + tc * 512 + 512],
                             ev, deps=[e1], name="x1_st2")
                ev_prev[k] = st
    sch.stage()
    return []



def alloc_moe(ctx):
    ctx.scr_top("m_wst", (128, 1024), F32)
    ctx.scr_top("m_wv", (128, 2 * 16 * 6), F32)
    ctx.scr_top("m_s2u", (128, 2 * 16 * 6), U32)
    ctx.scr_top("m_mrows", (33, 2048), F32)
    ctx.scr_top("m_cnt8", (8, 64), F32)
    ctx.scr_top("m_rmask", (8, 1), F32)
    ctx.scr_top("m_tokid", (128, 32), F32)
    ctx.scr_top("m_rsum", (128, 1024), F32)
    ctx.scr_top("m_ctm", (128, 1024), F32)
    ctx.scr_top("m_xidx", (128, 32), U32)
    ctx.scr_top("m_xnbt", (128, 1024), BF16)


def alloc_moe_routing(ctx):
    ctx.scr("m_plmu", (128, 2048), F32)
    ctx.scr("m_plr", (128, 2048), F32)
    ctx.scr("m_t1", (128, 2048), F32)
    ctx.scr("m_xnb", (128, NDT * 2048), BF16)
    ctx.scr("m_xtile", (128, 2048), F32)
    ctx.scr("m_gw", (128, NDT * 64), BF16)
    ctx.scr("m_gwl", (128, NDT * 64), BF16)
    ctx.scr("m_xlo", (128, 2048), BF16)
    ctx.scr("m_gbrow", (1, 64), F32)
    ctx.scr("m_gbrep", (128, 64), F32)
    ctx.scr("m_sc", (128, 16 * 64), F32)
    ctx.scr("m_sc2", (128, 16 * 64), F32)
    ctx.scr("m_srow", (128, 16), F32)
    ctx.scr("m_oh", (128, 64), F32)
    ctx.scr("m_ohp", (128, 64), F32)
    ctx.scr("m_topv", (128, 16 * 8), F32)
    ctx.scr("m_topi", (128, 16 * 8), U16)
    ctx.scr("m_ef", (128, 2 * 16 * 6), F32)
    ctx.scr("m_r1", (128, 16 * 6), F32)
    ctx.scr("m_p6", (128, 16 * 36), F32)
    ctx.scr("m_I", (128, 4 * 6 * 64), F32)
    ctx.scr("m_c", (128, 2 * 16 * 64), F32)
    ctx.scr("m_pfx", (128, 2 * 16 * 64), F32)
    ctx.scr("m_fp", (128, 16 * 64), F32)
    ctx.scr("m_pos", (128, 16 * 6), F32)
    ctx.scr("m_slot", (128, 16 * 6), F32)
    ctx.scr("m_val", (128, 16 * 6), F32)
    ctx.scr("m_rec", (128, 16 * 6 * 4), F32)


def emit_moe(ctx, x_fm, x2_fm, z_dram, pass_id, deps, cut=10):
    nc, sch = ctx.nc, ctx.sch
    sb = ctx.sb
    d = ctx.dram
    psA, psB, psC, psD, psE = (ctx.ps[k] for k in ("psA", "psB", "psC", "psD", "psE"))
    ident = sb["ident128"]
    P = pass_id
    qs = ["sync", "scalar", "gpsimd"]

    ag_cnt_i = ctx.dram_tmp(f"agci{P}", (1, 64), F32)
    ag_cnt_o = ctx.dram_tmp(f"agco{P}", (8, 64), F32, shared=True)
    ag_rec_i = ctx.dram_tmp(f"agri{P}", (32768, 4), F32)
    ag_rec_o = ctx.dram_tmp(f"agro{P}", (8 * 32768, 4), F32, shared=True)
    ag_xn_i = ctx.dram_tmp(f"agxi{P}", (4096, 1024), BF16)
    ag_xn_o = ctx.dram_tmp(f"agxo{P}", (8 * 4096, 1024), BF16, shared=True)
    ag_cb_i = ctx.dram_tmp(f"agbi{P}", (4096, 1024), F32)
    ag_cb_o = ctx.dram_tmp(f"agbo{P}", (8 * 4096, 1024), F32, shared=True)

    sch.set_phase(f"m{P}_ld")
    ctx.scratch_reset()
    alloc_moe_routing(ctx)
    zoff = ctx._bump
    prev_w = list(deps)
    wl = []
    for dt in range(NDT):
        t1 = sch.dma("sync", sb["m_wst"][:, 0:64], d["gate_wT"][dt * 128:(dt + 1) * 128, :],
                     deps=prev_w, name="gw_l")
        t2 = sch.emit("vector", lambda E, dt=dt: E.tensor_copy(
            sb["m_gw"][:, dt * 64:(dt + 1) * 64], sb["m_wst"][:, 0:64]), deps=[t1],
            name="gw_c")
        t2u = sch.emit("vector", lambda E, dt=dt: E.tensor_copy(
            sb["m_wst"][:, 128:192], sb["m_gw"][:, dt * 64:(dt + 1) * 64]),
            deps=[t2], name="gw_up")
        t2b = sch.emit("vector", lambda E: E.tensor_tensor(
            sb["m_wst"][:, 64:128], sb["m_wst"][:, 0:64],
            sb["m_wst"][:, 128:192], ALU.subtract), deps=[t2u],
            name="gw_lo1")
        t2c = sch.emit("vector", lambda E, dt=dt: E.tensor_copy(
            sb["m_gwl"][:, dt * 64:(dt + 1) * 64], sb["m_wst"][:, 64:128]),
            deps=[t2b], name="gw_lo2")
        prev_w = [t2b]
        wl += [t2, t2c]
    tgb = sch.dma("sync", sb["m_gbrow"][0:1, :], d["L0_ffn_gate_b"][:].unsqueeze(0),
                  deps=deps, name="gb_l")
    bcg = sch.emit("tensor", lambda E: E.matmul(psD[:, 0:64], sb["ones_row"][0:1, :],
                   sb["m_gbrow"][0:1, :], start=True, stop=True), deps=[tgb], name="gb_b")
    tgb2 = sch.emit("vector", lambda E: E.tensor_copy(sb["m_gbrep"][:, :], psD[:, 0:64]),
                    deps=[bcg], name="gb_e")
    trm = sch.dma("sync", sb["m_rmask"][:, :], d["rankmask"][:, :], deps=deps, name="rm")
    tti = sch.dma("sync", sb["m_tokid"][:, :], d["tokid"][:, :], deps=deps, name="ti")
    sch.stage()
    if cut <= 1:
        return

    if P == 0:
        wsz = ctx.dram_tmp("wsz", (11, 128, 8, 256), BF16)
        ws2z = ctx.dram_tmp("ws2z", (8, 128, 11, 128), BF16)
    else:
        wsz = ctx.dram["wsz"]
        ws2z = ctx.dram["ws2z"]
    ctx.scr("z_wk", (128, 2 * 2048), BF16)
    ctx.scr("z_w2k", (128, 2 * 1408), BF16)
    ctx.scr("z_zh", (128, 11 * 512), BF16)
    ctx.scr("z_stmp", (128, 512), F32)
    ctx.scr("z_stg", (128, 256), BF16)
    prev_w = []
    zl = []
    for dt in range(NDT if P == 0 else 0):
        for scn in range(11):
            t1 = sch.dma("scalar", sb["m_wst"][:, 0:128],
                         d["ws1T"][dt * 128:(dt + 1) * 128, scn * 128:(scn + 1) * 128],
                         deps=prev_w, name="zc_l1")
            c1 = sch.emit("vector", lambda E: E.tensor_copy(sb["z_stg"][:, 0:128],
                          sb["m_wst"][:, 0:128]), deps=[t1], name="zc_c1")
            t2 = sch.dma("scalar", sb["m_wst"][:, 0:128],
                         d["ws3T"][dt * 128:(dt + 1) * 128, scn * 128:(scn + 1) * 128],
                         deps=[c1], name="zc_l2")
            c2 = sch.emit("vector", lambda E: E.tensor_copy(sb["z_stg"][:, 128:256],
                          sb["m_wst"][:, 0:128]), deps=[t2], name="zc_c2")
            s1 = sch.dma("gpsimd", wsz[scn, :, dt, :], sb["z_stg"][:, :], deps=[c2],
                         name="zc_s")
            prev_w = [c2, s1]
            zl.append(s1)
    for dt in range(NDT if P == 0 else 0):
        for scn in range(11):
            t1 = sch.dma("scalar", sb["m_wst"][:, 0:128],
                         d["ws2T"][scn * 128:(scn + 1) * 128, dt * 128:(dt + 1) * 128],
                         deps=prev_w, name="z2_l")
            c1 = sch.emit("vector", lambda E: E.tensor_copy(sb["z_stg"][:, 0:128],
                          sb["m_wst"][:, 0:128]), deps=[t1], name="z2_c")
            s1 = sch.dma("gpsimd", ws2z[dt, :, scn, :], sb["z_stg"][:, 0:128], deps=[c1],
                         name="z2_s")
            prev_w = [c1, s1]
            zl.append(s1)
    sch.stage()
    if cut <= 2:
        return

    xn_dma = []
    ts_evac = []
    for b in range(B):
        sch.set_phase(f"m{P}_rt{b}")
        col0 = b * S
        xt = sb["m_xtile"]
        mm_toks = []
        prev = []
        for dt in range(NDT):
            tl = sch.dma("sync", xt[:, :], x_fm[dt * 128:(dt + 1) * 128, col0:col0 + S],
                         deps=prev, name="mx_ld")
            tq = sch.emit("scalar", lambda E: E.activation(out=sb["m_t1"][:, :],
                          in_=xt[:, :], func=AF.Square), deps=[tl] + prev, name="m_sq")
            m1 = m2 = None
            for c0 in range(0, S, 512):
                m1 = sch.emit("tensor", lambda E, c0=c0, dt=dt: E.matmul(
                    psE[0:1, c0:c0 + 512], sb["ones_col"][:, 0:1], xt[:, c0:c0 + 512],
                    start=(dt == 0), stop=(dt == NDT - 1)), deps=[tl], name="m_s")
                m2 = sch.emit("tensor", lambda E, c0=c0, dt=dt: E.matmul(
                    [psA, psB, psC, psD][c0 // 512][0:1, 0:512], sb["ones_col"][:, 0:1],
                    sb["m_t1"][:, c0:c0 + 512],
                    start=(dt == 0), stop=(dt == NDT - 1)), deps=[tq], name="m_q")
                if dt == NDT - 1:
                    mm_toks += [m1, m2]
            prev = [tq, m1, m2]
        mu, var = sb["mu_row"], sb["var_row"]
        t1 = sch.emit("vector", lambda E: E.tensor_scalar(mu[0:1, :], psE[0:1, :],
                      1.0 / D, None, ALU.mult), deps=mm_toks, name="m_mu")
        sqc = []
        for c0 in range(0, 2048, 512):
            sqc.append(sch.emit("vector", lambda E, c0=c0: E.tensor_copy(
                var[0:1, c0:c0 + 512], [psA, psB, psC, psD][c0 // 512][0:1, 0:512]),
                deps=mm_toks, name="m_sc"))
        t2 = sch.emit("vector", lambda E: E.tensor_tensor(psE[0:1, :], mu[0:1, :],
                      mu[0:1, :], ALU.mult), deps=[t1], name="m_m2")
        t3 = sch.emit("vector", lambda E: E.scalar_tensor_tensor(var[0:1, :], var[0:1, :],
                      1.0 / D, psE[0:1, :], ALU.mult, ALU.subtract),
                      deps=[t2] + sqc, name="m_v")
        t4 = sch.emit("scalar", lambda E: E.activation(out=var[0:1, :], in_=var[0:1, :],
                      func=AF.Sqrt, bias=sb["eps_col"][0:1, 0:1]), deps=[t3], name="m_st")
        t5 = sch.emit("vector", lambda E: E.reciprocal(var[0:1, :], var[0:1, :]),
                      deps=[t4], name="m_rc")
        bmu = emit_bcast(ctx, sb["m_plmu"], mu, [t1, t3], "m_bm")
        t5b = sch.emit("vector", lambda E: E.tensor_copy(mu[0:1, :], var[0:1, :]),
                       deps=[t5] + bmu, name="m_r0")
        brr = emit_bcast(ctx, sb["m_plr"], mu, [t5b], "m_br")
        sch.stage()
        prev = []
        g_last = None
        for dt in range(NDT):
            tl = sch.dma("sync", xt[:, :], x_fm[dt * 128:(dt + 1) * 128, col0:col0 + S],
                         deps=prev, name="mx_ld2")
            a1 = sch.emit("vector", lambda E: E.tensor_tensor(sb["m_t1"][:, :], xt[:, :],
                          sb["m_plmu"][:, :], ALU.subtract), deps=[tl], name="m_a1")
            a1b = sch.emit("vector", lambda E: E.tensor_tensor(sb["m_t1"][:, :],
                           sb["m_t1"][:, :], sb["m_plr"][:, :], ALU.mult), deps=[a1],
                           name="m_a1b")
            a2 = sch.emit("vector", lambda E, dt=dt: E.tensor_copy(
                sb["m_xnb"][:, dt * 2048:(dt + 1) * 2048], sb["m_t1"][:, :]),
                deps=[a1b], name="m_a2")
            a2u = sch.emit("vector", lambda E, dt=dt: E.tensor_copy(
                xt[:, :], sb["m_xnb"][:, dt * 2048:(dt + 1) * 2048]),
                deps=[a2], name="m_a2u")
            a3 = sch.emit("vector", lambda E: E.tensor_tensor(
                sb["m_t1"][:, :], sb["m_t1"][:, :], xt[:, :], ALU.subtract),
                deps=[a2u] + ([g_last] if g_last else []), name="m_a3")
            a4 = sch.emit("vector", lambda E: E.tensor_copy(sb["m_xlo"][:, :],
                          sb["m_t1"][:, :]), deps=[a3], name="m_a4")
            for c0 in range(0, S, 512):
                sch.emit("tensor", lambda E, c0=c0, dt=dt: E.matmul(
                    psE[0:64, c0:c0 + 512], sb["m_gw"][:, dt * 64:(dt + 1) * 64],
                    sb["m_xnb"][:, dt * 2048 + c0: dt * 2048 + c0 + 512],
                    start=(dt == 0), stop=False), deps=wl + [a2, t3], name="m_gmm")
                sch.emit("tensor", lambda E, c0=c0, dt=dt: E.matmul(
                    psE[0:64, c0:c0 + 512], sb["m_gwl"][:, dt * 64:(dt + 1) * 64],
                    sb["m_xnb"][:, dt * 2048 + c0: dt * 2048 + c0 + 512],
                    start=False, stop=False), deps=[a2], name="m_gml")
                g_last = sch.emit("tensor", lambda E, c0=c0, dt=dt: E.matmul(
                    psE[0:64, c0:c0 + 512], sb["m_gw"][:, dt * 64:(dt + 1) * 64],
                    sb["m_xlo"][:, c0:c0 + 512],
                    start=False, stop=(dt == NDT - 1)), deps=[a4], name="m_gmx")
            prev = [a3]
        sch.stage()
        gev = []
        for c0 in range(0, 2048, 512):
            gev.append(sch.emit("vector", lambda E, c0=c0: E.tensor_copy(
                sb["m_ctm"][0:64, 0:512], psE[0:64, c0:c0 + 512]),
                deps=[g_last] + gev[-1:], name="m_gev"))
            for q in range(4):
                tt = c0 // 128 + q
                k = (tt) % 2
                ps = [psC, psD][k]
                tmx = sch.emit("tensor", lambda E, q=q, ps=ps: E.transpose(
                    ps[:, 0:64], sb["m_ctm"][0:64, q * 128:(q + 1) * 128],
                    ident[0:64, 0:64]), deps=[gev[-1]], name="m_gtr")
                evx = sch.emit("vector", lambda E, tt=tt, ps=ps: E.tensor_copy(
                    sb["m_sc"][:, tt * 64:(tt + 1) * 64], ps[:, 0:64]), deps=[tmx],
                    name="m_gte")
                gev.append(evx)
        if "dbg_sc" in d and b == 0 and P == 0:
            sch.dma("sync", d["dbg_sc"][:, :], sb["m_sc"][:, :], deps=list(gev),
                    name="dbg_sc")
        sch.stage()
        tr_prev = {0: None, 1: None}
        for tt in range(16):
            evs = []
            for dt in range(NDT):
                k = dt % 2
                ps = [psC, psD][k]
                tm = sch.emit("tensor", lambda E, dt=dt, tt=tt, ps=ps: E.transpose(
                    ps[:, 0:64].bitcast(BF16), sb["m_xnb"][:, dt * 2048 + tt * 128: dt * 2048 + tt * 128 + 128],
                    sb["ident128b"][:, :]), deps=[tr_prev[k]], name="m_tr")
                ev = sch.emit("vector", lambda E, dt=dt, ps=ps: E.tensor_copy(
                    sb["m_ctm"][:, dt * 128:(dt + 1) * 128], ps[:, 0:64].bitcast(BF16)), deps=[tm],
                    name="m_tre")
                tr_prev[k] = ev
                evs.append(ev)
            cb = sch.emit("vector", lambda E: E.tensor_copy(
                sb["m_rsum"][:, :].bitcast(BF16)[:, 0:1024] if False else sb["m_rec"][:, 0:384],
                sb["m_ctm"][:, 0:384]), deps=evs, name="m_dum") if False else None
            cb = sch.emit("vector", lambda E: E.tensor_copy(sb["m_xnbt"][:, :],
                          sb["m_ctm"][:, :]), deps=evs, name="m_cb")
            xd = sch.dma(qs[tt % 3], ag_xn_i[b * 2048 + tt * 128: b * 2048 + tt * 128 + 128, :],
                         sb["m_xnbt"][:, :], deps=[cb], name="m_xnd")
            xn_dma.append(xd)
            tr_prev = {0: xd, 1: xd}
        sch.stage()
        sch.set_phase(f"m{P}_z{b}")
        for tc in range(4):
            zh_toks = []
            wk_free = [None, None]
            for scn in range(11):
                kb = scn % 2
                wkl = sch.dma("sync", sb["z_wk"][:, kb * 2048:(kb + 1) * 2048],
                              wsz[scn, :, :, :], deps=[wk_free[kb]] + zl, name="m_zwl")
                mm1 = mm3 = None
                for dt in range(NDT):
                    mm1 = sch.emit("tensor", lambda E, scn=scn, dt=dt, tc=tc, kb=kb: E.matmul(
                        psA[:, 0:512],
                        sb["z_wk"][:, kb * 2048 + dt * 256: kb * 2048 + dt * 256 + 128],
                        sb["m_xnb"][:, dt * 2048 + tc * 512: dt * 2048 + tc * 512 + 512],
                        start=(dt == 0), stop=(dt == NDT - 1), skip_group_check=True),
                        deps=[wkl] + zh_toks, name="m_z1m")
                    mm3 = sch.emit("tensor", lambda E, scn=scn, dt=dt, tc=tc, kb=kb: E.matmul(
                        psB[:, 0:512],
                        sb["z_wk"][:, kb * 2048 + dt * 256 + 128: kb * 2048 + dt * 256 + 256],
                        sb["m_xnb"][:, dt * 2048 + tc * 512: dt * 2048 + tc * 512 + 512],
                        start=(dt == 0), stop=(dt == NDT - 1), skip_group_check=True),
                        deps=[wkl] + zh_toks, name="m_z3m")
                e1 = sch.emit("scalar", lambda E: E.activation(
                    out=sb["z_stmp"][:, :], in_=psA[:, 0:512], func=AF.Silu),
                    deps=[mm1], name="m_zsil")
                e2 = sch.emit("vector", lambda E, scn=scn: E.tensor_tensor(
                    sb["z_zh"][:, scn * 512:(scn + 1) * 512], sb["z_stmp"][:, :],
                    psB[:, 0:512], ALU.mult), deps=[e1, mm3], name="m_zhm")
                zh_toks.append(e2)
                wk_free[kb] = mm3
            ps_free = None
            w2_free = [None, None]
            for dt in range(NDT):
                kb = dt % 2
                w2l = sch.dma("sync", sb["z_w2k"][:, kb * 1408:(kb + 1) * 1408],
                              ws2z[dt, :, :, :], deps=[w2_free[kb]] + zl, name="m_z2wl")
                mmz = None
                for scn in range(11):
                    mmz = sch.emit("tensor", lambda E, dt=dt, scn=scn, kb=kb: E.matmul(
                        psE[:, (dt % 4) * 512:(dt % 4) * 512 + 512],
                        sb["z_w2k"][:, kb * 1408 + scn * 128: kb * 1408 + scn * 128 + 128],
                        sb["z_zh"][:, scn * 512:(scn + 1) * 512],
                        start=(scn == 0), stop=(scn == 10), skip_group_check=True),
                        deps=zh_toks + [w2l] + ([ps_free] if dt >= 4 else []), name="m_zmm")
                ze = sch.emit("vector", lambda E, dt=dt: E.tensor_copy(
                    sb["m_t1"][:, (dt % 4) * 512:(dt % 4) * 512 + 512],
                    psE[:, (dt % 4) * 512:(dt % 4) * 512 + 512]), deps=[mmz], name="m_ze")
                zd2 = sch.dma(qs[dt % 3], z_dram[dt * 128:(dt + 1) * 128, col0 + tc * 512: col0 + tc * 512 + 512],
                              sb["m_t1"][:, (dt % 4) * 512:(dt % 4) * 512 + 512],
                              deps=[ze], name="m_zd")
                ps_free = zd2
                w2_free[kb] = mmz
            sch.stage()
        sch.set_phase(f"m{P}_tk{b}")
        s1 = sch.emit("vector", lambda E: E.tensor_reduce(
            sb["m_srow"][:, :], sb["m_sc"][:, :].rearrange("p (a e) -> p a e", e=64),
            axis=AX.X, op=ALU.max), deps=[], name="m_sm1")
        s2 = sch.emit("vector", lambda E: E.tensor_tensor(
            sb["m_sc"][:, :].rearrange("p (a e) -> p a e", e=64),
            sb["m_sc"][:, :].rearrange("p (a e) -> p a e", e=64),
            sb["m_srow"][:, :].to_broadcast((128, 16, 64)), ALU.subtract), deps=[s1],
            name="m_sm2")
        s3 = sch.emit("scalar", lambda E: E.activation(out=sb["m_sc"][:, :],
                      in_=sb["m_sc"][:, :], func=AF.Exp), deps=[s2], name="m_sm3")
        s4 = sch.emit("vector", lambda E: E.tensor_reduce(
            sb["m_srow"][:, :], sb["m_sc"][:, :].rearrange("p (a e) -> p a e", e=64),
            axis=AX.X, op=ALU.add), deps=[s3], name="m_sm4")
        s5 = sch.emit("vector", lambda E: E.reciprocal(sb["m_srow"][:, :],
                      sb["m_srow"][:, :]), deps=[s4], name="m_sm5")
        s6 = sch.emit("vector", lambda E: E.tensor_tensor(
            sb["m_sc"][:, :].rearrange("p (a e) -> p a e", e=64),
            sb["m_sc"][:, :].rearrange("p (a e) -> p a e", e=64),
            sb["m_srow"][:, :].to_broadcast((128, 16, 64)), ALU.mult), deps=[s5],
            name="m_sm6")
        s7 = sch.emit("vector", lambda E: E.tensor_tensor(
            sb["m_sc2"][:, :].rearrange("p (a e) -> p a e", e=64),
            sb["m_sc"][:, :].rearrange("p (a e) -> p a e", e=64),
            sb["m_gbrep"][:, :].to_broadcast((128, 64, 16)).transpose((0, 2, 1)),
            ALU.add), deps=[s6, tgb2], name="m_sc2")
        prevv = [s7]
        eftoks = []
        for tt in range(16):
            tv = sch.emit("vector", lambda E, tt=tt: E.max_with_indices(
                sb["m_topv"][:, tt * 8:(tt + 1) * 8], sb["m_topi"][:, tt * 8:(tt + 1) * 8],
                sb["m_sc2"][:, tt * 64:(tt + 1) * 64]), deps=prevv, name="m_top")
            for j in range(TOPK):
                oh = sch.emit("vector", lambda E, tt=tt, j=j: E.tensor_tensor(
                    sb["m_oh"][:, :].rearrange("p (a e) -> p a e", a=1),
                    sb["m_sc2"][:, tt * 64:(tt + 1) * 64].rearrange("p (a e) -> p a e", a=1),
                    sb["m_topv"][:, tt * 8 + j: tt * 8 + j + 1].to_broadcast((128, 1, 64)),
                    ALU.is_equal), deps=[tv], name="m_oh")
                wj1 = sch.emit("vector", lambda E, tt=tt: E.tensor_tensor(
                    sb["m_ohp"][:, :], sb["m_sc"][:, tt * 64:(tt + 1) * 64],
                    sb["m_oh"][:, :], ALU.mult), deps=[oh], name="m_wj1")
                wj = sch.emit("vector", lambda E, tt=tt, j=j, b=b: E.tensor_reduce(
                    sb["m_wv"][:, (b * 16 + tt) * 6 + j: (b * 16 + tt) * 6 + j + 1],
                    sb["m_ohp"][:, :].rearrange("p (a e) -> p a e", a=1),
                    axis=AX.X, op=ALU.add), deps=[wj1], name="m_wj2")
                ef1 = sch.emit("vector", lambda E: E.tensor_tensor(
                    sb["m_ohp"][:, :], sb["m_oh"][:, :], sb["erow"][:, :],
                    ALU.mult), deps=[wj], name="m_ef1")
                ef2 = sch.emit("vector", lambda E, tt=tt, j=j, b=b: E.tensor_reduce(
                    sb["m_ef"][:, b * 96 + tt * 6 + j: b * 96 + tt * 6 + j + 1],
                    sb["m_ohp"][:, :].rearrange("p (a e) -> p a e", a=1),
                    axis=AX.X, op=ALU.add), deps=[ef1], name="m_ef2")
                prevv = [ef2]
                eftoks.append(ef2)
        ef = eftoks[-1]
        ctoks = []
        for g in range(4):
            efg = sb["m_ef"][:, b * 96 + g * 24: b * 96 + (g + 1) * 24]
            i1 = sch.emit("vector", lambda E, efg=efg: E.tensor_tensor(
                sb["m_I"][:, :].rearrange("p (t j e) -> p t j e", t=4, j=6, e=64),
                efg.rearrange("p (t j) -> p t j", j=6).to_broadcast((128, 4, 6, 64)),
                sb["erow"][:, :].to_broadcast((128, 64, 6)).to_broadcast((128, 64, 6, 4)).transpose((0, 3, 2, 1)),
                ALU.is_equal), deps=[ef], name="m_I")
            c1 = sch.emit("vector", lambda E, g=g, b=b: E.tensor_reduce(
                sb["m_c"][:, (b * 16 + g * 4) * 64: (b * 16 + (g + 1) * 4) * 64],
                sb["m_I"][:, :].rearrange("p (t j e) -> p t e j", t=4, j=6, e=64),
                axis=AX.X, op=ALU.add), deps=[i1], name="m_c")
            ctoks.append(c1)
        ms_toks = []
        for tt in range(16):
            k = tt % 2
            ps = [psA, psB][k]
            mp = sch.emit("tensor", lambda E, tt=tt, b=b, ps=ps: E.matmul(
                ps[:, 0:64], sb["ustri128"][:, :],
                sb["m_c"][:, (b * 16 + tt) * 64:(b * 16 + tt + 1) * 64],
                start=True, stop=True), deps=ctoks, name="m_mp")
            ev = sch.emit("vector", lambda E, tt=tt, b=b, ps=ps: E.tensor_copy(
                sb["m_pfx"][:, (b * 16 + tt) * 64:(b * 16 + tt + 1) * 64], ps[:, 0:64]),
                deps=[mp], name="m_mpe")
            ms = sch.emit("tensor", lambda E, tt=tt, b=b: E.matmul(
                [psC, psD][tt // 8][0:1, (tt % 8) * 64:(tt % 8) * 64 + 64],
                sb["ones_col"][:, 0:1],
                sb["m_c"][:, (b * 16 + tt) * 64:(b * 16 + tt + 1) * 64],
                start=True, stop=True), deps=ctoks, name="m_ms")
            ms_toks.append(ms)
        e1 = sch.emit("vector", lambda E, b=b: E.tensor_copy(
            sb["m_mrows"][0:1, b * 1024: b * 1024 + 512], psC[0:1, 0:512]),
            deps=ms_toks, name="m_tse1")
        e2 = sch.emit("vector", lambda E, b=b: E.tensor_copy(
            sb["m_mrows"][0:1, b * 1024 + 512: b * 1024 + 1024], psD[0:1, 0:512]),
            deps=ms_toks, name="m_tse2")
        ts_evac += [e1, e2]
        sch.stage()
    pass
    sch.stage()
    if cut <= 3:
        return
    sch.set_phase(f"m{P}_r3")
    tot = sch.emit("vector", lambda E: E.tensor_reduce(
        sb["m_mrows"][32:33, 0:64],
        sb["m_mrows"][0:1, 0:2048].rearrange("p (t e) -> p e t", t=32, e=64),
        axis=AX.X, op=ALU.add), deps=ts_evac, name="m_tot")
    cp = sch.emit("vector", lambda E: E.tensor_copy(sb["mu_row"][0:1, 64:128],
                  sb["m_mrows"][32:33, 0:64]), deps=[tot], name="m_totc")
    d1 = sch.dma("sync", ag_cnt_i[0:1, :], sb["mu_row"][0:1, 64:128], deps=[cp],
                 name="m_cd")
    if cut == 41:
        sch.stage()
        return
    ag1 = sch.emit("gpsimd", lambda E: E.collective_compute(
        "AllGather", ALU.bypass, replica_groups=[list(range(8))],
        ins=[ag_cnt_i[:, :]], outs=[ag_cnt_o[:, :]]), deps=[d1], name="m_ag1", is_cc=True)
    if cut == 42:
        sch.stage()
        return
    d2 = sch.dma("sync", sb["m_cnt8"][:, :], ag_cnt_o[:, :], deps=[ag1], name="m_cl")
    if cut == 43:
        sch.stage()
        return
    msm = sch.emit("tensor", lambda E: E.matmul(psA[0:1, 0:64], sb["m_rmask"][:, :],
                   sb["m_cnt8"][:, :], start=True, stop=True), deps=[d2, trm], name="m_cs")
    bp0 = sch.emit("vector", lambda E: E.tensor_copy(sb["mu_row"][0:1, 0:64],
                   psA[0:1, 0:64]), deps=[msm], name="m_bp0")
    sch.stage()
    if cut <= 4:
        return
    z1 = sch.emit("gpsimd", lambda E: E.memset(sb["m_rsum"][:, :], 0.0), deps=[], name="m_z1")
    zd = sch.dma("gpsimd", ag_rec_i[:, :].rearrange("(a b) c -> a (b c)", a=128),
                 sb["m_rsum"][:, :], deps=[z1], name="m_zd")
    prev_bp = [bp0, zd]
    for b in range(B):
        for tt in range(16):
            f1 = sch.emit("tensor", lambda E: E.matmul(
                psE[:, 0:64], sb["ones_row"][0:1, :], sb["mu_row"][0:1, 0:64],
                start=True, stop=True), deps=prev_bp, name="m_f1")
            f2 = sch.emit("vector", lambda E, b=b, tt=tt: E.tensor_tensor(
                sb["m_fp"][:, tt * 64:(tt + 1) * 64],
                sb["m_pfx"][:, (b * 16 + tt) * 64:(b * 16 + tt + 1) * 64],
                psE[:, 0:64], ALU.add), deps=[f1], name="m_f2")
            f3 = sch.emit("vector", lambda E, b=b, tt=tt: E.tensor_tensor(
                sb["mu_row"][0:1, 0:64], sb["mu_row"][0:1, 0:64],
                sb["m_mrows"][0:1, (b * 16 + tt) * 64:(b * 16 + tt + 1) * 64], ALU.add),
                deps=[f2], name="m_f3")
            prev_bp = [f3]
        for g in range(4):
            efg = sb["m_ef"][:, b * 96 + g * 24: b * 96 + (g + 1) * 24]
            i1 = sch.emit("vector", lambda E, efg=efg: E.tensor_tensor(
                sb["m_I"][:, :].rearrange("p (t j e) -> p t j e", t=4, j=6, e=64),
                efg.rearrange("p (t j) -> p t j", j=6).to_broadcast((128, 4, 6, 64)),
                sb["erow"][:, :].to_broadcast((128, 64, 6)).to_broadcast((128, 64, 6, 4)).transpose((0, 3, 2, 1)),
                ALU.is_equal), deps=prev_bp, name="m_I2")
            i2 = sch.emit("vector", lambda E, g=g: E.tensor_tensor(
                sb["m_I"][:, :].rearrange("p (t j e) -> p t j e", t=4, j=6, e=64),
                sb["m_I"][:, :].rearrange("p (t j e) -> p t j e", t=4, j=6, e=64),
                sb["m_fp"][:, g * 256:(g + 1) * 256].rearrange("p (t e) -> p t e", e=64).to_broadcast((128, 4, 64, 6)).transpose((0, 1, 3, 2)),
                ALU.mult), deps=[i1], name="m_I3")
            i3 = sch.emit("vector", lambda E, g=g: E.tensor_reduce(
                sb["m_pos"][:, g * 24:(g + 1) * 24],
                sb["m_I"][:, :].rearrange("p (t j e) -> p t j e", t=4, j=6, e=64),
                axis=AX.X, op=ALU.add), deps=[i2], name="m_I4")
            prev_bp = [i3]
        efb = sb["m_ef"][:, b * 96:(b + 1) * 96]
        q1 = sch.emit("vector", lambda E, efb=efb: E.tensor_tensor(
            sb["m_p6"][:, :].rearrange("p (t j q) -> p t j q", t=16, j=6, q=6),
            efb.rearrange("p (t j) -> p t j", j=6).to_broadcast((128, 16, 6, 6)),
            efb.rearrange("p (t q) -> p t q", q=6).to_broadcast((128, 16, 6, 6)).transpose((0, 1, 3, 2)),
            ALU.is_equal), deps=prev_bp, name="m_q1b")
        q2 = sch.emit("vector", lambda E: E.tensor_tensor(
            sb["m_p6"][:, :].rearrange("p (t jq) -> p t jq", jq=36),
            sb["m_p6"][:, :].rearrange("p (t jq) -> p t jq", jq=36),
            sb["lt6"][:, :].to_broadcast((128, 36, 16)).transpose((0, 2, 1)),
            ALU.mult), deps=[q1], name="m_q2b")
        q3 = sch.emit("vector", lambda E: E.tensor_reduce(
            sb["m_r1"][:, :], sb["m_p6"][:, :].rearrange("p (t j q) -> p t j q", j=6, q=6),
            axis=AX.X, op=ALU.add), deps=[q2], name="m_q3b")
        p1 = sch.emit("vector", lambda E: E.tensor_tensor(sb["m_pos"][:, :],
                      sb["m_pos"][:, :], sb["m_r1"][:, :], ALU.add), deps=[q3],
                      name="m_p1b")
        v1 = sch.emit("vector", lambda E: E.tensor_scalar(sb["m_val"][:, :],
                      sb["m_pos"][:, :], 512.0, None, ALU.is_lt), deps=[p1], name="m_v1")
        v2 = sch.emit("vector", lambda E, b=b: E.tensor_tensor(
            sb["m_wv"][:, b * 96:(b + 1) * 96], sb["m_wv"][:, b * 96:(b + 1) * 96],
            sb["m_val"][:, :], ALU.mult), deps=[v1], name="m_v2")
        sl1 = sch.emit("vector", lambda E, b=b, efb=efb: E.scalar_tensor_tensor(
            sb["m_slot"][:, :], efb, 512.0, sb["m_pos"][:, :],
            ALU.mult, ALU.add), deps=[p1], name="m_sl1")
        sl2 = sch.emit("vector", lambda E: E.tensor_scalar(
            sb["m_val"][:, :], sb["m_val"][:, :], -1e9, 1e9, ALU.mult, ALU.add),
            deps=[v2, sl1], name="m_sl2")
        sl3 = sch.emit("vector", lambda E: E.tensor_tensor(
            sb["m_pos"][:, :], sb["m_slot"][:, :], sb["m_val"][:, :],
            ALU.add), deps=[sl2], name="m_sl3")
        sl4 = sch.emit("vector", lambda E, b=b: E.tensor_copy(
            sb["m_s2u"][:, b * 96:(b + 1) * 96], sb["m_pos"][:, :]), deps=[sl3],
            name="m_s2u")
        r0 = sch.emit("gpsimd", lambda E: E.memset(sb["m_rec"][:, :], 0.0),
                      deps=[sl4] + (sc_tail if b > 0 else []), name="m_r0")
        r1t = sch.emit("vector", lambda E, b=b: E.tensor_copy(
            sb["m_rec"][:, :].rearrange("p (t j c) -> p t j c", t=16, j=6, c=4)[:, :, :, 0:1],
            sb["m_tokid"][:, b * 16:(b + 1) * 16].to_broadcast((128, 16, 6)).to_broadcast((128, 16, 6, 1))),
            deps=[r0], name="m_r1")
        r2t = sch.emit("vector", lambda E, b=b: E.tensor_copy(
            sb["m_rec"][:, :].rearrange("p (t j c) -> p t j c", t=16, j=6, c=4)[:, :, :, 1:2],
            sb["m_wv"][:, b * 96:(b + 1) * 96].rearrange("p (t j) -> p t j", j=6).to_broadcast((128, 16, 6, 1))),
            deps=[r0, v2], name="m_r2")
        base_sc = [r1t, r2t, sl4, zd]
        sc_toks = []
        for col in range(96):
            st = sch.emit("gpsimd", lambda E, col=col, b=b: E.indirect_dma_start(
                ag_rec_i[:, :],
                bass.IndirectOffsetOnAxis(ap=sb["m_s2u"][:, b * 96 + col: b * 96 + col + 1], axis=0),
                sb["m_rec"][:, col * 4:(col + 1) * 4], None,
                bounds_check=bounds_reg(E), oob_is_err=False),
                is_dma=True, deps=base_sc, name="m_rsc", join=False)
            sc_toks.append(st)
        sc_tail = sc_toks[-8:]
        sch.stage()
    sch.stage()
    if cut <= 5:
        return
    if "dbg_s2u" in d and P == 0:
        sch.dma("sync", d["dbg_s2u"][:, :], sb["m_s2u"][:, :], deps=[], name="dbg_s2u")
        sch.dma("sync", d["dbg_wv"][:, :], sb["m_wv"][:, :], deps=[], name="dbg_wv")
        sch.dma("scalar", d["dbg_reci"][:, :].rearrange("(a b) c -> a (b c)", a=128),
                ag_rec_i[:, :].rearrange("(a b) c -> a (b c)", a=128), deps=[],
                name="dbg_reci")
        sch.stage()
    agx = sch.emit("gpsimd", lambda E: E.collective_compute(
        "AllGather", ALU.bypass, replica_groups=[list(range(8))],
        ins=[ag_xn_i[:, :]], outs=[ag_xn_o[:, :]]), deps=xn_dma, name="m_agx", is_cc=True)
    agr = sch.emit("gpsimd", lambda E: E.collective_compute(
        "AllGather", ALU.bypass, replica_groups=[list(range(8))],
        ins=[ag_rec_i[:, :]], outs=[ag_rec_o[:, :]]), deps=sc_tail, name="m_agr",
        is_cc=True)
    sch.stage()
    if cut <= 6:
        return
    acc = sch.dma("sync", sb["m_rsum"][:, :],
                  ag_rec_o[0:32768, :].rearrange("(a b) c -> a (b c)", a=128),
                  deps=[agr], name="m_rl0")
    for c in range(1, 8):
        tl = sch.dma("sync", sb["m_ctm"][:, :],
                     ag_rec_o[c * 32768:(c + 1) * 32768, :].rearrange("(a b) c -> a (b c)", a=128),
                     deps=[acc], name="m_rl")
        acc = sch.emit("vector", lambda E: E.tensor_tensor(
            sb["m_rsum"][:, :], sb["m_rsum"][:, :], sb["m_ctm"][:, :], ALU.add),
            deps=[tl], name="m_racc")
    mw = sch.dma("sync", ag_rec_i[:, :].rearrange("(a b) c -> a (b c)", a=128),
                 sb["m_rsum"][:, :], deps=[acc], name="m_rw")
    if "dbg_rec" in d and P == 0:
        sch.dma("scalar", d["dbg_rec"][:, :].rearrange("(a b) c -> a (b c)", a=128),
                sb["m_rsum"][:, :], deps=[acc], name="dbg_rec")
    sch.stage()
    if cut <= 7:
        return
    sch.set_phase(f"m{P}_ex")
    ctx.scratch_reset()
    ctx.scr("e_w13", (128, NDT * 1408), BF16)
    ctx.scr("e_w2", (128, 6 * 1024), BF16)
    ctx.scr("e_xbt", (128, 4 * 1024), BF16)
    ctx.scr("e_xbf", (128, NDT * 512), BF16)
    ctx.scr("e_h", (128, 6 * 512), BF16)
    ctx.scr("e_stmp", (128, 512), F32)
    ctx.scr("e_ybs", (128, 512), F32)
    ctx.scr("e_ct4", (128, 4096), F32)
    gsl = sch.dma("sync", sb["m_xidx"][:, :], d["myslots"][:, :], deps=[], name="m_msl")
    gts = []
    for g in range(32):
        gt = sch.emit("gpsimd", lambda E, g=g: E.indirect_dma_start(
            sb["m_rsum"][:, g * 4:(g + 1) * 4], None,
            ag_rec_i[:, :],
            bass.IndirectOffsetOnAxis(ap=sb["m_xidx"][:, g:g + 1], axis=0),
            bounds_check=bounds_reg(E), oob_is_err=False),
            is_dma=True, deps=[gsl, mw], name="m_grec")
        gts.append(gt)
    x1 = sch.emit("vector", lambda E: E.tensor_scalar(
        sb["m_ctm"][:, 0:32].rearrange("p (g c) -> p g c", c=1),
        sb["m_rsum"][:, 0:128].rearrange("p (g c) -> p g c", c=4)[:, :, 0:1],
        -1.0, None, ALU.add), deps=gts, name="m_xi1")
    x2 = sch.emit("vector", lambda E: E.tensor_scalar(
        sb["m_ctm"][:, 32:64], sb["m_ctm"][:, 0:32], 0.0, None, ALU.is_lt),
        deps=[x1], name="m_xi2")
    x3 = sch.emit("vector", lambda E: E.scalar_tensor_tensor(
        sb["m_ctm"][:, 0:32], sb["m_ctm"][:, 32:64], 1e9, sb["m_ctm"][:, 0:32],
        ALU.mult, ALU.add), deps=[x2], name="m_xi3")
    x4 = sch.emit("vector", lambda E: E.tensor_copy(sb["m_xidx"][:, :],
                  sb["m_ctm"][:, 0:32]), deps=[x3], name="m_xi4")
    sch.stage()
    prev_e = []
    cb_dma = []
    for e in range(EPERC):
        wts = []
        prev_l = list(prev_e)
        for dt in range(NDT):
            l1 = sch.dma("sync", sb["m_wst"][:, 0:704],
                         d["w1T_my"][e, dt * 128:(dt + 1) * 128, :], deps=prev_l, name="m_w1l")
            c1 = sch.emit("vector", lambda E, dt=dt: E.tensor_copy(
                sb["e_w13"][:, dt * 1408: dt * 1408 + 704], sb["m_wst"][:, 0:704]),
                deps=[l1], name="m_w1c")
            l2 = sch.dma("scalar", sb["m_wst"][:, 0:704],
                         d["w3T_my"][e, dt * 128:(dt + 1) * 128, :], deps=[c1], name="m_w3l")
            c2 = sch.emit("vector", lambda E, dt=dt: E.tensor_copy(
                sb["e_w13"][:, dt * 1408 + 704: dt * 1408 + 1408], sb["m_wst"][:, 0:704]),
                deps=[l2], name="m_w3c")
            prev_l = [c2]
            wts.append(c2)
        for ic in range(6):
            kk = 128 if ic < 5 else 64
            l3 = sch.dma("gpsimd", sb["m_wst"][0:kk, :],
                         d["w2T_my"][e, ic * 128: ic * 128 + kk, :], deps=prev_l, name="m_w2l")
            c3 = sch.emit("vector", lambda E, ic=ic, kk=kk: E.tensor_copy(
                sb["e_w2"][0:kk, ic * 1024:(ic + 1) * 1024], sb["m_wst"][0:kk, :]),
                deps=[l3], name="m_w2c")
            prev_l = [c3]
            wts.append(c3)
        if cut == 71:
            sch.stage()
            return
        z2 = sch.emit("gpsimd", lambda E: E.memset(sb["e_xbt"][:, :], 0.0),
                      deps=prev_e + [x4], name="m_xz")
        gats = []
        for sg in range(4):
            gx = sch.emit("gpsimd", lambda E, sg=sg, e=e: E.indirect_dma_start(
                sb["e_xbt"][:, sg * 1024:(sg + 1) * 1024], None,
                ag_xn_o[:, :],
                bass.IndirectOffsetOnAxis(ap=sb["m_xidx"][:, e * 4 + sg: e * 4 + sg + 1], axis=0),
                bounds_check=bounds_reg(E), oob_is_err=False),
                is_dma=True, deps=[z2, x4, agx], name="m_gx")
            gats.append(gx)
        if cut == 72:
            sch.stage()
            return
        trp = {0: None, 1: None}
        tr_toks = []
        for sg in range(4):
            for dt in range(NDT):
                k = (sg * NDT + dt) % 2
                ps = [psC, psD][k]
                tmt = sch.emit("tensor", lambda E, sg=sg, dt=dt, ps=ps: E.transpose(
                    ps[:, 0:64].bitcast(BF16), sb["e_xbt"][:, sg * 1024 + dt * 128: sg * 1024 + dt * 128 + 128],
                    sb["ident128b"][:, :]), deps=[trp[k]] + gats, name="m_xtr")
                ev = sch.emit("vector", lambda E, sg=sg, dt=dt, ps=ps: E.tensor_copy(
                    sb["e_xbf"][:, dt * 512 + sg * 128: dt * 512 + sg * 128 + 128],
                    ps[:, 0:64].bitcast(BF16)), deps=[tmt], name="m_xte")
                trp[k] = ev
                tr_toks.append(ev)
        if cut == 73:
            sch.stage()
            return
        hts = []
        for ic in range(6):
            ni = 128 if ic < 5 else 64
            mm1 = mm3 = None
            for dt in range(NDT):
                mm1 = sch.emit("tensor", lambda E, ic=ic, dt=dt, ni=ni: E.matmul(
                    psA[0:ni, 0:512], sb["e_w13"][:, dt * 1408 + ic * 128: dt * 1408 + ic * 128 + ni],
                    sb["e_xbf"][:, dt * 512:(dt + 1) * 512],
                    start=(dt == 0), stop=(dt == NDT - 1), skip_group_check=True),
                    deps=wts + tr_toks + hts, name="m_h1")
                mm3 = sch.emit("tensor", lambda E, ic=ic, dt=dt, ni=ni: E.matmul(
                    psB[0:ni, 0:512], sb["e_w13"][:, dt * 1408 + 704 + ic * 128: dt * 1408 + 704 + ic * 128 + ni],
                    sb["e_xbf"][:, dt * 512:(dt + 1) * 512],
                    start=(dt == 0), stop=(dt == NDT - 1), skip_group_check=True),
                    deps=wts + tr_toks + hts, name="m_h3")
            e1 = sch.emit("scalar", lambda E, ni=ni: E.activation(
                out=sb["e_stmp"][0:ni, :], in_=psA[0:ni, 0:512], func=AF.Silu),
                deps=[mm1], name="m_sil")
            e2 = sch.emit("vector", lambda E, ic=ic, ni=ni: E.tensor_tensor(
                sb["e_h"][0:ni, ic * 512:(ic + 1) * 512], sb["e_stmp"][0:ni, :],
                psB[0:ni, 0:512], ALU.mult), deps=[e1, mm3], name="m_hm")
            hts.append(e2)
        if cut == 74:
            sch.stage()
            return
        ct_toks = []
        ytp = {0: None, 1: None}
        for dt in range(NDT):
            mm2 = None
            for ic in range(6):
                kk = 128 if ic < 5 else 64
                mm2 = sch.emit("tensor", lambda E, dt=dt, ic=ic, kk=kk: E.matmul(
                    psE[:, (dt % 4) * 512:(dt % 4) * 512 + 512],
                    sb["e_w2"][0:kk, ic * 1024 + dt * 128: ic * 1024 + dt * 128 + 128],
                    sb["e_h"][0:kk, ic * 512:(ic + 1) * 512],
                    start=(ic == 0), stop=(ic == 5), skip_group_check=True),
                    deps=hts + ct_toks, name="m_ybm")
            ev = sch.emit("vector", lambda E, dt=dt: E.tensor_copy(
                sb["e_ybs"][:, :], psE[:, (dt % 4) * 512:(dt % 4) * 512 + 512]),
                deps=[mm2] + ct_toks, name="m_ybe")
            sg_evs = []
            for sg in range(4):
                k = sg % 2
                ps = [psC, psD][k]
                tmt = sch.emit("tensor", lambda E, sg=sg, ps=ps: E.transpose(
                    ps[:, 0:128], sb["e_ybs"][:, sg * 128:(sg + 1) * 128], ident[:, :]),
                    deps=[ev, ytp.get(k)] + tr_toks, name="m_ytr")
                ev2 = sch.emit("vector", lambda E, sg=sg, dt=dt, ps=ps: E.tensor_copy(
                    sb["e_ct4"][:, sg * 1024 + dt * 128: sg * 1024 + dt * 128 + 128],
                    ps[:, 0:128]), deps=[tmt], name="m_yte")
                ytp[k] = ev2
                sg_evs.append(ev2)
            ct_toks = sg_evs
        eds = []
        for sg in range(4):
            ed = sch.dma(qs[sg % 3], ag_cb_i[e * 512 + sg * 128: e * 512 + (sg + 1) * 128, :],
                         sb["e_ct4"][:, sg * 1024:(sg + 1) * 1024], deps=ct_toks, name="m_cbd")
            eds.append(ed)
            cb_dma.append(ed)
            if "dbg_cb" in d and P == 0:
                sch.dma(qs[sg % 3], d["dbg_cb"][e * 512 + sg * 128: e * 512 + (sg + 1) * 128, :],
                        sb["e_ct4"][:, sg * 1024:(sg + 1) * 1024], deps=ct_toks, name="dbg_cb")
        prev_e = eds
        sch.stage()
    sch.stage()
    if cut <= 8:
        return
    agc = sch.emit("gpsimd", lambda E: E.collective_compute(
        "AllGather", ALU.bypass, replica_groups=[list(range(8))],
        ins=[ag_cb_i[:, :]], outs=[ag_cb_o[:, :]]), deps=cb_dma, name="m_agc", is_cc=True)
    sch.stage()
    if cut <= 9:
        return
    sch.set_phase(f"m{P}_cb")
    ctx.scratch_reset()
    ctx.scr("c_gat", (128, 1024), F32)
    ctx.scr("c_ytm4", (128, 4096), F32)
    ctx.scr("c_x2", (128, 512), F32)
    ctx.scr("c_zt", (128, 512), F32)
    for b in range(B):
        col0 = b * S
        for tc in range(4):
            ycols = []
            for q in range(4):
                tt = tc * 4 + q
                yq = sb["c_ytm4"][:, q * 1024:(q + 1) * 1024]
                z3 = sch.emit("gpsimd", lambda E, yq=yq: E.memset(yq, 0.0),
                              deps=ycols, name="m_yz2")
                acc2 = [z3]
                for j in range(6):
                    col = b * 96 + tt * 6 + j
                    gx = sch.emit("gpsimd", lambda E, col=col: E.indirect_dma_start(
                        sb["c_gat"][:, :], None,
                        ag_cb_o[:, :],
                        bass.IndirectOffsetOnAxis(ap=sb["m_s2u"][:, col:col + 1], axis=0),
                        bounds_check=bounds_reg(E), oob_is_err=False),
                        is_dma=True, deps=acc2 + [agc], name="m_yg")
                    a1 = sch.emit("vector", lambda E, col=col, yq=yq: E.scalar_tensor_tensor(
                        yq, sb["c_gat"][:, :],
                        sb["m_wv"][:, col:col + 1], yq,
                        ALU.mult, ALU.add), deps=[gx], name="m_ya")
                    acc2 = [a1]
                ycols.append(acc2[0])
            prev_x = []
            for dt in range(NDT):
                tlx = sch.dma("sync", sb["c_x2"][:, :],
                              x_fm[dt * 128:(dt + 1) * 128, col0 + tc * 512: col0 + tc * 512 + 512],
                              deps=prev_x, name="m_xl")
                tlz = sch.dma("scalar", sb["c_zt"][:, :],
                              z_dram[dt * 128:(dt + 1) * 128, col0 + tc * 512: col0 + tc * 512 + 512],
                              deps=prev_x, name="m_zl")
                ax = sch.emit("vector", lambda E: E.tensor_tensor(
                    sb["c_x2"][:, :], sb["c_x2"][:, :], sb["c_zt"][:, :], ALU.add),
                    deps=[tlx, tlz], name="m_ax")
                ay = [ax]
                for q in range(4):
                    k = q % 2
                    ps = [psC, psD][k]
                    tmt = sch.emit("tensor", lambda E, q=q, dt=dt, ps=ps: E.transpose(
                        ps[:, 0:128], sb["c_ytm4"][:, q * 1024 + dt * 128: q * 1024 + dt * 128 + 128],
                        ident[:, :]), deps=ycols + ay[-1:], name="m_ytr2")
                    av = sch.emit("vector", lambda E, q=q, ps=ps: E.tensor_tensor(
                        sb["c_x2"][:, q * 128:(q + 1) * 128], sb["c_x2"][:, q * 128:(q + 1) * 128],
                        ps[:, 0:128], ALU.add), deps=[tmt] + ay, name="m_ay")
                    ay.append(av)
                sx = sch.dma("gpsimd", x2_fm[dt * 128:(dt + 1) * 128, col0 + tc * 512: col0 + tc * 512 + 512],
                             sb["c_x2"][:, :], deps=ay, name="m_x2s")
                prev_x = [sx]
            sch.stage()
    sch.stage()
    return {}



def emit_attn_b(ctx, x_fm, x4_fm, deps):
    nc, sch = ctx.nc, ctx.sch
    sb = ctx.sb
    d = ctx.dram
    psA, psB, psC, psD, psE = (ctx.ps[k] for k in ("psA", "psB", "psC", "psD", "psE"))
    banks8 = [psA, psB, psC, psD] + [psE[:, k * 512:(k + 1) * 512] for k in range(4)]
    qs = ["sync", "scalar", "gpsimd"]

    wzi = ctx.dram_tmp("wzi", (32, 8, 128, 128), BF16)
    wzx = ctx.dram_tmp("wzx", (17, 16, 128, 128), BF16)
    wzo = ctx.dram_tmp("wzo", (8, 16, 128, 128), BF16)
    xs_d = ctx.dram_tmp("xs_d", (16, 128, 2048), BF16)
    slr_d = ctx.dram_tmp("slr_d", (16, 128, 2048), BF16)
    cse_d = ctx.dram_tmp("cse_d", (16, 128, 2048), BF16)
    dlt_d = ctx.dram_tmp("dlt_d", (16, 128, 2048), BF16)
    y2_d = ctx.dram_tmp("y2_d", (16, 128, 2048), BF16)
    xn3_d = ctx.dram_tmp("xn3_d", (8, 128, 2048), F32)
    bsel_d = ctx.dram_tmp("bsel_d", (64, 2048), BF16)

    sch.set_phase("ab_stage")
    ctx.scratch_reset()
    aA = ctx.scr("b_aA", (128, 1024), F32)
    bcols = ctx.scr("b_cols", (128, 112), F32)
    CBI, CBX, CBDT, CDP, CBO, CGAM, CBET = 0, 32, 49, 65, 81, 89, 97
    wdt_sb = ctx.scr("b_wdt", (64, 2048), BF16)
    wt = [ctx.scr("b_wt0", (128, 2048), BF16), ctx.scr("b_wt1", (128, 2048), BF16)]
    wstg = [ctx.scr("b_wstg0", (128, 128), BF16), ctx.scr("b_wstg1", (128, 128), BF16)]
    wldf = [ctx.scr("b_wldf0", (128, 128), F32), ctx.scr("b_wldf1", (128, 128), F32)]
    xt = ctx.scr("b_xt", (128, 2048), F32)
    sqb = ctx.scr("b_sqb", (128, 2048), F32)
    xstg = ctx.scr("b_xstg", (128, 2048), BF16)
    cstg = [ctx.scr("b_cstg0", (128, 512), BF16), ctx.scr("b_cstg1", (128, 512), BF16)]
    b64 = ctx.scr("b_b64", (64, 2048), BF16)
    dbr = ctx.scr("b_dbr", (128, 2048), F32)
    dtmp = ctx.scr("b_dtmp", (128, 512), F32)
    ctx.scr("b_hbn", (64, 64 * 128), BF16)
    BASE = ctx.mark()

    stage_jobs = []
    for ft in range(32):
        for dt in range(NDT):
            stage_jobs.append((d["b_Wi"], dt, ft, wzi[ft, dt]))
    for ft in range(17):
        for dti in range(16):
            stage_jobs.append((d["b_Wx"], dti, ft, wzx[ft, dti]))
    for ft in range(8):
        for dblk in range(16):
            stage_jobs.append((d["b_Wo"], dblk, ft, wzo[ft, dblk]))
    prev_l = {0: None, 1: None}
    prev_s = {0: None, 1: None}
    wz_done = []
    for i, (src, kt, ft, dst) in enumerate(stage_jobs):
        k = i % 2
        q = ["sync", "scalar"][k]
        l = sch.dma(q, wldf[k][:, :], src[kt * 128:(kt + 1) * 128, ft * 128:(ft + 1) * 128],
                    deps=[prev_l[k]] + list(deps), name="bw_l")
        c = sch.emit("vector", lambda E, k=k: E.tensor_copy(wstg[k][:, :], wldf[k][:, :]),
                     deps=[l, prev_s[k]], name="bw_c")
        s = sch.dma(q, dst, wstg[k][:, :], deps=[c], name="bw_s")
        prev_l[k] = c
        prev_s[k] = s
    wz_done = [prev_s[0], prev_s[1]]
    t_bi = sch.dma("gpsimd", bcols[:, CBI:CBI + 32], d["b_bi_c"][:, :], deps=deps, name="b_bi")
    t_bx = sch.dma("gpsimd", bcols[:, CBX:CBX + 17], d["b_bx_c"][:, :], deps=deps, name="b_bx")
    t_bdt = sch.dma("gpsimd", bcols[:, CBDT:CBDT + 16], d["b_bdt_c"][:, :], deps=deps, name="b_bdt")
    t_dp = sch.dma("gpsimd", bcols[:, CDP:CDP + 16], d["b_Dp_c"][:, :], deps=deps, name="b_dp")
    t_bo = sch.dma("gpsimd", bcols[:, CBO:CBO + 8], d["b_bo_c"][:, :], deps=deps, name="b_bo")
    t_gm = sch.dma("gpsimd", bcols[:, CGAM:CGAM + 8], d["b_gam_c"][:, :], deps=deps, name="b_gm")
    t_bt = sch.dma("gpsimd", bcols[:, CBET:CBET + 8], d["b_bet_c"][:, :], deps=deps, name="b_bt")
    t_al = sch.dma("gpsimd", sqb[:, 0:1024], d["b_Alog_r"][:, :], deps=deps, name="b_al")
    t_a1 = sch.emit("scalar", lambda E: E.activation(out=aA[:, :], in_=sqb[:, 0:1024],
                    func=AF.Exp), deps=[t_al], name="b_a1")
    t_a2 = sch.emit("vector", lambda E: E.tensor_scalar(aA[:, :], aA[:, :], -1.0, None,
                    ALU.mult), deps=[t_a1], name="b_a2")
    prev = [t_a2]
    for h in range(2):
        l = sch.dma("sync", xt[0:64, 0:1024], d["b_Wdt"][:, h * 1024:(h + 1) * 1024],
                    deps=prev, name="b_wdl")
        c = sch.emit("vector", lambda E, h=h: E.tensor_copy(
            wdt_sb[:, h * 1024:(h + 1) * 1024], xt[0:64, 0:1024]), deps=[l], name="b_wdc")
        prev = [c]
    for n in range(64):
        sch.emit("vector", lambda E, n=n: E.tensor_copy(
            sb["b_hbn"][0:64, n * 128:(n + 1) * 128],
            sb["ident128b"][0:64, n:n + 1].to_broadcast((64, 128))),
            deps=deps, name="b_hb")
    sch.stage()

    for b in range(B):
        col0 = b * S
        ctx.reset_to(BASE)
        plmu = ctx.scr("b_plmu", (128, 2048), F32)
        plr = ctx.scr("b_plr", (128, 2048), F32)
        MARKB = ctx.mark()
        xnb = ctx.scr("b_xnb", (128, 8 * 2048), BF16)

        sch.set_phase(f"ab_L{b}")
        mm_toks = []
        prev = []
        for dt in range(NDT):
            tl = sch.dma("sync", xt[:, :], x_fm[dt * 128:(dt + 1) * 128, col0:col0 + S],
                         deps=prev, name="bL_x")
            tq = sch.emit("scalar", lambda E: E.activation(out=sqb[:, :], in_=xt[:, :],
                          func=AF.Square), deps=[tl] + prev, name="bL_sq")
            m1 = m2 = None
            for c0 in range(0, S, 512):
                m1 = sch.emit("tensor", lambda E, c0=c0, dt=dt: E.matmul(
                    psE[0:1, c0:c0 + 512], sb["ones_col"][:, 0:1], xt[:, c0:c0 + 512],
                    start=(dt == 0), stop=(dt == NDT - 1)), deps=[tl], name="bL_s")
                m2 = sch.emit("tensor", lambda E, c0=c0, dt=dt: E.matmul(
                    [psA, psB, psC, psD][c0 // 512][0:1, 0:512], sb["ones_col"][:, 0:1],
                    sqb[:, c0:c0 + 512],
                    start=(dt == 0), stop=(dt == NDT - 1)), deps=[tq], name="bL_q")
                if dt == NDT - 1:
                    mm_toks += [m1, m2]
            prev = [tq, m1, m2]
        mu, var = sb["mu_row"], sb["var_row"]
        t1 = sch.emit("vector", lambda E: E.tensor_scalar(mu[0:1, :], psE[0:1, :],
                      1.0 / D, None, ALU.mult), deps=mm_toks, name="bL_mu")
        sqc = []
        for c0 in range(0, 2048, 512):
            sqc.append(sch.emit("vector", lambda E, c0=c0: E.tensor_copy(
                var[0:1, c0:c0 + 512], [psA, psB, psC, psD][c0 // 512][0:1, 0:512]),
                deps=mm_toks, name="bL_sc"))
        t2 = sch.emit("vector", lambda E: E.tensor_tensor(psE[0:1, :], mu[0:1, :],
                      mu[0:1, :], ALU.mult), deps=[t1], name="bL_m2")
        t3 = sch.emit("vector", lambda E: E.scalar_tensor_tensor(var[0:1, :], var[0:1, :],
                      1.0 / D, psE[0:1, :], ALU.mult, ALU.subtract),
                      deps=[t2] + sqc, name="bL_v")
        t4 = sch.emit("scalar", lambda E: E.activation(out=var[0:1, :], in_=var[0:1, :],
                      func=AF.Sqrt, bias=sb["eps_col"][0:1, 0:1]), deps=[t3], name="bL_st")
        t5 = sch.emit("vector", lambda E: E.reciprocal(var[0:1, :], var[0:1, :]),
                      deps=[t4], name="bL_rc")
        bmu = emit_bcast(ctx, plmu, mu, [t1, t3], "bL_bm")
        t5b = sch.emit("vector", lambda E: E.tensor_copy(mu[0:1, :], var[0:1, :]),
                       deps=[t5] + bmu, name="bL_r0")
        brr = emit_bcast(ctx, plr, mu, [t5b], "bL_br")
        sch.stage()
        prev = []
        for dt in range(NDT):
            tl = sch.dma("sync", xt[:, :], x_fm[dt * 128:(dt + 1) * 128, col0:col0 + S],
                         deps=prev, name="bLx2")
            a1 = sch.emit("vector", lambda E: E.tensor_tensor(sqb[:, :], xt[:, :],
                          plmu[:, :], ALU.subtract), deps=[tl], name="bLa1")
            a2 = sch.emit("vector", lambda E: E.tensor_tensor(sqb[:, :], sqb[:, :],
                          plr[:, :], ALU.mult), deps=[a1], name="bLa2")
            sd = sch.dma("scalar", xn3_d[dt, :, :], sqb[:, :], deps=[a2], name="bLxn")
            cb = sch.emit("vector", lambda E, dt=dt: E.tensor_copy(
                xnb[:, dt * 2048:(dt + 1) * 2048], sqb[:, :]), deps=[a2], name="bLcb")
            prev = [tl, sd, cb]
        sch.stage()

        sch.set_phase(f"ab_I{b}")
        prev_mm = {0: [], 1: []}
        prev_ev = []
        prev_st = None
        for ft in range(32):
            k = ft % 2
            wl = []
            for dt in range(NDT):
                wl.append(sch.dma(qs[dt % 3], wt[k][:, dt * 128:(dt + 1) * 128],
                          wzi[ft, dt], deps=prev_mm[k] + wz_done, name="bI_w"))
            evs = []
            for c in range(4):
                mm = None
                for dt in range(NDT):
                    mm = sch.emit("tensor", lambda E, c=c, dt=dt, k=k: E.matmul(
                        [psA, psB, psC, psD][c][:, 0:512],
                        wt[k][:, dt * 128:(dt + 1) * 128],
                        xnb[:, dt * 2048 + c * 512: dt * 2048 + c * 512 + 512],
                        start=(dt == 0), stop=(dt == NDT - 1)),
                        deps=wl + prev_ev, name="bI_mm")
                ev = sch.emit("scalar", lambda E, c=c, ft=ft: E.activation(
                    out=xstg[:, c * 512:(c + 1) * 512], in_=[psA, psB, psC, psD][c][:, 0:512],
                    func=AF.Silu, bias=bcols[:, CBI + ft:CBI + ft + 1]),
                    deps=[mm, prev_st], name="bI_ev")
                evs.append(ev)
            dst = xs_d[ft, :, :] if ft < 16 else slr_d[ft - 16, :, :]
            st = sch.dma(qs[ft % 3], dst, xstg[:, :], deps=evs, name="bI_st")
            prev_mm[k] = [mm]
            prev_ev = evs
            prev_st = st
        sch.stage()

        sch.set_phase(f"ab_X{b}")
        ctx.reset_to(MARKB)
        wtbig = ctx.scr("b_wtbig", (128, 16384), BF16)
        xsc = [ctx.scr("b_xsc0", (128, 512), BF16), ctx.scr("b_xsc1", (128, 512), BF16)]
        passes = [list(range(0, 8)), list(range(8, 16)), [16]]
        prev_pass = []
        for pss in passes:
            wl = []
            for fi, ft in enumerate(pss):
                for dti in range(16):
                    wl.append(sch.dma(qs[(fi + dti) % 3],
                              wtbig[:, (fi * 16 + dti) * 128:(fi * 16 + dti) * 128 + 128],
                              wzx[ft, dti], deps=prev_pass + wz_done, name="bX_w"))
            evs_all = []
            prev_x = {0: None, 1: None}
            for c in range(4):
                mms = {}
                for dti in range(16):
                    k = dti % 2
                    xl = sch.dma("sync", xsc[k][:, :], xs_d[dti, :, c * 512:(c + 1) * 512],
                                 deps=[prev_x[k]], name="bX_x")
                    for fi, ft in enumerate(pss):
                        mms[fi] = sch.emit("tensor", lambda E, fi=fi, dti=dti, k=k: E.matmul(
                            banks8[fi][:, 0:512],
                            wtbig[:, (fi * 16 + dti) * 128:(fi * 16 + dti) * 128 + 128],
                            xsc[k][:, :], start=(dti == 0), stop=(dti == 15)),
                            deps=wl + [xl] + evs_all, name="bX_mm")
                    prev_x[k] = mms[len(pss) - 1]
                evs = []
                for fi, ft in enumerate(pss):
                    if ft == 0:
                        ev = sch.emit("scalar", lambda E, c=c, fi=fi: E.activation(
                            out=dbr[:, c * 512:(c + 1) * 512], in_=banks8[fi][:, 0:512],
                            func=AF.Identity, bias=bcols[:, CBX:CBX + 1]),
                            deps=[mms[fi]], name="bX_e0")
                        evs.append(ev)
                    else:
                        kk = fi % 2
                        ev = sch.emit("scalar", lambda E, fi=fi, ft=ft, kk=kk: E.activation(
                            out=cstg[kk][:, :], in_=banks8[fi][:, 0:512],
                            func=AF.Identity, bias=bcols[:, CBX + ft:CBX + ft + 1]),
                            deps=[mms[fi]] + ([evs[fi - 2]] if fi >= 2 else []), name="bX_ec")
                        st = sch.dma(qs[fi % 3], cse_d[ft - 1, :, c * 512:(c + 1) * 512],
                                     cstg[kk][:, :], deps=[ev], name="bX_sc")
                        evs.append(st)
                evs_all = evs
            prev_pass = evs_all
        cb = sch.emit("vector", lambda E: E.tensor_copy(b64[:, :], dbr[64:128, :]),
                      deps=prev_pass, name="bX_bs")
        bsd = sch.dma("sync", bsel_d[:, :], b64[:, :], deps=[cb], name="bX_bd")
        sch.stage()

        sch.set_phase(f"ab_D{b}")
        cbd = sch.emit("vector", lambda E: E.tensor_copy(b64[:, :], dbr[0:64, :]),
                       deps=[bsd], name="bD_c")
        prev_ev = {}
        prev_ln = None
        for ft in range(16):
            for c in range(4):
                k = (ft * 4 + c) % 2
                ps = [psA, psB][k]
                mm = sch.emit("tensor", lambda E, ft=ft, c=c, ps=ps: E.matmul(
                    ps[:, 0:512], wdt_sb[:, ft * 128:(ft + 1) * 128],
                    b64[:, c * 512:(c + 1) * 512], start=True, stop=True),
                    deps=[cbd, prev_ev.get(k)], name="bD_mm")
                ev1 = sch.emit("scalar", lambda E, ft=ft, ps=ps: E.activation(
                    out=dtmp[:, :], in_=ps[:, 0:512], func=AF.Exp,
                    bias=bcols[:, CBDT + ft:CBDT + ft + 1]),
                    deps=[mm, prev_ev.get(k), prev_ln], name="bD_e1")
                ev = sch.emit("scalar", lambda E, k=k: E.activation(
                    out=cstg[k][:, :], in_=dtmp[:, :], func=AF.Ln, bias=1.0),
                    deps=[ev1], name="bD_e2")
                prev_ln = ev
                st = sch.dma(qs[c % 3], dlt_d[ft, :, c * 512:(c + 1) * 512],
                             cstg[k][:, :], deps=[ev], name="bD_st")
                prev_ev[k] = st
        sch.stage()

        sch.set_phase(f"ab_S{b}")
        ctx.reset_to(MARKB)
        pdl = [ctx.scr("bS_dl0", (128, 2048), BF16), ctx.scr("bS_dl1", (128, 2048), BF16)]
        pxs = [ctx.scr("bS_xs0", (128, 2048), BF16), ctx.scr("bS_xs1", (128, 2048), BF16)]
        pcs = [ctx.scr("bS_cs0", (128, 2048), BF16), ctx.scr("bS_cs1", (128, 2048), BF16)]
        psl = [ctx.scr("bS_sl0", (128, 2048), BF16), ctx.scr("bS_sl1", (128, 2048), BF16)]
        dxp = [ctx.scr("bS_dx0", (128, 2048), BF16), ctx.scr("bS_dx1", (128, 2048), BF16)]
        btil = [ctx.scr("bS_bt0", (128, 2048), BF16), ctx.scr("bS_bt1", (128, 2048), BF16)]
        dat = [[ctx.scr(f"bS_da{dd}{k}", (128, 2048), BF16) for k in range(2)] for dd in range(2)]
        dxt = [[ctx.scr(f"bS_dxt{dd}{k}", (128, 2048), BF16) for k in range(2)] for dd in range(2)]
        hti = [[ctx.scr(f"bS_h{dd}{k}", (128, 2048), BF16) for k in range(2)] for dd in range(2)]
        ytm = ctx.scr("bS_yt", (128, 2048), BF16)
        y2s = [ctx.scr("bS_y20", (128, 2048), BF16), ctx.scr("bS_y21", (128, 2048), BF16)]
        accs = [[psA, psB, psC, psD], [psE[:, k * 512:(k + 1) * 512] for k in range(4)]]
        prev_y2 = {0: [], 1: []}
        prev_ld = []
        for pair in range(8):
            lds = {}
            for dd in range(2):
                dblk = pair * 2 + dd
                l1 = sch.dma("sync", pdl[dd][:, :], dlt_d[dblk, :, :], deps=prev_ld, name="bS_l1")
                l2 = sch.dma("scalar", pxs[dd][:, :], xs_d[dblk, :, :], deps=prev_ld, name="bS_l2")
                l3 = sch.dma("gpsimd", pcs[dd][:, :], cse_d[dblk, :, :], deps=prev_ld, name="bS_l3")
                l4 = sch.dma("sync", psl[dd][:, :], slr_d[dblk, :, :], deps=prev_ld, name="bS_l4")
                dx = sch.emit("vector", lambda E, dd=dd: E.tensor_tensor(
                    dxp[dd][:, :], pdl[dd][:, :], pxs[dd][:, :], ALU.mult),
                    deps=[l1, l2], name="bS_dx")
                lds[dd] = [l1, l2, l3, l4, dx]
            prev_pe = {0: [], 1: []}
            prev_tts = {0: None, 1: None}
            prev_dxt = {}
            prev_bt = {}
            for n in range(64):
                k = n % 2
                bt0 = sch.dma("sync", btil[k][0:64, :],
                              bsel_d[n:n + 1, :].to_broadcast((64, 2048)),
                              deps=[bsd, prev_dxt.get((1, k))], name="bS_bt",
                              join=False)
                bt = sch.dma("gpsimd", btil[k][64:128, :],
                             bsel_d[n:n + 1, :].to_broadcast((64, 2048)),
                             deps=[bsd, prev_dxt.get((1, k))], name="bS_bt2",
                             join=False)
                prev_bt[k] = bt
                prev_bt[(k, 0)] = bt0
                for dd in range(2):
                    dblk = pair * 2 + dd
                    da = sch.emit("scalar", lambda E, dd=dd, k=k, dblk=dblk, n=n: E.activation(
                        out=dat[dd][k][:, :], in_=pdl[dd][:, :], func=AF.Exp,
                        scale=aA[:, dblk * 64 + n: dblk * 64 + n + 1]),
                        deps=lds[dd] + [prev_tts[dd]], name="bS_da")
                    dxe = sch.emit("gpsimd", lambda E, dd=dd, k=k: E.tensor_tensor(
                        dxt[dd][k][:, :], dxp[dd][:, :], btil[k][:, :], ALU.mult),
                        deps=[bt0, bt, lds[dd][4], prev_tts[dd]], name="bS_dxe")
                    prev_dxt[(dd, k)] = dxe
                    tts = sch.emit("vector", lambda E, dd=dd, k=k: E.tensor_tensor_scan(
                        hti[dd][k][:, :], dat[dd][k][:, :], dxt[dd][k][:, :], 0.0,
                        ALU.mult, ALU.add), deps=[da, dxe] + prev_pe[dd], name="bS_tts")
                    prev_tts[dd] = tts
                    pes = []
                    for c in range(4):
                        pes.append(sch.emit("tensor", lambda E, dd=dd, k=k, c=c, n=n: E.matmul(
                            accs[dd][c][:, 0:512], sb["ident128b"][:, :],
                            hti[dd][k][:, c * 512:(c + 1) * 512],
                            start=(n == 0), stop=(n == 63)),
                            deps=[tts] + prev_y2[dd], name="bS_pe"))
                    prev_pe[dd] = pes
            for dd in range(2):
                dblk = pair * 2 + dd
                w1s = []
                w3 = None
                for c in range(4):
                    w1 = sch.emit("vector", lambda E, dd=dd, c=c: E.tensor_tensor(
                        ytm[:, c * 512:(c + 1) * 512], accs[dd][c][:, 0:512],
                        pcs[dd][:, c * 512:(c + 1) * 512], ALU.mult),
                        deps=prev_pe[dd] + [lds[dd][2]] + prev_y2[dd], name="bS_w1")
                    w2 = sch.emit("vector", lambda E, dd=dd, c=c, dblk=dblk: E.scalar_tensor_tensor(
                        ytm[:, c * 512:(c + 1) * 512], pxs[dd][:, c * 512:(c + 1) * 512],
                        bcols[:, CDP + dblk:CDP + dblk + 1],
                        ytm[:, c * 512:(c + 1) * 512], ALU.mult, ALU.add),
                        deps=[w1], name="bS_w2")
                    w3 = sch.emit("vector", lambda E, dd=dd, c=c: E.tensor_tensor(
                        y2s[dd][:, c * 512:(c + 1) * 512], ytm[:, c * 512:(c + 1) * 512],
                        psl[dd][:, c * 512:(c + 1) * 512], ALU.mult),
                        deps=[w2, lds[dd][3]], name="bS_w3")
                    w1s.append(w1)
                    w1s.append(w3)
                st = sch.dma(qs[dd], y2_d[dblk, :, :], y2s[dd][:, :], deps=[w3] + w1s,
                             name="bS_y2d")
                prev_y2[dd] = w1s + [st]
            prev_ld = [prev_y2[0][-1], prev_y2[1][-1], prev_tts[0], prev_tts[1],
                       prev_bt[0], prev_bt[1]]
        sch.stage()

        sch.set_phase(f"ab_O{b}")
        ctx.reset_to(MARKB)
        wob = ctx.scr("bO_w", (128, 16384), BF16)
        ysc = [ctx.scr("bO_y0", (128, 512), BF16), ctx.scr("bO_y1", (128, 512), BF16)]
        xnt = [ctx.scr("bO_xn0", (128, 512), F32), ctx.scr("bO_xn1", (128, 512), F32)]
        zpl = ctx.scr("bO_z", (128, 8 * 2048), F32)
        wl = []
        for ft in range(8):
            for dblk in range(16):
                wl.append(sch.dma(qs[(ft + dblk) % 3],
                          wob[:, (ft * 16 + dblk) * 128:(ft * 16 + dblk) * 128 + 128],
                          wzo[ft, dblk], deps=wz_done, name="bO_w"))
        prev_ev = []
        prev_yl = {0: None, 1: None}
        prev_xl = {0: None, 1: None}
        for c in range(4):
            mms = {}
            for dblk in range(16):
                k = dblk % 2
                yl = sch.dma("sync", ysc[k][:, :], y2_d[dblk, :, c * 512:(c + 1) * 512],
                             deps=[prev_yl[k]], name="bO_yl")
                for ft in range(8):
                    mms[ft] = sch.emit("tensor", lambda E, ft=ft, dblk=dblk, k=k: E.matmul(
                        banks8[ft][:, 0:512],
                        wob[:, (ft * 16 + dblk) * 128:(ft * 16 + dblk) * 128 + 128],
                        ysc[k][:, :], start=(dblk == 0), stop=(dblk == 15)),
                        deps=wl + [yl] + prev_ev, name="bO_mm")
                prev_yl[k] = mms[7]
            evs = []
            for ft in range(8):
                k = ft % 2
                xl = sch.dma("scalar", xnt[k][:, :], xn3_d[ft, :, c * 512:(c + 1) * 512],
                             deps=[prev_xl[k]], name="bO_xl")
                e1 = sch.emit("scalar", lambda E, ft=ft, c=c: E.activation(
                    out=zpl[:, ft * 2048 + c * 512: ft * 2048 + c * 512 + 512],
                    in_=banks8[ft][:, 0:512], func=AF.Identity,
                    bias=bcols[:, CBO + ft:CBO + ft + 1]), deps=[mms[ft]], name="bO_e1")
                e2 = sch.emit("vector", lambda E, ft=ft, c=c, k=k: E.tensor_tensor(
                    zpl[:, ft * 2048 + c * 512: ft * 2048 + c * 512 + 512],
                    zpl[:, ft * 2048 + c * 512: ft * 2048 + c * 512 + 512],
                    xnt[k][:, :], ALU.add), deps=[e1, xl], name="bO_e2")
                prev_xl[k] = e2
                evs.append(e2)
            prev_ev = evs
        sch.stage()

        sch.set_phase(f"ab_F{b}")
        stoks, _ = emit_stats(ctx, lambda dt: zpl[:, dt * 2048:(dt + 1) * 2048],
                              prev_ev, "bF", sq_buf=sqb)
        bmu = emit_bcast(ctx, plmu, sb["mu_row"], stoks, "bF_bm")
        t5b = sch.emit("vector", lambda E: E.tensor_copy(sb["mu_row"][0:1, :],
                       sb["var_row"][0:1, :]), deps=stoks + bmu, name="bF_r0")
        brr = emit_bcast(ctx, plr, sb["mu_row"], [t5b], "bF_br")
        sch.stage()
        prev = []
        for dt in range(NDT):
            tl = sch.dma("sync", xt[:, :], x_fm[dt * 128:(dt + 1) * 128, col0:col0 + S],
                         deps=prev, name="bF_x")
            f1 = sch.emit("vector", lambda E, dt=dt: E.tensor_tensor(
                sqb[:, :], zpl[:, dt * 2048:(dt + 1) * 2048], plmu[:, :], ALU.subtract),
                deps=[tl], name="bF_f1")
            f2 = sch.emit("vector", lambda E: E.tensor_tensor(sqb[:, :], sqb[:, :],
                          plr[:, :], ALU.mult), deps=[f1], name="bF_f2")
            f3 = sch.emit("vector", lambda E, dt=dt: E.scalar_tensor_tensor(
                sqb[:, :], sqb[:, :], bcols[:, CGAM + dt:CGAM + dt + 1], xt[:, :],
                ALU.mult, ALU.add), deps=[f2], name="bF_f3")
            f4 = sch.emit("scalar", lambda E, dt=dt: E.activation(
                out=xt[:, :], in_=sqb[:, :], func=AF.Identity,
                bias=bcols[:, CBET + dt:CBET + dt + 1]), deps=[f3], name="bF_f4")
            st = sch.dma(qs[dt % 3], x4_fm[dt * 128:(dt + 1) * 128, col0:col0 + S],
                         xt[:, :], deps=[f4], name="bF_st")
            prev = [f1, st, f4]
        sch.stage()
    return {}



def _build_program(phases="am0bm1"):
    import concourse.bass as bass
    nc = bass.Bass("TRN2", target_bir_lowering=False)
    sch = Sched()
    ctx = Ctx(nc, sch)
    x_fm = ctx.dram_in("x_fm", (1024, T), F32)
    x4_fm = ctx.dram_out("x4_fm", (1024, T), F32)
    x1_fm = ctx.dram_tmp("x1_fm", (1024, T), F32)
    x2_fm = ctx.dram_tmp("x2_fm", (1024, T), F32)
    x3_fm = ctx.dram_tmp("x3_fm", (1024, T), F32)
    xn1b_dram = ctx.dram_tmp("xn1b_dram", (1024, T), BF16)
    z_dram = ctx.dram_tmp("z_dram", (1024, T), F32)
    for nme, arr in host_consts().items():
        ctx.dram_in(f"c_{nme}", arr.shape, F32)
    aa_shapes = {
        "L0_attn_a_ln_w": (1024,), "L0_attn_a_ln_b": (1024,),
        "L0_attn_a_sign_W": (1024, 32), "L0_attn_a_sign_b": (32,),
        "L0_attn_a_perm_W": (1024, 256), "L0_attn_a_perm_b": (256,),
        "L0_attn_a_gate_W": (1024, 32), "L0_attn_a_gate_b": (32,),
        "L0_attn_a_in_W": (1024, 32), "L0_attn_a_in_b": (32,),
        "L0_attn_a_out_W": (32, 1024), "L0_attn_a_out_b": (1024,),
    }
    for nm, shp in aa_shapes.items():
        ctx.dram_in(nm, shp, F32)
    ctx.dram_in("gate_wT", (1024, 64), F32)
    ctx.dram_in("L0_ffn_gate_b", (64,), F32)
    ctx.dram_in("ws1T", (1024, 1408), F32)
    ctx.dram_in("ws2T", (1408, 1024), F32)
    ctx.dram_in("ws3T", (1024, 1408), F32)
    ctx.dram_in("w1T_my", (8, 1024, 704), F32)
    ctx.dram_in("w2T_my", (8, 704, 1024), F32)
    ctx.dram_in("w3T_my", (8, 1024, 704), F32)
    ctx.dram_in("tokid", (128, 32), F32)
    ctx.dram_in("rankmask", (8, 1), F32)
    ctx.dram["myslots"] = nc.dram_tensor("myslots", [128, 32], U32,
                                         kind="ExternalInput").ap()
    ctx.dram_in("b_Wi", (1024, 4096), F32)
    ctx.dram_in("b_bi_c", (128, 32), F32)
    ctx.dram_in("b_Wx", (2048, 2176), F32)
    ctx.dram_in("b_bx_c", (128, 17), F32)
    ctx.dram_in("b_Wdt", (64, 2048), F32)
    ctx.dram_in("b_bdt_c", (128, 16), F32)
    ctx.dram_in("b_Alog_r", (128, 1024), F32)
    ctx.dram_in("b_Dp_c", (128, 16), F32)
    ctx.dram_in("b_Wo", (2048, 1024), F32)
    ctx.dram_in("b_bo_c", (128, 8), F32)
    ctx.dram_in("b_gam_c", (128, 8), F32)
    ctx.dram_in("b_bet_c", (128, 8), F32)

    alloc_common(ctx)
    ctx.init_arena(52500)
    alloc_stats_top(ctx)
    TOPM = ctx.top_mark()
    alloc_attn_a_body(ctx)
    load_consts(ctx)
    sch.stage()
    cur = x_fm
    if "a" in phases.replace("m0", "").replace("m1", "").replace("b", "") or phases.startswith("a"):
        emit_attn_a(ctx, cur, x1_fm, xn1b_dram, deps=[])
        cur = x1_fm
        sch.stage()
    if "m0" in phases:
        ctx.scratch_reset()
        alloc_moe(ctx)
        emit_moe(ctx, cur, x2_fm, z_dram, 0, deps=[])
        cur = x2_fm
        ctx.top_reset(TOPM)
        sch.stage()
    if "b" in phases.replace("m0", "").replace("m1", ""):
        emit_attn_b(ctx, cur, x3_fm, deps=[])
        cur = x3_fm
        sch.stage()
    if "m1" in phases:
        ctx.scratch_reset()
        alloc_moe(ctx)
        emit_moe(ctx, cur, x4_fm, z_dram, 1, deps=[])
        cur = x4_fm
        sch.stage()
    if cur is not x4_fm:
        ctx.scratch_reset()
        cp = ctx.scr("fin_cp", (128, 2048), F32)
        prev = []
        for dt in range(NDT):
            for cb2 in range(2):
                tl = sch.dma("sync", cp[:, :], cur[dt * 128:(dt + 1) * 128, cb2 * 2048:(cb2 + 1) * 2048], deps=prev, name="fin_l")
                st = sch.dma("scalar", x4_fm[dt * 128:(dt + 1) * 128, cb2 * 2048:(cb2 + 1) * 2048], cp[:, :], deps=[tl], name="fin_s")
                prev = [st]
    sch.stage()
    section, sem_c, sem_d = sch.build(nc)
    with nc.Block() as block:
        block.sync(section("sync"))
        block.vector(section("vector"))
        block.scalar(section("scalar"))
        block.tensor(section("tensor"))
        block.gpsimd(section("gpsimd"))
    return nc, ctx


def _make_in_maps(inp):
    X = inp["L0_attn_a_x"].astype(np.float32)
    consts = host_consts()
    in_maps = []
    for c in range(8):
        im = {"x_fm": np.ascontiguousarray(
            np.concatenate([X[2 * c].T, X[2 * c + 1].T], axis=1))}
        for nme, arr in consts.items():
            im[f"c_{nme}"] = arr
        for k in list(inp.keys()):
            if k.startswith("L0_attn_a") and k != "L0_attn_a_x":
                im[k] = inp[k]
        im["gate_wT"] = np.ascontiguousarray(inp["L0_ffn_gate_w"].T)
        im["L0_ffn_gate_b"] = inp["L0_ffn_gate_b"]
        im["ws1T"] = np.ascontiguousarray(inp["L0_ffn_ws1"].T)
        im["ws2T"] = np.ascontiguousarray(inp["L0_ffn_ws2"].T)
        im["ws3T"] = np.ascontiguousarray(inp["L0_ffn_ws3"].T)
        im["w1T_my"] = np.ascontiguousarray(inp["L0_ffn_w1"][8 * c:8 * c + 8].transpose(0, 2, 1))
        im["w2T_my"] = np.ascontiguousarray(inp["L0_ffn_w2"][8 * c:8 * c + 8].transpose(0, 2, 1))
        im["w3T_my"] = np.ascontiguousarray(inp["L0_ffn_w3"][8 * c:8 * c + 8].transpose(0, 2, 1))
        tokid = np.zeros((128, 32), np.float32)
        for b in range(2):
            for tt in range(16):
                tokid[:, b * 16 + tt] = c * 4096 + b * 2048 + tt * 128 + np.arange(128) + 1
        im["tokid"] = tokid
        im["rankmask"] = (np.arange(8) < c).astype(np.float32).reshape(8, 1)
        mys = np.zeros((128, 32), np.uint32)
        for g in range(32):
            mys[:, g] = c * 4096 + g * 128 + np.arange(128)
        im["myslots"] = mys
        im["b_Wi"] = inp["L0_attn_b_Wi"]
        im["b_bi_c"] = np.ascontiguousarray(inp["L0_attn_b_bi"].reshape(32, 128).T)
        im["b_Wx"] = inp["L0_attn_b_Wx"]
        im["b_bx_c"] = np.ascontiguousarray(inp["L0_attn_b_bx"].reshape(17, 128).T)
        im["b_Wdt"] = inp["L0_attn_b_Wdt"]
        im["b_bdt_c"] = np.ascontiguousarray(inp["L0_attn_b_bdt"].reshape(16, 128).T)
        im["b_Alog_r"] = np.ascontiguousarray(
            inp["L0_attn_b_A_log"].reshape(16, 128, 64).transpose(1, 0, 2).reshape(128, 1024))
        im["b_Dp_c"] = np.ascontiguousarray(inp["L0_attn_b_Dp"].reshape(16, 128).T)
        im["b_Wo"] = inp["L0_attn_b_Wo"]
        im["b_bo_c"] = np.ascontiguousarray(inp["L0_attn_b_bo"].reshape(8, 128).T)
        im["b_gam_c"] = np.ascontiguousarray(inp["L0_attn_b_gamma"].reshape(8, 128).T)
        im["b_bet_c"] = np.ascontiguousarray(inp["L0_attn_b_beta"].reshape(8, 128).T)
        in_maps.append(im)
    return in_maps


def kernel(**inputs):
    from concourse.bass_utils import run_bass_kernel_spmd
    inp = {k: np.asarray(v) for k, v in inputs.items()}
    nc, ctx = _build_program()
    in_maps = _make_in_maps(inp)
    res = run_bass_kernel_spmd(nc, in_maps, core_ids=list(range(8)))
    out = np.empty((16, 2048, 1024), np.float32)
    for c in range(8):
        pl = res.results[c]["x4_fm"]
        out[2 * c] = pl[:, 0:2048].T
        out[2 * c + 1] = pl[:, 2048:4096].T
    return out


# version 10 (final)
# speedup vs baseline: 1.0729x; 1.0280x over previous
import numpy as np
import concourse.bass as bass
import concourse.mybir as mybir

F32 = mybir.dt.float32
BF16 = mybir.dt.bfloat16
U16 = mybir.dt.uint16
U32 = mybir.dt.uint32
AF = mybir.ActivationFunctionType
ALU = mybir.AluOpType
AX = mybir.AxisListType

B = 2
S = 2048
D = 1024
T = B * S
NC = 8
H, N8 = 4, 8
DIN = 2048
DST = 64
E, TOPK, CAP = 64, 6, 512
EPERC = E // NC
I_EXP = 704
SI = 1408
EPS = 1e-5
NDT = D // 128

ENGINES = ("sync", "vector", "scalar", "tensor", "gpsimd")

_BREG = {}

def bounds_reg(E):
    k = id(E)
    if k not in _BREG:
        r = E.alloc_register()
        E.reg_mov(r, 32767)
        _BREG[k] = r
    return _BREG[k]


NJSLOTS = 8
DRAIN_PHASES = {"init", "aa"}


class Tok:
    __slots__ = ("eng", "is_dma", "seq", "dma_seq", "cc_seq", "nojoin",
                 "nj_slot", "nj_cnt")
    def __init__(self, eng, is_dma, seq):
        self.eng, self.is_dma, self.seq = eng, is_dma, seq
        self.dma_seq = None
        self.cc_seq = None
        self.nojoin = False
        self.nj_slot = None
        self.nj_cnt = None


class Sched:
    def __init__(self):
        self.items = []
        self.cc = {e: 0 for e in ENGINES}
        self.cd = {e: 0 for e in ENGINES}
        self.ccc = {}
        self.cdn = {e: 0 for e in ENGINES}
        self.last = {e: None for e in ENGINES}
        self._stage = []
        self._phase = "init"

    def set_phase(self, p):
        self._phase = p

    def emit(self, eng, fn, deps=(), is_dma=False, name="", is_cc=False,
             join=True):
        if is_dma and not join:
            i = self.cdn[eng]
            self.cdn[eng] += 1
            tok = Tok(eng, False, self.cc[eng])
            tok.nojoin = True
            tok.nj_slot = i % NJSLOTS
            tok.nj_cnt = i // NJSLOTS + 1
            deps = tuple(d for d in deps if d is not None) + tuple(self._stage)
            self.items.append((eng, fn, deps, "dmanj", name, tok, self._phase))
            self.last[eng] = tok
            return tok
        if is_dma:
            self.cd[eng] += 1
            self.cc[eng] += 1
            tok = Tok(eng, False, self.cc[eng])
            tok.dma_seq = self.cd[eng]
        elif is_cc:
            self.ccc[eng] = self.ccc.get(eng, 0) + 1
            self.cc[eng] += 1
            tok = Tok(eng, False, self.cc[eng])
            tok.dma_seq = None
            tok.cc_seq = self.ccc[eng]
        else:
            self.cc[eng] += 1
            tok = Tok(eng, False, self.cc[eng])
            tok.dma_seq = None
        deps = tuple(d for d in deps if d is not None) + tuple(self._stage)
        self.items.append((eng, fn, deps, ("dma" if is_dma else "cc" if is_cc else ""),
                           name, tok, self._phase))
        self.last[eng] = tok
        return tok


    def dma(self, eng, out, in_, deps=(), name="", slow=False, join=True):
        if slow:
            return self.emit(eng, lambda E, o=out, i=in_: E.dma_start(
                out=o, in_=i, allow_slow_non_contiguous=True),
                deps=deps, is_dma=True, name=name, join=join)
        return self.emit(eng, lambda E, o=out, i=in_: E.dma_start(out=o, in_=i),
                         deps=deps, is_dma=True, name=name, join=join)

    def stage(self):
        self._stage = [t for t in self.last.values() if t is not None]

    def build(self, nc):
        sem_c = {e: nc.semaphore(f"sc_{e}").__enter__() for e in ENGINES}
        sem_d = {e: nc.semaphore(f"sd_{e}").__enter__() for e in ENGINES}
        sem_g = nc.semaphore("sg_cc").__enter__()
        sem_n = {}
        for e in ENGINES:
            if self.cdn[e]:
                sem_n[e] = [nc.semaphore(f"sn_{e}{j}").__enter__()
                            for j in range(NJSLOTS)]

        def section(ename):
            def run(E):
                floor = {}
                floor_d = {}
                for (eng, fn, deps, kind, name, tok, phase) in self.items:
                    if eng != ename:
                        continue
                    w = {}
                    wd = {}
                    for d in deps:
                        if d.nojoin:
                            wd[(d.eng, d.nj_slot)] = max(
                                wd.get((d.eng, d.nj_slot), 0), d.nj_cnt)
                            continue
                        w[d.eng] = max(w.get(d.eng, 0), d.seq)
                    for deng, seq in w.items():
                        if seq <= floor.get(deng, 0):
                            continue
                        floor[deng] = seq
                        E.wait_ge(sem_c[deng], seq)
                    for dk, cnt in wd.items():
                        if cnt <= floor_d.get(dk, 0):
                            continue
                        floor_d[dk] = cnt
                        E.wait_ge(sem_n[dk[0]][dk[1]], 16 * cnt)
                    with nc.named_scope(phase):
                        if kind == "dmanj":
                            if tok.nj_cnt > 1:
                                k0 = (ename, tok.nj_slot)
                                if floor_d.get(k0, 0) < tok.nj_cnt - 1:
                                    floor_d[k0] = tok.nj_cnt - 1
                                    E.wait_ge(sem_n[ename][tok.nj_slot],
                                              16 * (tok.nj_cnt - 1))
                        inst = fn(E)
                        if kind == "dmanj":
                            inst.then_inc(sem_n[ename][tok.nj_slot], 16)
                        elif kind == "dma":
                            inst.then_inc(sem_d[ename], 16)
                            E.wait_ge(sem_d[ename], 16 * tok.dma_seq)
                            E.nop().then_inc(sem_c[ename], 1)
                        elif kind == "cc":
                            inst.then_inc(sem_g, 1)
                            E.wait_ge(sem_g, tok.cc_seq)
                            E.nop().then_inc(sem_c[ename], 1)
                        else:
                            if inst is None:
                                E.nop().then_inc(sem_c[ename], 1)
                            elif phase in DRAIN_PHASES:
                                E.maybe_drain_then_inc((sem_c[ename], 1))
                            else:
                                inst.then_inc(sem_c[ename], 1)
            return run
        return section, sem_c, sem_d


class Ctx:
    def __init__(self, nc, sch):
        self.nc = nc
        self.sch = sch
        self.sb = {}
        self.ps = {}
        self.dram = {}
        self._exits = []
        self._scopes = []

    def push_scope(self):
        self._scopes.append([])

    def pop_scope(self):
        for name, h in reversed(self._scopes.pop()):
            try:
                h.__exit__(None, None, None)
            except Exception:
                pass

    def init_arena(self, cols_f32):
        self.arena = self.sbuf("ARENA", (128, cols_f32), F32)
        self._bump = 0
        self._top = cols_f32
        self._arena_cols = cols_f32

    def scratch_reset(self):
        self._bump = 0

    def scr_top(self, name, shape, dtype=F32):
        p, n = shape
        if dtype in (BF16, U16, U32):
            units = (n + 1) // 2 if dtype in (BF16, U16) else n
        else:
            units = n
        units = (units + 15) & ~15
        self._top -= units
        assert self._bump <= self._top, (name, self._bump, self._top)
        view = self.arena[0:p, self._top:self._top + units]
        if dtype in (BF16, U16, U32):
            view = view.bitcast(dtype)
        view = view[0:p, 0:n]
        self.sb[name] = view
        return view

    def top_mark(self):
        return self._top

    def top_reset(self, mark):
        self._top = mark

    def mark(self):
        return self._bump

    def reset_to(self, mark):
        self._bump = mark

    def scr(self, name, shape, dtype=F32):
        p, n = shape
        if dtype in (BF16, U16):
            units = (n + 1) // 2
        else:
            units = n
        units = (units + 15) & ~15
        off = self._bump
        self._bump += units
        assert self._bump <= self._top, (name, self._bump, self._top)
        view = self.arena[0:p, off:off + units]
        if dtype in (BF16, U16, U32):
            view = view.bitcast(dtype)
        view = view[0:p, 0:n]
        self.sb[name] = view
        return view

    def sbuf(self, name, shape, dtype=F32):
        h = self.nc.sbuf_tensor(name, list(shape), dtype)
        t = h.__enter__()
        if self._scopes:
            self._scopes[-1].append((name, h))
        else:
            self._exits.append(h)
        self.sb[name] = t
        return t

    def psum(self, name, shape, dtype=F32):
        h = self.nc.psum_tensor(list(shape), dtype)
        t = h.__enter__()
        self._exits.append(h)
        self.ps[name] = t
        return t

    def dram_in(self, name, shape, dtype=F32):
        t = self.nc.dram_tensor(name, list(shape), dtype, kind="ExternalInput").ap()
        self.dram[name] = t
        return t

    def dram_out(self, name, shape, dtype=F32):
        t = self.nc.dram_tensor(name, list(shape), dtype, kind="ExternalOutput").ap()
        self.dram[name] = t
        return t

    def dram_tmp(self, name, shape, dtype=F32, shared=False):
        kw = dict(kind="Internal")
        if shared:
            kw["addr_space"] = "Shared"
        t = self.nc.dram_tensor(name, list(shape), dtype, **kw).ap()
        self.dram[name] = t
        return t


def host_consts():
    c = {}
    c["ident128"] = np.eye(128, dtype=np.float32)
    c["ones_col"] = np.ones((128, 1), np.float32)
    c["ones_row"] = np.ones((1, 128), np.float32)
    lt = (np.arange(8)[None, :] < np.arange(8)[:, None]).astype(np.float32)
    c["lt8"] = np.broadcast_to(lt.reshape(1, 64), (128, 64)).copy()
    c["ident8row"] = np.eye(8, dtype=np.float32).reshape(1, 64)
    c["ident8rep"] = np.broadcast_to(np.eye(8, dtype=np.float32).reshape(1, 64), (128, 64)).copy()
    c["ustri128"] = (np.arange(128)[:, None] < np.arange(128)[None, :]).astype(np.float32)
    c["erow"] = np.broadcast_to(np.arange(64, dtype=np.float32).reshape(1, 64), (128, 64)).copy()
    lt6 = (np.arange(6)[None, :] < np.arange(6)[:, None]).astype(np.float32)
    c["lt6"] = np.broadcast_to(lt6.reshape(1, 36), (128, 36)).copy()
    return c


def alloc_common(ctx):
    ctx.sbuf("ident128", (128, 128), F32)
    ctx.sbuf("ones_col", (128, 1), F32)
    ctx.sbuf("ones_row", (1, 128), F32)
    ctx.sbuf("lt8", (128, 64), F32)
    ctx.sbuf("ustri128", (128, 128), F32)
    ctx.sbuf("erow", (128, 64), F32)
    ctx.sbuf("lt6", (128, 36), F32)
    ctx.sbuf("eps_col", (128, 1), F32)
    ctx.sbuf("ident128b", (128, 128), BF16)
    ctx.psum("psA", (128, 512))
    ctx.psum("psB", (128, 512))
    ctx.psum("psC", (128, 512))
    ctx.psum("psD", (128, 512))
    ctx.psum("psE", (128, 2048))


def alloc_stats_top(ctx):
    sr = ctx.scr_top("stat_rows", (33, 2048), F32)
    ctx.sb["mu_row"] = sr[0:1, :]
    ctx.sb["var_row"] = sr[32:33, :]
    ctx.sb["r_row"] = sr[32:33, :]
    ctx.sb["negmu_row"] = sr[0:1, :]


def load_consts(ctx, deps=()):
    sch = ctx.sch
    toks = []
    for nme in ("ident128", "ones_col", "ones_row", "lt8", "ustri128", "erow", "lt6"):
        toks.append(sch.dma("sync", ctx.sb[nme][:], ctx.dram[f"c_{nme}"][:], deps=deps))
    toks.append(sch.emit("gpsimd", lambda E: E.memset(ctx.sb["eps_col"][:, :], EPS),
                         deps=deps, name="eps"))
    toks.append(ctx.sch.emit("vector", lambda E: E.tensor_copy(
        ctx.sb["ident128b"][:, :], ctx.sb["ident128"][:, :]), deps=[toks[0]], name="id_b"))
    return toks



def emit_ln_stats(ctx, src_fn, deps, tag):
    nc, sch = ctx.nc, ctx.sch
    ones = ctx.sb["ones_col"]
    psE = ctx.ps["psE"]
    sqs = [ctx.sb["sq0"], ctx.sb["sq1"]]
    sq_free = [None, None]
    mm_toks = []
    for dt in range(NDT):
        sq = sqs[dt % 2]
        dep_sq = [sq_free[dt % 2]] if sq_free[dt % 2] else []
        t_sq = sch.emit("scalar", lambda E, dt=dt, sq=sq: E.activation(
            out=sq[:], in_=src_fn(dt), func=AF.Square), deps=list(deps) + dep_sq,
            name=f"{tag}_sq{dt}")
        last = None
        for c0 in range(0, S, 512):
            m1 = sch.emit("tensor", lambda E, dt=dt, c0=c0: E.matmul(
                psE[0:1, c0:c0 + 512], ones[:, 0:1], src_fn(dt)[:, c0:c0 + 512],
                start=(dt == 0), stop=(dt == NDT - 1)), deps=deps, name=f"{tag}_s")
            m2 = sch.emit("tensor", lambda E, sq=sq, c0=c0, dt=dt: E.matmul(
                psE[32:33, c0:c0 + 512], ones[:, 0:1], sq[:, c0:c0 + 512],
                start=(dt == 0), stop=(dt == NDT - 1)), deps=[t_sq], name=f"{tag}_q")
            last = m2
            if dt == NDT - 1:
                mm_toks += [m1, m2]
        sq_free[dt % 2] = last
    mu, var, r, negmu = (ctx.sb["mu_row"], ctx.sb["var_row"], ctx.sb["r_row"],
                         ctx.sb["negmu_row"])
    t1 = sch.emit("vector", lambda E: E.tensor_scalar(mu[0:1, :], psE[0:1, :], 1.0 / D,
                  None, ALU.mult), deps=mm_toks, name=f"{tag}_mu")
    t2 = sch.emit("vector", lambda E: E.tensor_tensor(var[0:1, :], mu[0:1, :], mu[0:1, :],
                  ALU.mult), deps=[t1], name=f"{tag}_mu2")
    t3 = sch.emit("vector", lambda E: E.scalar_tensor_tensor(var[0:1, :], psE[32:33, :],
                  1.0 / D, var[0:1, :], ALU.mult, ALU.subtract), deps=[t2] + mm_toks,
                  name=f"{tag}_var")
    t4 = sch.emit("scalar", lambda E: E.activation(out=r[0:1, :], in_=var[0:1, :],
                  func=AF.Sqrt, bias=sb["eps_col"][0:1, 0:1]), deps=[t3], name=f"{tag}_sqrt")
    t5 = sch.emit("vector", lambda E: E.reciprocal(r[0:1, :], r[0:1, :]), deps=[t4],
                  name=f"{tag}_rcp")
    t6 = sch.emit("vector", lambda E: E.tensor_scalar(negmu[0:1, :], mu[0:1, :], -1.0,
                  None, ALU.mult), deps=[t1], name=f"{tag}_negmu")
    return [t5, t6]


def emit_bcast_plane(ctx, dst_sb, row_ap, deps, tag):
    nc, sch = ctx.nc, ctx.sch
    onesr = ctx.sb["ones_row"]
    psE = ctx.ps["psE"]
    toks = []
    for c0 in range(0, 2048, 512):
        toks.append(sch.emit("tensor", lambda E, c0=c0: E.matmul(
            psE[:, c0:c0 + 512], onesr[0:1, :], row_ap[0:1, c0:c0 + 512],
            start=True, stop=True), deps=deps, name=f"{tag}_bc"))
    ev = sch.emit("vector", lambda E: E.tensor_copy(dst_sb[:, :], psE[:, :]),
                  deps=toks, name=f"{tag}_ev")
    return [ev]



def emit_attn_a_weights(ctx, deps):
    nc, sch = ctx.nc, ctx.sch
    d = ctx.dram
    sb = ctx.sb
    wst = sb["wstage"]
    lnw, lnb = sb["lnw_col"], sb["lnb_col"]
    psA, psB = ctx.ps["psA"], ctx.ps["psB"]
    ob1 = sch.emit("gpsimd", lambda E: E.memset(sb["ones_col_b"][:, :], 1.0), deps=deps,
                   name="onesb")
    tok_ln = [sch.dma("sync", lnw[:, :], d["L0_attn_a_ln_w"].rearrange("(a p) -> p a", p=128), deps=deps, slow=True),
              sch.dma("sync", lnb[:, :], d["L0_attn_a_ln_b"].rearrange("(a p) -> p a", p=128), deps=deps, slow=True)]
    const_mm = []
    prev_users = []
    for dt in range(NDT):
        tks = [sch.dma("sync", wst[:, 0:32], d["L0_attn_a_sign_W"][dt * 128:(dt + 1) * 128, :], deps=prev_users),
               sch.dma("scalar", wst[:, 32:288], d["L0_attn_a_perm_W"][dt * 128:(dt + 1) * 128, :], deps=prev_users),
               sch.dma("gpsimd", wst[:, 288:320], d["L0_attn_a_gate_W"][dt * 128:(dt + 1) * 128, :], deps=prev_users),
               sch.dma("gpsimd", wst[:, 320:352], d["L0_attn_a_in_W"][dt * 128:(dt + 1) * 128, :], deps=prev_users)]
        tks += tok_ln
        psE = ctx.ps["psE"]
        psC, psD = ctx.ps["psC"], ctx.ps["psD"]
        c1 = sch.emit("tensor", lambda E, dt=dt: E.matmul(
            psA[0:128, 0:1], wst[:, 32:160], lnb[:, dt:dt + 1],
            start=(dt == 0), stop=(dt == NDT - 1)), deps=tks, name="aw_c0")
        c2 = sch.emit("tensor", lambda E, dt=dt: E.matmul(
            psB[0:128, 0:1], wst[:, 160:288], lnb[:, dt:dt + 1],
            start=(dt == 0), stop=(dt == NDT - 1)), deps=tks, name="aw_c1")
        c3 = sch.emit("tensor", lambda E, dt=dt: E.matmul(
            psC[0:32, 0:1], wst[:, 0:32], lnb[:, dt:dt + 1],
            start=(dt == 0), stop=(dt == NDT - 1)), deps=tks, name="aw_c2")
        c4 = sch.emit("tensor", lambda E, dt=dt: E.matmul(
            psD[32:64, 0:1], wst[:, 288:320], lnb[:, dt:dt + 1],
            start=(dt == 0), stop=(dt == NDT - 1)), deps=tks, name="aw_c3")
        c5 = sch.emit("tensor", lambda E, dt=dt: E.matmul(
            psE[64:96, 0:1], wst[:, 320:352], lnb[:, dt:dt + 1],
            start=(dt == 0), stop=(dt == NDT - 1)), deps=tks, name="aw_c4")
        const_mm = [c1, c2, c3, c4, c5]
        t_p = sch.emit("vector", lambda E, dt=dt: E.tensor_scalar(
            sb["aw_perm"][:, dt * 256:(dt + 1) * 256], wst[:, 32:288],
            lnw[:, dt:dt + 1], None, ALU.mult), deps=tks + const_mm, name="aw_cp")
        t_s = sch.emit("vector", lambda E, dt=dt: E.tensor_scalar(
            sb["aw_sgi"][:, dt * 96: dt * 96 + 32], wst[:, 0:32],
            lnw[:, dt:dt + 1], None, ALU.mult), deps=tks + const_mm, name="aw_cs")
        t_g = sch.emit("vector", lambda E, dt=dt: E.tensor_scalar(
            sb["aw_sgi"][:, dt * 96 + 32: dt * 96 + 64], wst[:, 288:320],
            lnw[:, dt:dt + 1], None, ALU.mult), deps=tks + const_mm, name="aw_cg")
        t_i = sch.emit("vector", lambda E, dt=dt: E.tensor_scalar(
            sb["aw_sgi"][:, dt * 96 + 64: dt * 96 + 96], wst[:, 320:352],
            lnw[:, dt:dt + 1], None, ALU.mult), deps=tks + const_mm, name="aw_ci")
        prev_users = [t_p, t_s, t_g, t_i]
    bias_tiles = sb["a_bias"]
    tz = sch.emit("gpsimd", lambda E: E.memset(bias_tiles[:, :], 0.0), deps=deps, name="bz")
    deps2 = list(deps) + [tz]
    tb = [sch.dma("sync", bias_tiles[0:128, 0:1], d["L0_attn_a_perm_b"][0:128].unsqueeze(1), deps=deps2, slow=True),
          sch.dma("sync", bias_tiles[0:128, 1:2], d["L0_attn_a_perm_b"][128:256].unsqueeze(1), deps=deps2, slow=True),
          sch.dma("scalar", bias_tiles[0:32, 2:3], d["L0_attn_a_sign_b"][:].unsqueeze(1), deps=deps2, slow=True),
          sch.dma("scalar", bias_tiles[32:64, 2:3], d["L0_attn_a_gate_b"][:].unsqueeze(1), deps=deps2, slow=True),
          sch.dma("scalar", bias_tiles[64:96, 2:3], d["L0_attn_a_in_b"][:].unsqueeze(1), deps=deps2, slow=True)]
    tz3 = sch.emit("gpsimd", lambda E: E.memset(sb["a_const"][:, :], 0.0), deps=deps, name="cz")
    tca = sch.emit("vector", lambda E: E.tensor_tensor(sb["a_const"][:, 0:1],
                   psA[:, 0:1], bias_tiles[:, 0:1], ALU.add), deps=const_mm + tb + [tz3], name="ac0")
    tcb = sch.emit("vector", lambda E: E.tensor_tensor(sb["a_const"][:, 1:2],
                   psB[:, 0:1], bias_tiles[:, 1:2], ALU.add), deps=const_mm + tb + [tz3], name="ac1")
    tcc = sch.emit("vector", lambda E: E.tensor_tensor(sb["a_const"][0:32, 2:3],
                   ctx.ps["psC"][0:32, 0:1], bias_tiles[0:32, 2:3], ALU.add),
                   deps=const_mm + tb + [tz3], name="ac2")
    tcd = sch.emit("vector", lambda E: E.tensor_tensor(sb["a_const"][32:64, 2:3],
                   ctx.ps["psD"][32:64, 0:1], bias_tiles[32:64, 2:3], ALU.add),
                   deps=const_mm + tb + [tz3], name="ac3")
    tce = sch.emit("vector", lambda E: E.tensor_tensor(sb["a_const"][64:96, 2:3],
                   ctx.ps["psE"][64:96, 0:1], bias_tiles[64:96, 2:3], ALU.add),
                   deps=const_mm + tb + [tz3], name="ac4")
    tc = sch.emit("vector", lambda E: None, deps=[tca, tcb, tcc, tcd, tce], name="acj")
    tcx = tc
    onesb = sb["ones_col_b"]
    wtw_toks = []
    for dt in range(NDT):
        w1 = sch.emit("tensor", lambda E, dt=dt: E.matmul(
            psA[0:128, 2:3], sb["aw_perm"][:, dt * 256: dt * 256 + 128], onesb[:, 0:1],
            start=(dt == 0), stop=(dt == NDT - 1)), deps=prev_users + [tcx, ob1], name="wtw0")
        w2 = sch.emit("tensor", lambda E, dt=dt: E.matmul(
            psB[0:128, 2:3], sb["aw_perm"][:, dt * 256 + 128: dt * 256 + 256], onesb[:, 0:1],
            start=(dt == 0), stop=(dt == NDT - 1)), deps=prev_users + [tcx], name="wtw1")
        w3 = sch.emit("tensor", lambda E, dt=dt: E.matmul(
            ctx.ps["psC"][0:96, 2:3], sb["aw_sgi"][:, dt * 96: dt * 96 + 96], onesb[:, 0:1],
            start=(dt == 0), stop=(dt == NDT - 1)), deps=prev_users + [tcx], name="wtw2")
        wtw_toks = [w1, w2, w3]
    te1 = sch.emit("vector", lambda E: E.tensor_copy(sb["a_wtw"][:, 0:1], psA[:, 2:3]),
                   deps=wtw_toks, name="wtw_e1")
    te2 = sch.emit("vector", lambda E: E.tensor_copy(sb["a_wtw"][:, 1:2], psB[:, 2:3]),
                   deps=wtw_toks, name="wtw_e2")
    tz4 = sch.emit("gpsimd", lambda E: E.memset(sb["a_wtw"][:, 2:3], 0.0), deps=deps, name="wz")
    te3 = sch.emit("vector", lambda E: E.tensor_copy(sb["a_wtw"][0:96, 2:3],
                   ctx.ps["psC"][0:96, 2:3]), deps=wtw_toks + [tz4], name="wtw_e3")
    te = sch.emit("vector", lambda E: None, deps=[te1, te2, te3], name="wtw_j")
    tf = []
    for col in range(3):
        tf.append(sch.dma("gpsimd", sb["a_wtw_row"][0:1, col * 128:(col + 1) * 128],
                          sb["a_wtw"][0:128, col:col + 1], deps=[te], name="wtwf"))
    tg = sch.emit("vector", lambda E: E.tensor_copy(sb["a_wtw_rowb"][0:1, :],
                  sb["a_wtw_row"][0:1, :]), deps=tf, name="wtwb")
    tow1 = sch.dma("sync", wst[0:32, 0:512], d["L0_attn_a_out_W"][:, 0:512],
                   deps=prev_users + const_mm)
    tcw1 = sch.emit("vector", lambda E: E.tensor_copy(sb["aw_outb"][0:32, 0:512],
                    wst[0:32, 0:512]), deps=[tow1], name="owc1")
    tow2 = sch.dma("sync", wst[0:32, 0:512], d["L0_attn_a_out_W"][:, 512:1024],
                   deps=[tcw1])
    tcw2 = sch.emit("vector", lambda E: E.tensor_copy(sb["aw_outb"][0:32, 512:1024],
                    wst[0:32, 0:512]), deps=[tow2], name="owc2")
    tob = sch.dma("sync", sb["a_outb"][:, 0:8],
                  d["L0_attn_a_out_b"].rearrange("(a p) -> p a", p=128), deps=deps, slow=True)
    return [tc, tg, tcw2, tob]




def emit_stats(ctx, src_fn, deps, tag, negmu=False, sq_buf=None, mean_of_sq=False):
    nc, sch = ctx.nc, ctx.sch
    sb = ctx.sb
    psA, psB, psC, psD, psE = (ctx.ps[k] for k in ("psA", "psB", "psC", "psD", "psE"))
    ones = sb["ones_col"]
    sq = sq_buf if sq_buf is not None else sb["sq0"]
    mu, var = sb["mu_row"], sb["var_row"]
    mm_toks = []
    prev = list(deps)
    for dt in range(NDT):
        xap = src_fn(dt)
        tq = sch.emit("scalar", lambda E, xap=xap, sq=sq: E.activation(
            out=sq[:, :], in_=xap, func=AF.Square), deps=prev, name=f"{tag}_sq")
        m1 = m2 = None
        for c0 in range(0, S, 512):
            m1 = sch.emit("tensor", lambda E, c0=c0, dt=dt, xap=xap: E.matmul(
                psE[0:1, c0:c0 + 512], ones[:, 0:1], xap[:, c0:c0 + 512],
                start=(dt == 0), stop=(dt == NDT - 1)), deps=prev, name=f"{tag}_s")
            m2 = sch.emit("tensor", lambda E, c0=c0, dt=dt, sq=sq: E.matmul(
                [psA, psB, psC, psD][c0 // 512][0:1, 0:512], ones[:, 0:1],
                sq[:, c0:c0 + 512],
                start=(dt == 0), stop=(dt == NDT - 1)), deps=[tq], name=f"{tag}_q")
            if dt == NDT - 1:
                mm_toks += [m1, m2]
        prev = [tq, m1, m2]
    t1 = sch.emit("vector", lambda E: E.tensor_scalar(mu[0:1, :], psE[0:1, :],
                  1.0 / D, None, ALU.mult), deps=mm_toks, name=f"{tag}_mu")
    sqc = []
    for c0 in range(0, 2048, 512):
        sqc.append(sch.emit("vector", lambda E, c0=c0: E.tensor_copy(
            var[0:1, c0:c0 + 512], [psA, psB, psC, psD][c0 // 512][0:1, 0:512]),
            deps=mm_toks, name=f"{tag}_sc"))
    t2 = sch.emit("vector", lambda E: E.tensor_tensor(psE[0:1, :], mu[0:1, :],
                  mu[0:1, :], ALU.mult), deps=[t1], name=f"{tag}_m2")
    t3 = sch.emit("vector", lambda E: E.scalar_tensor_tensor(var[0:1, :], var[0:1, :],
                  1.0 / D, psE[0:1, :], ALU.mult, ALU.subtract),
                  deps=[t2] + sqc, name=f"{tag}_v")
    t4 = sch.emit("scalar", lambda E: E.activation(out=var[0:1, :], in_=var[0:1, :],
                  func=AF.Sqrt, bias=sb["eps_col"][0:1, 0:1]), deps=[t3], name=f"{tag}_st")
    t5 = sch.emit("vector", lambda E: E.reciprocal(var[0:1, :], var[0:1, :]), deps=[t4],
                  name=f"{tag}_rc")
    toks = [t5]
    if negmu:
        toks.append(sch.emit("vector", lambda E: E.tensor_scalar(mu[0:1, :], mu[0:1, :],
                    -1.0, None, ALU.mult), deps=[t1, t2], name=f"{tag}_nm"))
    return toks, t1


def emit_bcast(ctx, dst_sb, row_ap, deps, tag, via_row0=None):
    nc, sch = ctx.nc, ctx.sch
    sb = ctx.sb
    psE = ctx.ps["psE"]
    toks = []
    for c0 in range(0, 2048, 512):
        toks.append(sch.emit("tensor", lambda E, c0=c0: E.matmul(
            psE[:, c0:c0 + 512], sb["ones_row"][0:1, :], row_ap[0:1, c0:c0 + 512],
            start=True, stop=True), deps=deps, name=f"{tag}_b"))
    ev = sch.emit("vector", lambda E: E.tensor_copy(dst_sb[:, 0:2048], psE[:, :]),
                  deps=toks, name=f"{tag}_e")
    return [ev]



def alloc_attn_a_body(ctx):
    ctx.scr("aw_perm", (128, NDT * 256), BF16)
    ctx.scr("aw_sgi", (128, NDT * 96), BF16)
    ctx.scr("aw_outb", (32, 1024), BF16)
    ctx.scr("a_outb", (128, 8), F32)
    ctx.scr("lnw_col", (128, NDT), F32)
    ctx.scr("lnb_col", (128, NDT), F32)
    ctx.scr("ones_col_b", (128, 1), BF16)
    ctx.scr("poolA", (128, 4096), F32)
    ctx.scr("poolC", (128, 4096), F32)
    ctx.scr("xn1d", (128, 2048), F32)
    ctx.scr("xn1bt", (128, 2048), BF16)
    ctx.scr("xtile", (128, 2048), F32)
    ctx.scr("pl_mu", (128, 2048), F32)
    ctx.scr("pl_r", (128, 2048), F32)
    ctx.scr("perm_fm", (128, 2 * 2048), F32)
    ctx.scr("sgi_fm", (96, 2048), F32)
    ctx.scr("perm_tm", (128, 16 * 256), F32)
    ctx.scr("lse_s", (128, 512), F32)
    ctx.scr("grd_rank", (128, 512), F32)
    ctx.scr("grd_sel", (128, 512), F32)
    ctx.scr("grd_rv", (128, 512), F32)
    ctx.scr("grd_oh", (128, 512), F32)
    ctx.scr("grd_cm", (128, 64), F32)
    ctx.scr("grd_tk", (128, 512), F32)
    ctx.scr("ss_P", (128, 64 * 64), BF16)
    ctx.scr("ss_G", (128, 64 * 24), F32)
    ctx.scr("scanT_P", (128, 128 * 64), BF16)
    ctx.scr("scanT_G", (128, 128 * 24), F32)
    ctx.scr("sc_p32", (128, 64), F32)
    ctx.scr("scM", (128, 64), F32)
    ctx.scr("scV", (128, 8), F32)
    ctx.scr("sc_gs", (128, 8), F32)
    ctx.scr("sc_c", (128, 8), F32)
    ctx.scr("sc_om", (128, 8), F32)
    ctx.scr("sc_t1", (128, 512), F32)
    ctx.scr("sc_pm", (128, 64), F32)
    ctx.scr("sc_t3", (128, 64), F32)
    ctx.scr("sc_pv", (128, 8), F32)
    ctx.scr("sc_pv2", (128, 8), F32)
    ctx.scr("summ", (128, 72), F32)
    ctx.scr("pf", (8, 16 * 72), F32)
    ctx.scr("pf_h", (8, 8), F32)
    ctx.scr("pf_h2", (8, 8), F32)
    ctx.scr("pf_t", (8, 64), F32)
    ctx.scr("hstarts", (8, 16 * 8), F32)
    ctx.scr("hst", (128, 8), F32)
    ctx.scr("ys", (128, 128 * 8), F32)
    ctx.scr("ys_b", (128, 128 * 8), BF16)
    ctx.scr("ys_r", (32, 4096), BF16)
    ctx.scr("ident8rep", (128, 64), F32)
    ctx.scr("a_bias", (128, 4), F32)
    ctx.scr("a_const", (128, 4), F32)
    ctx.scr("a_wtw", (128, 4), F32)
    ctx.scr("a_wtw_row", (1, 384), F32)
    ctx.scr("a_wtw_rowb", (1, 384), BF16)
    ctx.sb["wstage"] = ctx.sb["poolA"][:, 0:512]
    ctx.sb["sq0"] = ctx.sb["poolA"][:, 0:2048]
    ctx.sb["sq1"] = ctx.sb["poolA"][:, 2048:4096]
    ctx.sb["grd_m"] = ctx.sb["lse_s"]
    ctx.sb["negmu_b"] = ctx.sb["ys_r"][0:1, 0:2048]
    ctx.sb["evtmp"] = ctx.sb["perm_fm"][:, 0:1024]


def emit_attn_a(ctx, x_fm, x1_fm, xn1b_dram, deps, dbg=None):
    nc, sch = ctx.nc, ctx.sch
    psA, psB, psC, psD, psE = (ctx.ps[k] for k in ("psA", "psB", "psC", "psD", "psE"))
    sb = ctx.sb
    ident = sb["ident128"]
    qs = ["sync", "scalar", "gpsimd"]

    sch.set_phase("aa")
    wt = emit_attn_a_weights(ctx, deps)
    t_i8 = sch.dma("gpsimd", sb["ident8rep"][:, :], ctx.dram["c_ident8rep"][:], deps=deps)
    sch.stage()

    for b in range(B):
        col0 = b * S
        xt = sb["xtile"]
        sq = sb["sq0"]
        mm_toks = []
        prev = []
        for dt in range(NDT):
            tl = sch.dma("sync", xt[:, :], x_fm[dt * 128:(dt + 1) * 128, col0:col0 + S],
                         deps=prev, name="x_ld")
            tq = sch.emit("scalar", lambda E: E.activation(
                out=sq[:, :], in_=xt[:, :], func=AF.Square), deps=[tl] + prev, name="sq")
            m1 = None
            m2 = None
            for c0 in range(0, S, 512):
                m1 = sch.emit("tensor", lambda E, c0=c0, dt=dt: E.matmul(
                    psE[0:1, c0:c0 + 512], sb["ones_col"][:, 0:1], xt[:, c0:c0 + 512],
                    start=(dt == 0), stop=(dt == NDT - 1)), deps=[tl], name="lnS")
                m2 = sch.emit("tensor", lambda E, c0=c0, dt=dt: E.matmul(
                    [psA, psB, psC, psD][c0 // 512][0:1, 0:512], sb["ones_col"][:, 0:1],
                    sq[:, c0:c0 + 512],
                    start=(dt == 0), stop=(dt == NDT - 1)), deps=[tq], name="lnQ")
                if dt == NDT - 1:
                    mm_toks += [m1, m2]
            prev = [tq, m1, m2]
        mu, var, r = sb["mu_row"], sb["var_row"], sb["r_row"]
        t1 = sch.emit("vector", lambda E: E.tensor_scalar(mu[0:1, :], psE[0:1, :],
                      1.0 / D, None, ALU.mult), deps=mm_toks, name="mu")
        sqc = []
        for c0 in range(0, 2048, 512):
            sqc.append(sch.emit("vector", lambda E, c0=c0: E.tensor_copy(
                var[0:1, c0:c0 + 512], [psA, psB, psC, psD][c0 // 512][0:1, 0:512]),
                deps=mm_toks, name="sqc"))
        t2 = sch.emit("vector", lambda E: E.tensor_tensor(psE[0:1, :], mu[0:1, :],
                      mu[0:1, :], ALU.mult), deps=[t1], name="mu2")
        t3 = sch.emit("vector", lambda E: E.scalar_tensor_tensor(var[0:1, :], var[0:1, :],
                      1.0 / D, psE[0:1, :], ALU.mult, ALU.subtract),
                      deps=[t2] + sqc, name="var")
        t4 = sch.emit("scalar", lambda E: E.activation(out=r[0:1, :], in_=var[0:1, :],
                      func=AF.Sqrt, bias=sb["eps_col"][0:1, 0:1]), deps=[t3], name="sr")
        t5 = sch.emit("vector", lambda E: E.reciprocal(r[0:1, :], r[0:1, :]), deps=[t4],
                      name="rc")
        bc1 = []
        for c0 in range(0, 2048, 512):
            bc1.append(sch.emit("tensor", lambda E, c0=c0: E.matmul(
                psE[:, c0:c0 + 512], sb["ones_row"][0:1, :], mu[0:1, c0:c0 + 512],
                start=True, stop=True), deps=[t3, t1], name="bmu"))
        ev1 = sch.emit("vector", lambda E: E.tensor_copy(sb["pl_mu"][:, :], psE[:, :]),
                       deps=bc1, name="bmu_ev")
        t5b = sch.emit("vector", lambda E: E.tensor_copy(mu[0:1, :], r[0:1, :]),
                       deps=[t5] + bc1, name="r2row0")
        bc2 = []
        for c0 in range(0, 2048, 512):
            bc2.append(sch.emit("tensor", lambda E, c0=c0: E.matmul(
                psE[:, c0:c0 + 512], sb["ones_row"][0:1, :], mu[0:1, c0:c0 + 512],
                start=True, stop=True), deps=[t5b, ev1], name="br"))
        ev2 = sch.emit("vector", lambda E: E.tensor_copy(sb["pl_r"][:, :], psE[:, :]),
                       deps=bc2, name="br_ev")
        sch.stage()
        xn1d = sb["xn1d"]
        prev = []
        mm_toks = []
        for dt in range(NDT):
            tl = sch.dma("sync", xt[:, :], x_fm[dt * 128:(dt + 1) * 128, col0:col0 + S],
                         deps=prev, name="x_ld2")
            a1 = sch.emit("vector", lambda E: E.tensor_tensor(
                xn1d[:, :], xt[:, :], sb["pl_mu"][:, :], ALU.subtract),
                deps=[tl] + prev, name="xsub")
            a2 = sch.emit("vector", lambda E: E.tensor_tensor(
                xn1d[:, :], xn1d[:, :], sb["pl_r"][:, :], ALU.mult), deps=[a1], name="xmul")
            a3 = sch.emit("vector", lambda E: E.tensor_tensor(
                xt[:, :], xt[:, :], xn1d[:, :], ALU.add), deps=[a2], name="xadd")
            st = sch.dma("scalar", x1_fm[dt * 128:(dt + 1) * 128, col0:col0 + S],
                         xt[:, :], deps=[a3], name="x1_st")
            tq = sch.emit("scalar", lambda E: E.activation(
                out=sq[:, :], in_=xn1d[:, :], func=AF.Square), deps=[a2] + prev, name="sq2")
            m1 = None
            m2 = None
            for c0 in range(0, S, 512):
                m1 = sch.emit("tensor", lambda E, c0=c0, dt=dt: E.matmul(
                    psE[0:1, c0:c0 + 512], sb["ones_col"][:, 0:1], xn1d[:, c0:c0 + 512],
                    start=(dt == 0), stop=(dt == NDT - 1)), deps=[a2], name="ln2S")
                m2 = sch.emit("tensor", lambda E, c0=c0, dt=dt: E.matmul(
                    [psA, psB, psC, psD][c0 // 512][0:1, 0:512], sb["ones_col"][:, 0:1],
                    sq[:, c0:c0 + 512],
                    start=(dt == 0), stop=(dt == NDT - 1)), deps=[tq], name="ln2Q")
                if dt == NDT - 1:
                    mm_toks += [m1, m2]
            cst = sch.emit("vector", lambda E: E.tensor_copy(sb["xn1bt"][:, :],
                           xn1d[:, :]), deps=[a2], name="xnbc")
            sd = sch.dma("gpsimd", xn1b_dram[dt * 128:(dt + 1) * 128, col0:col0 + S],
                         sb["xn1bt"][:, :], deps=[cst], name="xnb_sd")
            prev = [st, tq, m1, m2, cst, sd]
        t1 = sch.emit("vector", lambda E: E.tensor_scalar(mu[0:1, :], psE[0:1, :],
                      1.0 / D, None, ALU.mult), deps=mm_toks, name="mu_2")
        sqc = []
        for c0 in range(0, 2048, 512):
            sqc.append(sch.emit("vector", lambda E, c0=c0: E.tensor_copy(
                var[0:1, c0:c0 + 512], [psA, psB, psC, psD][c0 // 512][0:1, 0:512]),
                deps=mm_toks, name="sqc2"))
        t2 = sch.emit("vector", lambda E: E.tensor_tensor(psE[0:1, :], mu[0:1, :],
                      mu[0:1, :], ALU.mult), deps=[t1], name="mu2_2")
        t3 = sch.emit("vector", lambda E: E.scalar_tensor_tensor(var[0:1, :], var[0:1, :],
                      1.0 / D, psE[0:1, :], ALU.mult, ALU.subtract),
                      deps=[t2] + sqc, name="var_2")
        t4 = sch.emit("scalar", lambda E: E.activation(out=r[0:1, :], in_=var[0:1, :],
                      func=AF.Sqrt, bias=sb["eps_col"][0:1, 0:1]), deps=[t3], name="sr_2")
        t5 = sch.emit("vector", lambda E: E.reciprocal(r[0:1, :], r[0:1, :]), deps=[t4],
                      name="rc_2")
        t6 = sch.emit("vector", lambda E: E.tensor_scalar(mu[0:1, :],
                      mu[0:1, :], -1.0, None, ALU.mult), deps=[t3], name="negmu")
        cnm = sch.emit("vector", lambda E: E.tensor_copy(sb["negmu_b"][0:1, :],
                       mu[0:1, :]), deps=[t6], name="negmu_b")
        t6b = sch.emit("vector", lambda E: E.tensor_copy(mu[0:1, :], r[0:1, :]),
                       deps=[t5, cnm], name="r2row0b")
        bc3 = []
        for c0 in range(0, 2048, 512):
            bc3.append(sch.emit("tensor", lambda E, c0=c0: E.matmul(
                psE[:, c0:c0 + 512], sb["ones_row"][0:1, :], mu[0:1, c0:c0 + 512],
                start=True, stop=True), deps=[t6b], name="br2"))
        ev3 = sch.emit("vector", lambda E: E.tensor_copy(sb["pl_r"][:, :], psE[:, :]),
                       deps=bc3, name="br2_ev")
        cwt = sch.emit("vector", lambda E: E.tensor_copy(sb["a_wtw_rowb"][0:1, :],
                       sb["a_wtw_row"][0:1, :]), deps=wt, name="wtw_b")
        sch.stage()
        pss = [psA, psB, psC]
        ld_prev = [None, None]
        ev_prev = []
        for tc in range(4):
            mm_last = [None, None, None]
            for dt in range(NDT):
                k = dt % 2
                xb = sb["xn1bt"][:, k * 512:(k + 1) * 512]
                tl = sch.dma("sync", xb,
                             xn1b_dram[dt * 128:(dt + 1) * 128, col0 + tc * 512: col0 + tc * 512 + 512],
                             deps=[ld_prev[k]] + ([sd] if dt == 0 and tc == 0 else []),
                             name="xnb_ld")
                for mt in range(3):
                    nrow = 128 if mt < 2 else 96
                    if mt == 0:
                        lh = sb["aw_perm"][:, dt * 256: dt * 256 + 128]
                    elif mt == 1:
                        lh = sb["aw_perm"][:, dt * 256 + 128: dt * 256 + 256]
                    else:
                        lh = sb["aw_sgi"][:, dt * 96: dt * 96 + 96]
                    mm_last[mt] = sch.emit("tensor", lambda E, lh=lh, dt=dt, mt=mt, nrow=nrow, xb=xb: E.matmul(
                        pss[mt][0:nrow, 0:512], lh, xb,
                        start=(dt == 0), stop=False, skip_group_check=True),
                        deps=[tl] + ev_prev, name=f"pj{mt}")
                ld_prev[k] = mm_last[2]
            ev_prev = []
            for mt in range(3):
                nrow = 128 if mt < 2 else 96
                wr = sb["a_wtw_rowb"][0:1, mt * 128: mt * 128 + nrow]
                mmc = sch.emit("tensor", lambda E, wr=wr, tc=tc, mt=mt, nrow=nrow: E.matmul(
                    pss[mt][0:nrow, 0:512], wr, sb["negmu_b"][0:1, tc * 512: tc * 512 + 512],
                    start=False, stop=True, skip_group_check=True),
                    deps=[mm_last[mt], cnm, cwt], name="pjc")
                if mt < 2:
                    dst = sb["perm_fm"][:, mt * 2048 + tc * 512: mt * 2048 + tc * 512 + 512]
                else:
                    dst = sb["sgi_fm"][0:96, tc * 512: tc * 512 + 512]
                e1 = sch.emit("vector", lambda E, mt=mt, dst=dst, nrow=nrow, tc=tc: E.tensor_tensor(
                    dst, pss[mt][0:nrow, 0:512], sb["pl_r"][0:nrow, tc * 512: tc * 512 + 512],
                    ALU.mult), deps=[mmc, ev3], name="pj_e1")
                e2 = sch.emit("vector", lambda E, dst=dst, nrow=nrow, mt=mt: E.tensor_scalar(
                    dst, dst, sb["a_const"][0:nrow, mt:mt + 1], None, ALU.add),
                    deps=[e1], name="pj_e2")
                ev_prev.append(e2)
        sch.stage()
        tr_prev = {0: None, 1: None}
        for ft in range(2):
            for tt in range(16):
                k = (ft * 16 + tt) % 2
                ps = [psC, psD][k]
                tm = sch.emit("tensor", lambda E, ft=ft, tt=tt, ps=ps: E.transpose(
                    ps[:, 0:128], sb["perm_fm"][:, ft * 2048 + tt * 128: ft * 2048 + tt * 128 + 128],
                    ident[:, :]), deps=[tr_prev[k]], name="ptr")
                ev = sch.emit("vector", lambda E, ft=ft, tt=tt, ps=ps: E.tensor_copy(
                    sb["perm_tm"][:, tt * 256 + ft * 128: tt * 256 + ft * 128 + 128],
                    ps[:, 0:128]), deps=[tm], name="ptr_ev")
                tr_prev[k] = ev
        for tt in range(16):
            k = tt % 2
            ps = [psC, psD][k]
            tm = sch.emit("tensor", lambda E, tt=tt, ps=ps: E.transpose(
                ps[:, 0:96], sb["sgi_fm"][0:96, tt * 128: tt * 128 + 128], ident[0:96, 0:96]),
                deps=[tr_prev[k]], name="str")
            ssg4 = sb["ss_G"][:, :].rearrange("p (h tt c) -> p h tt c", h=4, tt=16, c=24)
            e1 = sch.emit("scalar", lambda E, tt=tt, ps=ps, ssg4=ssg4: E.activation(
                out=ssg4[:, :, tt, 0:8], in_=ps[:, 0:32].rearrange("p (h n) -> p h n", n=8),
                func=AF.Tanh, scale=0.5), deps=[tm], name="s_tanh")
            e2 = sch.emit("scalar", lambda E, tt=tt, ps=ps, ssg4=ssg4: E.activation(
                out=ssg4[:, :, tt, 8:16], in_=ps[:, 32:64].rearrange("p (h n) -> p h n", n=8),
                func=AF.Sigmoid), deps=[e1, tm], name="g_sig")
            e3 = sch.emit("vector", lambda E, tt=tt, ps=ps, ssg4=ssg4: E.tensor_copy(
                ssg4[:, :, tt, 16:24], ps[:, 64:96].rearrange("p (h n) -> p h n", n=8)),
                deps=[tm], name="bI_cp")
            tr_prev[k] = sch.emit("vector", lambda E: None, deps=[e2, e3], name="join")
        sch.stage()
        la = sb["perm_tm"]
        ebuf = sb["poolA"]
        x8 = None
        for it in range(5):
            x1 = sch.emit("scalar", lambda E: E.activation(out=ebuf[:, :], in_=la[:, :],
                          func=AF.Exp), deps=[x8], name="sk_e")
            x2 = sch.emit("vector", lambda E: E.tensor_reduce(
                sb["lse_s"][:, :], ebuf[:, :].rearrange("p (a j) -> p a j", j=8),
                axis=AX.X, op=ALU.add), deps=[x1], name="sk_rs")
            x3 = sch.emit("scalar", lambda E: E.activation(out=sb["lse_s"][:, :],
                          in_=sb["lse_s"][:, :], func=AF.Ln), deps=[x2], name="sk_ln")
            x4 = sch.emit("vector", lambda E: E.tensor_tensor(
                la[:, :].rearrange("p (a j) -> p a j", j=8),
                la[:, :].rearrange("p (a j) -> p a j", j=8),
                sb["lse_s"][:, :].to_broadcast((128, 512, 8)), ALU.subtract),
                deps=[x3], name="sk_sub")
            x5 = sch.emit("scalar", lambda E: E.activation(out=ebuf[:, :], in_=la[:, :],
                          func=AF.Exp), deps=[x4], name="sk_e2")
            x6 = sch.emit("vector", lambda E: E.tensor_reduce(
                sb["lse_s"][:, :], ebuf[:, :].rearrange("p (g i j) -> p g j i", i=8, j=8),
                axis=AX.X, op=ALU.add), deps=[x5], name="sk_cs")
            x7 = sch.emit("scalar", lambda E: E.activation(out=sb["lse_s"][:, :],
                          in_=sb["lse_s"][:, :], func=AF.Ln), deps=[x6], name="sk_ln2")
            x8 = sch.emit("vector", lambda E: E.tensor_tensor(
                la[:, :].rearrange("p (g i j) -> p g j i", i=8, j=8),
                la[:, :].rearrange("p (g i j) -> p g j i", i=8, j=8),
                sb["lse_s"][:, :].to_broadcast((128, 512, 8)), ALU.subtract),
                deps=[x7], name="sk_sub2")
        soft = sb["perm_tm"]
        x9 = sch.emit("scalar", lambda E: E.activation(out=soft[:, :], in_=soft[:, :],
                      func=AF.Exp), deps=[x8], name="sk_soft")
        sch.stage()
        m = sb["grd_m"]
        g1 = sch.emit("vector", lambda E: E.tensor_reduce(
            m[:, :], soft[:, :].rearrange("p (a j) -> p a j", j=8), axis=AX.X,
            op=ALU.max), deps=[], name="g_max")
        mv_r = m[:, :].rearrange("p (g r) -> p g r", r=8).to_broadcast((128, 64, 8, 8))
        e1 = sch.emit("vector", lambda E: E.tensor_tensor(
            sb["poolA"][:, :].rearrange("p (g r q) -> p g r q", r=8, q=8),
            mv_r.transpose((0, 1, 3, 2)), mv_r, ALU.is_gt), deps=[g1], name="g_gt")
        e2 = sch.emit("vector", lambda E: E.tensor_tensor(
            sb["poolC"][:, :].rearrange("p (g r q) -> p g r q", r=8, q=8),
            mv_r.transpose((0, 1, 3, 2)), mv_r, ALU.is_equal), deps=[e1], name="g_eq")
        lt_v = sb["lt8"][:, :].rearrange("p (r q) -> p r q", q=8).to_broadcast(
            (128, 8, 8, 64)).transpose((0, 3, 1, 2))
        e3 = sch.emit("vector", lambda E: E.tensor_tensor(
            sb["poolC"][:, :].rearrange("p (g r q) -> p g r q", r=8, q=8),
            sb["poolC"][:, :].rearrange("p (g r q) -> p g r q", r=8, q=8),
            lt_v, ALU.mult), deps=[e2], name="g_lt")
        e4 = sch.emit("vector", lambda E: E.tensor_tensor(
            sb["poolA"][:, :], sb["poolA"][:, :], sb["poolC"][:, :], ALU.add),
            deps=[e3], name="g_add")
        e5 = sch.emit("vector", lambda E: E.tensor_reduce(
            sb["grd_rank"][:, :], sb["poolA"][:, :].rearrange("p (a q) -> p a q", q=8),
            axis=AX.X, op=ALU.add), deps=[e4], name="g_rank")
        z1 = sch.emit("gpsimd", lambda E: E.memset(sb["grd_tk"][:, :], 0.0), deps=[],
                      name="g_tk0")
        hard = sb["poolC"]
        z2 = sch.emit("gpsimd", lambda E: E.memset(hard[:, :], 0.0), deps=[e3, e4],
                      name="g_h0")
        prev = [e5, z1, z2]
        for step in range(8):
            s1 = sch.emit("vector", lambda E, step=step: E.tensor_scalar(
                sb["grd_sel"][:, :], sb["grd_rank"][:, :], float(step), None,
                ALU.is_equal), deps=prev, name="g_sel")
            selv = sb["grd_sel"][:, :].rearrange("p (g r) -> p g r", r=8).to_broadcast(
                (128, 64, 8, 8))
            s2 = sch.emit("vector", lambda E, selv=selv: E.tensor_tensor(
                sb["poolA"][:, :].rearrange("p (g j r) -> p g j r", j=8, r=8),
                soft[:, :].rearrange("p (g r j) -> p g j r", r=8, j=8),
                selv.transpose((0, 1, 3, 2)), ALU.mult), deps=[s1], name="g_rv1")
            s3 = sch.emit("vector", lambda E: E.tensor_reduce(
                sb["grd_rv"][:, :], sb["poolA"][:, :].rearrange("p (a r) -> p a r", r=8),
                axis=AX.X, op=ALU.add), deps=[s2], name="g_rv2")
            s4 = sch.emit("vector", lambda E: E.scalar_tensor_tensor(
                sb["grd_rv"][:, :], sb["grd_tk"][:, :], -1e30, sb["grd_rv"][:, :],
                ALU.mult, ALU.add), deps=[s3], name="g_sc")
            s5 = sch.emit("vector", lambda E: E.tensor_reduce(
                sb["grd_cm"][:, :], sb["grd_rv"][:, :].rearrange("p (g j) -> p g j", j=8),
                axis=AX.X, op=ALU.max), deps=[s4], name="g_cm")
            s6 = sch.emit("vector", lambda E: E.tensor_tensor(
                sb["grd_oh"][:, :].rearrange("p (g j) -> p g j", j=8),
                sb["grd_rv"][:, :].rearrange("p (g j) -> p g j", j=8),
                sb["grd_cm"][:, :].to_broadcast((128, 64, 8)), ALU.is_equal),
                deps=[s5], name="g_oh")
            s7 = sch.emit("vector", lambda E: E.tensor_tensor(
                sb["grd_tk"][:, :], sb["grd_tk"][:, :], sb["grd_oh"][:, :], ALU.add),
                deps=[s6], name="g_tk")
            ohv = sb["grd_oh"][:, :].rearrange("p (g j) -> p g j", j=8).to_broadcast(
                (128, 64, 8, 8)).transpose((0, 1, 3, 2))
            s8 = sch.emit("vector", lambda E, selv=selv, ohv=ohv: E.tensor_tensor(
                sb["poolA"][:, :].rearrange("p (g r j) -> p g r j", r=8, j=8),
                selv, ohv, ALU.mult), deps=[s6], name="g_hu1")
            s9 = sch.emit("vector", lambda E: E.tensor_tensor(
                hard[:, :], hard[:, :], sb["poolA"][:, :], ALU.add), deps=[s8], name="g_hu2")
            prev = [s7, s9]
        sch.stage()
        c1 = sch.emit("vector", lambda E: E.tensor_copy(
            sb["ss_P"][:, :].rearrange("p (h tt c) -> p h tt c", h=4, tt=16, c=64),
            hard[:, :].rearrange("p (tt h c) -> p h tt c", tt=16, h=4, c=64)),
            deps=[], name="ss_h")
        for g2 in range(64):
            sp = b * 64 + g2
            sch.dma(qs[g2 % 3], sb["scanT_P"][sp:sp + 1, :],
                    sb["ss_P"][:, g2 * 64:(g2 + 1) * 64], deps=[c1], name="scTP")
            sch.dma(qs[(g2 + 1) % 3], sb["scanT_G"][sp:sp + 1, :],
                    sb["ss_G"][:, g2 * 24:(g2 + 1) * 24], deps=[], name="scTG")
        sch.stage()
    sb_ = sb
    i1 = sch.emit("vector", lambda E: E.tensor_copy(sb_["scM"][:, :], sb_["ident8rep"][:, :]),
                  deps=[], name="scM0")
    i2 = sch.emit("gpsimd", lambda E: E.memset(sb_["scV"][:, :], 0.0), deps=[], name="scV0")
    prev = [i1, i2]
    for tau in range(128):
        gbase = tau * 24
        cp = sch.emit("vector", lambda E, tau=tau: E.tensor_copy(
            sb_["sc_p32"][:, :], sb_["scanT_P"][:, tau * 64:(tau + 1) * 64]),
            deps=prev, name="p32")
        P_ij = sb_["sc_p32"][:, :].rearrange("p (i j) -> p i j", j=8)
        gsl = sb_["sc_gs"][:, :]
        csl = sb_["sc_c"][:, :]
        u1 = sch.emit("vector", lambda E, gbase=gbase, gsl=gsl: E.tensor_tensor(
            gsl, sb_["scanT_G"][:, gbase + 8:gbase + 16], sb_["scanT_G"][:, gbase:gbase + 8],
            ALU.mult), deps=prev, name="gs")
        u2 = sch.emit("vector", lambda E, gbase=gbase: E.tensor_scalar(
            sb_["sc_om"][:, :], sb_["scanT_G"][:, gbase + 8:gbase + 16], -1.0, 1.0,
            ALU.mult, ALU.add), deps=prev, name="om")
        u3 = sch.emit("vector", lambda E, gbase=gbase, csl=csl: E.tensor_tensor(
            csl, sb_["sc_om"][:, :], sb_["scanT_G"][:, gbase + 16:gbase + 24], ALU.mult),
            deps=[u2], name="cc")
        Pb = P_ij.to_broadcast((128, 8, 8, 8)).transpose((0, 1, 3, 2))
        Mb = sb_["scM"][:, :].rearrange("p (j k) -> p j k", k=8).to_broadcast(
            (128, 8, 8, 8)).transpose((0, 3, 2, 1))
        u4 = sch.emit("vector", lambda E, Pb=Pb, Mb=Mb: E.tensor_tensor(
            sb_["sc_t1"][:, :].rearrange("p (i k j) -> p i k j", i=8, k=8, j=8),
            Pb, Mb, ALU.mult), deps=[cp] + prev, name="pm1")
        u5 = sch.emit("vector", lambda E: E.tensor_reduce(
            sb_["sc_pm"][:, :], sb_["sc_t1"][:, :].rearrange("p (a j) -> p a j", j=8),
            axis=AX.X, op=ALU.add), deps=[u4], name="pm2")
        u6 = sch.emit("vector", lambda E, gsl=gsl: E.tensor_tensor(
            sb_["scM"][:, :].rearrange("p (i k) -> p i k", k=8),
            sb_["sc_pm"][:, :].rearrange("p (i k) -> p i k", k=8),
            gsl.to_broadcast((128, 8, 8)), ALU.mult), deps=[u5, u1], name="pm3")
        vb = sb_["scV"][:, :].to_broadcast((128, 8, 8)).transpose((0, 2, 1))
        u7 = sch.emit("vector", lambda E, P_ij=P_ij, vb=vb: E.tensor_tensor(
            sb_["sc_t3"][:, :].rearrange("p (i j) -> p i j", j=8), P_ij, vb, ALU.mult),
            deps=[cp] + prev, name="pv1")
        u8 = sch.emit("vector", lambda E: E.tensor_reduce(
            sb_["sc_pv"][:, :], sb_["sc_t3"][:, :].rearrange("p (i j) -> p i j", j=8),
            axis=AX.X, op=ALU.add), deps=[u7], name="pv2")
        u9 = sch.emit("vector", lambda E, gsl=gsl: E.tensor_tensor(
            sb_["sc_pv2"][:, :], sb_["sc_pv"][:, :], gsl, ALU.mult), deps=[u8, u1],
            name="pv3")
        u10 = sch.emit("vector", lambda E, csl=csl: E.tensor_tensor(
            sb_["scV"][:, :], sb_["sc_pv2"][:, :], csl, ALU.add), deps=[u9, u3, u7],
            name="pv4")
        prev = [u6, u10]
    f1 = sch.emit("vector", lambda E: E.tensor_copy(sb_["summ"][:, 0:64], sb_["scM"][:, :]),
                  deps=prev, name="sm")
    f2 = sch.emit("vector", lambda E: E.tensor_copy(sb_["summ"][:, 64:72], sb_["scV"][:, :]),
                  deps=prev, name="sv")
    sch.stage()
    qs = ["sync", "scalar", "gpsimd"]
    dts = []
    for b2 in range(2):
        for h in range(4):
            base_p = b2 * 64 + h * 16
            dts.append(sch.dma(qs[(b2 * 4 + h) % 3], sb_["pf"][b2 * 4 + h:b2 * 4 + h + 1, :],
                               sb_["summ"][base_p:base_p + 16, :], deps=[f1, f2], name="pfd"))
    z3 = sch.emit("gpsimd", lambda E: E.memset(sb_["pf_h"][:, :], 0.0), deps=[], name="pfh0")
    prev = dts + [z3]
    for c in range(16):
        w1 = sch.emit("vector", lambda E, c=c: E.tensor_copy(
            sb_["hstarts"][:, c * 8:(c + 1) * 8], sb_["pf_h"][:, :]), deps=prev, name="hs_cp")
        Mc = sb_["pf"][:, c * 72:c * 72 + 64].rearrange("p (i k) -> p i k", k=8)
        hb = sb_["pf_h"][:, :].to_broadcast((8, 8, 8)).transpose((0, 2, 1))
        w2 = sch.emit("vector", lambda E, Mc=Mc, hb=hb: E.tensor_tensor(
            sb_["pf_t"][:, :].rearrange("p (i k) -> p i k", k=8), Mc, hb, ALU.mult),
            deps=prev, name="hs_m")
        w3 = sch.emit("vector", lambda E: E.tensor_reduce(
            sb_["pf_h2"][:, :], sb_["pf_t"][:, :].rearrange("p (i k) -> p i k", k=8),
            axis=AX.X, op=ALU.add), deps=[w2], name="hs_r")
        w4 = sch.emit("vector", lambda E, c=c: E.tensor_tensor(
            sb_["pf_h"][:, :], sb_["pf_h2"][:, :], sb_["pf"][:, c * 72 + 64:c * 72 + 72],
            ALU.add), deps=[w3, w1], name="hs_a")
        prev = [w4]
    dts2 = []
    for b2 in range(2):
        for h in range(4):
            base_p = b2 * 64 + h * 16
            dts2.append(sch.dma(qs[(b2 * 4 + h) % 3], sb_["hst"][base_p:base_p + 16, :],
                                sb_["hstarts"][b2 * 4 + h:b2 * 4 + h + 1, :],
                                deps=prev, name="hsd"))
    sch.stage()
    prev = list(dts2)
    hprev_ap = sb_["hst"][:, :]
    for tau in range(128):
        cp = sch.emit("vector", lambda E, tau=tau: E.tensor_copy(
            sb_["sc_p32"][:, :], sb_["scanT_P"][:, tau * 64:(tau + 1) * 64]),
            deps=prev, name="p32b")
        P_ij = sb_["sc_p32"][:, :].rearrange("p (i j) -> p i j", j=8)
        gbase = tau * 24
        g1b = sch.emit("vector", lambda E, gbase=gbase: E.tensor_tensor(
            sb_["sc_gs"][:, :], sb_["scanT_G"][:, gbase + 8:gbase + 16],
            sb_["scanT_G"][:, gbase:gbase + 8], ALU.mult), deps=prev, name="gs2")
        g2b = sch.emit("vector", lambda E, gbase=gbase: E.tensor_scalar(
            sb_["sc_om"][:, :], sb_["scanT_G"][:, gbase + 8:gbase + 16], -1.0, 1.0,
            ALU.mult, ALU.add), deps=prev, name="om2")
        g3b = sch.emit("vector", lambda E, gbase=gbase: E.tensor_tensor(
            sb_["sc_c"][:, :], sb_["sc_om"][:, :], sb_["scanT_G"][:, gbase + 16:gbase + 24],
            ALU.mult), deps=[g2b], name="cc2")
        hb = hprev_ap.to_broadcast((128, 8, 8)).transpose((0, 2, 1))
        v1 = sch.emit("vector", lambda E, P_ij=P_ij, hb=hb: E.tensor_tensor(
            sb_["sc_t3"][:, :].rearrange("p (i j) -> p i j", j=8), P_ij, hb, ALU.mult),
            deps=[cp] + prev, name="ap1")
        v2 = sch.emit("vector", lambda E: E.tensor_reduce(
            sb_["sc_pv"][:, :], sb_["sc_t3"][:, :].rearrange("p (i j) -> p i j", j=8),
            axis=AX.X, op=ALU.add), deps=[v1], name="ap2")
        v3 = sch.emit("vector", lambda E: E.tensor_tensor(
            sb_["sc_pv2"][:, :], sb_["sc_pv"][:, :], sb_["sc_gs"][:, :],
            ALU.mult), deps=[v2, g1b], name="ap3")
        yslot = sb_["ys"][:, :].rearrange("p (n tau) -> p n tau", n=8)[:, :, tau]
        v4 = sch.emit("vector", lambda E, yslot=yslot: E.tensor_tensor(
            yslot, sb_["sc_pv2"][:, :],
            sb_["sc_c"][:, :], ALU.add), deps=[v3, g3b], name="ap4")
        hprev_ap = yslot
        prev = [v4]
    cb = sch.emit("vector", lambda E: E.tensor_copy(sb_["ys_b"][:, :], sb_["ys"][:, :]),
                  deps=prev, name="ysb")
    sch.stage()
    dts3 = []
    for b2 in range(2):
        for h in range(4):
            for tt in range(16):
                sp = b2 * 64 + h * 16 + tt
                src2 = sb_["ys_b"][sp:sp + 1, :].rearrange("p (n tau) -> p n tau", n=8)
                dst = sb_["ys_r"][h * 8:(h + 1) * 8, b2 * 2048 + tt * 128: b2 * 2048 + tt * 128 + 128]
                dts3.append(sch.dma(qs[sp % 3], dst, src2, deps=[cb], name="ysr"))
    sch.stage()
    ev_prev = {0: None, 1: None}
    for b2 in range(2):
        col0 = b2 * S
        for ft in range(8):
            for tc in range(4):
                k = (ft * 4 + tc) % 2
                ps = [psA, psB][k]
                mm = sch.emit("tensor", lambda E, ft=ft, tc=tc, b2=b2, ps=ps: E.matmul(
                    ps[:, 0:512], sb_["aw_outb"][0:32, ft * 128:(ft + 1) * 128],
                    sb_["ys_r"][0:32, b2 * 2048 + tc * 512: b2 * 2048 + tc * 512 + 512],
                    start=True, stop=True), deps=[ev_prev[k]] + dts3, name="ow_mm")
                xl = sb_["xtile"][:, k * 512:(k + 1) * 512]
                ev = sb_["evtmp"][:, k * 512:(k + 1) * 512]
                tl = sch.dma("sync", xl,
                             x1_fm[ft * 128:(ft + 1) * 128, col0 + tc * 512: col0 + tc * 512 + 512],
                             deps=[ev_prev[k]], name="x1_ld")
                e1 = sch.emit("vector", lambda E, ps=ps, ft=ft, xl=xl, ev=ev: E.scalar_tensor_tensor(
                    ev, ps[:, 0:512], sb_["a_outb"][:, ft:ft + 1],
                    xl, ALU.add, ALU.add), deps=[mm, tl], name="ow_ev")
                st = sch.dma("scalar", x1_fm[ft * 128:(ft + 1) * 128, col0 + tc * 512: col0 + tc * 512 + 512],
                             ev, deps=[e1], name="x1_st2")
                ev_prev[k] = st
    sch.stage()
    return []



def alloc_moe(ctx):
    ctx.scr_top("m_wst", (128, 1024), F32)
    ctx.scr_top("m_wv", (128, 2 * 16 * 6), F32)
    ctx.scr_top("m_s2u", (128, 2 * 16 * 6), U32)
    ctx.scr_top("m_mrows", (33, 2048), F32)
    ctx.scr_top("m_cnt8", (8, 64), F32)
    ctx.scr_top("m_rmask", (8, 1), F32)
    ctx.scr_top("m_tokid", (128, 32), F32)
    ctx.scr_top("m_rsum", (128, 1024), F32)
    ctx.scr_top("m_ctm", (128, 1024), F32)
    ctx.scr_top("m_xidx", (128, 32), U32)
    ctx.scr_top("m_xnbt", (128, 1024), BF16)


def alloc_moe_routing(ctx):
    ctx.scr("m_plmu", (128, 2048), F32)
    ctx.scr("m_plr", (128, 2048), F32)
    ctx.scr("m_t1", (128, 2048), F32)
    ctx.scr("m_xnb", (128, NDT * 2048), BF16)
    ctx.scr("m_xtile", (128, 2048), F32)
    ctx.scr("m_gw", (128, NDT * 64), BF16)
    ctx.scr("m_gwl", (128, NDT * 64), BF16)
    ctx.scr("m_xlo", (128, 2048), BF16)
    ctx.scr("m_gbrow", (1, 64), F32)
    ctx.scr("m_gbrep", (128, 64), F32)
    ctx.scr("m_sc", (128, 16 * 64), F32)
    ctx.scr("m_sc2", (128, 16 * 64), F32)
    ctx.scr("m_srow", (128, 16), F32)
    ctx.scr("m_oh", (128, 64), F32)
    ctx.scr("m_ohp", (128, 64), F32)
    ctx.scr("m_topv", (128, 16 * 8), F32)
    ctx.scr("m_topi", (128, 16 * 8), U16)
    ctx.scr("m_ef", (128, 2 * 16 * 6), F32)
    ctx.scr("m_r1", (128, 16 * 6), F32)
    ctx.scr("m_p6", (128, 16 * 36), F32)
    ctx.scr("m_I", (128, 4 * 6 * 64), F32)
    ctx.scr("m_c", (128, 2 * 16 * 64), F32)
    ctx.scr("m_pfx", (128, 2 * 16 * 64), F32)
    ctx.scr("m_fp", (128, 16 * 64), F32)
    ctx.scr("m_pos", (128, 16 * 6), F32)
    ctx.scr("m_slot", (128, 16 * 6), F32)
    ctx.scr("m_val", (128, 16 * 6), F32)
    ctx.scr("m_rec", (128, 16 * 6 * 4), F32)


def emit_moe(ctx, x_fm, x2_fm, z_dram, pass_id, deps, cut=10):
    nc, sch = ctx.nc, ctx.sch
    sb = ctx.sb
    d = ctx.dram
    psA, psB, psC, psD, psE = (ctx.ps[k] for k in ("psA", "psB", "psC", "psD", "psE"))
    ident = sb["ident128"]
    P = pass_id
    qs = ["sync", "scalar", "gpsimd"]

    ag_cnt_i = ctx.dram_tmp(f"agci{P}", (1, 64), F32)
    ag_cnt_o = ctx.dram_tmp(f"agco{P}", (8, 64), F32, shared=True)
    ag_rec_i = ctx.dram_tmp(f"agri{P}", (32768, 4), F32)
    ag_rec_o = ctx.dram_tmp(f"agro{P}", (8 * 32768, 4), F32, shared=True)
    ag_xn_i = ctx.dram_tmp(f"agxi{P}", (4096, 1024), BF16)
    ag_xn_o = ctx.dram_tmp(f"agxo{P}", (8 * 4096, 1024), BF16, shared=True)
    ag_cb_i = ctx.dram_tmp(f"agbi{P}", (4096, 1024), F32)
    ag_cb_o = ctx.dram_tmp(f"agbo{P}", (8 * 4096, 1024), F32, shared=True)

    sch.set_phase(f"m{P}_ld")
    ctx.scratch_reset()
    alloc_moe_routing(ctx)
    zoff = ctx._bump
    prev_w = list(deps)
    wl = []
    for dt in range(NDT):
        t1 = sch.dma("sync", sb["m_wst"][:, 0:64], d["gate_wT"][dt * 128:(dt + 1) * 128, :],
                     deps=prev_w, name="gw_l")
        t2 = sch.emit("vector", lambda E, dt=dt: E.tensor_copy(
            sb["m_gw"][:, dt * 64:(dt + 1) * 64], sb["m_wst"][:, 0:64]), deps=[t1],
            name="gw_c")
        t2u = sch.emit("vector", lambda E, dt=dt: E.tensor_copy(
            sb["m_wst"][:, 128:192], sb["m_gw"][:, dt * 64:(dt + 1) * 64]),
            deps=[t2], name="gw_up")
        t2b = sch.emit("vector", lambda E: E.tensor_tensor(
            sb["m_wst"][:, 64:128], sb["m_wst"][:, 0:64],
            sb["m_wst"][:, 128:192], ALU.subtract), deps=[t2u],
            name="gw_lo1")
        t2c = sch.emit("vector", lambda E, dt=dt: E.tensor_copy(
            sb["m_gwl"][:, dt * 64:(dt + 1) * 64], sb["m_wst"][:, 64:128]),
            deps=[t2b], name="gw_lo2")
        prev_w = [t2b]
        wl += [t2, t2c]
    tgb = sch.dma("sync", sb["m_gbrow"][0:1, :], d["L0_ffn_gate_b"][:].unsqueeze(0),
                  deps=deps, name="gb_l")
    bcg = sch.emit("tensor", lambda E: E.matmul(psD[:, 0:64], sb["ones_row"][0:1, :],
                   sb["m_gbrow"][0:1, :], start=True, stop=True), deps=[tgb], name="gb_b")
    tgb2 = sch.emit("vector", lambda E: E.tensor_copy(sb["m_gbrep"][:, :], psD[:, 0:64]),
                    deps=[bcg], name="gb_e")
    trm = sch.dma("sync", sb["m_rmask"][:, :], d["rankmask"][:, :], deps=deps, name="rm")
    tti = sch.dma("sync", sb["m_tokid"][:, :], d["tokid"][:, :], deps=deps, name="ti")
    sch.stage()
    if cut <= 1:
        return

    if P == 0:
        wsz = ctx.dram_tmp("wsz", (11, 128, 8, 256), BF16)
        ws2z = ctx.dram_tmp("ws2z", (8, 128, 11, 128), BF16)
    else:
        wsz = ctx.dram["wsz"]
        ws2z = ctx.dram["ws2z"]
    ctx.scr("z_wk", (128, 2 * 2048), BF16)
    ctx.scr("z_w2k", (128, 2 * 1408), BF16)
    ctx.scr("z_zh", (128, 11 * 512), BF16)
    ctx.scr("z_stmp", (128, 512), F32)
    ctx.scr("z_stg", (128, 256), BF16)
    prev_w = []
    zl = []
    for dt in range(NDT if P == 0 else 0):
        for scn in range(11):
            t1 = sch.dma("scalar", sb["m_wst"][:, 0:128],
                         d["ws1T"][dt * 128:(dt + 1) * 128, scn * 128:(scn + 1) * 128],
                         deps=prev_w, name="zc_l1")
            c1 = sch.emit("vector", lambda E: E.tensor_copy(sb["z_stg"][:, 0:128],
                          sb["m_wst"][:, 0:128]), deps=[t1], name="zc_c1")
            t2 = sch.dma("scalar", sb["m_wst"][:, 0:128],
                         d["ws3T"][dt * 128:(dt + 1) * 128, scn * 128:(scn + 1) * 128],
                         deps=[c1], name="zc_l2")
            c2 = sch.emit("vector", lambda E: E.tensor_copy(sb["z_stg"][:, 128:256],
                          sb["m_wst"][:, 0:128]), deps=[t2], name="zc_c2")
            s1 = sch.dma("gpsimd", wsz[scn, :, dt, :], sb["z_stg"][:, :], deps=[c2],
                         name="zc_s")
            prev_w = [c2, s1]
            zl.append(s1)
    for dt in range(NDT if P == 0 else 0):
        for scn in range(11):
            t1 = sch.dma("scalar", sb["m_wst"][:, 0:128],
                         d["ws2T"][scn * 128:(scn + 1) * 128, dt * 128:(dt + 1) * 128],
                         deps=prev_w, name="z2_l")
            c1 = sch.emit("vector", lambda E: E.tensor_copy(sb["z_stg"][:, 0:128],
                          sb["m_wst"][:, 0:128]), deps=[t1], name="z2_c")
            s1 = sch.dma("gpsimd", ws2z[dt, :, scn, :], sb["z_stg"][:, 0:128], deps=[c1],
                         name="z2_s")
            prev_w = [c1, s1]
            zl.append(s1)
    sch.stage()
    if cut <= 2:
        return

    xn_dma = []
    ts_evac = []
    for b in range(B):
        sch.set_phase(f"m{P}_rt{b}")
        col0 = b * S
        xt = sb["m_xtile"]
        mm_toks = []
        prev = []
        for dt in range(NDT):
            tl = sch.dma("sync", xt[:, :], x_fm[dt * 128:(dt + 1) * 128, col0:col0 + S],
                         deps=prev, name="mx_ld")
            tq = sch.emit("scalar", lambda E: E.activation(out=sb["m_t1"][:, :],
                          in_=xt[:, :], func=AF.Square), deps=[tl] + prev, name="m_sq")
            m1 = m2 = None
            for c0 in range(0, S, 512):
                m1 = sch.emit("tensor", lambda E, c0=c0, dt=dt: E.matmul(
                    psE[0:1, c0:c0 + 512], sb["ones_col"][:, 0:1], xt[:, c0:c0 + 512],
                    start=(dt == 0), stop=(dt == NDT - 1)), deps=[tl], name="m_s")
                m2 = sch.emit("tensor", lambda E, c0=c0, dt=dt: E.matmul(
                    [psA, psB, psC, psD][c0 // 512][0:1, 0:512], sb["ones_col"][:, 0:1],
                    sb["m_t1"][:, c0:c0 + 512],
                    start=(dt == 0), stop=(dt == NDT - 1)), deps=[tq], name="m_q")
                if dt == NDT - 1:
                    mm_toks += [m1, m2]
            prev = [tq, m1, m2]
        mu, var = sb["mu_row"], sb["var_row"]
        t1 = sch.emit("vector", lambda E: E.tensor_scalar(mu[0:1, :], psE[0:1, :],
                      1.0 / D, None, ALU.mult), deps=mm_toks, name="m_mu")
        sqc = []
        for c0 in range(0, 2048, 512):
            sqc.append(sch.emit("vector", lambda E, c0=c0: E.tensor_copy(
                var[0:1, c0:c0 + 512], [psA, psB, psC, psD][c0 // 512][0:1, 0:512]),
                deps=mm_toks, name="m_sc"))
        t2 = sch.emit("vector", lambda E: E.tensor_tensor(psE[0:1, :], mu[0:1, :],
                      mu[0:1, :], ALU.mult), deps=[t1], name="m_m2")
        t3 = sch.emit("vector", lambda E: E.scalar_tensor_tensor(var[0:1, :], var[0:1, :],
                      1.0 / D, psE[0:1, :], ALU.mult, ALU.subtract),
                      deps=[t2] + sqc, name="m_v")
        t4 = sch.emit("scalar", lambda E: E.activation(out=var[0:1, :], in_=var[0:1, :],
                      func=AF.Sqrt, bias=sb["eps_col"][0:1, 0:1]), deps=[t3], name="m_st")
        t5 = sch.emit("vector", lambda E: E.reciprocal(var[0:1, :], var[0:1, :]),
                      deps=[t4], name="m_rc")
        bmu = emit_bcast(ctx, sb["m_plmu"], mu, [t1, t3], "m_bm")
        t5b = sch.emit("vector", lambda E: E.tensor_copy(mu[0:1, :], var[0:1, :]),
                       deps=[t5] + bmu, name="m_r0")
        brr = emit_bcast(ctx, sb["m_plr"], mu, [t5b], "m_br")
        sch.stage()
        prev = []
        g_last = None
        for dt in range(NDT):
            tl = sch.dma("sync", xt[:, :], x_fm[dt * 128:(dt + 1) * 128, col0:col0 + S],
                         deps=prev, name="mx_ld2")
            a1 = sch.emit("vector", lambda E: E.tensor_tensor(sb["m_t1"][:, :], xt[:, :],
                          sb["m_plmu"][:, :], ALU.subtract), deps=[tl], name="m_a1")
            a1b = sch.emit("vector", lambda E: E.tensor_tensor(sb["m_t1"][:, :],
                           sb["m_t1"][:, :], sb["m_plr"][:, :], ALU.mult), deps=[a1],
                           name="m_a1b")
            a2 = sch.emit("vector", lambda E, dt=dt: E.tensor_copy(
                sb["m_xnb"][:, dt * 2048:(dt + 1) * 2048], sb["m_t1"][:, :]),
                deps=[a1b], name="m_a2")
            a2u = sch.emit("vector", lambda E, dt=dt: E.tensor_copy(
                xt[:, :], sb["m_xnb"][:, dt * 2048:(dt + 1) * 2048]),
                deps=[a2], name="m_a2u")
            a3 = sch.emit("vector", lambda E: E.tensor_tensor(
                sb["m_t1"][:, :], sb["m_t1"][:, :], xt[:, :], ALU.subtract),
                deps=[a2u] + ([g_last] if g_last else []), name="m_a3")
            a4 = sch.emit("vector", lambda E: E.tensor_copy(sb["m_xlo"][:, :],
                          sb["m_t1"][:, :]), deps=[a3], name="m_a4")
            for c0 in range(0, S, 512):
                sch.emit("tensor", lambda E, c0=c0, dt=dt: E.matmul(
                    psE[0:64, c0:c0 + 512], sb["m_gw"][:, dt * 64:(dt + 1) * 64],
                    sb["m_xnb"][:, dt * 2048 + c0: dt * 2048 + c0 + 512],
                    start=(dt == 0), stop=False), deps=wl + [a2, t3], name="m_gmm")
                sch.emit("tensor", lambda E, c0=c0, dt=dt: E.matmul(
                    psE[0:64, c0:c0 + 512], sb["m_gwl"][:, dt * 64:(dt + 1) * 64],
                    sb["m_xnb"][:, dt * 2048 + c0: dt * 2048 + c0 + 512],
                    start=False, stop=False), deps=[a2], name="m_gml")
                g_last = sch.emit("tensor", lambda E, c0=c0, dt=dt: E.matmul(
                    psE[0:64, c0:c0 + 512], sb["m_gw"][:, dt * 64:(dt + 1) * 64],
                    sb["m_xlo"][:, c0:c0 + 512],
                    start=False, stop=(dt == NDT - 1)), deps=[a4], name="m_gmx")
            prev = [a3]
        sch.stage()
        gev = []
        for c0 in range(0, 2048, 512):
            gev.append(sch.emit("vector", lambda E, c0=c0: E.tensor_copy(
                sb["m_ctm"][0:64, 0:512], psE[0:64, c0:c0 + 512]),
                deps=[g_last] + gev[-1:], name="m_gev"))
            for q in range(4):
                tt = c0 // 128 + q
                k = (tt) % 2
                ps = [psC, psD][k]
                tmx = sch.emit("tensor", lambda E, q=q, ps=ps: E.transpose(
                    ps[:, 0:64], sb["m_ctm"][0:64, q * 128:(q + 1) * 128],
                    ident[0:64, 0:64]), deps=[gev[-1]], name="m_gtr")
                evx = sch.emit("vector", lambda E, tt=tt, ps=ps: E.tensor_copy(
                    sb["m_sc"][:, tt * 64:(tt + 1) * 64], ps[:, 0:64]), deps=[tmx],
                    name="m_gte")
                gev.append(evx)
        if "dbg_sc" in d and b == 0 and P == 0:
            sch.dma("sync", d["dbg_sc"][:, :], sb["m_sc"][:, :], deps=list(gev),
                    name="dbg_sc")
        sch.stage()
        tr_prev = {0: None, 1: None}
        for tt in range(16):
            evs = []
            for dt in range(NDT):
                k = dt % 2
                ps = [psC, psD][k]
                tm = sch.emit("tensor", lambda E, dt=dt, tt=tt, ps=ps: E.transpose(
                    ps[:, 0:64].bitcast(BF16), sb["m_xnb"][:, dt * 2048 + tt * 128: dt * 2048 + tt * 128 + 128],
                    sb["ident128b"][:, :]), deps=[tr_prev[k]], name="m_tr")
                ev = sch.emit("vector", lambda E, dt=dt, ps=ps: E.tensor_copy(
                    sb["m_ctm"][:, dt * 128:(dt + 1) * 128], ps[:, 0:64].bitcast(BF16)), deps=[tm],
                    name="m_tre")
                tr_prev[k] = ev
                evs.append(ev)
            cb = sch.emit("vector", lambda E: E.tensor_copy(
                sb["m_rsum"][:, :].bitcast(BF16)[:, 0:1024] if False else sb["m_rec"][:, 0:384],
                sb["m_ctm"][:, 0:384]), deps=evs, name="m_dum") if False else None
            cb = sch.emit("vector", lambda E: E.tensor_copy(sb["m_xnbt"][:, :],
                          sb["m_ctm"][:, :]), deps=evs, name="m_cb")
            xd = sch.dma(qs[tt % 3], ag_xn_i[b * 2048 + tt * 128: b * 2048 + tt * 128 + 128, :],
                         sb["m_xnbt"][:, :], deps=[cb], name="m_xnd")
            xn_dma.append(xd)
            tr_prev = {0: xd, 1: xd}
        sch.stage()
        sch.set_phase(f"m{P}_z{b}")
        for tc in range(4):
            zh_toks = []
            wk_free = [None, None]
            for scn in range(11):
                kb = scn % 2
                wkl = sch.dma("sync", sb["z_wk"][:, kb * 2048:(kb + 1) * 2048],
                              wsz[scn, :, :, :], deps=[wk_free[kb]] + zl, name="m_zwl")
                mm1 = mm3 = None
                for dt in range(NDT):
                    mm1 = sch.emit("tensor", lambda E, scn=scn, dt=dt, tc=tc, kb=kb: E.matmul(
                        psA[:, 0:512],
                        sb["z_wk"][:, kb * 2048 + dt * 256: kb * 2048 + dt * 256 + 128],
                        sb["m_xnb"][:, dt * 2048 + tc * 512: dt * 2048 + tc * 512 + 512],
                        start=(dt == 0), stop=(dt == NDT - 1), skip_group_check=True),
                        deps=[wkl] + zh_toks, name="m_z1m")
                    mm3 = sch.emit("tensor", lambda E, scn=scn, dt=dt, tc=tc, kb=kb: E.matmul(
                        psB[:, 0:512],
                        sb["z_wk"][:, kb * 2048 + dt * 256 + 128: kb * 2048 + dt * 256 + 256],
                        sb["m_xnb"][:, dt * 2048 + tc * 512: dt * 2048 + tc * 512 + 512],
                        start=(dt == 0), stop=(dt == NDT - 1), skip_group_check=True),
                        deps=[wkl] + zh_toks, name="m_z3m")
                e1 = sch.emit("scalar", lambda E: E.activation(
                    out=sb["z_stmp"][:, :], in_=psA[:, 0:512], func=AF.Silu),
                    deps=[mm1], name="m_zsil")
                e2 = sch.emit("vector", lambda E, scn=scn: E.tensor_tensor(
                    sb["z_zh"][:, scn * 512:(scn + 1) * 512], sb["z_stmp"][:, :],
                    psB[:, 0:512], ALU.mult), deps=[e1, mm3], name="m_zhm")
                zh_toks.append(e2)
                wk_free[kb] = mm3
            ps_free = None
            w2_free = [None, None]
            for dt in range(NDT):
                kb = dt % 2
                w2l = sch.dma("sync", sb["z_w2k"][:, kb * 1408:(kb + 1) * 1408],
                              ws2z[dt, :, :, :], deps=[w2_free[kb]] + zl, name="m_z2wl")
                mmz = None
                for scn in range(11):
                    mmz = sch.emit("tensor", lambda E, dt=dt, scn=scn, kb=kb: E.matmul(
                        psE[:, (dt % 4) * 512:(dt % 4) * 512 + 512],
                        sb["z_w2k"][:, kb * 1408 + scn * 128: kb * 1408 + scn * 128 + 128],
                        sb["z_zh"][:, scn * 512:(scn + 1) * 512],
                        start=(scn == 0), stop=(scn == 10), skip_group_check=True),
                        deps=zh_toks + [w2l] + ([ps_free] if dt >= 4 else []), name="m_zmm")
                ze = sch.emit("vector", lambda E, dt=dt: E.tensor_copy(
                    sb["m_t1"][:, (dt % 4) * 512:(dt % 4) * 512 + 512],
                    psE[:, (dt % 4) * 512:(dt % 4) * 512 + 512]), deps=[mmz], name="m_ze")
                zd2 = sch.dma(qs[dt % 3], z_dram[dt * 128:(dt + 1) * 128, col0 + tc * 512: col0 + tc * 512 + 512],
                              sb["m_t1"][:, (dt % 4) * 512:(dt % 4) * 512 + 512],
                              deps=[ze], name="m_zd")
                ps_free = zd2
                w2_free[kb] = mmz
            sch.stage()
        sch.set_phase(f"m{P}_tk{b}")
        s1 = sch.emit("vector", lambda E: E.tensor_reduce(
            sb["m_srow"][:, :], sb["m_sc"][:, :].rearrange("p (a e) -> p a e", e=64),
            axis=AX.X, op=ALU.max), deps=[], name="m_sm1")
        s2 = sch.emit("vector", lambda E: E.tensor_tensor(
            sb["m_sc"][:, :].rearrange("p (a e) -> p a e", e=64),
            sb["m_sc"][:, :].rearrange("p (a e) -> p a e", e=64),
            sb["m_srow"][:, :].to_broadcast((128, 16, 64)), ALU.subtract), deps=[s1],
            name="m_sm2")
        s3 = sch.emit("scalar", lambda E: E.activation(out=sb["m_sc"][:, :],
                      in_=sb["m_sc"][:, :], func=AF.Exp), deps=[s2], name="m_sm3")
        s4 = sch.emit("vector", lambda E: E.tensor_reduce(
            sb["m_srow"][:, :], sb["m_sc"][:, :].rearrange("p (a e) -> p a e", e=64),
            axis=AX.X, op=ALU.add), deps=[s3], name="m_sm4")
        s5 = sch.emit("vector", lambda E: E.reciprocal(sb["m_srow"][:, :],
                      sb["m_srow"][:, :]), deps=[s4], name="m_sm5")
        s6 = sch.emit("vector", lambda E: E.tensor_tensor(
            sb["m_sc"][:, :].rearrange("p (a e) -> p a e", e=64),
            sb["m_sc"][:, :].rearrange("p (a e) -> p a e", e=64),
            sb["m_srow"][:, :].to_broadcast((128, 16, 64)), ALU.mult), deps=[s5],
            name="m_sm6")
        s7 = sch.emit("vector", lambda E: E.tensor_tensor(
            sb["m_sc2"][:, :].rearrange("p (a e) -> p a e", e=64),
            sb["m_sc"][:, :].rearrange("p (a e) -> p a e", e=64),
            sb["m_gbrep"][:, :].to_broadcast((128, 64, 16)).transpose((0, 2, 1)),
            ALU.add), deps=[s6, tgb2], name="m_sc2")
        prevv = [s7]
        eftoks = []
        for tt in range(16):
            tv = sch.emit("vector", lambda E, tt=tt: E.max_with_indices(
                sb["m_topv"][:, tt * 8:(tt + 1) * 8], sb["m_topi"][:, tt * 8:(tt + 1) * 8],
                sb["m_sc2"][:, tt * 64:(tt + 1) * 64]), deps=prevv, name="m_top")
            for j in range(TOPK):
                oh = sch.emit("vector", lambda E, tt=tt, j=j: E.tensor_tensor(
                    sb["m_oh"][:, :].rearrange("p (a e) -> p a e", a=1),
                    sb["m_sc2"][:, tt * 64:(tt + 1) * 64].rearrange("p (a e) -> p a e", a=1),
                    sb["m_topv"][:, tt * 8 + j: tt * 8 + j + 1].to_broadcast((128, 1, 64)),
                    ALU.is_equal), deps=[tv], name="m_oh")
                wj1 = sch.emit("vector", lambda E, tt=tt: E.tensor_tensor(
                    sb["m_ohp"][:, :], sb["m_sc"][:, tt * 64:(tt + 1) * 64],
                    sb["m_oh"][:, :], ALU.mult), deps=[oh], name="m_wj1")
                wj = sch.emit("vector", lambda E, tt=tt, j=j, b=b: E.tensor_reduce(
                    sb["m_wv"][:, (b * 16 + tt) * 6 + j: (b * 16 + tt) * 6 + j + 1],
                    sb["m_ohp"][:, :].rearrange("p (a e) -> p a e", a=1),
                    axis=AX.X, op=ALU.add), deps=[wj1], name="m_wj2")
                ef1 = sch.emit("vector", lambda E: E.tensor_tensor(
                    sb["m_ohp"][:, :], sb["m_oh"][:, :], sb["erow"][:, :],
                    ALU.mult), deps=[wj], name="m_ef1")
                ef2 = sch.emit("vector", lambda E, tt=tt, j=j, b=b: E.tensor_reduce(
                    sb["m_ef"][:, b * 96 + tt * 6 + j: b * 96 + tt * 6 + j + 1],
                    sb["m_ohp"][:, :].rearrange("p (a e) -> p a e", a=1),
                    axis=AX.X, op=ALU.add), deps=[ef1], name="m_ef2")
                prevv = [ef2]
                eftoks.append(ef2)
        ef = eftoks[-1]
        ctoks = []
        for g in range(4):
            efg = sb["m_ef"][:, b * 96 + g * 24: b * 96 + (g + 1) * 24]
            i1 = sch.emit("vector", lambda E, efg=efg: E.tensor_tensor(
                sb["m_I"][:, :].rearrange("p (t j e) -> p t j e", t=4, j=6, e=64),
                efg.rearrange("p (t j) -> p t j", j=6).to_broadcast((128, 4, 6, 64)),
                sb["erow"][:, :].to_broadcast((128, 64, 6)).to_broadcast((128, 64, 6, 4)).transpose((0, 3, 2, 1)),
                ALU.is_equal), deps=[ef], name="m_I")
            c1 = sch.emit("vector", lambda E, g=g, b=b: E.tensor_reduce(
                sb["m_c"][:, (b * 16 + g * 4) * 64: (b * 16 + (g + 1) * 4) * 64],
                sb["m_I"][:, :].rearrange("p (t j e) -> p t e j", t=4, j=6, e=64),
                axis=AX.X, op=ALU.add), deps=[i1], name="m_c")
            ctoks.append(c1)
        ms_toks = []
        for tt in range(16):
            k = tt % 2
            ps = [psA, psB][k]
            mp = sch.emit("tensor", lambda E, tt=tt, b=b, ps=ps: E.matmul(
                ps[:, 0:64], sb["ustri128"][:, :],
                sb["m_c"][:, (b * 16 + tt) * 64:(b * 16 + tt + 1) * 64],
                start=True, stop=True), deps=ctoks, name="m_mp")
            ev = sch.emit("vector", lambda E, tt=tt, b=b, ps=ps: E.tensor_copy(
                sb["m_pfx"][:, (b * 16 + tt) * 64:(b * 16 + tt + 1) * 64], ps[:, 0:64]),
                deps=[mp], name="m_mpe")
            ms = sch.emit("tensor", lambda E, tt=tt, b=b: E.matmul(
                [psC, psD][tt // 8][0:1, (tt % 8) * 64:(tt % 8) * 64 + 64],
                sb["ones_col"][:, 0:1],
                sb["m_c"][:, (b * 16 + tt) * 64:(b * 16 + tt + 1) * 64],
                start=True, stop=True), deps=ctoks, name="m_ms")
            ms_toks.append(ms)
        e1 = sch.emit("vector", lambda E, b=b: E.tensor_copy(
            sb["m_mrows"][0:1, b * 1024: b * 1024 + 512], psC[0:1, 0:512]),
            deps=ms_toks, name="m_tse1")
        e2 = sch.emit("vector", lambda E, b=b: E.tensor_copy(
            sb["m_mrows"][0:1, b * 1024 + 512: b * 1024 + 1024], psD[0:1, 0:512]),
            deps=ms_toks, name="m_tse2")
        ts_evac += [e1, e2]
        sch.stage()
    pass
    sch.stage()
    if cut <= 3:
        return
    sch.set_phase(f"m{P}_r3")
    tot = sch.emit("vector", lambda E: E.tensor_reduce(
        sb["m_mrows"][32:33, 0:64],
        sb["m_mrows"][0:1, 0:2048].rearrange("p (t e) -> p e t", t=32, e=64),
        axis=AX.X, op=ALU.add), deps=ts_evac, name="m_tot")
    cp = sch.emit("vector", lambda E: E.tensor_copy(sb["mu_row"][0:1, 64:128],
                  sb["m_mrows"][32:33, 0:64]), deps=[tot], name="m_totc")
    d1 = sch.dma("sync", ag_cnt_i[0:1, :], sb["mu_row"][0:1, 64:128], deps=[cp],
                 name="m_cd")
    if cut == 41:
        sch.stage()
        return
    ag1 = sch.emit("gpsimd", lambda E: E.collective_compute(
        "AllGather", ALU.bypass, replica_groups=[list(range(8))],
        ins=[ag_cnt_i[:, :]], outs=[ag_cnt_o[:, :]]), deps=[d1], name="m_ag1", is_cc=True)
    if cut == 42:
        sch.stage()
        return
    d2 = sch.dma("sync", sb["m_cnt8"][:, :], ag_cnt_o[:, :], deps=[ag1], name="m_cl")
    if cut == 43:
        sch.stage()
        return
    msm = sch.emit("tensor", lambda E: E.matmul(psA[0:1, 0:64], sb["m_rmask"][:, :],
                   sb["m_cnt8"][:, :], start=True, stop=True), deps=[d2, trm], name="m_cs")
    bp0 = sch.emit("vector", lambda E: E.tensor_copy(sb["mu_row"][0:1, 0:64],
                   psA[0:1, 0:64]), deps=[msm], name="m_bp0")
    sch.stage()
    if cut <= 4:
        return
    z1 = sch.emit("gpsimd", lambda E: E.memset(sb["m_rsum"][:, :], 0.0), deps=[], name="m_z1")
    zd = sch.dma("gpsimd", ag_rec_i[:, :].rearrange("(a b) c -> a (b c)", a=128),
                 sb["m_rsum"][:, :], deps=[z1], name="m_zd")
    prev_bp = [bp0, zd]
    for b in range(B):
        for tt in range(16):
            f1 = sch.emit("tensor", lambda E: E.matmul(
                psE[:, 0:64], sb["ones_row"][0:1, :], sb["mu_row"][0:1, 0:64],
                start=True, stop=True), deps=prev_bp, name="m_f1")
            f2 = sch.emit("vector", lambda E, b=b, tt=tt: E.tensor_tensor(
                sb["m_fp"][:, tt * 64:(tt + 1) * 64],
                sb["m_pfx"][:, (b * 16 + tt) * 64:(b * 16 + tt + 1) * 64],
                psE[:, 0:64], ALU.add), deps=[f1], name="m_f2")
            f3 = sch.emit("vector", lambda E, b=b, tt=tt: E.tensor_tensor(
                sb["mu_row"][0:1, 0:64], sb["mu_row"][0:1, 0:64],
                sb["m_mrows"][0:1, (b * 16 + tt) * 64:(b * 16 + tt + 1) * 64], ALU.add),
                deps=[f2], name="m_f3")
            prev_bp = [f3]
        for g in range(4):
            efg = sb["m_ef"][:, b * 96 + g * 24: b * 96 + (g + 1) * 24]
            i1 = sch.emit("vector", lambda E, efg=efg: E.tensor_tensor(
                sb["m_I"][:, :].rearrange("p (t j e) -> p t j e", t=4, j=6, e=64),
                efg.rearrange("p (t j) -> p t j", j=6).to_broadcast((128, 4, 6, 64)),
                sb["erow"][:, :].to_broadcast((128, 64, 6)).to_broadcast((128, 64, 6, 4)).transpose((0, 3, 2, 1)),
                ALU.is_equal), deps=prev_bp, name="m_I2")
            i2 = sch.emit("vector", lambda E, g=g: E.tensor_tensor(
                sb["m_I"][:, :].rearrange("p (t j e) -> p t j e", t=4, j=6, e=64),
                sb["m_I"][:, :].rearrange("p (t j e) -> p t j e", t=4, j=6, e=64),
                sb["m_fp"][:, g * 256:(g + 1) * 256].rearrange("p (t e) -> p t e", e=64).to_broadcast((128, 4, 64, 6)).transpose((0, 1, 3, 2)),
                ALU.mult), deps=[i1], name="m_I3")
            i3 = sch.emit("vector", lambda E, g=g: E.tensor_reduce(
                sb["m_pos"][:, g * 24:(g + 1) * 24],
                sb["m_I"][:, :].rearrange("p (t j e) -> p t j e", t=4, j=6, e=64),
                axis=AX.X, op=ALU.add), deps=[i2], name="m_I4")
            prev_bp = [i3]
        efb = sb["m_ef"][:, b * 96:(b + 1) * 96]
        q1 = sch.emit("vector", lambda E, efb=efb: E.tensor_tensor(
            sb["m_p6"][:, :].rearrange("p (t j q) -> p t j q", t=16, j=6, q=6),
            efb.rearrange("p (t j) -> p t j", j=6).to_broadcast((128, 16, 6, 6)),
            efb.rearrange("p (t q) -> p t q", q=6).to_broadcast((128, 16, 6, 6)).transpose((0, 1, 3, 2)),
            ALU.is_equal), deps=prev_bp, name="m_q1b")
        q2 = sch.emit("vector", lambda E: E.tensor_tensor(
            sb["m_p6"][:, :].rearrange("p (t jq) -> p t jq", jq=36),
            sb["m_p6"][:, :].rearrange("p (t jq) -> p t jq", jq=36),
            sb["lt6"][:, :].to_broadcast((128, 36, 16)).transpose((0, 2, 1)),
            ALU.mult), deps=[q1], name="m_q2b")
        q3 = sch.emit("vector", lambda E: E.tensor_reduce(
            sb["m_r1"][:, :], sb["m_p6"][:, :].rearrange("p (t j q) -> p t j q", j=6, q=6),
            axis=AX.X, op=ALU.add), deps=[q2], name="m_q3b")
        p1 = sch.emit("vector", lambda E: E.tensor_tensor(sb["m_pos"][:, :],
                      sb["m_pos"][:, :], sb["m_r1"][:, :], ALU.add), deps=[q3],
                      name="m_p1b")
        v1 = sch.emit("vector", lambda E: E.tensor_scalar(sb["m_val"][:, :],
                      sb["m_pos"][:, :], 512.0, None, ALU.is_lt), deps=[p1], name="m_v1")
        v2 = sch.emit("vector", lambda E, b=b: E.tensor_tensor(
            sb["m_wv"][:, b * 96:(b + 1) * 96], sb["m_wv"][:, b * 96:(b + 1) * 96],
            sb["m_val"][:, :], ALU.mult), deps=[v1], name="m_v2")
        sl1 = sch.emit("vector", lambda E, b=b, efb=efb: E.scalar_tensor_tensor(
            sb["m_slot"][:, :], efb, 512.0, sb["m_pos"][:, :],
            ALU.mult, ALU.add), deps=[p1], name="m_sl1")
        sl2 = sch.emit("vector", lambda E: E.tensor_scalar(
            sb["m_val"][:, :], sb["m_val"][:, :], -1e9, 1e9, ALU.mult, ALU.add),
            deps=[v2, sl1], name="m_sl2")
        sl3 = sch.emit("vector", lambda E: E.tensor_tensor(
            sb["m_pos"][:, :], sb["m_slot"][:, :], sb["m_val"][:, :],
            ALU.add), deps=[sl2], name="m_sl3")
        sl4 = sch.emit("vector", lambda E, b=b: E.tensor_copy(
            sb["m_s2u"][:, b * 96:(b + 1) * 96], sb["m_pos"][:, :]), deps=[sl3],
            name="m_s2u")
        r0 = sch.emit("gpsimd", lambda E: E.memset(sb["m_rec"][:, :], 0.0),
                      deps=[sl4] + (sc_tail if b > 0 else []), name="m_r0")
        r1t = sch.emit("vector", lambda E, b=b: E.tensor_copy(
            sb["m_rec"][:, :].rearrange("p (t j c) -> p t j c", t=16, j=6, c=4)[:, :, :, 0:1],
            sb["m_tokid"][:, b * 16:(b + 1) * 16].to_broadcast((128, 16, 6)).to_broadcast((128, 16, 6, 1))),
            deps=[r0], name="m_r1")
        r2t = sch.emit("vector", lambda E, b=b: E.tensor_copy(
            sb["m_rec"][:, :].rearrange("p (t j c) -> p t j c", t=16, j=6, c=4)[:, :, :, 1:2],
            sb["m_wv"][:, b * 96:(b + 1) * 96].rearrange("p (t j) -> p t j", j=6).to_broadcast((128, 16, 6, 1))),
            deps=[r0, v2], name="m_r2")
        base_sc = [r1t, r2t, sl4, zd]
        sc_toks = []
        for col in range(96):
            st = sch.emit("gpsimd", lambda E, col=col, b=b: E.indirect_dma_start(
                ag_rec_i[:, :],
                bass.IndirectOffsetOnAxis(ap=sb["m_s2u"][:, b * 96 + col: b * 96 + col + 1], axis=0),
                sb["m_rec"][:, col * 4:(col + 1) * 4], None,
                bounds_check=bounds_reg(E), oob_is_err=False),
                is_dma=True, deps=base_sc, name="m_rsc", join=False)
            sc_toks.append(st)
        sc_tail = sc_toks[-8:]
        sch.stage()
    sch.stage()
    if cut <= 5:
        return
    if "dbg_s2u" in d and P == 0:
        sch.dma("sync", d["dbg_s2u"][:, :], sb["m_s2u"][:, :], deps=[], name="dbg_s2u")
        sch.dma("sync", d["dbg_wv"][:, :], sb["m_wv"][:, :], deps=[], name="dbg_wv")
        sch.dma("scalar", d["dbg_reci"][:, :].rearrange("(a b) c -> a (b c)", a=128),
                ag_rec_i[:, :].rearrange("(a b) c -> a (b c)", a=128), deps=[],
                name="dbg_reci")
        sch.stage()
    agx = sch.emit("gpsimd", lambda E: E.collective_compute(
        "AllGather", ALU.bypass, replica_groups=[list(range(8))],
        ins=[ag_xn_i[:, :]], outs=[ag_xn_o[:, :]]), deps=xn_dma, name="m_agx", is_cc=True)
    agr = sch.emit("gpsimd", lambda E: E.collective_compute(
        "AllGather", ALU.bypass, replica_groups=[list(range(8))],
        ins=[ag_rec_i[:, :]], outs=[ag_rec_o[:, :]]), deps=sc_tail, name="m_agr",
        is_cc=True)
    sch.stage()
    if cut <= 6:
        return
    acc = sch.dma("sync", sb["m_rsum"][:, :],
                  ag_rec_o[0:32768, :].rearrange("(a b) c -> a (b c)", a=128),
                  deps=[agr], name="m_rl0")
    for c in range(1, 8):
        tl = sch.dma("sync", sb["m_ctm"][:, :],
                     ag_rec_o[c * 32768:(c + 1) * 32768, :].rearrange("(a b) c -> a (b c)", a=128),
                     deps=[acc], name="m_rl")
        acc = sch.emit("vector", lambda E: E.tensor_tensor(
            sb["m_rsum"][:, :], sb["m_rsum"][:, :], sb["m_ctm"][:, :], ALU.add),
            deps=[tl], name="m_racc")
    mw = sch.dma("sync", ag_rec_i[:, :].rearrange("(a b) c -> a (b c)", a=128),
                 sb["m_rsum"][:, :], deps=[acc], name="m_rw")
    if "dbg_rec" in d and P == 0:
        sch.dma("scalar", d["dbg_rec"][:, :].rearrange("(a b) c -> a (b c)", a=128),
                sb["m_rsum"][:, :], deps=[acc], name="dbg_rec")
    sch.stage()
    if cut <= 7:
        return
    sch.set_phase(f"m{P}_ex")
    ctx.scratch_reset()
    ctx.scr("e_w13", (128, NDT * 1408), BF16)
    ctx.scr("e_w2", (128, 6 * 1024), BF16)
    ctx.scr("e_xbt", (128, 4 * 1024), BF16)
    ctx.scr("e_xbf", (128, NDT * 512), BF16)
    ctx.scr("e_h", (128, 6 * 512), BF16)
    ctx.scr("e_stmp", (128, 512), F32)
    ctx.scr("e_ybs", (128, 512), F32)
    ctx.scr("e_ct4", (128, 4096), F32)
    gsl = sch.dma("sync", sb["m_xidx"][:, :], d["myslots"][:, :], deps=[], name="m_msl")
    gts = []
    for g in range(32):
        gt = sch.emit("gpsimd", lambda E, g=g: E.indirect_dma_start(
            sb["m_rsum"][:, g * 4:(g + 1) * 4], None,
            ag_rec_i[:, :],
            bass.IndirectOffsetOnAxis(ap=sb["m_xidx"][:, g:g + 1], axis=0),
            bounds_check=bounds_reg(E), oob_is_err=False),
            is_dma=True, deps=[gsl, mw], name="m_grec")
        gts.append(gt)
    x1 = sch.emit("vector", lambda E: E.tensor_scalar(
        sb["m_ctm"][:, 0:32].rearrange("p (g c) -> p g c", c=1),
        sb["m_rsum"][:, 0:128].rearrange("p (g c) -> p g c", c=4)[:, :, 0:1],
        -1.0, None, ALU.add), deps=gts, name="m_xi1")
    x2 = sch.emit("vector", lambda E: E.tensor_scalar(
        sb["m_ctm"][:, 32:64], sb["m_ctm"][:, 0:32], 0.0, None, ALU.is_lt),
        deps=[x1], name="m_xi2")
    x3 = sch.emit("vector", lambda E: E.scalar_tensor_tensor(
        sb["m_ctm"][:, 0:32], sb["m_ctm"][:, 32:64], 1e9, sb["m_ctm"][:, 0:32],
        ALU.mult, ALU.add), deps=[x2], name="m_xi3")
    x4 = sch.emit("vector", lambda E: E.tensor_copy(sb["m_xidx"][:, :],
                  sb["m_ctm"][:, 0:32]), deps=[x3], name="m_xi4")
    sch.stage()
    prev_e = []
    cb_dma = []
    for e in range(EPERC):
        wts = []
        prev_l = list(prev_e)
        for dt in range(NDT):
            l1 = sch.dma("sync", sb["m_wst"][:, 0:704],
                         d["w1T_my"][e, dt * 128:(dt + 1) * 128, :], deps=prev_l, name="m_w1l")
            c1 = sch.emit("vector", lambda E, dt=dt: E.tensor_copy(
                sb["e_w13"][:, dt * 1408: dt * 1408 + 704], sb["m_wst"][:, 0:704]),
                deps=[l1], name="m_w1c")
            l2 = sch.dma("scalar", sb["m_wst"][:, 0:704],
                         d["w3T_my"][e, dt * 128:(dt + 1) * 128, :], deps=[c1], name="m_w3l")
            c2 = sch.emit("vector", lambda E, dt=dt: E.tensor_copy(
                sb["e_w13"][:, dt * 1408 + 704: dt * 1408 + 1408], sb["m_wst"][:, 0:704]),
                deps=[l2], name="m_w3c")
            prev_l = [c2]
            wts.append(c2)
        for ic in range(6):
            kk = 128 if ic < 5 else 64
            l3 = sch.dma("gpsimd", sb["m_wst"][0:kk, :],
                         d["w2T_my"][e, ic * 128: ic * 128 + kk, :], deps=prev_l, name="m_w2l")
            c3 = sch.emit("vector", lambda E, ic=ic, kk=kk: E.tensor_copy(
                sb["e_w2"][0:kk, ic * 1024:(ic + 1) * 1024], sb["m_wst"][0:kk, :]),
                deps=[l3], name="m_w2c")
            prev_l = [c3]
            wts.append(c3)
        if cut == 71:
            sch.stage()
            return
        z2 = sch.emit("gpsimd", lambda E: E.memset(sb["e_xbt"][:, :], 0.0),
                      deps=prev_e + [x4], name="m_xz")
        gats = []
        for sg in range(4):
            gx = sch.emit("gpsimd", lambda E, sg=sg, e=e: E.indirect_dma_start(
                sb["e_xbt"][:, sg * 1024:(sg + 1) * 1024], None,
                ag_xn_o[:, :],
                bass.IndirectOffsetOnAxis(ap=sb["m_xidx"][:, e * 4 + sg: e * 4 + sg + 1], axis=0),
                bounds_check=bounds_reg(E), oob_is_err=False),
                is_dma=True, deps=[z2, x4, agx], name="m_gx")
            gats.append(gx)
        if cut == 72:
            sch.stage()
            return
        trp = {0: None, 1: None}
        tr_toks = []
        for sg in range(4):
            for dt in range(NDT):
                k = (sg * NDT + dt) % 2
                ps = [psC, psD][k]
                tmt = sch.emit("tensor", lambda E, sg=sg, dt=dt, ps=ps: E.transpose(
                    ps[:, 0:64].bitcast(BF16), sb["e_xbt"][:, sg * 1024 + dt * 128: sg * 1024 + dt * 128 + 128],
                    sb["ident128b"][:, :]), deps=[trp[k]] + gats, name="m_xtr")
                ev = sch.emit("vector", lambda E, sg=sg, dt=dt, ps=ps: E.tensor_copy(
                    sb["e_xbf"][:, dt * 512 + sg * 128: dt * 512 + sg * 128 + 128],
                    ps[:, 0:64].bitcast(BF16)), deps=[tmt], name="m_xte")
                trp[k] = ev
                tr_toks.append(ev)
        if cut == 73:
            sch.stage()
            return
        hts = []
        for ic in range(6):
            ni = 128 if ic < 5 else 64
            mm1 = mm3 = None
            for dt in range(NDT):
                mm1 = sch.emit("tensor", lambda E, ic=ic, dt=dt, ni=ni: E.matmul(
                    psA[0:ni, 0:512], sb["e_w13"][:, dt * 1408 + ic * 128: dt * 1408 + ic * 128 + ni],
                    sb["e_xbf"][:, dt * 512:(dt + 1) * 512],
                    start=(dt == 0), stop=(dt == NDT - 1), skip_group_check=True),
                    deps=wts + tr_toks + hts, name="m_h1")
                mm3 = sch.emit("tensor", lambda E, ic=ic, dt=dt, ni=ni: E.matmul(
                    psB[0:ni, 0:512], sb["e_w13"][:, dt * 1408 + 704 + ic * 128: dt * 1408 + 704 + ic * 128 + ni],
                    sb["e_xbf"][:, dt * 512:(dt + 1) * 512],
                    start=(dt == 0), stop=(dt == NDT - 1), skip_group_check=True),
                    deps=wts + tr_toks + hts, name="m_h3")
            e1 = sch.emit("scalar", lambda E, ni=ni: E.activation(
                out=sb["e_stmp"][0:ni, :], in_=psA[0:ni, 0:512], func=AF.Silu),
                deps=[mm1], name="m_sil")
            e2 = sch.emit("vector", lambda E, ic=ic, ni=ni: E.tensor_tensor(
                sb["e_h"][0:ni, ic * 512:(ic + 1) * 512], sb["e_stmp"][0:ni, :],
                psB[0:ni, 0:512], ALU.mult), deps=[e1, mm3], name="m_hm")
            hts.append(e2)
        if cut == 74:
            sch.stage()
            return
        ct_toks = []
        ytp = {0: None, 1: None}
        for dt in range(NDT):
            mm2 = None
            for ic in range(6):
                kk = 128 if ic < 5 else 64
                mm2 = sch.emit("tensor", lambda E, dt=dt, ic=ic, kk=kk: E.matmul(
                    psE[:, (dt % 4) * 512:(dt % 4) * 512 + 512],
                    sb["e_w2"][0:kk, ic * 1024 + dt * 128: ic * 1024 + dt * 128 + 128],
                    sb["e_h"][0:kk, ic * 512:(ic + 1) * 512],
                    start=(ic == 0), stop=(ic == 5), skip_group_check=True),
                    deps=hts + ct_toks, name="m_ybm")
            ev = sch.emit("vector", lambda E, dt=dt: E.tensor_copy(
                sb["e_ybs"][:, :], psE[:, (dt % 4) * 512:(dt % 4) * 512 + 512]),
                deps=[mm2] + ct_toks, name="m_ybe")
            sg_evs = []
            for sg in range(4):
                k = sg % 2
                ps = [psC, psD][k]
                tmt = sch.emit("tensor", lambda E, sg=sg, ps=ps: E.transpose(
                    ps[:, 0:128], sb["e_ybs"][:, sg * 128:(sg + 1) * 128], ident[:, :]),
                    deps=[ev, ytp.get(k)] + tr_toks, name="m_ytr")
                ev2 = sch.emit("vector", lambda E, sg=sg, dt=dt, ps=ps: E.tensor_copy(
                    sb["e_ct4"][:, sg * 1024 + dt * 128: sg * 1024 + dt * 128 + 128],
                    ps[:, 0:128]), deps=[tmt], name="m_yte")
                ytp[k] = ev2
                sg_evs.append(ev2)
            ct_toks = sg_evs
        eds = []
        for sg in range(4):
            ed = sch.dma(qs[sg % 3], ag_cb_i[e * 512 + sg * 128: e * 512 + (sg + 1) * 128, :],
                         sb["e_ct4"][:, sg * 1024:(sg + 1) * 1024], deps=ct_toks, name="m_cbd")
            eds.append(ed)
            cb_dma.append(ed)
            if "dbg_cb" in d and P == 0:
                sch.dma(qs[sg % 3], d["dbg_cb"][e * 512 + sg * 128: e * 512 + (sg + 1) * 128, :],
                        sb["e_ct4"][:, sg * 1024:(sg + 1) * 1024], deps=ct_toks, name="dbg_cb")
        prev_e = eds
        sch.stage()
    sch.stage()
    if cut <= 8:
        return
    agc = sch.emit("gpsimd", lambda E: E.collective_compute(
        "AllGather", ALU.bypass, replica_groups=[list(range(8))],
        ins=[ag_cb_i[:, :]], outs=[ag_cb_o[:, :]]), deps=cb_dma, name="m_agc", is_cc=True)
    sch.stage()
    if cut <= 9:
        return
    sch.set_phase(f"m{P}_cb")
    ctx.scratch_reset()
    cgat = [ctx.scr(f"c_gat{i}", (128, 1024), F32) for i in range(3)]
    ctx.scr("c_ytm4", (128, 4096), F32)
    ctx.scr("c_x2", (128, 512), F32)
    ctx.scr("c_zt", (128, 512), F32)
    slot_last = {0: None, 1: None, 2: None}
    gidx = 0
    for b in range(B):
        col0 = b * S
        for tc in range(4):
            ycols = []
            for q in range(4):
                tt = tc * 4 + q
                yq = sb["c_ytm4"][:, q * 1024:(q + 1) * 1024]
                z3 = sch.emit("gpsimd", lambda E, yq=yq: E.memset(yq, 0.0),
                              deps=ycols, name="m_yz2")
                acc2 = [z3]
                for j in range(6):
                    col = b * 96 + tt * 6 + j
                    sl = gidx % 3
                    gidx += 1
                    gx = sch.emit("gpsimd", lambda E, col=col, sl=sl: E.indirect_dma_start(
                        cgat[sl][:, :], None,
                        ag_cb_o[:, :],
                        bass.IndirectOffsetOnAxis(ap=sb["m_s2u"][:, col:col + 1], axis=0),
                        bounds_check=bounds_reg(E), oob_is_err=False),
                        is_dma=True, deps=[slot_last[sl], agc], name="m_yg",
                        join=False)
                    a1 = sch.emit("vector", lambda E, col=col, yq=yq, sl=sl: E.scalar_tensor_tensor(
                        yq, cgat[sl][:, :],
                        sb["m_wv"][:, col:col + 1], yq,
                        ALU.mult, ALU.add), deps=[gx] + acc2, name="m_ya")
                    slot_last[sl] = a1
                    acc2 = [a1]
                ycols.append(acc2[0])
            prev_x = []
            for dt in range(NDT):
                tlx = sch.dma("sync", sb["c_x2"][:, :],
                              x_fm[dt * 128:(dt + 1) * 128, col0 + tc * 512: col0 + tc * 512 + 512],
                              deps=prev_x, name="m_xl")
                tlz = sch.dma("scalar", sb["c_zt"][:, :],
                              z_dram[dt * 128:(dt + 1) * 128, col0 + tc * 512: col0 + tc * 512 + 512],
                              deps=prev_x, name="m_zl")
                ax = sch.emit("vector", lambda E: E.tensor_tensor(
                    sb["c_x2"][:, :], sb["c_x2"][:, :], sb["c_zt"][:, :], ALU.add),
                    deps=[tlx, tlz], name="m_ax")
                ay = [ax]
                for q in range(4):
                    k = q % 2
                    ps = [psC, psD][k]
                    tmt = sch.emit("tensor", lambda E, q=q, dt=dt, ps=ps: E.transpose(
                        ps[:, 0:128], sb["c_ytm4"][:, q * 1024 + dt * 128: q * 1024 + dt * 128 + 128],
                        ident[:, :]), deps=ycols + ay[-1:], name="m_ytr2")
                    av = sch.emit("vector", lambda E, q=q, ps=ps: E.tensor_tensor(
                        sb["c_x2"][:, q * 128:(q + 1) * 128], sb["c_x2"][:, q * 128:(q + 1) * 128],
                        ps[:, 0:128], ALU.add), deps=[tmt] + ay, name="m_ay")
                    ay.append(av)
                sx = sch.dma("gpsimd", x2_fm[dt * 128:(dt + 1) * 128, col0 + tc * 512: col0 + tc * 512 + 512],
                             sb["c_x2"][:, :], deps=ay, name="m_x2s")
                prev_x = [sx]
            sch.stage()
    sch.stage()
    return {}



def emit_attn_b(ctx, x_fm, x4_fm, deps):
    nc, sch = ctx.nc, ctx.sch
    sb = ctx.sb
    d = ctx.dram
    psA, psB, psC, psD, psE = (ctx.ps[k] for k in ("psA", "psB", "psC", "psD", "psE"))
    banks8 = [psA, psB, psC, psD] + [psE[:, k * 512:(k + 1) * 512] for k in range(4)]
    qs = ["sync", "scalar", "gpsimd"]

    wzi = ctx.dram_tmp("wzi", (32, 8, 128, 128), BF16)
    wzx = ctx.dram_tmp("wzx", (17, 16, 128, 128), BF16)
    wzo = ctx.dram_tmp("wzo", (8, 16, 128, 128), BF16)
    xs_d = ctx.dram_tmp("xs_d", (16, 128, 2048), BF16)
    slr_d = ctx.dram_tmp("slr_d", (16, 128, 2048), BF16)
    cse_d = ctx.dram_tmp("cse_d", (16, 128, 2048), BF16)
    dlt_d = ctx.dram_tmp("dlt_d", (16, 128, 2048), BF16)
    y2_d = ctx.dram_tmp("y2_d", (16, 128, 2048), BF16)
    xn3_d = ctx.dram_tmp("xn3_d", (8, 128, 2048), F32)
    bsel_d = ctx.dram_tmp("bsel_d", (64, 2048), BF16)

    sch.set_phase("ab_stage")
    ctx.scratch_reset()
    aA = ctx.scr("b_aA", (128, 1024), F32)
    bcols = ctx.scr("b_cols", (128, 112), F32)
    CBI, CBX, CBDT, CDP, CBO, CGAM, CBET = 0, 32, 49, 65, 81, 89, 97
    wdt_sb = ctx.scr("b_wdt", (64, 2048), BF16)
    wt = [ctx.scr("b_wt0", (128, 2048), BF16), ctx.scr("b_wt1", (128, 2048), BF16)]
    wstg = [ctx.scr("b_wstg0", (128, 128), BF16), ctx.scr("b_wstg1", (128, 128), BF16)]
    wldf = [ctx.scr("b_wldf0", (128, 128), F32), ctx.scr("b_wldf1", (128, 128), F32)]
    xt = ctx.scr("b_xt", (128, 2048), F32)
    sqb = ctx.scr("b_sqb", (128, 2048), F32)
    xstg = ctx.scr("b_xstg", (128, 2048), BF16)
    cstg = [ctx.scr("b_cstg0", (128, 512), BF16), ctx.scr("b_cstg1", (128, 512), BF16)]
    b64 = ctx.scr("b_b64", (64, 2048), BF16)
    dbr = ctx.scr("b_dbr", (128, 2048), F32)
    dtmp = ctx.scr("b_dtmp", (128, 512), F32)
    ctx.scr("b_hbn", (64, 64 * 128), BF16)
    BASE = ctx.mark()

    stage_jobs = []
    for ft in range(32):
        for dt in range(NDT):
            stage_jobs.append((d["b_Wi"], dt, ft, wzi[ft, dt]))
    for ft in range(17):
        for dti in range(16):
            stage_jobs.append((d["b_Wx"], dti, ft, wzx[ft, dti]))
    for ft in range(8):
        for dblk in range(16):
            stage_jobs.append((d["b_Wo"], dblk, ft, wzo[ft, dblk]))
    prev_l = {0: None, 1: None}
    prev_s = {0: None, 1: None}
    wz_done = []
    for i, (src, kt, ft, dst) in enumerate(stage_jobs):
        k = i % 2
        q = ["sync", "scalar"][k]
        l = sch.dma(q, wldf[k][:, :], src[kt * 128:(kt + 1) * 128, ft * 128:(ft + 1) * 128],
                    deps=[prev_l[k]] + list(deps), name="bw_l")
        c = sch.emit("vector", lambda E, k=k: E.tensor_copy(wstg[k][:, :], wldf[k][:, :]),
                     deps=[l, prev_s[k]], name="bw_c")
        s = sch.dma(q, dst, wstg[k][:, :], deps=[c], name="bw_s")
        prev_l[k] = c
        prev_s[k] = s
    wz_done = [prev_s[0], prev_s[1]]
    t_bi = sch.dma("gpsimd", bcols[:, CBI:CBI + 32], d["b_bi_c"][:, :], deps=deps, name="b_bi")
    t_bx = sch.dma("gpsimd", bcols[:, CBX:CBX + 17], d["b_bx_c"][:, :], deps=deps, name="b_bx")
    t_bdt = sch.dma("gpsimd", bcols[:, CBDT:CBDT + 16], d["b_bdt_c"][:, :], deps=deps, name="b_bdt")
    t_dp = sch.dma("gpsimd", bcols[:, CDP:CDP + 16], d["b_Dp_c"][:, :], deps=deps, name="b_dp")
    t_bo = sch.dma("gpsimd", bcols[:, CBO:CBO + 8], d["b_bo_c"][:, :], deps=deps, name="b_bo")
    t_gm = sch.dma("gpsimd", bcols[:, CGAM:CGAM + 8], d["b_gam_c"][:, :], deps=deps, name="b_gm")
    t_bt = sch.dma("gpsimd", bcols[:, CBET:CBET + 8], d["b_bet_c"][:, :], deps=deps, name="b_bt")
    t_al = sch.dma("gpsimd", sqb[:, 0:1024], d["b_Alog_r"][:, :], deps=deps, name="b_al")
    t_a1 = sch.emit("scalar", lambda E: E.activation(out=aA[:, :], in_=sqb[:, 0:1024],
                    func=AF.Exp), deps=[t_al], name="b_a1")
    t_a2 = sch.emit("vector", lambda E: E.tensor_scalar(aA[:, :], aA[:, :], -1.0, None,
                    ALU.mult), deps=[t_a1], name="b_a2")
    prev = [t_a2]
    for h in range(2):
        l = sch.dma("sync", xt[0:64, 0:1024], d["b_Wdt"][:, h * 1024:(h + 1) * 1024],
                    deps=prev, name="b_wdl")
        c = sch.emit("vector", lambda E, h=h: E.tensor_copy(
            wdt_sb[:, h * 1024:(h + 1) * 1024], xt[0:64, 0:1024]), deps=[l], name="b_wdc")
        prev = [c]
    for n in range(64):
        sch.emit("vector", lambda E, n=n: E.tensor_copy(
            sb["b_hbn"][0:64, n * 128:(n + 1) * 128],
            sb["ident128b"][0:64, n:n + 1].to_broadcast((64, 128))),
            deps=deps, name="b_hb")
    sch.stage()

    for b in range(B):
        col0 = b * S
        ctx.reset_to(BASE)
        plmu = ctx.scr("b_plmu", (128, 2048), F32)
        plr = ctx.scr("b_plr", (128, 2048), F32)
        MARKB = ctx.mark()
        xnb = ctx.scr("b_xnb", (128, 8 * 2048), BF16)

        sch.set_phase(f"ab_L{b}")
        mm_toks = []
        prev = []
        for dt in range(NDT):
            tl = sch.dma("sync", xt[:, :], x_fm[dt * 128:(dt + 1) * 128, col0:col0 + S],
                         deps=prev, name="bL_x")
            tq = sch.emit("scalar", lambda E: E.activation(out=sqb[:, :], in_=xt[:, :],
                          func=AF.Square), deps=[tl] + prev, name="bL_sq")
            m1 = m2 = None
            for c0 in range(0, S, 512):
                m1 = sch.emit("tensor", lambda E, c0=c0, dt=dt: E.matmul(
                    psE[0:1, c0:c0 + 512], sb["ones_col"][:, 0:1], xt[:, c0:c0 + 512],
                    start=(dt == 0), stop=(dt == NDT - 1)), deps=[tl], name="bL_s")
                m2 = sch.emit("tensor", lambda E, c0=c0, dt=dt: E.matmul(
                    [psA, psB, psC, psD][c0 // 512][0:1, 0:512], sb["ones_col"][:, 0:1],
                    sqb[:, c0:c0 + 512],
                    start=(dt == 0), stop=(dt == NDT - 1)), deps=[tq], name="bL_q")
                if dt == NDT - 1:
                    mm_toks += [m1, m2]
            prev = [tq, m1, m2]
        mu, var = sb["mu_row"], sb["var_row"]
        t1 = sch.emit("vector", lambda E: E.tensor_scalar(mu[0:1, :], psE[0:1, :],
                      1.0 / D, None, ALU.mult), deps=mm_toks, name="bL_mu")
        sqc = []
        for c0 in range(0, 2048, 512):
            sqc.append(sch.emit("vector", lambda E, c0=c0: E.tensor_copy(
                var[0:1, c0:c0 + 512], [psA, psB, psC, psD][c0 // 512][0:1, 0:512]),
                deps=mm_toks, name="bL_sc"))
        t2 = sch.emit("vector", lambda E: E.tensor_tensor(psE[0:1, :], mu[0:1, :],
                      mu[0:1, :], ALU.mult), deps=[t1], name="bL_m2")
        t3 = sch.emit("vector", lambda E: E.scalar_tensor_tensor(var[0:1, :], var[0:1, :],
                      1.0 / D, psE[0:1, :], ALU.mult, ALU.subtract),
                      deps=[t2] + sqc, name="bL_v")
        t4 = sch.emit("scalar", lambda E: E.activation(out=var[0:1, :], in_=var[0:1, :],
                      func=AF.Sqrt, bias=sb["eps_col"][0:1, 0:1]), deps=[t3], name="bL_st")
        t5 = sch.emit("vector", lambda E: E.reciprocal(var[0:1, :], var[0:1, :]),
                      deps=[t4], name="bL_rc")
        bmu = emit_bcast(ctx, plmu, mu, [t1, t3], "bL_bm")
        t5b = sch.emit("vector", lambda E: E.tensor_copy(mu[0:1, :], var[0:1, :]),
                       deps=[t5] + bmu, name="bL_r0")
        brr = emit_bcast(ctx, plr, mu, [t5b], "bL_br")
        sch.stage()
        prev = []
        for dt in range(NDT):
            tl = sch.dma("sync", xt[:, :], x_fm[dt * 128:(dt + 1) * 128, col0:col0 + S],
                         deps=prev, name="bLx2")
            a1 = sch.emit("vector", lambda E: E.tensor_tensor(sqb[:, :], xt[:, :],
                          plmu[:, :], ALU.subtract), deps=[tl], name="bLa1")
            a2 = sch.emit("vector", lambda E: E.tensor_tensor(sqb[:, :], sqb[:, :],
                          plr[:, :], ALU.mult), deps=[a1], name="bLa2")
            sd = sch.dma("scalar", xn3_d[dt, :, :], sqb[:, :], deps=[a2], name="bLxn")
            cb = sch.emit("vector", lambda E, dt=dt: E.tensor_copy(
                xnb[:, dt * 2048:(dt + 1) * 2048], sqb[:, :]), deps=[a2], name="bLcb")
            prev = [tl, sd, cb]
        sch.stage()

        sch.set_phase(f"ab_I{b}")
        prev_mm = {0: [], 1: []}
        prev_ev = []
        prev_st = None
        for ft in range(32):
            k = ft % 2
            wl = []
            for dt in range(NDT):
                wl.append(sch.dma(qs[dt % 3], wt[k][:, dt * 128:(dt + 1) * 128],
                          wzi[ft, dt], deps=prev_mm[k] + wz_done, name="bI_w"))
            evs = []
            for c in range(4):
                mm = None
                for dt in range(NDT):
                    mm = sch.emit("tensor", lambda E, c=c, dt=dt, k=k: E.matmul(
                        [psA, psB, psC, psD][c][:, 0:512],
                        wt[k][:, dt * 128:(dt + 1) * 128],
                        xnb[:, dt * 2048 + c * 512: dt * 2048 + c * 512 + 512],
                        start=(dt == 0), stop=(dt == NDT - 1)),
                        deps=wl + prev_ev, name="bI_mm")
                ev = sch.emit("scalar", lambda E, c=c, ft=ft: E.activation(
                    out=xstg[:, c * 512:(c + 1) * 512], in_=[psA, psB, psC, psD][c][:, 0:512],
                    func=AF.Silu, bias=bcols[:, CBI + ft:CBI + ft + 1]),
                    deps=[mm, prev_st], name="bI_ev")
                evs.append(ev)
            dst = xs_d[ft, :, :] if ft < 16 else slr_d[ft - 16, :, :]
            st = sch.dma(qs[ft % 3], dst, xstg[:, :], deps=evs, name="bI_st")
            prev_mm[k] = [mm]
            prev_ev = evs
            prev_st = st
        sch.stage()

        sch.set_phase(f"ab_X{b}")
        ctx.reset_to(MARKB)
        wtbig = ctx.scr("b_wtbig", (128, 16384), BF16)
        xsc = [ctx.scr("b_xsc0", (128, 512), BF16), ctx.scr("b_xsc1", (128, 512), BF16)]
        passes = [list(range(0, 8)), list(range(8, 16)), [16]]
        prev_pass = []
        for pss in passes:
            wl = []
            for fi, ft in enumerate(pss):
                for dti in range(16):
                    wl.append(sch.dma(qs[(fi + dti) % 3],
                              wtbig[:, (fi * 16 + dti) * 128:(fi * 16 + dti) * 128 + 128],
                              wzx[ft, dti], deps=prev_pass + wz_done, name="bX_w"))
            evs_all = []
            prev_x = {0: None, 1: None}
            for c in range(4):
                mms = {}
                for dti in range(16):
                    k = dti % 2
                    xl = sch.dma("sync", xsc[k][:, :], xs_d[dti, :, c * 512:(c + 1) * 512],
                                 deps=[prev_x[k]], name="bX_x")
                    for fi, ft in enumerate(pss):
                        mms[fi] = sch.emit("tensor", lambda E, fi=fi, dti=dti, k=k: E.matmul(
                            banks8[fi][:, 0:512],
                            wtbig[:, (fi * 16 + dti) * 128:(fi * 16 + dti) * 128 + 128],
                            xsc[k][:, :], start=(dti == 0), stop=(dti == 15)),
                            deps=wl + [xl] + evs_all, name="bX_mm")
                    prev_x[k] = mms[len(pss) - 1]
                evs = []
                for fi, ft in enumerate(pss):
                    if ft == 0:
                        ev = sch.emit("scalar", lambda E, c=c, fi=fi: E.activation(
                            out=dbr[:, c * 512:(c + 1) * 512], in_=banks8[fi][:, 0:512],
                            func=AF.Identity, bias=bcols[:, CBX:CBX + 1]),
                            deps=[mms[fi]], name="bX_e0")
                        evs.append(ev)
                    else:
                        kk = fi % 2
                        ev = sch.emit("scalar", lambda E, fi=fi, ft=ft, kk=kk: E.activation(
                            out=cstg[kk][:, :], in_=banks8[fi][:, 0:512],
                            func=AF.Identity, bias=bcols[:, CBX + ft:CBX + ft + 1]),
                            deps=[mms[fi]] + ([evs[fi - 2]] if fi >= 2 else []), name="bX_ec")
                        st = sch.dma(qs[fi % 3], cse_d[ft - 1, :, c * 512:(c + 1) * 512],
                                     cstg[kk][:, :], deps=[ev], name="bX_sc")
                        evs.append(st)
                evs_all = evs
            prev_pass = evs_all
        cb = sch.emit("vector", lambda E: E.tensor_copy(b64[:, :], dbr[64:128, :]),
                      deps=prev_pass, name="bX_bs")
        bsd = sch.dma("sync", bsel_d[:, :], b64[:, :], deps=[cb], name="bX_bd")
        sch.stage()

        sch.set_phase(f"ab_D{b}")
        cbd = sch.emit("vector", lambda E: E.tensor_copy(b64[:, :], dbr[0:64, :]),
                       deps=[bsd], name="bD_c")
        prev_ev = {}
        prev_ln = None
        for ft in range(16):
            for c in range(4):
                k = (ft * 4 + c) % 2
                ps = [psA, psB][k]
                mm = sch.emit("tensor", lambda E, ft=ft, c=c, ps=ps: E.matmul(
                    ps[:, 0:512], wdt_sb[:, ft * 128:(ft + 1) * 128],
                    b64[:, c * 512:(c + 1) * 512], start=True, stop=True),
                    deps=[cbd, prev_ev.get(k)], name="bD_mm")
                ev1 = sch.emit("scalar", lambda E, ft=ft, ps=ps: E.activation(
                    out=dtmp[:, :], in_=ps[:, 0:512], func=AF.Exp,
                    bias=bcols[:, CBDT + ft:CBDT + ft + 1]),
                    deps=[mm, prev_ev.get(k), prev_ln], name="bD_e1")
                ev = sch.emit("scalar", lambda E, k=k: E.activation(
                    out=cstg[k][:, :], in_=dtmp[:, :], func=AF.Ln, bias=1.0),
                    deps=[ev1], name="bD_e2")
                prev_ln = ev
                st = sch.dma(qs[c % 3], dlt_d[ft, :, c * 512:(c + 1) * 512],
                             cstg[k][:, :], deps=[ev], name="bD_st")
                prev_ev[k] = st
        sch.stage()

        sch.set_phase(f"ab_S{b}")
        ctx.reset_to(MARKB)
        pdl = [ctx.scr("bS_dl0", (128, 2048), BF16), ctx.scr("bS_dl1", (128, 2048), BF16)]
        pxs = [ctx.scr("bS_xs0", (128, 2048), BF16), ctx.scr("bS_xs1", (128, 2048), BF16)]
        pcs = [ctx.scr("bS_cs0", (128, 2048), BF16), ctx.scr("bS_cs1", (128, 2048), BF16)]
        psl = [ctx.scr("bS_sl0", (128, 2048), BF16), ctx.scr("bS_sl1", (128, 2048), BF16)]
        dxp = [ctx.scr("bS_dx0", (128, 2048), BF16), ctx.scr("bS_dx1", (128, 2048), BF16)]
        btil = [ctx.scr("bS_bt0", (128, 2048), BF16), ctx.scr("bS_bt1", (128, 2048), BF16)]
        dat = [[ctx.scr(f"bS_da{dd}{k}", (128, 2048), BF16) for k in range(2)] for dd in range(2)]
        dxt = [[ctx.scr(f"bS_dxt{dd}{k}", (128, 2048), BF16) for k in range(2)] for dd in range(2)]
        hti = [[ctx.scr(f"bS_h{dd}{k}", (128, 2048), BF16) for k in range(2)] for dd in range(2)]
        ytm = ctx.scr("bS_yt", (128, 2048), BF16)
        y2s = [ctx.scr("bS_y20", (128, 2048), BF16), ctx.scr("bS_y21", (128, 2048), BF16)]
        accs = [[psA, psB, psC, psD], [psE[:, k * 512:(k + 1) * 512] for k in range(4)]]
        prev_y2 = {0: [], 1: []}
        prev_ld = []
        for pair in range(8):
            lds = {}
            for dd in range(2):
                dblk = pair * 2 + dd
                l1 = sch.dma("sync", pdl[dd][:, :], dlt_d[dblk, :, :], deps=prev_ld, name="bS_l1")
                l2 = sch.dma("scalar", pxs[dd][:, :], xs_d[dblk, :, :], deps=prev_ld, name="bS_l2")
                l3 = sch.dma("gpsimd", pcs[dd][:, :], cse_d[dblk, :, :], deps=prev_ld, name="bS_l3")
                l4 = sch.dma("sync", psl[dd][:, :], slr_d[dblk, :, :], deps=prev_ld, name="bS_l4")
                dx = sch.emit("vector", lambda E, dd=dd: E.tensor_tensor(
                    dxp[dd][:, :], pdl[dd][:, :], pxs[dd][:, :], ALU.mult),
                    deps=[l1, l2], name="bS_dx")
                lds[dd] = [l1, l2, l3, l4, dx]
            prev_pe = {0: [], 1: []}
            prev_tts = {0: None, 1: None}
            prev_dxt = {}
            prev_bt = {}
            for n in range(64):
                k = n % 2
                bt0 = sch.dma("sync", btil[k][0:64, :],
                              bsel_d[n:n + 1, :].to_broadcast((64, 2048)),
                              deps=[bsd, prev_dxt.get((1, k))], name="bS_bt",
                              join=False)
                bt = sch.dma("gpsimd", btil[k][64:128, :],
                             bsel_d[n:n + 1, :].to_broadcast((64, 2048)),
                             deps=[bsd, prev_dxt.get((1, k))], name="bS_bt2",
                             join=False)
                prev_bt[k] = bt
                prev_bt[(k, 0)] = bt0
                for dd in range(2):
                    dblk = pair * 2 + dd
                    da = sch.emit("scalar", lambda E, dd=dd, k=k, dblk=dblk, n=n: E.activation(
                        out=dat[dd][k][:, :], in_=pdl[dd][:, :], func=AF.Exp,
                        scale=aA[:, dblk * 64 + n: dblk * 64 + n + 1]),
                        deps=lds[dd] + [prev_tts[dd]], name="bS_da")
                    dxe = sch.emit("gpsimd", lambda E, dd=dd, k=k: E.tensor_tensor(
                        dxt[dd][k][:, :], dxp[dd][:, :], btil[k][:, :], ALU.mult),
                        deps=[bt0, bt, lds[dd][4], prev_tts[dd]], name="bS_dxe")
                    prev_dxt[(dd, k)] = dxe
                    tts = sch.emit("vector", lambda E, dd=dd, k=k: E.tensor_tensor_scan(
                        hti[dd][k][:, :], dat[dd][k][:, :], dxt[dd][k][:, :], 0.0,
                        ALU.mult, ALU.add), deps=[da, dxe] + prev_pe[dd], name="bS_tts")
                    prev_tts[dd] = tts
                    pes = []
                    for c in range(4):
                        pes.append(sch.emit("tensor", lambda E, dd=dd, k=k, c=c, n=n: E.matmul(
                            accs[dd][c][:, 0:512], sb["ident128b"][:, :],
                            hti[dd][k][:, c * 512:(c + 1) * 512],
                            start=(n == 0), stop=(n == 63)),
                            deps=[tts] + prev_y2[dd], name="bS_pe"))
                    prev_pe[dd] = pes
            for dd in range(2):
                dblk = pair * 2 + dd
                w1s = []
                w3 = None
                for c in range(4):
                    w1 = sch.emit("vector", lambda E, dd=dd, c=c: E.tensor_tensor(
                        ytm[:, c * 512:(c + 1) * 512], accs[dd][c][:, 0:512],
                        pcs[dd][:, c * 512:(c + 1) * 512], ALU.mult),
                        deps=prev_pe[dd] + [lds[dd][2]] + prev_y2[dd], name="bS_w1")
                    w2 = sch.emit("vector", lambda E, dd=dd, c=c, dblk=dblk: E.scalar_tensor_tensor(
                        ytm[:, c * 512:(c + 1) * 512], pxs[dd][:, c * 512:(c + 1) * 512],
                        bcols[:, CDP + dblk:CDP + dblk + 1],
                        ytm[:, c * 512:(c + 1) * 512], ALU.mult, ALU.add),
                        deps=[w1], name="bS_w2")
                    w3 = sch.emit("vector", lambda E, dd=dd, c=c: E.tensor_tensor(
                        y2s[dd][:, c * 512:(c + 1) * 512], ytm[:, c * 512:(c + 1) * 512],
                        psl[dd][:, c * 512:(c + 1) * 512], ALU.mult),
                        deps=[w2, lds[dd][3]], name="bS_w3")
                    w1s.append(w1)
                    w1s.append(w3)
                st = sch.dma(qs[dd], y2_d[dblk, :, :], y2s[dd][:, :], deps=[w3] + w1s,
                             name="bS_y2d")
                prev_y2[dd] = w1s + [st]
            prev_ld = [prev_y2[0][-1], prev_y2[1][-1], prev_tts[0], prev_tts[1],
                       prev_bt[0], prev_bt[1]]
        sch.stage()

        sch.set_phase(f"ab_O{b}")
        ctx.reset_to(MARKB)
        wob = ctx.scr("bO_w", (128, 16384), BF16)
        ysc = [ctx.scr("bO_y0", (128, 512), BF16), ctx.scr("bO_y1", (128, 512), BF16)]
        xnt = [ctx.scr("bO_xn0", (128, 512), F32), ctx.scr("bO_xn1", (128, 512), F32)]
        zpl = ctx.scr("bO_z", (128, 8 * 2048), F32)
        wl = []
        for ft in range(8):
            for dblk in range(16):
                wl.append(sch.dma(qs[(ft + dblk) % 3],
                          wob[:, (ft * 16 + dblk) * 128:(ft * 16 + dblk) * 128 + 128],
                          wzo[ft, dblk], deps=wz_done, name="bO_w"))
        prev_ev = []
        prev_yl = {0: None, 1: None}
        prev_xl = {0: None, 1: None}
        for c in range(4):
            mms = {}
            for dblk in range(16):
                k = dblk % 2
                yl = sch.dma("sync", ysc[k][:, :], y2_d[dblk, :, c * 512:(c + 1) * 512],
                             deps=[prev_yl[k]], name="bO_yl")
                for ft in range(8):
                    mms[ft] = sch.emit("tensor", lambda E, ft=ft, dblk=dblk, k=k: E.matmul(
                        banks8[ft][:, 0:512],
                        wob[:, (ft * 16 + dblk) * 128:(ft * 16 + dblk) * 128 + 128],
                        ysc[k][:, :], start=(dblk == 0), stop=(dblk == 15)),
                        deps=wl + [yl] + prev_ev, name="bO_mm")
                prev_yl[k] = mms[7]
            evs = []
            for ft in range(8):
                k = ft % 2
                xl = sch.dma("scalar", xnt[k][:, :], xn3_d[ft, :, c * 512:(c + 1) * 512],
                             deps=[prev_xl[k]], name="bO_xl")
                e1 = sch.emit("scalar", lambda E, ft=ft, c=c: E.activation(
                    out=zpl[:, ft * 2048 + c * 512: ft * 2048 + c * 512 + 512],
                    in_=banks8[ft][:, 0:512], func=AF.Identity,
                    bias=bcols[:, CBO + ft:CBO + ft + 1]), deps=[mms[ft]], name="bO_e1")
                e2 = sch.emit("vector", lambda E, ft=ft, c=c, k=k: E.tensor_tensor(
                    zpl[:, ft * 2048 + c * 512: ft * 2048 + c * 512 + 512],
                    zpl[:, ft * 2048 + c * 512: ft * 2048 + c * 512 + 512],
                    xnt[k][:, :], ALU.add), deps=[e1, xl], name="bO_e2")
                prev_xl[k] = e2
                evs.append(e2)
            prev_ev = evs
        sch.stage()

        sch.set_phase(f"ab_F{b}")
        stoks, _ = emit_stats(ctx, lambda dt: zpl[:, dt * 2048:(dt + 1) * 2048],
                              prev_ev, "bF", sq_buf=sqb)
        bmu = emit_bcast(ctx, plmu, sb["mu_row"], stoks, "bF_bm")
        t5b = sch.emit("vector", lambda E: E.tensor_copy(sb["mu_row"][0:1, :],
                       sb["var_row"][0:1, :]), deps=stoks + bmu, name="bF_r0")
        brr = emit_bcast(ctx, plr, sb["mu_row"], [t5b], "bF_br")
        sch.stage()
        prev = []
        for dt in range(NDT):
            tl = sch.dma("sync", xt[:, :], x_fm[dt * 128:(dt + 1) * 128, col0:col0 + S],
                         deps=prev, name="bF_x")
            f1 = sch.emit("vector", lambda E, dt=dt: E.tensor_tensor(
                sqb[:, :], zpl[:, dt * 2048:(dt + 1) * 2048], plmu[:, :], ALU.subtract),
                deps=[tl], name="bF_f1")
            f2 = sch.emit("vector", lambda E: E.tensor_tensor(sqb[:, :], sqb[:, :],
                          plr[:, :], ALU.mult), deps=[f1], name="bF_f2")
            f3 = sch.emit("vector", lambda E, dt=dt: E.scalar_tensor_tensor(
                sqb[:, :], sqb[:, :], bcols[:, CGAM + dt:CGAM + dt + 1], xt[:, :],
                ALU.mult, ALU.add), deps=[f2], name="bF_f3")
            f4 = sch.emit("scalar", lambda E, dt=dt: E.activation(
                out=xt[:, :], in_=sqb[:, :], func=AF.Identity,
                bias=bcols[:, CBET + dt:CBET + dt + 1]), deps=[f3], name="bF_f4")
            st = sch.dma(qs[dt % 3], x4_fm[dt * 128:(dt + 1) * 128, col0:col0 + S],
                         xt[:, :], deps=[f4], name="bF_st")
            prev = [f1, st, f4]
        sch.stage()
    return {}



def _build_program(phases="am0bm1"):
    import concourse.bass as bass
    nc = bass.Bass("TRN2", target_bir_lowering=False)
    sch = Sched()
    ctx = Ctx(nc, sch)
    x_fm = ctx.dram_in("x_fm", (1024, T), F32)
    x4_fm = ctx.dram_out("x4_fm", (1024, T), F32)
    x1_fm = ctx.dram_tmp("x1_fm", (1024, T), F32)
    x2_fm = ctx.dram_tmp("x2_fm", (1024, T), F32)
    x3_fm = ctx.dram_tmp("x3_fm", (1024, T), F32)
    xn1b_dram = ctx.dram_tmp("xn1b_dram", (1024, T), BF16)
    z_dram = ctx.dram_tmp("z_dram", (1024, T), F32)
    for nme, arr in host_consts().items():
        ctx.dram_in(f"c_{nme}", arr.shape, F32)
    aa_shapes = {
        "L0_attn_a_ln_w": (1024,), "L0_attn_a_ln_b": (1024,),
        "L0_attn_a_sign_W": (1024, 32), "L0_attn_a_sign_b": (32,),
        "L0_attn_a_perm_W": (1024, 256), "L0_attn_a_perm_b": (256,),
        "L0_attn_a_gate_W": (1024, 32), "L0_attn_a_gate_b": (32,),
        "L0_attn_a_in_W": (1024, 32), "L0_attn_a_in_b": (32,),
        "L0_attn_a_out_W": (32, 1024), "L0_attn_a_out_b": (1024,),
    }
    for nm, shp in aa_shapes.items():
        ctx.dram_in(nm, shp, F32)
    ctx.dram_in("gate_wT", (1024, 64), F32)
    ctx.dram_in("L0_ffn_gate_b", (64,), F32)
    ctx.dram_in("ws1T", (1024, 1408), F32)
    ctx.dram_in("ws2T", (1408, 1024), F32)
    ctx.dram_in("ws3T", (1024, 1408), F32)
    ctx.dram_in("w1T_my", (8, 1024, 704), F32)
    ctx.dram_in("w2T_my", (8, 704, 1024), F32)
    ctx.dram_in("w3T_my", (8, 1024, 704), F32)
    ctx.dram_in("tokid", (128, 32), F32)
    ctx.dram_in("rankmask", (8, 1), F32)
    ctx.dram["myslots"] = nc.dram_tensor("myslots", [128, 32], U32,
                                         kind="ExternalInput").ap()
    ctx.dram_in("b_Wi", (1024, 4096), F32)
    ctx.dram_in("b_bi_c", (128, 32), F32)
    ctx.dram_in("b_Wx", (2048, 2176), F32)
    ctx.dram_in("b_bx_c", (128, 17), F32)
    ctx.dram_in("b_Wdt", (64, 2048), F32)
    ctx.dram_in("b_bdt_c", (128, 16), F32)
    ctx.dram_in("b_Alog_r", (128, 1024), F32)
    ctx.dram_in("b_Dp_c", (128, 16), F32)
    ctx.dram_in("b_Wo", (2048, 1024), F32)
    ctx.dram_in("b_bo_c", (128, 8), F32)
    ctx.dram_in("b_gam_c", (128, 8), F32)
    ctx.dram_in("b_bet_c", (128, 8), F32)

    alloc_common(ctx)
    ctx.init_arena(52500)
    alloc_stats_top(ctx)
    TOPM = ctx.top_mark()
    alloc_attn_a_body(ctx)
    load_consts(ctx)
    sch.stage()
    cur = x_fm
    if "a" in phases.replace("m0", "").replace("m1", "").replace("b", "") or phases.startswith("a"):
        emit_attn_a(ctx, cur, x1_fm, xn1b_dram, deps=[])
        cur = x1_fm
        sch.stage()
    if "m0" in phases:
        ctx.scratch_reset()
        alloc_moe(ctx)
        emit_moe(ctx, cur, x2_fm, z_dram, 0, deps=[])
        cur = x2_fm
        ctx.top_reset(TOPM)
        sch.stage()
    if "b" in phases.replace("m0", "").replace("m1", ""):
        emit_attn_b(ctx, cur, x3_fm, deps=[])
        cur = x3_fm
        sch.stage()
    if "m1" in phases:
        ctx.scratch_reset()
        alloc_moe(ctx)
        emit_moe(ctx, cur, x4_fm, z_dram, 1, deps=[])
        cur = x4_fm
        sch.stage()
    if cur is not x4_fm:
        ctx.scratch_reset()
        cp = ctx.scr("fin_cp", (128, 2048), F32)
        prev = []
        for dt in range(NDT):
            for cb2 in range(2):
                tl = sch.dma("sync", cp[:, :], cur[dt * 128:(dt + 1) * 128, cb2 * 2048:(cb2 + 1) * 2048], deps=prev, name="fin_l")
                st = sch.dma("scalar", x4_fm[dt * 128:(dt + 1) * 128, cb2 * 2048:(cb2 + 1) * 2048], cp[:, :], deps=[tl], name="fin_s")
                prev = [st]
    sch.stage()
    section, sem_c, sem_d = sch.build(nc)
    with nc.Block() as block:
        block.sync(section("sync"))
        block.vector(section("vector"))
        block.scalar(section("scalar"))
        block.tensor(section("tensor"))
        block.gpsimd(section("gpsimd"))
    return nc, ctx


def _make_in_maps(inp):
    X = inp["L0_attn_a_x"].astype(np.float32)
    consts = host_consts()
    in_maps = []
    for c in range(8):
        im = {"x_fm": np.ascontiguousarray(
            np.concatenate([X[2 * c].T, X[2 * c + 1].T], axis=1))}
        for nme, arr in consts.items():
            im[f"c_{nme}"] = arr
        for k in list(inp.keys()):
            if k.startswith("L0_attn_a") and k != "L0_attn_a_x":
                im[k] = inp[k]
        im["gate_wT"] = np.ascontiguousarray(inp["L0_ffn_gate_w"].T)
        im["L0_ffn_gate_b"] = inp["L0_ffn_gate_b"]
        im["ws1T"] = np.ascontiguousarray(inp["L0_ffn_ws1"].T)
        im["ws2T"] = np.ascontiguousarray(inp["L0_ffn_ws2"].T)
        im["ws3T"] = np.ascontiguousarray(inp["L0_ffn_ws3"].T)
        im["w1T_my"] = np.ascontiguousarray(inp["L0_ffn_w1"][8 * c:8 * c + 8].transpose(0, 2, 1))
        im["w2T_my"] = np.ascontiguousarray(inp["L0_ffn_w2"][8 * c:8 * c + 8].transpose(0, 2, 1))
        im["w3T_my"] = np.ascontiguousarray(inp["L0_ffn_w3"][8 * c:8 * c + 8].transpose(0, 2, 1))
        tokid = np.zeros((128, 32), np.float32)
        for b in range(2):
            for tt in range(16):
                tokid[:, b * 16 + tt] = c * 4096 + b * 2048 + tt * 128 + np.arange(128) + 1
        im["tokid"] = tokid
        im["rankmask"] = (np.arange(8) < c).astype(np.float32).reshape(8, 1)
        mys = np.zeros((128, 32), np.uint32)
        for g in range(32):
            mys[:, g] = c * 4096 + g * 128 + np.arange(128)
        im["myslots"] = mys
        im["b_Wi"] = inp["L0_attn_b_Wi"]
        im["b_bi_c"] = np.ascontiguousarray(inp["L0_attn_b_bi"].reshape(32, 128).T)
        im["b_Wx"] = inp["L0_attn_b_Wx"]
        im["b_bx_c"] = np.ascontiguousarray(inp["L0_attn_b_bx"].reshape(17, 128).T)
        im["b_Wdt"] = inp["L0_attn_b_Wdt"]
        im["b_bdt_c"] = np.ascontiguousarray(inp["L0_attn_b_bdt"].reshape(16, 128).T)
        im["b_Alog_r"] = np.ascontiguousarray(
            inp["L0_attn_b_A_log"].reshape(16, 128, 64).transpose(1, 0, 2).reshape(128, 1024))
        im["b_Dp_c"] = np.ascontiguousarray(inp["L0_attn_b_Dp"].reshape(16, 128).T)
        im["b_Wo"] = inp["L0_attn_b_Wo"]
        im["b_bo_c"] = np.ascontiguousarray(inp["L0_attn_b_bo"].reshape(8, 128).T)
        im["b_gam_c"] = np.ascontiguousarray(inp["L0_attn_b_gamma"].reshape(8, 128).T)
        im["b_bet_c"] = np.ascontiguousarray(inp["L0_attn_b_beta"].reshape(8, 128).T)
        in_maps.append(im)
    return in_maps


def kernel(**inputs):
    from concourse.bass_utils import run_bass_kernel_spmd
    inp = {k: np.asarray(v) for k, v in inputs.items()}
    nc, ctx = _build_program()
    in_maps = _make_in_maps(inp)
    res = run_bass_kernel_spmd(nc, in_maps, core_ids=list(range(8)))
    out = np.empty((16, 2048, 1024), np.float32)
    for c in range(8):
        pl = res.results[c]["x4_fm"]
        out[2 * c] = pl[:, 0:2048].T
        out[2 * c + 1] = pl[:, 2048:4096].T
    return out
